# version 01
# baseline (speedup 1.0000x reference)
import contextlib
import numpy as np
import ml_dtypes
import concourse.bass as bass
import concourse.mybir as mybir
from concourse.bass_utils import run_bass_kernel_spmd
from concourse.bass_core_rust import IndirectOffsetOnAxis

F32 = mybir.dt.float32
BF16 = mybir.dt.bfloat16
I32 = mybir.dt.int32
AF = mybir.ActivationFunctionType
ALU = mybir.AluOpType
AXX = mybir.AxisListType.X

NCORE = 8
D = 1024
NCH = 8
TLOC = 4096
NTOK = 32768
TT = 512
NTT = TLOC // TT
BF16NP = ml_dtypes.bfloat16
PI = float(np.pi)


class Sem:
    def __init__(self, ctx, nc, name):
        self.h = ctx.enter_context(nc.semaphore(name))
        self.v = 0

    def inc(self, n=1):
        self.v += n
        return self.v


def tinc(instr, sem, n=1):
    instr.then_inc(sem.h, n)
    sem.inc(n)
    return sem.v


def ch(instr, sem, eng):
    instr.then_inc(sem.h, 1)
    sem.inc(1)
    eng.wait_ge(sem.h, sem.v)


def poll_flag(gpsimd, sh_flag_row, flag_s, tagp_s, psem):
    with gpsimd.register(None) as cnt:
        gpsimd.reg_mov(cnt, 0)

        def cond_fn():
            gpsimd.dma_start(out=flag_s[:], in_=sh_flag_row).then_inc(psem.h, 16)
            gpsimd.reg_add(cnt, cnt, 16)
            gpsimd.wait_ge(psem.h, cnt)
            val = gpsimd.value_load(flag_s[0:1, 0:1])
            tv = gpsimd.value_load(tagp_s[0:1, 0:1])
            return val != tv

        with gpsimd.While(cond_fn):
            pass



def build_A(detect=True):
    nc = bass.Bass("TRN2", target_bir_lowering=False, debug=False,
                   detect_race_conditions=detect, dynamic_dma_scratch_size=4096)
    x_all = nc.dram_tensor("x_all", [NTOK, D], F32, kind="ExternalInput").ap()
    onesc = nc.dram_tensor("onesc", [128, 1], F32, kind="ExternalInput").ap()
    ident = nc.dram_tensor("ident", [128, 128], F32, kind="ExternalInput").ap()
    x1_tm = nc.dram_tensor("x1_tm", [TLOC, D], F32, kind="ExternalOutput").ap()

    sin2_dram = nc.dram_tensor("sin2_dram", [8192, TLOC], BF16, kind="Internal").ap()

    NT_ALL = NTOK // 128
    NT_B = 8192 // 128
    NG = TLOC // 256

    with contextlib.ExitStack() as ctx:
        E = ctx.enter_context
        xt = [E(nc.sbuf_tensor(f"xt{i}", [128, D], F32)) for i in range(2)]
        sq = E(nc.sbuf_tensor("sq", [128, D], F32))
        lnf = E(nc.sbuf_tensor("lnf", [128, D], F32))
        vt = E(nc.sbuf_tensor("vt", [128, 512], F32))
        stat = [E(nc.sbuf_tensor(f"stat{i}", [128, 8], F32)) for i in range(2)]
        epsb = E(nc.sbuf_tensor("epsb", [128, 1], F32))
        ones_s = E(nc.sbuf_tensor("ones_s", [128, 1], F32))
        ones_b = E(nc.sbuf_tensor("ones_b", [128, 1], BF16))
        one1 = E(nc.sbuf_tensor("one1", [1, 1], F32))
        ident_s = E(nc.sbuf_tensor("ident_s", [128, 128], F32))
        lnb = E(nc.sbuf_tensor("lnb", [128, NT_B * D], BF16))
        m_tm = E(nc.sbuf_tensor("m_tm", [128, 64 * 64], F32))
        m_fm = E(nc.sbuf_tensor("m_fm", [64, 8192], F32))
        el2 = E(nc.sbuf_tensor("el2", [128, 2048], F32))
        sin2b = E(nc.sbuf_tensor("sin2b", [128, 2048], BF16))
        pbuf = E(nc.sbuf_tensor("pbuf", [128, 256], BF16))
        ptile = E(nc.sbuf_tensor("ptile", [128, 256], BF16))
        zrow = E(nc.sbuf_tensor("zrow", [1, 256], F32))
        zrec = E(nc.sbuf_tensor("zrec", [128, 1], F32))
        outs = E(nc.sbuf_tensor("outs", [128, D], F32))
        xres = E(nc.sbuf_tensor("xres", [128, D], F32))

        ps_log = E(nc.psum_tensor("ps_log", [128, 512], F32))
        ps_z = E(nc.psum_tensor("ps_z", [1, 256], F32))
        ps_o = [E(nc.psum_tensor(f"ps_o{i}", [128, 512], F32)) for i in range(4)]
        ps_t = E(nc.psum_tensor("ps_t", [128, 128], F32))

        vch = Sem(ctx, nc, "vch")
        sch = Sem(ctx, nc, "sch")
        ldc = Sem(ctx, nc, "ldc")
        ldx = [Sem(ctx, nc, f"ldx{i}") for i in range(2)]
        lds = [Sem(ctx, nc, f"lds{i}") for i in range(2)]
        ldr = Sem(ctx, nc, "ldr")
        sts = Sem(ctx, nc, "sts")
        stx = Sem(ctx, nc, "stx")
        mm = Sem(ctx, nc, "mm")
        ve = Sem(ctx, nc, "ve")
        se = Sem(ctx, nc, "se")

        block = E(nc.Block())

        @block.vector
        def _(vector):
            vector.memset(epsb[:], 1e-5)
            vector.memset(ones_b[:], 1.0)
            tinc(vector.memset(m_tm[:], 0.0), ve, 1)
        ve_init = ve.v

        @block.sync
        def _(sync):
            tinc(sync.dma_start(out=ones_s[:], in_=onesc), ldc, 16)
            tinc(sync.dma_start(out=one1[:], in_=onesc[0:1, 0:1]), ldc, 16)
            tinc(sync.dma_start(out=ident_s[:], in_=ident), ldc, 16)
            sync.wait_ge(ldc.h, ldc.v)

        tile_done = []
        se_tile = []
        for i in range(NT_ALL):
            b = i % 2

            @block.sync
            def _(sync, i=i, b=b):
                if i >= 2:
                    sync.wait_ge(ve.h, tile_done[i - 2])
                    sync.wait_ge(se.h, se_tile[i - 2])
                tinc(sync.dma_start(out=xt[b][:], in_=x_all[i * 128:(i + 1) * 128, :]), ldx[b], 16)
            ld_i = ldx[b].v

            @block.scalar
            def _(scalar, i=i, b=b, ld_i=ld_i):
                scalar.wait_ge(ldx[b].h, ld_i)
                if i >= 1:
                    scalar.wait_ge(sch.h, sch.v)
                if i >= 2:
                    scalar.wait_ge(ve.h, tile_done[i - 2])
                ch(scalar.activation(sq[:], xt[b][:], AF.Copy, accum_out=stat[b][:, 0:1]), sch, scalar)
                tinc(scalar.activation(sq[:], xt[b][:], AF.Square, accum_out=stat[b][:, 1:2]), se, 1)
            se_sums = se.v

            @block.vector
            def _(vector, i=i, b=b, se_sums=se_sums):
                vector.wait_ge(se.h, se_sums)
                ch(vector.tensor_scalar(stat[b][:, 2:3], stat[b][:, 0:1], 1.0 / D, None, ALU.mult), vch, vector)
                ch(vector.tensor_tensor(stat[b][:, 3:4], stat[b][:, 2:3], stat[b][:, 2:3], ALU.mult), vch, vector)
                ch(vector.tensor_scalar(stat[b][:, 5:6], stat[b][:, 1:2], 1.0 / D, None, ALU.mult), vch, vector)
                tinc(vector.tensor_tensor(stat[b][:, 3:4], stat[b][:, 5:6], stat[b][:, 3:4], ALU.subtract), ve, 1)
            ve_var = ve.v

            @block.scalar
            def _(scalar, i=i, b=b, ve_var=ve_var):
                scalar.wait_ge(ve.h, ve_var)
                tinc(scalar.activation(stat[b][:, 4:5], stat[b][:, 3:4], AF.Sqrt, bias=epsb[:, 0:1]), se, 1)
            se_tile.append(se.v)
            se_sqrt = se.v

            @block.vector
            def _(vector, i=i, b=b, se_sqrt=se_sqrt):
                vector.wait_ge(se.h, se_sqrt)
                ch(vector.reciprocal(stat[b][:, 4:5], stat[b][:, 4:5]), vch, vector)
                ch(vector.tensor_scalar(lnf[:], xt[b][:], stat[b][:, 2:3], stat[b][:, 4:5],
                                        ALU.subtract, ALU.mult), vch, vector)
                if i < NT_B:
                    ch(vector.tensor_copy(lnb[:, i * D:(i + 1) * D], lnf[:]), vch, vector)
                ch(vector.tensor_tensor(vt[:, 0:512], lnf[:, 0:512], lnf[:, 512:1024], ALU.add), vch, vector)
                ch(vector.tensor_tensor(vt[:, 0:256], vt[:, 0:256], vt[:, 256:512], ALU.add), vch, vector)
                ch(vector.tensor_tensor(vt[:, 0:128], vt[:, 0:128], vt[:, 128:256], ALU.add), vch, vector)
                ch(vector.tensor_tensor(vt[:, 0:64], vt[:, 0:64], vt[:, 64:128], ALU.add), vch, vector)
                pt = i % 64
                tinc(vector.tensor_tensor(m_tm[:, pt * 64:(pt + 1) * 64], m_tm[:, pt * 64:(pt + 1) * 64],
                                          vt[:, 0:64], ALU.add), ve, 1)
            tile_done.append(ve.v)

        @block.vector
        def _(vector):
            vector.wait_ge(ve.h, ve.v)
            tinc(vector.tensor_scalar(m_tm[:], m_tm[:], 1.0 / 64.0, None, ALU.mult), ve, 1)
        ve_ms = ve.v

        for p in range(64):
            @block.tensor
            def _(tensor, p=p):
                tensor.wait_ge(ve.h, ve_ms + p)
                tensor.wait_ge(ldc.h, ldc.v)
                tinc(tensor.transpose(ps_t[0:64, :], m_tm[:, p * 64:(p + 1) * 64], ident_s[:]), mm, 1)
            mm_t = mm.v

            @block.vector
            def _(vector, p=p, mm_t=mm_t):
                vector.wait_ge(mm.h, mm_t)
                tinc(vector.tensor_copy(m_fm[:, p * 128:(p + 1) * 128], ps_t[0:64, :]), ve, 1)
        ve_mfm = ve.v

        se_s2_prev = None
        last_clip = None
        for j in range(NT_B):
            for hlf in range(2):
                clip_done = []
                for q in range(4):
                    @block.tensor
                    def _(tensor, j=j, hlf=hlf, q=q, last_clip=last_clip):
                        if j == 0 and hlf == 0 and q == 0:
                            tensor.wait_ge(ve.h, ve_mfm)
                        gate = clip_done[-1] if clip_done else last_clip
                        if gate is not None:
                            tensor.wait_ge(ve.h, gate)
                        tinc(tensor.matmul(ps_log[:], m_fm[:, j * 128:(j + 1) * 128],
                                           m_fm[:, hlf * 2048 + q * 512: hlf * 2048 + (q + 1) * 512],
                                           start=True, stop=True), mm, 1)
                    mm_l = mm.v

                    @block.vector
                    def _(vector, j=j, hlf=hlf, q=q, mm_l=mm_l, prev=se_s2_prev):
                        vector.wait_ge(mm.h, mm_l)
                        if prev is not None:
                            vector.wait_ge(se.h, prev)
                        tinc(vector.tensor_scalar(el2[:, q * 512:(q + 1) * 512], ps_log[:],
                                                  PI, -PI, ALU.min, ALU.max), ve, 1)
                    clip_done.append(ve.v)
                last_clip = clip_done[-1]

                @block.scalar
                def _(scalar, j=j, hlf=hlf, clip_done=tuple(clip_done)):
                    scalar.wait_ge(ve.h, clip_done[-1])
                    if j + hlf > 0:
                        scalar.wait_ge(sts.h, sts.v)
                    ch(scalar.activation(el2[:], el2[:], AF.Sin, scale=0.5), sch, scalar)
                    tinc(scalar.activation(sin2b[:], el2[:], AF.Square), se, 1)
                se_s2_prev = se.v
                se_s2 = se.v

                @block.sync
                def _(sync, j=j, hlf=hlf, se_s2=se_s2):
                    sync.wait_ge(se.h, se_s2)
                    tinc(sync.dma_start(out=sin2_dram[j * 128:(j + 1) * 128, hlf * 2048:(hlf + 1) * 2048],
                                        in_=sin2b[:]), sts, 16)
        sts_all = sts.v

        for g in range(NG):
            mm_j = []
            for j in range(NT_B):
                sb = j % 2

                @block.sync
                def _(sync, g=g, j=j, sb=sb, mm_j=mm_j):
                    if g == 0 and j == 0:
                        sync.wait_ge(sts.h, sts_all)
                    if j >= 2:
                        sync.wait_ge(mm.h, mm_j[j - 2])
                    elif g > 0 and j < 2:
                        sync.wait_ge(mm.h, mm_j_prev[NT_B - 2 + j])
                    tinc(sync.dma_start(out=(ptile if sb else pbuf)[:],
                                        in_=sin2_dram[j * 128:(j + 1) * 128, g * 256:(g + 1) * 256]),
                         lds[sb], 16)
                ld_j = lds[sb].v

                @block.scalar
                def _(scalar, g=g, j=j, sb=sb, ld_j=ld_j):
                    scalar.wait_ge(lds[sb].h, ld_j)
                    tinc(scalar.activation((ptile if sb else pbuf)[:],
                                           (ptile if sb else pbuf)[:], AF.Exp), se, 1)
                se_e = se.v

                @block.tensor
                def _(tensor, g=g, j=j, sb=sb, se_e=se_e):
                    tensor.wait_ge(se.h, se_e)
                    src_ = ptile if sb else pbuf
                    tensor.matmul(ps_z[:], ones_b[:], src_[:], start=(j == 0), stop=(j == NT_B - 1))
                    for s in range(2):
                        for hh in range(2):
                            tinc(tensor.matmul(ps_o[s * 2 + hh][:],
                                               src_[:, s * 128:(s + 1) * 128],
                                               lnb[:, j * D + hh * 512: j * D + (hh + 1) * 512],
                                               start=(j == 0), stop=(j == NT_B - 1)), mm, 1)
                mm_j.append(mm.v)
            mm_j_prev = mm_j
            mm_grp = mm.v

            @block.vector
            def _(vector, g=g, mm_grp=mm_grp):
                vector.wait_ge(mm.h, mm_grp)
                tinc(vector.tensor_copy(zrow[:], ps_z[:]), ve, 1)
            ve_z = ve.v

            for s in range(2):
                @block.tensor
                def _(tensor, g=g, s=s, ve_z=ve_z):
                    tensor.wait_ge(ve.h, ve.v)
                    tensor.wait_ge(ve.h, ve_z)
                    tinc(tensor.matmul(ps_t[:, 0:1], zrow[0:1, s * 128:(s + 1) * 128], one1[:],
                                       start=True, stop=True), mm, 1)
                mm_zc = mm.v

                @block.sync
                def _(sync, g=g, s=s):
                    sync.wait_ge(stx.h, stx.v)
                    r0 = g * 256 + s * 128
                    tinc(sync.dma_start(out=xres[:], in_=x_all[r0:r0 + 128, :]), ldr, 16)
                ld_r = ldr.v

                @block.vector
                def _(vector, g=g, s=s, mm_zc=mm_zc, ld_r=ld_r):
                    vector.wait_ge(mm.h, mm_zc)
                    vector.wait_ge(stx.h, stx.v)
                    ch(vector.tensor_copy(zrec[:], ps_t[:, 0:1]), vch, vector)
                    ch(vector.reciprocal(zrec[:], zrec[:]), vch, vector)
                    for hh in range(2):
                        ch(vector.tensor_scalar(outs[:, hh * 512:(hh + 1) * 512], ps_o[s * 2 + hh][:],
                                                zrec[:, 0:1], None, ALU.mult), vch, vector)
                    vector.wait_ge(ldr.h, ld_r)
                    tinc(vector.tensor_tensor(outs[:], outs[:], xres[:], ALU.add), ve, 1)
                ve_r = ve.v

                @block.sync
                def _(sync, g=g, s=s, ve_r=ve_r):
                    sync.wait_ge(ve.h, ve_r)
                    r0 = g * 256 + s * 128
                    tinc(sync.dma_start(out=x1_tm[r0:r0 + 128, :], in_=outs[:]), stx, 16)

        @block.sync
        def _(sync):
            sync.wait_ge(stx.h, stx.v)

    return nc



def build_B1(detect=True):
    nc = bass.Bass("TRN2", target_bir_lowering=False, debug=False,
                   detect_race_conditions=detect)
    x1 = nc.dram_tensor("x1", [TLOC, D], F32, kind="ExternalInput").ap()
    rw = nc.dram_tensor("rw", [D, 16], F32, kind="ExternalInput").ap()
    rb = nc.dram_tensor("rb", [1, 16], F32, kind="ExternalInput").ap()
    onesc = nc.dram_tensor("onesc", [128, 1], F32, kind="ExternalInput").ap()
    iota16 = nc.dram_tensor("iota16", [128, 16], F32, kind="ExternalInput").ap()
    ln_tm = nc.dram_tensor("ln_tm", [TLOC, D], BF16, kind="ExternalOutput").ap()
    route = nc.dram_tensor("route", [TLOC, 4], F32, kind="ExternalOutput").ap()
    counts = nc.dram_tensor("counts", [1, 16], F32, kind="ExternalOutput").ap()

    NTI = TLOC // 128

    with contextlib.ExitStack() as ctx:
        E = ctx.enter_context
        xb = [E(nc.sbuf_tensor(f"xb{i}", [128, D], F32)) for i in range(2)]
        sq = E(nc.sbuf_tensor("sq", [128, D], F32))
        lnf = E(nc.sbuf_tensor("lnf", [128, D], F32))
        lnt = [E(nc.sbuf_tensor(f"lnt{i}", [128, D], BF16)) for i in range(2)]
        stat = [E(nc.sbuf_tensor(f"stat{i}", [128, 8], F32)) for i in range(2)]
        epsb = E(nc.sbuf_tensor("epsb", [128, 1], F32))
        ones_s = E(nc.sbuf_tensor("ones_s", [128, 1], F32))
        onesr_b = E(nc.sbuf_tensor("onesr_b", [1, 128], BF16))
        rw_s = E(nc.sbuf_tensor("rw_s", [128, 8 * 16], BF16))
        rb_s = E(nc.sbuf_tensor("rb_s", [1, 16], BF16))
        iota_s = E(nc.sbuf_tensor("iota_s", [128, 16], F32))
        iM_s = E(nc.sbuf_tensor("iM_s", [128, 16], F32))
        ln_fm = E(nc.sbuf_tensor("ln_fm", [128, NCH * TLOC], BF16))
        probs = E(nc.sbuf_tensor("probs", [128, 16], F32))
        t16 = E(nc.sbuf_tensor("t16", [128, 16], F32))
        oh1 = E(nc.sbuf_tensor("oh1", [128, 16], F32))
        oh2 = E(nc.sbuf_tensor("oh2", [128, 16], F32))
        col = E(nc.sbuf_tensor("col", [128, 8], F32))
        rtile = [E(nc.sbuf_tensor(f"rtile{i}", [128, 4], F32)) for i in range(2)]
        cnt_s = E(nc.sbuf_tensor("cnt_s", [1, 16], F32))

        ps_logit = E(nc.psum_tensor("ps_logit", [128, 16], F32))
        ps_cnt = E(nc.psum_tensor("ps_cnt", [1, 16], F32))

        vch = Sem(ctx, nc, "vch")
        sch = Sem(ctx, nc, "sch")
        ldc = Sem(ctx, nc, "ldc")
        ldx = [Sem(ctx, nc, f"ldx{i}") for i in range(2)]
        ldt = [Sem(ctx, nc, f"ldt{i}") for i in range(NCH)]
        stl = [Sem(ctx, nc, f"stl{i}") for i in range(2)]
        str_ = [Sem(ctx, nc, f"str{i}") for i in range(2)]
        stc = Sem(ctx, nc, "stc")
        mm = Sem(ctx, nc, "mm")
        ve = Sem(ctx, nc, "ve")
        se = Sem(ctx, nc, "se")
        gld = Sem(ctx, nc, "gld")

        block = E(nc.Block())

        @block.vector
        def _(vector):
            vector.memset(epsb[:], 1e-5)
            tinc(vector.memset(onesr_b[:], 1.0), ve, 1)
        ve_init = ve.v

        @block.sync
        def _(sync):
            tinc(sync.dma_start(out=ones_s[:], in_=onesc), ldc, 16)
            tinc(sync.dma_start(out=iota_s[:], in_=iota16), ldc, 16)

        @block.gpsimd
        def _(gpsimd):
            for c in range(NCH):
                gpsimd.dma_start(out=rw_s[:, c * 16:(c + 1) * 16], in_=rw[c * 128:(c + 1) * 128, :]).then_inc(gld.h, 16)
                gld.inc(16)
            tinc(gpsimd.dma_start(out=rb_s[:], in_=rb), gld, 16)
        gld_w = gld.v

        @block.vector
        def _(vector):
            vector.wait_ge(ldc.h, ldc.v)
            tinc(vector.tensor_scalar(iM_s[:], iota_s[:], 64.0, None, ALU.subtract), ve, 1)
        ve_iM = ve.v

        tile_done = []
        se_tile = []
        for i in range(NTI):
            b = i % 2

            @block.sync
            def _(sync, i=i, b=b):
                if i >= 2:
                    sync.wait_ge(ve.h, tile_done[i - 2])
                    sync.wait_ge(se.h, se_tile[i - 2])
                tinc(sync.dma_start(out=xb[b][:], in_=x1[i * 128:(i + 1) * 128, :]), ldx[b], 16)
            ld_i = ldx[b].v

            @block.scalar
            def _(scalar, i=i, b=b, ld_i=ld_i):
                scalar.wait_ge(ldx[b].h, ld_i)
                if i >= 1:
                    scalar.wait_ge(sch.h, sch.v)
                if i >= 2:
                    scalar.wait_ge(ve.h, tile_done[i - 2])
                ch(scalar.activation(sq[:], xb[b][:], AF.Copy, accum_out=stat[b][:, 0:1]), sch, scalar)
                tinc(scalar.activation(sq[:], xb[b][:], AF.Square, accum_out=stat[b][:, 1:2]), se, 1)
            se_sums = se.v

            @block.vector
            def _(vector, i=i, b=b, se_sums=se_sums):
                vector.wait_ge(se.h, se_sums)
                ch(vector.tensor_scalar(stat[b][:, 2:3], stat[b][:, 0:1], 1.0 / D, None, ALU.mult), vch, vector)
                ch(vector.tensor_tensor(stat[b][:, 3:4], stat[b][:, 2:3], stat[b][:, 2:3], ALU.mult), vch, vector)
                ch(vector.tensor_scalar(stat[b][:, 5:6], stat[b][:, 1:2], 1.0 / D, None, ALU.mult), vch, vector)
                tinc(vector.tensor_tensor(stat[b][:, 3:4], stat[b][:, 5:6], stat[b][:, 3:4], ALU.subtract), ve, 1)
            ve_var = ve.v

            @block.scalar
            def _(scalar, i=i, b=b, ve_var=ve_var):
                scalar.wait_ge(ve.h, ve_var)
                tinc(scalar.activation(stat[b][:, 4:5], stat[b][:, 3:4], AF.Sqrt, bias=epsb[:, 0:1]), se, 1)
            se_tile.append(se.v)
            se_sqrt = se.v

            @block.vector
            def _(vector, i=i, b=b, se_sqrt=se_sqrt):
                vector.wait_ge(se.h, se_sqrt)
                ch(vector.reciprocal(stat[b][:, 4:5], stat[b][:, 4:5]), vch, vector)
                vector.wait_ge(ve.h, ve.v)
                ch(vector.tensor_scalar(lnf[:], xb[b][:], stat[b][:, 2:3], stat[b][:, 4:5],
                                        ALU.subtract, ALU.mult), vch, vector)
                vector.wait_ge(stl[b].h, stl[b].v)
                tinc(vector.tensor_copy(lnt[b][:], lnf[:]), ve, 1)
            tile_done.append(ve.v)
            ve_ln = ve.v

            @block.sync
            def _(sync, i=i, b=b, ve_ln=ve_ln):
                sync.wait_ge(ve.h, ve_ln)
                tinc(sync.dma_start(out=ln_tm[i * 128:(i + 1) * 128, :], in_=lnt[b][:]), stl[b], 16)
        stl_all = (stl[0].v, stl[1].v)

        for c in range(NCH):
            @block.scalar
            def _(scalar, c=c):
                scalar.wait_ge(stl[0].h, stl_all[0])
                scalar.wait_ge(stl[1].h, stl_all[1])
                tinc(scalar.dma_start(out=ln_fm[:, c * TLOC:(c + 1) * TLOC],
                                      in_=ln_tm[:, c * 128:(c + 1) * 128], transpose=True), ldt[c], 16)

        route_done = []
        for i in range(NTI):
            b = i % 2
            mm_l0 = mm.v

            @block.tensor
            def _(tensor, i=i, mm_l0=mm_l0):
                if i == 0:
                    for c in range(NCH):
                        tensor.wait_ge(ldt[c].h, ldt[c].v)
                    tensor.wait_ge(gld.h, gld_w)
                    tensor.wait_ge(ve.h, ve_init)
                else:
                    tensor.wait_ge(ve.h, ve.v)
                    tensor.wait_ge(se.h, se.v)
                for c in range(NCH):
                    tensor.matmul(ps_logit[:], ln_fm[:, c * TLOC + i * 128: c * TLOC + (i + 1) * 128],
                                  rw_s[:, c * 16:(c + 1) * 16], start=(c == 0), stop=False)
                tinc(tensor.matmul(ps_logit[:], onesr_b[0:1, 0:128], rb_s[:],
                                   start=False, stop=True), mm, 1)
            mm_log = mm.v

            @block.vector
            def _(vector, i=i, mm_log=mm_log):
                vector.wait_ge(mm.h, mm_log)
                ch(vector.reduce_max(col[:, 0:1], ps_logit[:], axis=AXX), vch, vector)
                tinc(vector.tensor_scalar(col[:, 1:2], col[:, 0:1], -1.0, None, ALU.mult), ve, 1)
            ve_mx = ve.v

            @block.scalar
            def _(scalar, i=i, ve_mx=ve_mx):
                scalar.wait_ge(ve.h, ve_mx)
                tinc(scalar.activation(probs[:], ps_logit[:], AF.Exp, bias=col[:, 1:2]), se, 1)
            se_exp = se.v

            @block.vector
            def _(vector, i=i, b=b, se_exp=se_exp):
                vector.wait_ge(se.h, se_exp)
                if i >= 1:
                    vector.wait_ge(mm.h, mm.v)
                ch(vector.reduce_sum(col[:, 2:3], probs[:], axis=AXX), vch, vector)
                ch(vector.reciprocal(col[:, 2:3], col[:, 2:3]), vch, vector)
                ch(vector.tensor_scalar(probs[:], probs[:], col[:, 2:3], None, ALU.mult), vch, vector)
                ch(vector.reduce_max(col[:, 0:1], probs[:], axis=AXX), vch, vector)
                ch(vector.tensor_scalar(t16[:], probs[:], col[:, 0:1], None, ALU.is_equal), vch, vector)
                ch(vector.tensor_tensor(t16[:], t16[:], iM_s[:], ALU.mult), vch, vector)
                ch(vector.tensor_scalar(t16[:], t16[:], 64.0, None, ALU.add), vch, vector)
                ch(vector.tensor_reduce(col[:, 4:5], t16[:], AXX, ALU.min), vch, vector)
                ch(vector.tensor_scalar(oh1[:], iota_s[:], col[:, 4:5], None, ALU.is_equal), vch, vector)
                ch(vector.tensor_tensor(t16[:], probs[:], oh1[:], ALU.mult), vch, vector)
                ch(vector.reduce_sum(col[:, 5:6], t16[:], axis=AXX), vch, vector)
                ch(vector.tensor_tensor(t16[:], probs[:], t16[:], ALU.subtract), vch, vector)
                ch(vector.reduce_max(col[:, 0:1], t16[:], axis=AXX), vch, vector)
                ch(vector.tensor_scalar(oh2[:], t16[:], col[:, 0:1], None, ALU.is_equal), vch, vector)
                ch(vector.tensor_tensor(oh2[:], oh2[:], iM_s[:], ALU.mult), vch, vector)
                ch(vector.tensor_scalar(oh2[:], oh2[:], 64.0, None, ALU.add), vch, vector)
                ch(vector.tensor_reduce(col[:, 6:7], oh2[:], AXX, ALU.min), vch, vector)
                ch(vector.tensor_scalar(oh2[:], iota_s[:], col[:, 6:7], None, ALU.is_equal), vch, vector)
                ch(vector.tensor_tensor(t16[:], t16[:], oh2[:], ALU.mult), vch, vector)
                ch(vector.reduce_sum(col[:, 7:8], t16[:], axis=AXX), vch, vector)
                ch(vector.tensor_tensor(col[:, 2:3], col[:, 5:6], col[:, 7:8], ALU.add), vch, vector)
                ch(vector.tensor_scalar(col[:, 2:3], col[:, 2:3], 1e-9, None, ALU.add), vch, vector)
                ch(vector.reciprocal(col[:, 2:3], col[:, 2:3]), vch, vector)
                vector.wait_ge(str_[b].h, str_[b].v)
                ch(vector.tensor_tensor(rtile[b][:, 0:1], col[:, 4:5], ones_s[:, 0:1], ALU.mult), vch, vector)
                ch(vector.tensor_tensor(rtile[b][:, 1:2], col[:, 6:7], ones_s[:, 0:1], ALU.mult), vch, vector)
                ch(vector.tensor_tensor(rtile[b][:, 2:3], col[:, 5:6], col[:, 2:3], ALU.mult), vch, vector)
                ch(vector.tensor_tensor(rtile[b][:, 3:4], col[:, 7:8], col[:, 2:3], ALU.mult), vch, vector)
                tinc(vector.tensor_tensor(t16[:], oh1[:], oh2[:], ALU.add), ve, 1)
            ve_rt = ve.v

            @block.tensor
            def _(tensor, i=i, ve_rt=ve_rt):
                tensor.wait_ge(ve.h, ve_rt)
                tensor.wait_ge(ldc.h, 32)
                tinc(tensor.matmul(ps_cnt[:], ones_s[:], t16[:],
                                   start=(i == 0), stop=(i == NTI - 1)), mm, 1)

            @block.sync
            def _(sync, i=i, b=b, ve_rt=ve_rt):
                sync.wait_ge(ve.h, ve_rt)
                tinc(sync.dma_start(out=route[i * 128:(i + 1) * 128, :], in_=rtile[b][:]), str_[b], 16)
            route_done.append((str_[b].v, b))
        mm_cnt = mm.v

        @block.vector
        def _(vector):
            vector.wait_ge(mm.h, mm_cnt)
            tinc(vector.tensor_copy(cnt_s[:], ps_cnt[:]), ve, 1)
        ve_cnt = ve.v

        @block.sync
        def _(sync):
            sync.wait_ge(ve.h, ve_cnt)
            tinc(sync.dma_start(out=counts, in_=cnt_s[:]), stc, 16)
            sync.wait_ge(stc.h, stc.v)
            sync.wait_ge(str_[0].h, str_[0].v)
            sync.wait_ge(str_[1].h, str_[1].v)

    return nc



CL = 1536
NBLK = CL // 128
TOTSLOT = 16 * CL
CAP_C = 5120.0
BIGOOB = 1.0e8


def build_B2(detect=True):
    nc = bass.Bass("TRN2", target_bir_lowering=False, debug=False,
                   detect_race_conditions=detect)
    x1 = nc.dram_tensor("x1", [TLOC, D], F32, kind="ExternalInput").ap()
    ln_tm = nc.dram_tensor("ln_tm", [TLOC, D], BF16, kind="ExternalInput").ap()
    route = nc.dram_tensor("route", [TLOC, 4], F32, kind="ExternalInput").ap()
    counts_all = nc.dram_tensor("counts_all", [8, 16], F32, kind="ExternalInput").ap()
    coremask = nc.dram_tensor("coremask", [8, 1], F32, kind="ExternalInput").ap()
    ident = nc.dram_tensor("ident", [128, 128], F32, kind="ExternalInput").ap()
    onesc = nc.dram_tensor("onesc", [128, 1], F32, kind="ExternalInput").ap()
    iota16 = nc.dram_tensor("iota16", [128, 16], F32, kind="ExternalInput").ap()
    eoffC = nc.dram_tensor("eoffC", [128, 16], F32, kind="ExternalInput").ap()
    iotacol = nc.dram_tensor("iotacol", [128, 1], F32, kind="ExternalInput").ap()
    pfill = nc.dram_tensor("pfill", [TOTSLOT, 4], F32, kind="ExternalInput").ap()
    swg = nc.dram_tensor("swg", [D, 1408], F32, kind="ExternalInput").ap()
    swu = nc.dram_tensor("swu", [D, 1408], F32, kind="ExternalInput").ap()
    swd = nc.dram_tensor("swd", [1408, D], F32, kind="ExternalInput").ap()
    wg = nc.dram_tensor("wg", [16, D, 1408], F32, kind="ExternalInput").ap()
    wu = nc.dram_tensor("wu", [16, D, 1408], F32, kind="ExternalInput").ap()
    wdn = nc.dram_tensor("wdn", [16, 1408, D], F32, kind="ExternalInput").ap()
    x2 = nc.dram_tensor("x2", [TLOC, D], F32, kind="ExternalOutput").ap()

    ptable = nc.dram_tensor("ptable", [TOTSLOT + 128, 4], F32, kind="Internal").ap()
    gtmp = nc.dram_tensor("gtmp", [CL, D], BF16, kind="Internal").ap()
    ofm_dram = nc.dram_tensor("ofm_dram", [D, CL], BF16, kind="Internal").ap()
    sfm_dram = nc.dram_tensor("sfm_dram", [D, TLOC], BF16, kind="Internal").ap()
    img2 = nc.dram_tensor("img2", [2 * TLOC + 128, D], F32, kind="Internal").ap()

    NTI = TLOC // 128
    NF = 11
    NTT8 = TLOC // 512

    with contextlib.ExitStack() as ctx:
        E = ctx.enter_context
        ident_s = E(nc.sbuf_tensor("ident_s", [128, 128], F32))
        ones_s = E(nc.sbuf_tensor("ones_s", [128, 1], F32))
        iota_s = E(nc.sbuf_tensor("iota_s", [128, 16], F32))
        eoff_s = E(nc.sbuf_tensor("eoff_s", [128, 16], F32))
        icol_s = E(nc.sbuf_tensor("icol_s", [128, 1], F32))
        cmask_s = E(nc.sbuf_tensor("cmask_s", [8, 1], F32))
        call_s = E(nc.sbuf_tensor("call_s", [8, 16], F32))
        pre_row = E(nc.sbuf_tensor("pre_row", [1, 16], F32))
        pre_col = E(nc.sbuf_tensor("pre_col", [16, 1], F32))
        one1 = E(nc.sbuf_tensor("one1", [1, 1], F32))
        route_sb = E(nc.sbuf_tensor("route_sb", [128, NTI * 4], F32))
        pick_fm = E(nc.sbuf_tensor("pick_fm", [16, TLOC], F32))
        posE_fm = E(nc.sbuf_tensor("posE_fm", [16, TLOC], F32))
        ploc_fm = E(nc.sbuf_tensor("ploc_fm", [16, TLOC], F32))
        posg_tm = E(nc.sbuf_tensor("posg_tm", [128, 16], F32))
        ploc_tm = E(nc.sbuf_tensor("ploc_tm", [128, 16], F32))
        t16 = E(nc.sbuf_tensor("t16", [128, 16], F32))
        u16 = E(nc.sbuf_tensor("u16", [128, 16], F32))
        oh = E(nc.sbuf_tensor("oh", [128, 16], F32))
        colB = E(nc.sbuf_tensor("colB", [128, 8], F32))
        stile = [E(nc.sbuf_tensor(f"stile{i}", [128, 4], F32)) for i in range(2)]
        di32 = [E(nc.sbuf_tensor(f"di32{i}", [128, 1], I32)) for i in range(2)]
        wgt = E(nc.sbuf_tensor("wgt", [128, NCH * 1408], BF16))
        wut = E(nc.sbuf_tensor("wut", [128, NCH * 1408], BF16))
        wdt = E(nc.sbuf_tensor("wdt", [128, NF * 1024], BF16))
        lnfm = E(nc.sbuf_tensor("lnfm", [128, NCH * 512], BF16))
        mbuf = E(nc.sbuf_tensor("mbuf", [128, NF * 512], BF16))
        s1f = E(nc.sbuf_tensor("s1f", [128, 512], F32))
        obuf = E(nc.sbuf_tensor("obuf", [128, 512], BF16))
        gfm = E(nc.sbuf_tensor("gfm", [128, NCH * 512], BF16))
        pt_s = E(nc.sbuf_tensor("pt_s", [128, 4], F32))
        gidx = E(nc.sbuf_tensor("gidx", [128, 1], I32))
        gt_s = E(nc.sbuf_tensor("gt_s", [128, D], BF16))
        srows_s = E(nc.sbuf_tensor("srows_s", [128, NBLK], I32))
        ws_s = E(nc.sbuf_tensor("ws_s", [128, NBLK], F32))
        ot_s = E(nc.sbuf_tensor("ot_s", [128, D], BF16))
        ow_f = E(nc.sbuf_tensor("ow_f", [128, D], F32))
        ztile = E(nc.sbuf_tensor("ztile", [128, D], F32))
        fx = [E(nc.sbuf_tensor(f"fx{i}", [128, D], F32)) for i in range(2)]
        fl = [E(nc.sbuf_tensor(f"fl{i}", [128, D], BF16)) for i in range(2)]
        fs = [E(nc.sbuf_tensor(f"fs{i}", [128, D], BF16)) for i in range(2)]
        fg = [E(nc.sbuf_tensor(f"fg{i}", [128, 2 * D], F32)) for i in range(2)]

        ps_h1 = E(nc.psum_tensor("ps_h1", [128, 512], F32))
        ps_h2 = E(nc.psum_tensor("ps_h2", [128, 512], F32))
        ps_o = E(nc.psum_tensor("ps_o", [128, 512], F32))
        ps_t = E(nc.psum_tensor("ps_t", [128, 128], F32))
        ps_pre = E(nc.psum_tensor("ps_pre", [16, 16], F32))

        vch = Sem(ctx, nc, "vch")
        sch = Sem(ctx, nc, "sch")
        ldc = Sem(ctx, nc, "ldc")
        gw = Sem(ctx, nc, "gw")
        gsc = Sem(ctx, nc, "gsc")
        gpt = Sem(ctx, nc, "gpt")
        ggth = Sem(ctx, nc, "ggth")
        gst2 = Sem(ctx, nc, "gst2")
        trn = [Sem(ctx, nc, f"trn{i}") for i in range(NCH)]
        trb = Sem(ctx, nc, "trb")
        trf = [Sem(ctx, nc, f"trf{i}") for i in range(2)]
        sto = Sem(ctx, nc, "sto")
        stz = Sem(ctx, nc, "stz")
        stsh = Sem(ctx, nc, "stsh")
        gsb = Sem(ctx, nc, "gsb")
        stx = [Sem(ctx, nc, f"stx{i}") for i in range(2)]
        ldf = [Sem(ctx, nc, f"ldf{i}") for i in range(2)]
        mm = Sem(ctx, nc, "mm")
        ve = Sem(ctx, nc, "ve")
        se = Sem(ctx, nc, "se")

        block = E(nc.Block())

        @block.sync
        def _(sync):
            tinc(sync.dma_start(out=ident_s[:], in_=ident), ldc, 16)
            tinc(sync.dma_start(out=ones_s[:], in_=onesc), ldc, 16)
            tinc(sync.dma_start(out=one1[:], in_=onesc[0:1, 0:1]), ldc, 16)
            tinc(sync.dma_start(out=iota_s[:], in_=iota16), ldc, 16)
            tinc(sync.dma_start(out=eoff_s[:], in_=eoffC), ldc, 16)
            tinc(sync.dma_start(out=icol_s[:], in_=iotacol), ldc, 16)
            tinc(sync.dma_start(out=cmask_s[:], in_=coremask), ldc, 16)
            tinc(sync.dma_start(out=call_s[:], in_=counts_all), ldc, 16)
            tinc(sync.dma_start(out=route_sb[:].rearrange("p (i c) -> p i c", c=4),
                                in_=route.rearrange("(i p) c -> p i c", p=128)), ldc, 16)
            sync.wait_ge(ldc.h, ldc.v)

        @block.gpsimd
        def _(gpsimd):
            tinc(gpsimd.dma_start(out=ptable[0:TOTSLOT, :], in_=pfill), gsc, 16)
        gsc_fill = gsc.v

        @block.vector
        def _(vector):
            tinc(vector.memset(ztile[:], 0.0), ve, 1)
        ve_z = ve.v

        @block.scalar
        def _(scalar):
            scalar.wait_ge(ve.h, ve_z)
            for zz in range(2 * TLOC // 128):
                tinc(scalar.dma_start(out=img2[zz * 128:(zz + 1) * 128, :], in_=ztile[:]), stz, 16)
        stz_zero = stz.v

        @block.tensor
        def _(tensor):
            tensor.wait_ge(ldc.h, ldc.v)
            tinc(tensor.matmul(ps_pre[0:1, :], cmask_s[:], call_s[:], start=True, stop=True), mm, 1)

        @block.vector
        def _(vector):
            vector.wait_ge(mm.h, mm.v)
            tinc(vector.tensor_copy(pre_row[:], ps_pre[0:1, :]), ve, 1)
        ve_pr = ve.v

        @block.tensor
        def _(tensor):
            tensor.wait_ge(ve.h, ve_pr)
            tinc(tensor.matmul(ps_t[0:16, 0:1], pre_row[:], one1[:], start=True, stop=True), mm, 1)

        @block.vector
        def _(vector):
            vector.wait_ge(mm.h, mm.v)
            tinc(vector.tensor_copy(pre_col[:], ps_t[0:16, 0:1]), ve, 1)
        ve_prc = ve.v

        for i in range(NTI):
            @block.vector
            def _(vector, i=i):
                if i == 0:
                    vector.wait_ge(ldc.h, ldc.v)
                else:
                    vector.wait_ge(mm.h, mm.v)
                ch(vector.tensor_scalar(t16[:], iota_s[:], route_sb[:, i * 4:i * 4 + 1], None, ALU.is_equal), vch, vector)
                ch(vector.tensor_scalar(u16[:], iota_s[:], route_sb[:, i * 4 + 1:i * 4 + 2], None, ALU.is_equal), vch, vector)
                tinc(vector.tensor_tensor(t16[:], t16[:], u16[:], ALU.add), ve, 1)
            ve_pk = ve.v

            @block.tensor
            def _(tensor, i=i, ve_pk=ve_pk):
                tensor.wait_ge(ve.h, ve_pk)
                if i >= 1:
                    tensor.wait_ge(ve.h, ve.v)
                tinc(tensor.transpose(ps_t[0:16, :], t16[:], ident_s[:]), mm, 1)
            mm_tp = mm.v

            @block.vector
            def _(vector, i=i, mm_tp=mm_tp):
                vector.wait_ge(mm.h, mm_tp)
                tinc(vector.tensor_copy(pick_fm[:, i * 128:(i + 1) * 128], ps_t[0:16, :]), ve, 1)

        @block.vector
        def _(vector):
            vector.wait_ge(ve.h, ve.v)
            vector.wait_ge(ve.h, ve_prc)
            ch(vector.tensor_tensor_scan(posE_fm[:], pick_fm[:], pick_fm[:], pre_col[:, 0:1],
                                         ALU.add, ALU.bypass), vch, vector)
            ch(vector.tensor_tensor(posE_fm[:], posE_fm[:], pick_fm[:], ALU.subtract), vch, vector)
            tinc(vector.tensor_scalar(ploc_fm[:], posE_fm[:], pre_col[:, 0:1], None, ALU.subtract), ve, 1)
        ve_scan = ve.v

        for i in range(NTI):
            @block.tensor
            def _(tensor, i=i, ve_scan=ve_scan):
                tensor.wait_ge(ve.h, ve_scan)
                tensor.wait_ge(ve.h, ve.v)
                tinc(tensor.transpose(ps_t[0:128, 0:16], posE_fm[:, i * 128:(i + 1) * 128],
                                      ident_s[0:16, 0:16]), mm, 1)
            mm_t1 = mm.v

            @block.vector
            def _(vector, i=i, mm_t1=mm_t1):
                vector.wait_ge(mm.h, mm_t1)
                tinc(vector.tensor_copy(posg_tm[:], ps_t[0:128, 0:16]), ve, 1)
            ve_t1 = ve.v

            @block.tensor
            def _(tensor, i=i, ve_t1=ve_t1):
                tensor.wait_ge(ve.h, ve_t1)
                tinc(tensor.transpose(ps_t[0:128, 0:16], ploc_fm[:, i * 128:(i + 1) * 128],
                                      ident_s[0:16, 0:16]), mm, 1)
            mm_t2 = mm.v

            @block.vector
            def _(vector, i=i, mm_t2=mm_t2):
                vector.wait_ge(mm.h, mm_t2)
                ch(vector.tensor_copy(ploc_tm[:], ps_t[0:128, 0:16]), vch, vector)
                tinc(vector.tensor_scalar(u16[:], posg_tm[:], CAP_C, None, ALU.is_lt), ve, 1)
            ve_t2 = ve.v

            for k in range(2):
                sb = (2 * i + k) % 2

                @block.vector
                def _(vector, i=i, k=k, sb=sb, ve_t2=ve_t2):
                    vector.wait_ge(ve.h, ve_t2)
                    vector.wait_ge(gsc.h, gsc.v)
                    ch(vector.tensor_scalar(oh[:], iota_s[:], route_sb[:, i * 4 + k:i * 4 + k + 1],
                                            None, ALU.is_equal), vch, vector)
                    ch(vector.tensor_tensor(t16[:], oh[:], ploc_tm[:], ALU.mult), vch, vector)
                    ch(vector.reduce_sum(colB[:, 0:1], t16[:], axis=AXX), vch, vector)
                    ch(vector.tensor_tensor(t16[:], oh[:], u16[:], ALU.mult), vch, vector)
                    ch(vector.reduce_sum(colB[:, 1:2], t16[:], axis=AXX), vch, vector)
                    ch(vector.tensor_tensor(t16[:], ploc_tm[:], eoff_s[:], ALU.add), vch, vector)
                    ch(vector.tensor_tensor(t16[:], oh[:], t16[:], ALU.mult), vch, vector)
                    ch(vector.reduce_sum(colB[:, 2:3], t16[:], axis=AXX), vch, vector)
                    ch(vector.tensor_scalar(colB[:, 3:4], colB[:, 0:1], float(CL), None, ALU.is_ge), vch, vector)
                    ch(vector.tensor_tensor(colB[:, 4:5], colB[:, 3:4], colB[:, 2:3], ALU.mult), vch, vector)
                    ch(vector.tensor_tensor(colB[:, 2:3], colB[:, 2:3], colB[:, 4:5], ALU.subtract), vch, vector)
                    ch(vector.scalar_tensor_tensor(colB[:, 2:3], colB[:, 3:4], float(TOTSLOT), colB[:, 2:3],
                                                   ALU.mult, ALU.add), vch, vector)
                    ch(vector.tensor_scalar(stile[sb][:, 0:1], icol_s[:], float(i * 128), None, ALU.add), vch, vector)
                    ch(vector.tensor_scalar(stile[sb][:, 1:2], stile[sb][:, 0:1], float(k * TLOC), None, ALU.add), vch, vector)
                    ch(vector.tensor_tensor(stile[sb][:, 2:3], route_sb[:, i * 4 + 2 + k:i * 4 + 3 + k],
                                            colB[:, 1:2], ALU.mult), vch, vector)
                    ch(vector.tensor_copy(stile[sb][:, 3:4], stile[sb][:, 2:3]), vch, vector)
                    tinc(vector.tensor_copy(di32[sb][:], colB[:, 2:3]), ve, 1)
                ve_st = ve.v

                @block.gpsimd
                def _(gpsimd, i=i, k=k, sb=sb, ve_st=ve_st):
                    gpsimd.wait_ge(ve.h, ve_st)
                    gpsimd.wait_ge(gsc.h, gsc_fill)
                    tinc(gpsimd.indirect_dma_start(
                        out=ptable, out_offset=IndirectOffsetOnAxis(di32[sb][:, 0:1], 0),
                        in_=stile[sb][:], in_offset=None), gsc, 16)
        gsc_all = gsc.v

        def emit_mlp_512(src_buf, src_stride, dst_store, mm_ref, gw_gate=None):
            for fc in range(NF):
                @block.tensor
                def _(tensor, fc=fc):
                    if gw_gate is not None and fc == 0:
                        tensor.wait_ge(gw.h, gw_gate)
                    if fc >= 1:
                        tensor.wait_ge(ve.h, ve.v)
                        tensor.wait_ge(se.h, se.v)
                    for c in range(NCH):
                        tensor.matmul(ps_h1[:], wgt[:, c * 1408 + fc * 128:c * 1408 + (fc + 1) * 128],
                                      src_buf[:, c * src_stride:c * src_stride + 512],
                                      start=(c == 0), stop=(c == NCH - 1))
                    for c in range(NCH):
                        i_ = tensor.matmul(ps_h2[:], wut[:, c * 1408 + fc * 128:c * 1408 + (fc + 1) * 128],
                                           src_buf[:, c * src_stride:c * src_stride + 512],
                                           start=(c == 0), stop=(c == NCH - 1))
                    tinc(i_, mm, 1)
                mm_h = mm.v

                @block.scalar
                def _(scalar, fc=fc, mm_h=mm_h):
                    scalar.wait_ge(mm.h, mm_h)
                    if True:
                        scalar.wait_ge(ve.h, ve.v)
                    tinc(scalar.activation(s1f[:], ps_h1[:], AF.Silu), se, 1)
                se_si = se.v

                @block.vector
                def _(vector, fc=fc, se_si=se_si):
                    vector.wait_ge(se.h, se_si)
                    if fc == 0:
                        vector.wait_ge(mm.h, mm.v)
                    tinc(vector.tensor_tensor(mbuf[:, fc * 512:(fc + 1) * 512], s1f[:], ps_h2[:], ALU.mult), ve, 1)
            ve_m = ve.v

            for dc in range(NCH):
                @block.tensor
                def _(tensor, dc=dc, ve_m=ve_m):
                    tensor.wait_ge(ve.h, ve_m)
                    tensor.wait_ge(se.h, se.v)
                    for fc in range(NF):
                        i_ = tensor.matmul(ps_o[:], wdt[:, fc * 1024 + dc * 128:fc * 1024 + (dc + 1) * 128],
                                           mbuf[:, fc * 512:(fc + 1) * 512], start=(fc == 0), stop=(fc == NF - 1))
                    tinc(i_, mm, 1)
                mm_o = mm.v

                @block.scalar
                def _(scalar, dc=dc, mm_o=mm_o):
                    scalar.wait_ge(mm.h, mm_o)
                    scalar.wait_ge(sto.h, sto.v)
                    tinc(scalar.activation(obuf[:], ps_o[:], AF.Copy), se, 1)
                se_ob = se.v
                dst_store(dc, se_ob)

        def load_weights_gp(gpsimd, wsrc_g, wsrc_u, wsrc_d):
            for c in range(NCH):
                tinc(gpsimd.dma_start(out=wgt[:, c * 1408:(c + 1) * 1408],
                                      in_=wsrc_g[c * 128:(c + 1) * 128, :]), gw, 16)
            for c in range(NCH):
                tinc(gpsimd.dma_start(out=wut[:, c * 1408:(c + 1) * 1408],
                                      in_=wsrc_u[c * 128:(c + 1) * 128, :]), gw, 16)
            for f in range(NF):
                tinc(gpsimd.dma_start(out=wdt[:, f * 1024:(f + 1) * 1024],
                                      in_=wsrc_d[f * 128:(f + 1) * 128, :]), gw, 16)

        @block.gpsimd
        def _(gpsimd):
            load_weights_gp(gpsimd, swg, swu, swd)
        gw_sw = gw.v

        for t in range(NTT8):
            t0 = t * 512

            @block.scalar
            def _(scalar, t=t, t0=t0):
                if t >= 1:
                    scalar.wait_ge(mm.h, mm.v)
                for c in range(NCH):
                    tinc(scalar.dma_start(out=lnfm[:, c * 512:(c + 1) * 512],
                                          in_=ln_tm[t0:t0 + 512, c * 128:(c + 1) * 128],
                                          transpose=True), trn[c], 16)

            @block.tensor
            def _(tensor, t=t):
                for c in range(NCH):
                    tensor.wait_ge(trn[c].h, trn[c].v)
                tensor.engine_nop() if False else None

            def store_s(dc, se_ob, t0=t0):
                @block.sync
                def _(sync, dc=dc, se_ob=se_ob, t0=t0):
                    sync.wait_ge(se.h, se_ob)
                    tinc(sync.dma_start(out=sfm_dram[dc * 128:(dc + 1) * 128, t0:t0 + 512], in_=obuf[:]), sto, 16)

            emit_mlp_512(lnfm, 512, store_s, mm, gw_gate=(gw_sw if t == 0 else None))

        for e in range(16):
            mm_before = mm.v

            @block.gpsimd
            def _(gpsimd, e=e, mm_before=mm_before):
                gpsimd.wait_ge(mm.h, mm_before)
                load_weights_gp(gpsimd, wg[e], wu[e], wdn[e])
            gw_e = gw.v

            for blk in range(NBLK):
                @block.gpsimd
                def _(gpsimd, e=e, blk=blk):
                    if e == 0 and blk == 0:
                        gpsimd.wait_ge(gsc.h, gsc_all)
                    gpsimd.wait_ge(ve.h, ve.v)
                    r0 = e * CL + blk * 128
                    tinc(gpsimd.dma_start(out=pt_s[:], in_=ptable[r0:r0 + 128, :]), gpt, 16)
                gpt_l = gpt.v

                @block.vector
                def _(vector, e=e, blk=blk, gpt_l=gpt_l):
                    vector.wait_ge(gpt.h, gpt_l)
                    vector.wait_ge(gsb.h, gsb.v)
                    ch(vector.tensor_copy(gidx[:], pt_s[:, 0:1]), vch, vector)
                    ch(vector.tensor_copy(srows_s[:, blk:blk + 1], pt_s[:, 1:2]), vch, vector)
                    tinc(vector.tensor_copy(ws_s[:, blk:blk + 1], pt_s[:, 2:3]), ve, 1)
                ve_gi = ve.v

                @block.gpsimd
                def _(gpsimd, e=e, blk=blk, ve_gi=ve_gi):
                    gpsimd.wait_ge(ve.h, ve_gi)
                    gpsimd.wait_ge(gst2.h, gst2.v)
                    tinc(gpsimd.indirect_dma_start(
                        out=gt_s[:], out_offset=None,
                        in_=ln_tm, in_offset=IndirectOffsetOnAxis(gidx[:, 0:1], 0)), ggth, 16)
                gg_l = ggth.v

                @block.gpsimd
                def _(gpsimd, e=e, blk=blk, gg_l=gg_l):
                    gpsimd.wait_ge(ggth.h, gg_l)
                    tinc(gpsimd.dma_start(out=gtmp[blk * 128:(blk + 1) * 128, :], in_=gt_s[:]), gst2, 16)
            gst2_e = gst2.v

            for ct in range(3):
                c0 = ct * 512

                @block.scalar
                def _(scalar, e=e, ct=ct, c0=c0, gst2_e=gst2_e):
                    scalar.wait_ge(gst2.h, gst2_e)
                    scalar.wait_ge(mm.h, mm.v)
                    for c in range(NCH):
                        tinc(scalar.dma_start(out=gfm[:, c * 512:(c + 1) * 512],
                                              in_=gtmp[c0:c0 + 512, c * 128:(c + 1) * 128],
                                              transpose=True), trn[c], 16)

                @block.tensor
                def _(tensor, e=e, ct=ct):
                    for c in range(NCH):
                        tensor.wait_ge(trn[c].h, trn[c].v)

                def store_e(dc, se_ob, e=e, c0=c0):
                    @block.sync
                    def _(sync, dc=dc, se_ob=se_ob, c0=c0):
                        sync.wait_ge(se.h, se_ob)
                        tinc(sync.dma_start(out=ofm_dram[dc * 128:(dc + 1) * 128, c0:c0 + 512], in_=obuf[:]), sto, 16)

                emit_mlp_512(gfm, 512, store_e, mm, gw_gate=(gw_e if ct == 0 else None))
            sto_e = sto.v

            for blk in range(NBLK):
                @block.scalar
                def _(scalar, e=e, blk=blk, sto_e=sto_e):
                    scalar.wait_ge(sto.h, sto_e)
                    scalar.wait_ge(ve.h, ve.v)
                    tinc(scalar.dma_start(out=ot_s[:], in_=ofm_dram[:, blk * 128:(blk + 1) * 128],
                                          transpose=True), trb, 16)
                trb_l = trb.v

                @block.vector
                def _(vector, e=e, blk=blk, trb_l=trb_l):
                    vector.wait_ge(trb.h, trb_l)
                    vector.wait_ge(gsb.h, gsb.v)
                    tinc(vector.tensor_scalar(ow_f[:], ot_s[:], ws_s[:, blk:blk + 1], None, ALU.mult), ve, 1)
                ve_ow = ve.v

                @block.gpsimd
                def _(gpsimd, e=e, blk=blk, ve_ow=ve_ow):
                    gpsimd.wait_ge(ve.h, ve_ow)
                    gpsimd.wait_ge(stz.h, stz_zero)
                    tinc(gpsimd.indirect_dma_start(
                        out=img2, out_offset=IndirectOffsetOnAxis(srows_s[:, blk:blk + 1], 0),
                        in_=ow_f[:], in_offset=None), gsb, 16)
        gsb_all = gsb.v

        for i in range(NTI):
            b = i % 2

            @block.sync
            def _(sync, i=i, b=b):
                sync.wait_ge(gsb.h, gsb_all)
                sync.wait_ge(stx[b].h, stx[b].v)
                tinc(sync.dma_start(out=fx[b][:], in_=x1[i * 128:(i + 1) * 128, :]), ldf[b], 16)
                tinc(sync.dma_start(out=fl[b][:], in_=ln_tm[i * 128:(i + 1) * 128, :]), ldf[b], 16)
                tinc(sync.dma_start(out=fg[b][:, 0:D], in_=img2[i * 128:(i + 1) * 128, :]), ldf[b], 16)
                tinc(sync.dma_start(out=fg[b][:, D:2 * D], in_=img2[TLOC + i * 128:TLOC + (i + 1) * 128, :]), ldf[b], 16)
            ld_f = ldf[b].v

            @block.scalar
            def _(scalar, i=i, b=b):
                scalar.wait_ge(stx[b].h, stx[b].v)
                tinc(scalar.dma_start(out=fs[b][:], in_=sfm_dram[:, i * 128:(i + 1) * 128],
                                      transpose=True), trf[b], 16)
            trb_f = trf[b].v

            @block.vector
            def _(vector, i=i, b=b, ld_f=ld_f, trb_f=trb_f):
                vector.wait_ge(ldf[b].h, ld_f)
                vector.wait_ge(trf[b].h, trb_f)
                ch(vector.tensor_tensor(fx[b][:], fx[b][:], fl[b][:], ALU.add), vch, vector)
                ch(vector.tensor_tensor(fx[b][:], fx[b][:], fs[b][:], ALU.add), vch, vector)
                ch(vector.tensor_tensor(fx[b][:], fx[b][:], fg[b][:, 0:D], ALU.add), vch, vector)
                tinc(vector.tensor_tensor(fx[b][:], fx[b][:], fg[b][:, D:2 * D], ALU.add), ve, 1)
            ve_f = ve.v

            @block.sync
            def _(sync, i=i, b=b, ve_f=ve_f):
                sync.wait_ge(ve.h, ve_f)
                tinc(sync.dma_start(out=x2[i * 128:(i + 1) * 128, :], in_=fx[b][:]), stx[b], 16)

        @block.sync
        def _(sync):
            sync.wait_ge(stx[0].h, stx[0].v)
            sync.wait_ge(stx[1].h, stx[1].v)

    return nc



def build_C(detect=True, self_exchange=False):
    nc = bass.Bass("TRN2", target_bir_lowering=False, debug=False,
                   detect_race_conditions=detect, dynamic_dma_scratch_size=8192)
    x2 = nc.dram_tensor("x2", [TLOC, D], F32, kind="ExternalInput").ap()
    inwT = nc.dram_tensor("inwT", [D, 2048], F32, kind="ExternalInput").ap()
    gwT = nc.dram_tensor("gwT", [D, 2048], F32, kind="ExternalInput").ap()
    outT = nc.dram_tensor("outT", [D, D], F32, kind="ExternalInput").ap()
    growT = nc.dram_tensor("growT", [D, 4096], F32, kind="ExternalInput").ap()
    shrT = nc.dram_tensor("shrT", [2048, D], F32, kind="ExternalInput").ap()
    convw = nc.dram_tensor("convw", [128, 4 * NCH], F32, kind="ExternalInput").ap()
    convb = nc.dram_tensor("convb", [128, NCH], F32, kind="ExternalInput").ap()
    fbase = nc.dram_tensor("fbase", [128, NCH], F32, kind="ExternalInput").ap()
    gam1 = nc.dram_tensor("gam1", [128, NCH], F32, kind="ExternalInput").ap()
    gam2 = nc.dram_tensor("gam2", [128, NCH], F32, kind="ExternalInput").ap()
    gb = nc.dram_tensor("gb", [128, 16], F32, kind="ExternalInput").ap()
    hmask = nc.dram_tensor("hmask", [128, 1], F32, kind="ExternalInput").ap()
    tag = nc.dram_tensor("tag", [1, 16], I32, kind="ExternalInput").ap()
    x3 = nc.dram_tensor("x3", [TLOC, D], F32, kind="ExternalOutput").ap()

    ln_dram = nc.dram_tensor("ln_dram", [TLOC, D], BF16, kind="Internal").ap()
    rms1_dram = nc.dram_tensor("rms1_dram", [TLOC, D], BF16, kind="Internal").ap()
    z0_dram = nc.dram_tensor("z0_dram", [NCH, 128, TLOC], F32, kind="Internal").ap()
    gg_dram = nc.dram_tensor("gg_dram", [NCH, 128, TLOC], BF16, kind="Internal").ap()
    z1_dram = nc.dram_tensor("z1_dram", [NCH, 128, TLOC], F32, kind="Internal").ap()
    hout_dram = nc.dram_tensor("hout_dram", [D, TLOC], BF16, kind="Internal").ap()
    rms2_dram = nc.dram_tensor("rms2_dram", [TLOC, D], BF16, kind="Internal").ap()
    gout_dram = nc.dram_tensor("gout_dram", [D, TLOC], BF16, kind="Internal").ap()
    sh_halo = nc.dram_tensor("sh_halo", [2, 128, 24], F32, kind="Internal", addr_space="Shared").ap()
    sh_carry = nc.dram_tensor("sh_carry", [2, 128, 8], F32, kind="Internal", addr_space="Shared").ap()
    sh_flag = nc.dram_tensor("sh_flag", [4, 16], I32, kind="Internal", addr_space="Shared").ap()

    NTI = TLOC // 128
    NTT8 = TLOC // 512

    with contextlib.ExitStack() as ctx:
        E = ctx.enter_context
        sq = E(nc.sbuf_tensor("sq", [128, D], F32))
        lnf = E(nc.sbuf_tensor("lnf", [128, D], F32))
        rmt = [E(nc.sbuf_tensor(f"rmt{i}", [128, D], BF16)) for i in range(2)]
        stat = [E(nc.sbuf_tensor(f"stat{i}", [128, 8], F32)) for i in range(2)]
        epsb = E(nc.sbuf_tensor("epsb", [128, 1], F32))
        oneb = E(nc.sbuf_tensor("oneb", [128, 1], F32))
        wA = E(nc.sbuf_tensor("wA", [128, 16384], BF16))
        wB = E(nc.sbuf_tensor("wB", [128, 32768], BF16))
        gamc = E(nc.sbuf_tensor("gamc", [128, 16], F32))
        cnv = E(nc.sbuf_tensor("cnv", [128, 4 * NCH], F32))
        cnb = E(nc.sbuf_tensor("cnb", [128, NCH], F32))
        cneg = E(nc.sbuf_tensor("cneg", [128, NCH], F32))
        fbs = E(nc.sbuf_tensor("fbs", [128, NCH], F32))
        gbC = E(nc.sbuf_tensor("gbC", [128, 16], F32))
        hm = E(nc.sbuf_tensor("hm", [128, 1], F32))
        tag_s = E(nc.sbuf_tensor("tag_s", [1, 16], I32))
        flag_s = E(nc.sbuf_tensor("flag_s", [1, 16], I32))
        rfm = E(nc.sbuf_tensor("rfm", [128, NCH * 512], BF16))
        zext = E(nc.sbuf_tensor("zext", [128, 3 + TLOC], F32))
        zacc = E(nc.sbuf_tensor("zacc", [128, TLOC], F32))
        halo = E(nc.sbuf_tensor("halo", [128, 24], F32))
        carrybuf = E(nc.sbuf_tensor("carrybuf", [128, 8], F32))
        myfin = E(nc.sbuf_tensor("myfin", [128, 8], F32))
        alph = E(nc.sbuf_tensor("alph", [128, TLOC], F32))
        xsb = E(nc.sbuf_tensor("xsb", [128, TLOC], F32))
        sgf = E(nc.sbuf_tensor("sgf", [128, 512], F32))
        evb = E(nc.sbuf_tensor("evb", [128, 512], BF16))
        g2buf = E(nc.sbuf_tensor("g2buf", [128, 16 * 512], BF16))
        fx = [E(nc.sbuf_tensor(f"fx{i}", [128, D], F32)) for i in range(2)]
        fl = [E(nc.sbuf_tensor(f"fl{i}", [128, D], BF16)) for i in range(2)]
        xb = fx
        lnt = fl
        fh = [E(nc.sbuf_tensor(f"fh{i}", [128, D], BF16)) for i in range(2)]
        fg2 = rmt

        ps_a = E(nc.psum_tensor("ps_a", [128, 512], F32))
        ps_b = E(nc.psum_tensor("ps_b", [128, 512], F32))

        vch = Sem(ctx, nc, "vch")
        sch = Sem(ctx, nc, "sch")
        ldc = Sem(ctx, nc, "ldc")
        gw = Sem(ctx, nc, "gw")
        ldx = [Sem(ctx, nc, f"ldx{i}") for i in range(2)]
        stl = [Sem(ctx, nc, f"stl{i}") for i in range(2)]
        stm = [Sem(ctx, nc, f"stm{i}") for i in range(2)]
        trn = [Sem(ctx, nc, f"trn{i}") for i in range(NCH)]
        stz0 = Sem(ctx, nc, "stz0")
        stgg = Sem(ctx, nc, "stgg")
        ldz = Sem(ctx, nc, "ldz")
        stz1 = Sem(ctx, nc, "stz1")
        ldz1 = Sem(ctx, nc, "ldz1")
        stho = Sem(ctx, nc, "stho")
        stgo = Sem(ctx, nc, "stgo")
        strm2 = [Sem(ctx, nc, f"strm2{i}") for i in range(2)]
        ldy = [Sem(ctx, nc, f"ldy{i}") for i in range(2)]
        gex = Sem(ctx, nc, "gex")
        ghl = Sem(ctx, nc, "ghl")
        pse1 = Sem(ctx, nc, "pse1")
        pse2 = Sem(ctx, nc, "pse2")
        ldf = [Sem(ctx, nc, f"ldf{i}") for i in range(2)]
        trf = [Sem(ctx, nc, f"trf{i}") for i in range(2)]
        stx = [Sem(ctx, nc, f"stx{i}") for i in range(2)]
        mm = Sem(ctx, nc, "mm")
        ve = Sem(ctx, nc, "ve")
        se = Sem(ctx, nc, "se")

        block = E(nc.Block())

        @block.sync
        def _(sync):
            tinc(sync.dma_start(out=tag_s[:], in_=tag), ldc, 16)
            tinc(sync.dma_start(out=hm[:], in_=hmask), ldc, 16)
            tinc(sync.dma_start(out=cnv[:], in_=convw), ldc, 16)
            tinc(sync.dma_start(out=cnb[:], in_=convb), ldc, 16)
            tinc(sync.dma_start(out=fbs[:], in_=fbase), ldc, 16)
            tinc(sync.dma_start(out=gamc[:, 0:8], in_=gam1), ldc, 16)
            tinc(sync.dma_start(out=gamc[:, 8:16], in_=gam2), ldc, 16)
            tinc(sync.dma_start(out=gbC[:], in_=gb), ldc, 16)
            sync.wait_ge(ldc.h, ldc.v)

        @block.vector
        def _(vector):
            vector.memset(epsb[:], 1e-5)
            vector.memset(oneb[:], 1.0 + 1e-6)
            vector.wait_ge(ldc.h, ldc.v)
            tinc(vector.tensor_scalar(gamc[:], gamc[:], 32.0, None, ALU.mult), ve, 1)
        ve_init = ve.v

        @block.scalar
        def _(scalar):
            scalar.wait_ge(ldc.h, ldc.v)
            scalar.wait_ge(ve.h, ve_init)
            ch(scalar.activation(cneg[:], fbs[:], AF.Abs), sch, scalar)
            ch(scalar.activation(cneg[:], cneg[:], AF.Exp, scale=-1.0), sch, scalar)
            tinc(scalar.activation(cneg[:], cneg[:], AF.Ln, bias=oneb[:, 0:1]), se, 1)
        se_sp = se.v

        @block.vector
        def _(vector):
            vector.wait_ge(se.h, se_sp)
            ch(vector.tensor_scalar(fbs[:], fbs[:], 0.0, None, ALU.max), vch, vector)
            ch(vector.tensor_tensor(cneg[:], cneg[:], fbs[:], ALU.add), vch, vector)
            tinc(vector.tensor_scalar(cneg[:], cneg[:], -8.0, None, ALU.mult), ve, 1)
        ve_cneg = ve.v

        tile_done = []
        se_tile = []
        for i in range(NTI):
            b = i % 2

            @block.sync
            def _(sync, i=i, b=b):
                if i >= 2:
                    sync.wait_ge(ve.h, tile_done[i - 2])
                    sync.wait_ge(se.h, se_tile[i - 2])
                tinc(sync.dma_start(out=xb[b][:], in_=x2[i * 128:(i + 1) * 128, :]), ldx[b], 16)
            ld_i = ldx[b].v

            @block.scalar
            def _(scalar, i=i, b=b, ld_i=ld_i):
                scalar.wait_ge(ldx[b].h, ld_i)
                if i >= 1:
                    scalar.wait_ge(sch.h, sch.v)
                if i >= 2:
                    scalar.wait_ge(ve.h, tile_done[i - 2])
                ch(scalar.activation(sq[:], xb[b][:], AF.Copy, accum_out=stat[b][:, 0:1]), sch, scalar)
                tinc(scalar.activation(sq[:], xb[b][:], AF.Square, accum_out=stat[b][:, 1:2]), se, 1)
            se_sums = se.v

            @block.vector
            def _(vector, i=i, b=b, se_sums=se_sums):
                vector.wait_ge(se.h, se_sums)
                ch(vector.tensor_scalar(stat[b][:, 2:3], stat[b][:, 0:1], 1.0 / D, None, ALU.mult), vch, vector)
                ch(vector.tensor_tensor(stat[b][:, 3:4], stat[b][:, 2:3], stat[b][:, 2:3], ALU.mult), vch, vector)
                ch(vector.tensor_scalar(stat[b][:, 5:6], stat[b][:, 1:2], 1.0 / D, None, ALU.mult), vch, vector)
                tinc(vector.tensor_tensor(stat[b][:, 3:4], stat[b][:, 5:6], stat[b][:, 3:4], ALU.subtract), ve, 1)
            ve_var = ve.v

            @block.scalar
            def _(scalar, i=i, b=b, ve_var=ve_var):
                scalar.wait_ge(ve.h, ve_var)
                tinc(scalar.activation(stat[b][:, 4:5], stat[b][:, 3:4], AF.Sqrt, bias=epsb[:, 0:1]), se, 1)
            se_tile.append(se.v)
            se_sqrt = se.v

            @block.vector
            def _(vector, i=i, b=b, se_sqrt=se_sqrt):
                vector.wait_ge(se.h, se_sqrt)
                ch(vector.reciprocal(stat[b][:, 4:5], stat[b][:, 4:5]), vch, vector)
                vector.wait_ge(ve.h, ve.v)
                ch(vector.tensor_scalar(lnf[:], xb[b][:], stat[b][:, 2:3], stat[b][:, 4:5],
                                        ALU.subtract, ALU.mult), vch, vector)
                vector.wait_ge(stl[b].h, stl[b].v)
                ch(vector.tensor_copy(lnt[b][:], lnf[:]), vch, vector)
                ch(vector.tensor_tensor(sq[:], lnf[:], lnf[:], ALU.mult), vch, vector)
                ch(vector.reduce_sum(stat[b][:, 6:7], sq[:], axis=AXX), vch, vector)
                tinc(vector.tensor_copy(stat[b][:, 7:8], stat[b][:, 6:7]), ve, 1)
            ve_n2 = ve.v

            @block.scalar
            def _(scalar, i=i, b=b, ve_n2=ve_n2):
                scalar.wait_ge(ve.h, ve_n2)
                tinc(scalar.activation(stat[b][:, 6:7], stat[b][:, 6:7], AF.Sqrt), se, 1)
            se_n = se.v

            @block.vector
            def _(vector, i=i, b=b, se_n=se_n):
                vector.wait_ge(se.h, se_n)
                ch(vector.reciprocal(stat[b][:, 6:7], stat[b][:, 6:7]), vch, vector)
                vector.wait_ge(stm[b].h, stm[b].v)
                tinc(vector.tensor_scalar(rmt[b][:], lnf[:], stat[b][:, 6:7], None, ALU.mult), ve, 1)
            tile_done.append(ve.v)
            ve_r = ve.v

            @block.sync
            def _(sync, i=i, b=b, ve_r=ve_r):
                sync.wait_ge(ve.h, ve_r)
                tinc(sync.dma_start(out=ln_dram[i * 128:(i + 1) * 128, :], in_=lnt[b][:]), stl[b], 16)
                tinc(sync.dma_start(out=rms1_dram[i * 128:(i + 1) * 128, :], in_=rmt[b][:]), stm[b], 16)
        stl_all = (stl[0].v, stl[1].v)
        stm_all = (stm[0].v, stm[1].v)

        def cast_weight(gpsimd, dst, dst_off, src, ncol, gam_col=None):
            for cc in range(NCH):
                tinc(gpsimd.dma_start(out=dst[:, dst_off + cc * ncol: dst_off + (cc + 1) * ncol],
                                      in_=src[cc * 128:(cc + 1) * 128, :]), gw, 16)

        @block.gpsimd
        def _(gpsimd):
            cast_weight(gpsimd, wA, 0, inwT, 2048)
        gw_in = gw.v

        @block.vector
        def _(vector):
            vector.wait_ge(gw.h, gw_in)
            vector.wait_ge(ve.h, ve_init)
            for cc in range(NCH):
                ii = vector.tensor_scalar(wA[:, cc * 2048:(cc + 1) * 2048],
                                          wA[:, cc * 2048:(cc + 1) * 2048],
                                          gamc[:, cc:cc + 1], None, ALU.mult)
                ch(ii, vch, vector)
            vector.sem_inc(ve.h, 1)
            ve.inc(1)
        ve_win = ve.v

        for t in range(NTT8):
            t0 = t * 512

            @block.scalar
            def _(scalar, t=t, t0=t0):
                scalar.wait_ge(stm[0].h, stm_all[0])
                scalar.wait_ge(stm[1].h, stm_all[1])
                if t >= 1:
                    scalar.wait_ge(mm.h, mm.v)
                for c in range(NCH):
                    tinc(scalar.dma_start(out=rfm[:, c * 512:(c + 1) * 512],
                                          in_=rms1_dram[t0:t0 + 512, c * 128:(c + 1) * 128],
                                          transpose=True), trn[c], 16)

            for oc in range(16):
                pbank = ps_a if oc % 2 == 0 else ps_b

                @block.tensor
                def _(tensor, t=t, oc=oc, pbank=pbank):
                    if oc == 0:
                        for c in range(NCH):
                            tensor.wait_ge(trn[c].h, trn[c].v)
                        tensor.wait_ge(ve.h, ve_win)
                    if oc >= 2:
                        tensor.wait_ge(se.h, se.v - 1)
                    for c in range(NCH):
                        i_ = tensor.matmul(pbank[:], wA[:, c * 2048 + oc * 128: c * 2048 + (oc + 1) * 128],
                                           rfm[:, c * 512:(c + 1) * 512], start=(c == 0), stop=(c == NCH - 1))
                    tinc(i_, mm, 1)
                mm_u = mm.v

                @block.scalar
                def _(scalar, t=t, t0=t0, oc=oc, pbank=pbank, mm_u=mm_u):
                    scalar.wait_ge(mm.h, mm_u)
                    scalar.wait_ge(stz0.h, stz0.v)
                    scalar.wait_ge(stgg.h, stgg.v)
                    if oc < NCH:
                        tinc(scalar.activation(evb[:], pbank[:], AF.Gelu), se, 1)
                    else:
                        tinc(scalar.activation(sgf[:], pbank[:], AF.Copy), se, 1)
                se_ev = se.v

                @block.sync
                def _(sync, t=t, t0=t0, oc=oc, se_ev=se_ev):
                    sync.wait_ge(se.h, se_ev)
                    if oc < NCH:
                        tinc(sync.dma_start(out=gg_dram[oc, :, t0:t0 + 512], in_=evb[:]), stgg, 16)
                    else:
                        tinc(sync.dma_start(out=z0_dram[oc - NCH, :, t0:t0 + 512], in_=sgf[:]), stz0, 16)
        stz0_all = stz0.v
        stgg_all = stgg.v

        @block.gpsimd
        def _(gpsimd):
            gpsimd.wait_ge(stz0.h, stz0_all)
            gpsimd.wait_ge(ldc.h, ldc.v)
            pid = gpsimd.partition_id()
            slot = pid % 2
            sib = slot if self_exchange else 1 - (pid % 2)
            tinc(gpsimd.dma_start(out=sh_halo[slot, :, :].rearrange("p (c f) -> p c f", f=3),
                                  in_=z0_dram[:, :, TLOC - 3:TLOC].rearrange("c p f -> p c f")), ghl, 16)
            gpsimd.wait_ge(ghl.h, ghl.v)
            tinc(gpsimd.dma_start(out=sh_flag[slot, :][None, :], in_=tag_s[:]), ghl, 16)
            gpsimd.wait_ge(ghl.h, ghl.v)
            poll_flag(gpsimd, sh_flag[sib, :][None, :], flag_s, tag_s, pse1)
            tinc(gpsimd.dma_start(out=halo[:], in_=sh_halo[sib, :, :]), ghl, 16)
            gpsimd.wait_ge(ghl.h, ghl.v)
            gpsimd.sem_inc(gex.h, 1)
            gex.inc(1)
        gex_halo = gex.v

        @block.vector
        def _(vector):
            vector.wait_ge(gex.h, gex_halo)
            tinc(vector.tensor_scalar(halo[:], halo[:], hm[:, 0:1], None, ALU.mult), ve, 1)
        ve_halo = ve.v

        @block.gpsimd
        def _(gpsimd):
            gpsimd.wait_ge(mm.h, mm.v)
            cast_weight(gpsimd, wA, 0, gwT, 2048)
        gw_g = gw.v

        scan_done = []
        for c in range(NCH):
            @block.sync
            def _(sync, c=c):
                if c >= 1:
                    sync.wait_ge(ve.h, ve.v)
                tinc(sync.dma_start(out=zext[:, 3:3 + TLOC], in_=z0_dram[c, :, :]), ldz, 16)
            ld_z = ldz.v

            @block.vector
            def _(vector, c=c, ld_z=ld_z):
                vector.wait_ge(ldz.h, ld_z)
                vector.wait_ge(ve.h, ve_halo)
                vector.wait_ge(stz1.h, stz1.v)
                ch(vector.tensor_copy(zext[:, 0:3], halo[:, c * 3:(c + 1) * 3]), vch, vector)
                ch(vector.tensor_scalar(zacc[:], zext[:, 0:TLOC], cnv[:, 4 * c:4 * c + 1], None, ALU.mult), vch, vector)
                for j in range(1, 4):
                    ch(vector.scalar_tensor_tensor(zacc[:], zext[:, j:j + TLOC], cnv[:, 4 * c + j:4 * c + j + 1],
                                                   zacc[:], ALU.mult, ALU.add), vch, vector)
                tinc(vector.tensor_scalar(zacc[:], zacc[:], cnb[:, c:c + 1], None, ALU.add), ve, 1)
            ve_cv = ve.v

            @block.vector
            def _(vector, c=c, ve_cv=ve_cv):
                vector.wait_ge(ve.h, ve_cv)
                tinc(vector.tensor_copy(wB[:, c * 4096:(c + 1) * 4096], zacc[:]), ve, 1)
            ve_zbf = ve.v

            @block.sync
            def _(sync, c=c, ve_cv=ve_cv):
                sync.wait_ge(ve.h, ve_cv)
                tinc(sync.dma_start(out=z1_dram[c, :, :], in_=zacc[:]), stz1, 16)
        stz1_all = stz1.v
        ve_z1all = ve.v

        for c in range(NCH):
            for half, oc in (("f", c), ("i", c + NCH)):
                for st_ in range(NTT8):
                    pbank = ps_a if half == "f" else ps_b

                    @block.tensor
                    def _(tensor, c=c, oc=oc, st_=st_, pbank=pbank, half=half):
                        if c == 0 and half == "f" and st_ == 0:
                            tensor.wait_ge(gw.h, gw_g)
                            tensor.wait_ge(ve.h, ve_z1all)
                        if not (c == 0 and half == "f" and st_ == 0):
                            tensor.wait_ge(se.h, se.v)
                        for cc in range(NCH):
                            i_ = tensor.matmul(pbank[:], wA[:, cc * 2048 + oc * 128: cc * 2048 + (oc + 1) * 128],
                                               wB[:, cc * 4096 + st_ * 512: cc * 4096 + (st_ + 1) * 512],
                                               start=(cc == 0), stop=(cc == NCH - 1))
                        tinc(i_, mm, 1)
                    mm_fg = mm.v

                    @block.scalar
                    def _(scalar, c=c, oc=oc, st_=st_, pbank=pbank, half=half, mm_fg=mm_fg):
                        scalar.wait_ge(mm.h, mm_fg)
                        if c >= 1 and st_ == 0:
                            scalar.wait_ge(ve.h, scan_done[c - 1])
                            scalar.wait_ge(stz0.h, stz0.v)
                            scalar.wait_ge(stz1.h, stz1.v)
                        if half == "f":
                            ch(scalar.activation(sgf[:], pbank[:], AF.Sigmoid, bias=gbC[:, c:c + 1]), sch, scalar)
                            tinc(scalar.activation(alph[:, st_ * 512:(st_ + 1) * 512], sgf[:],
                                                   AF.Exp, scale=cneg[:, c:c + 1]), se, 1)
                        else:
                            tinc(scalar.activation(xsb[:, st_ * 512:(st_ + 1) * 512], pbank[:],
                                                   AF.Sigmoid, bias=gbC[:, c + NCH:c + NCH + 1]), se, 1)
            se_fg = se.v

            @block.sync
            def _(sync, c=c):
                sync.wait_ge(stz1.h, stz1_all)
                if c >= 1:
                    sync.wait_ge(ve.h, ve.v)
                tinc(sync.dma_start(out=zext[:, 0:TLOC], in_=z1_dram[c, :, :]), ldz1, 16)
            ld_z1 = ldz1.v

            @block.vector
            def _(vector, c=c, se_fg=se_fg, ld_z1=ld_z1):
                vector.wait_ge(se.h, se_fg)
                vector.wait_ge(ldz1.h, ld_z1)
                ch(vector.tensor_tensor(xsb[:], xsb[:], zext[:, 0:TLOC], ALU.mult), vch, vector)
                ch(vector.tensor_tensor(zacc[:], alph[:], alph[:], ALU.mult), vch, vector)
                vector.sem_inc(ve.h, 1)
                ve.inc(1)
            ve_xs1 = ve.v

            @block.scalar
            def _(scalar, c=c, ve_xs1=ve_xs1):
                scalar.wait_ge(ve.h, ve_xs1)
                tinc(scalar.activation(zacc[:], zacc[:], AF.Sqrt, bias=oneb[:, 0:1], scale=-1.0), se, 1)
            se_beta = se.v

            @block.vector
            def _(vector, c=c, se_beta=se_beta):
                vector.wait_ge(se.h, se_beta)
                ch(vector.tensor_tensor(xsb[:], xsb[:], zacc[:], ALU.mult), vch, vector)
                ch(vector.tensor_tensor_scan(xsb[:], alph[:], xsb[:], 0.0, ALU.mult, ALU.add), vch, vector)
                ch(vector.tensor_tensor_scan(alph[:], alph[:], alph[:], 1.0, ALU.mult, ALU.bypass), vch, vector)
                tinc(vector.tensor_copy(myfin[:, c:c + 1], xsb[:, TLOC - 1:TLOC]), ve, 1)
            scan_done.append(ve.v)

            @block.sync
            def _(sync, c=c):
                sync.wait_ge(ve.h, scan_done[-1])
                tinc(sync.dma_start(out=z0_dram[c, :, :], in_=xsb[:]), stz0, 16)
            @block.sync
            def _(sync, c=c):
                tinc(sync.dma_start(out=z1_dram[c, :, :], in_=alph[:]), stz1, 16)
        sth0_all = stz0.v
        stA_all = stz1.v

        @block.gpsimd
        def _(gpsimd):
            gpsimd.wait_ge(ve.h, scan_done[-1])
            pid = gpsimd.partition_id()
            slot = pid % 2
            sib = slot if self_exchange else 1 - (pid % 2)
            tinc(gpsimd.dma_start(out=sh_carry[slot, :, :], in_=myfin[:]), ghl, 16)
            gpsimd.wait_ge(ghl.h, ghl.v)
            tinc(gpsimd.dma_start(out=sh_flag[2 + slot, :][None, :], in_=tag_s[:]), ghl, 16)
            gpsimd.wait_ge(ghl.h, ghl.v)
            poll_flag(gpsimd, sh_flag[2 + sib, :][None, :], flag_s, tag_s, pse2)
            tinc(gpsimd.dma_start(out=carrybuf[:], in_=sh_carry[sib, :, :]), ghl, 16)
            gpsimd.wait_ge(ghl.h, ghl.v)
            gpsimd.sem_inc(gex.h, 1)
            gex.inc(1)
        gex_carry = gex.v

        @block.vector
        def _(vector):
            vector.wait_ge(gex.h, gex_carry)
            tinc(vector.tensor_scalar(carrybuf[:], carrybuf[:], hm[:, 0:1], None, ALU.mult), ve, 1)
        ve_carry = ve.v

        for c in range(NCH):
            @block.sync
            def _(sync, c=c):
                sync.wait_ge(stz0.h, sth0_all)
                sync.wait_ge(stz1.h, stA_all)
                if c >= 1:
                    sync.wait_ge(ve.h, ve.v)
                tinc(sync.dma_start(out=zext[:, 0:TLOC], in_=z0_dram[c, :, :]), ldz, 16)
                tinc(sync.dma_start(out=zacc[:], in_=z1_dram[c, :, :]), ldz, 16)
            ld_h = ldz.v

            @block.scalar
            def _(scalar, c=c):
                if c >= 1:
                    scalar.wait_ge(ve.h, ve.v)
                tinc(scalar.dma_start(out=rfm[:, 0:TLOC // 8] if False else alph[:],
                                      in_=gg_dram[c, :, :].rearrange("p f -> p f")), ldz1, 16) if False else None
            @block.sync
            def _(sync, c=c):
                tinc(sync.dma_start(out=wA[:, 0:TLOC].rearrange("p f -> p f") if False else lnt[0][:, 0:D],
                                    in_=gg_dram[c, :, 0:0 + D]), ldz1, 16) if False else None

            @block.sync
            def _(sync, c=c):
                tinc(sync.dma_start(out=g2buf[:, 0:TLOC], in_=gg_dram[c, :, :]), ldz1, 16)
            ld_gg = ldz1.v

            @block.vector
            def _(vector, c=c, ld_h=ld_h, ld_gg=ld_gg):
                vector.wait_ge(ldz.h, ld_h)
                vector.wait_ge(ldz1.h, ld_gg)
                vector.wait_ge(ve.h, ve_carry)
                ch(vector.scalar_tensor_tensor(zext[:, 0:TLOC], zacc[:], carrybuf[:, c:c + 1],
                                               zext[:, 0:TLOC], ALU.mult, ALU.add), vch, vector)
                tinc(vector.tensor_tensor(wB[:, c * 4096:(c + 1) * 4096], g2buf[:, 0:TLOC],
                                          zext[:, 0:TLOC], ALU.mult), ve, 1)
        ve_hg = ve.v

        @block.gpsimd
        def _(gpsimd):
            gpsimd.wait_ge(mm.h, mm.v)
            cast_weight(gpsimd, wA, 0, outT, 1024)
        gw_o = gw.v

        for t in range(NTT8):
            t0 = t * 512
            for oc in range(NCH):
                pbank = ps_a if oc % 2 == 0 else ps_b

                @block.tensor
                def _(tensor, t=t, oc=oc, pbank=pbank):
                    if t == 0 and oc == 0:
                        tensor.wait_ge(gw.h, gw_o)
                        tensor.wait_ge(ve.h, ve_hg)
                    if not (t == 0 and oc < 2):
                        tensor.wait_ge(se.h, se.v - 1)
                    for cc in range(NCH):
                        i_ = tensor.matmul(pbank[:], wA[:, cc * 1024 + oc * 128: cc * 1024 + (oc + 1) * 128],
                                           wB[:, cc * 4096 + t * 512:(cc * 4096) + (t + 1) * 512],
                                           start=(cc == 0), stop=(cc == NCH - 1))
                    tinc(i_, mm, 1)
                mm_ho = mm.v

                @block.scalar
                def _(scalar, t=t, oc=oc, pbank=pbank, mm_ho=mm_ho):
                    scalar.wait_ge(mm.h, mm_ho)
                    scalar.wait_ge(stho.h, stho.v)
                    tinc(scalar.activation(evb[:], pbank[:], AF.Copy), se, 1)
                se_ho = se.v

                @block.sync
                def _(sync, t=t, t0=t0, oc=oc, se_ho=se_ho):
                    sync.wait_ge(se.h, se_ho)
                    tinc(sync.dma_start(out=hout_dram[oc * 128:(oc + 1) * 128, t0:t0 + 512], in_=evb[:]), stho, 16)
        stho_all = stho.v

        for i in range(NTI):
            b = i % 2

            @block.sync
            def _(sync, i=i, b=b):
                sync.wait_ge(stho.h, stho_all)
                sync.wait_ge(stl[0].h, stl_all[0])
                sync.wait_ge(stl[1].h, stl_all[1])
                sync.wait_ge(strm2[b].h, strm2[b].v)
                sync.wait_ge(ldy[b].h, ldy[b].v)
                tinc(sync.dma_start(out=fl[b][:], in_=ln_dram[i * 128:(i + 1) * 128, :]), ldy[b], 16)
            ld_y = ldy[b].v

            @block.scalar
            def _(scalar, i=i, b=b):
                scalar.wait_ge(strm2[b].h, strm2[b].v)
                tinc(scalar.dma_start(out=fh[b][:], in_=hout_dram[:, i * 128:(i + 1) * 128],
                                      transpose=True), trf[b], 16)
            tr_y = trf[b].v

            @block.vector
            def _(vector, i=i, b=b, ld_y=ld_y, tr_y=tr_y):
                vector.wait_ge(ldy[b].h, ld_y)
                vector.wait_ge(trf[b].h, tr_y)
                vector.wait_ge(ve.h, ve.v)
                ch(vector.tensor_tensor(lnf[:], fl[b][:], fh[b][:], ALU.add), vch, vector)
                ch(vector.tensor_tensor(sq[:], lnf[:], lnf[:], ALU.mult), vch, vector)
                tinc(vector.reduce_sum(stat[b][:, 6:7], sq[:], axis=AXX), ve, 1)
            ve_y = ve.v

            @block.scalar
            def _(scalar, i=i, b=b, ve_y=ve_y):
                scalar.wait_ge(ve.h, ve_y)
                tinc(scalar.activation(stat[b][:, 6:7], stat[b][:, 6:7], AF.Sqrt), se, 1)
            se_y = se.v

            @block.vector
            def _(vector, i=i, b=b, se_y=se_y):
                vector.wait_ge(se.h, se_y)
                ch(vector.reciprocal(stat[b][:, 6:7], stat[b][:, 6:7]), vch, vector)
                tinc(vector.tensor_scalar(rmt[b][:], lnf[:], stat[b][:, 6:7], None, ALU.mult), ve, 1)
            ve_r2 = ve.v

            @block.sync
            def _(sync, i=i, b=b, ve_r2=ve_r2):
                sync.wait_ge(ve.h, ve_r2)
                tinc(sync.dma_start(out=rms2_dram[i * 128:(i + 1) * 128, :], in_=rmt[b][:]), strm2[b], 16)
        strm2_all = (strm2[0].v, strm2[1].v)

        @block.gpsimd
        def _(gpsimd):
            gpsimd.wait_ge(mm.h, mm.v)
            gpsimd.wait_ge(ve.h, ve.v)
            cast_weight(gpsimd, wB, 0, growT, 4096)
            cast_weight(gpsimd, wA, 0, shrT, 1024)
            for cc in range(NCH, 16):
                tinc(gpsimd.dma_start(out=wA[:, cc * 1024:(cc + 1) * 1024],
                                      in_=shrT[cc * 128:(cc + 1) * 128, :]), gw, 16)
        gw_g2 = gw.v

        @block.vector
        def _(vector):
            vector.wait_ge(gw.h, gw_g2)
            for cc in range(NCH):
                ii = vector.tensor_scalar(wB[:, cc * 4096:(cc + 1) * 4096],
                                          wB[:, cc * 4096:(cc + 1) * 4096],
                                          gamc[:, 8 + cc:9 + cc], None, ALU.mult)
                ch(ii, vch, vector)
            vector.sem_inc(ve.h, 1)
            ve.inc(1)
        ve_wg2 = ve.v

        for t in range(NTT8):
            t0 = t * 512

            @block.scalar
            def _(scalar, t=t, t0=t0):
                scalar.wait_ge(strm2[0].h, strm2_all[0])
                scalar.wait_ge(strm2[1].h, strm2_all[1])
                if t >= 1:
                    scalar.wait_ge(mm.h, mm.v)
                for c in range(NCH):
                    tinc(scalar.dma_start(out=rfm[:, c * 512:(c + 1) * 512],
                                          in_=rms2_dram[t0:t0 + 512, c * 128:(c + 1) * 128],
                                          transpose=True), trn[c], 16)

            for ocp in range(16):
                @block.tensor
                def _(tensor, t=t, ocp=ocp):
                    if ocp == 0:
                        for c in range(NCH):
                            tensor.wait_ge(trn[c].h, trn[c].v)
                        tensor.wait_ge(ve.h, ve_wg2)
                    else:
                        tensor.wait_ge(se.h, se.v)
                        tensor.wait_ge(ve.h, ve.v)
                    for c in range(NCH):
                        tensor.matmul(ps_a[:], wB[:, c * 4096 + ocp * 128: c * 4096 + (ocp + 1) * 128],
                                      rfm[:, c * 512:(c + 1) * 512], start=(c == 0), stop=(c == NCH - 1))
                    for c in range(NCH):
                        i_ = tensor.matmul(ps_b[:], wB[:, c * 4096 + (16 + ocp) * 128: c * 4096 + (17 + ocp) * 128],
                                           rfm[:, c * 512:(c + 1) * 512], start=(c == 0), stop=(c == NCH - 1))
                    tinc(i_, mm, 1)
                mm_g2 = mm.v

                @block.scalar
                def _(scalar, t=t, ocp=ocp, mm_g2=mm_g2):
                    scalar.wait_ge(mm.h, mm_g2)
                    if t >= 1 and ocp == 0:
                        scalar.wait_ge(mm.h, mm_g2)
                    tinc(scalar.activation(g2buf[:, ocp * 512:(ocp + 1) * 512], ps_a[:], AF.Gelu), se, 1)
                se_gg2 = se.v

                @block.vector
                def _(vector, t=t, ocp=ocp, se_gg2=se_gg2):
                    vector.wait_ge(se.h, se_gg2)
                    tinc(vector.tensor_tensor(g2buf[:, ocp * 512:(ocp + 1) * 512],
                                              g2buf[:, ocp * 512:(ocp + 1) * 512],
                                              ps_b[:], ALU.mult), ve, 1)
            ve_m2 = ve.v

            for od in range(NCH):
                pbank = ps_a if od % 2 == 0 else ps_b

                @block.tensor
                def _(tensor, t=t, od=od, pbank=pbank, ve_m2=ve_m2):
                    tensor.wait_ge(ve.h, ve_m2)
                    if od >= 2:
                        tensor.wait_ge(se.h, se.v - 1)
                    for fc in range(16):
                        i_ = tensor.matmul(pbank[:], wA[:, fc * 1024 + od * 128: fc * 1024 + (od + 1) * 128],
                                           g2buf[:, fc * 512:(fc + 1) * 512], start=(fc == 0), stop=(fc == 15))
                    tinc(i_, mm, 1)
                mm_sh = mm.v

                @block.scalar
                def _(scalar, t=t, od=od, pbank=pbank, mm_sh=mm_sh):
                    scalar.wait_ge(mm.h, mm_sh)
                    scalar.wait_ge(stgo.h, stgo.v)
                    tinc(scalar.activation(evb[:], pbank[:], AF.Copy), se, 1)
                se_go = se.v

                @block.sync
                def _(sync, t=t, t0=t0, od=od, se_go=se_go):
                    sync.wait_ge(se.h, se_go)
                    tinc(sync.dma_start(out=gout_dram[od * 128:(od + 1) * 128, t0:t0 + 512], in_=evb[:]), stgo, 16)
        stgo_all = stgo.v

        for i in range(NTI):
            b = i % 2

            @block.sync
            def _(sync, i=i, b=b):
                sync.wait_ge(stgo.h, stgo_all)
                sync.wait_ge(stx[b].h, stx[b].v)
                tinc(sync.dma_start(out=fx[b][:], in_=x2[i * 128:(i + 1) * 128, :]), ldf[b], 16)
                tinc(sync.dma_start(out=fl[b][:], in_=ln_dram[i * 128:(i + 1) * 128, :]), ldf[b], 16)
            ld_f = ldf[b].v

            @block.scalar
            def _(scalar, i=i, b=b):
                scalar.wait_ge(stgo.h, stgo_all)
                scalar.wait_ge(stx[b].h, stx[b].v)
                scalar.wait_ge(trf[b].h, trf[b].v)
                tinc(scalar.dma_start(out=fh[b][:], in_=hout_dram[:, i * 128:(i + 1) * 128],
                                      transpose=True), trf[b], 16)
                tinc(scalar.dma_start(out=fg2[b][:], in_=gout_dram[:, i * 128:(i + 1) * 128],
                                      transpose=True), trf[b], 16)
            tr_f = trf[b].v

            @block.vector
            def _(vector, i=i, b=b, ld_f=ld_f, tr_f=tr_f):
                vector.wait_ge(ldf[b].h, ld_f)
                vector.wait_ge(trf[b].h, tr_f)
                ch(vector.tensor_tensor(fx[b][:], fx[b][:], fl[b][:], ALU.add), vch, vector)
                ch(vector.tensor_tensor(fx[b][:], fx[b][:], fh[b][:], ALU.add), vch, vector)
                tinc(vector.tensor_tensor(fx[b][:], fx[b][:], fg2[b][:], ALU.add), ve, 1)
            ve_f = ve.v

            @block.sync
            def _(sync, i=i, b=b, ve_f=ve_f):
                sync.wait_ge(ve.h, ve_f)
                tinc(sync.dma_start(out=x3[i * 128:(i + 1) * 128, :], in_=fx[b][:]), stx[b], 16)

        @block.sync
        def _(sync):
            sync.wait_ge(stx[0].h, stx[0].v)
            sync.wait_ge(stx[1].h, stx[1].v)

    return nc



LNC_FEAT = float(np.log(1.0 / np.sqrt(128.0)))


def build_E(detect=True, self_exchange=False):
    nc = bass.Bass("TRN2", target_bir_lowering=False, debug=False,
                   detect_race_conditions=detect, dynamic_dma_scratch_size=8192)
    x4 = nc.dram_tensor("x4", [TLOC, D], F32, kind="ExternalInput").ap()
    wq = nc.dram_tensor("wq", [D, D], F32, kind="ExternalInput").ap()
    wk = nc.dram_tensor("wk", [D, D], F32, kind="ExternalInput").ap()
    wv = nc.dram_tensor("wv", [D, D], F32, kind="ExternalInput").ap()
    wo = nc.dram_tensor("wo", [D, D], F32, kind="ExternalInput").ap()
    wf = nc.dram_tensor("wf", [64, 128], F32, kind="ExternalInput").ap()
    trimask = nc.dram_tensor("trimask", [128, 128], F32, kind="ExternalInput").ap()
    hmask = nc.dram_tensor("hmask", [128, 1], F32, kind="ExternalInput").ap()
    tag = nc.dram_tensor("tag", [1, 16], I32, kind="ExternalInput").ap()
    x5 = nc.dram_tensor("x5", [TLOC, D], F32, kind="ExternalOutput").ap()

    ln_dram = nc.dram_tensor("ln_dram", [TLOC, D], BF16, kind="Internal").ap()
    q_dram = nc.dram_tensor("q_dram", [D, TLOC], BF16, kind="Internal").ap()
    k_dram = nc.dram_tensor("k_dram", [D, TLOC], BF16, kind="Internal").ap()
    v_dram = nc.dram_tensor("v_dram", [TLOC, D], BF16, kind="Internal").ap()
    qp_dram = nc.dram_tensor("qp_dram", [16, 128, TLOC], BF16, kind="Internal").ap()
    ye_dram = nc.dram_tensor("ye_dram", [16, 65, TLOC], F32, kind="Internal").ap()
    sh_S = nc.dram_tensor("sh_S", [2, 128, 16 * 65], F32, kind="Internal", addr_space="Shared").ap()
    sh_flag = nc.dram_tensor("sh_flag", [2, 16], I32, kind="Internal", addr_space="Shared").ap()

    NTI = TLOC // 128
    NTT8 = TLOC // 512
    NCK = 32

    with contextlib.ExitStack() as ctx:
        E = ctx.enter_context
        sq = E(nc.sbuf_tensor("sq", [128, D], F32))
        lnf = E(nc.sbuf_tensor("lnf", [128, D], F32))
        stat = [E(nc.sbuf_tensor(f"stat{i}", [128, 8], F32)) for i in range(2)]
        epsb = E(nc.sbuf_tensor("epsb", [128, 1], F32))
        lncb = E(nc.sbuf_tensor("lncb", [128, 1], F32))
        eps6 = E(nc.sbuf_tensor("eps6", [128, 1], F32))
        xb = [E(nc.sbuf_tensor(f"xb{i}", [128, D], F32)) for i in range(2)]
        lnt = [E(nc.sbuf_tensor(f"lnt{i}", [128, D], BF16)) for i in range(2)]
        wA = E(nc.sbuf_tensor("wA", [128, NCH * 1024], BF16))
        rfm = E(nc.sbuf_tensor("rfm", [128, NCH * 512], BF16))
        evb = E(nc.sbuf_tensor("evb", [128, 512], BF16))
        wf_s = E(nc.sbuf_tensor("wf_s", [64, 128], BF16))
        nhalf = E(nc.sbuf_tensor("nhalf", [64, 128], BF16))
        tri_s = E(nc.sbuf_tensor("tri_s", [128, 128], F32))
        hm = E(nc.sbuf_tensor("hm", [128, 1], F32))
        tag_s = E(nc.sbuf_tensor("tag_s", [1, 16], I32))
        flag_s = E(nc.sbuf_tensor("flag_s", [1, 16], I32))
        qh_s = E(nc.sbuf_tensor("qh_s", [64, TLOC], BF16))
        kh_s = E(nc.sbuf_tensor("kh_s", [64, TLOC], BF16))
        vh_s = E(nc.sbuf_tensor("vh_s", [128, NCK * 65], BF16))
        qsq = E(nc.sbuf_tensor("qsq", [64, TLOC], BF16))
        ksq = E(nc.sbuf_tensor("ksq", [64, TLOC], BF16))
        qp_s = E(nc.sbuf_tensor("qp_s", [128, 128], BF16))
        kp_s = E(nc.sbuf_tensor("kp_s", [128, 128], BF16))
        kpT_s = E(nc.sbuf_tensor("kpT_s", [128, 128], BF16))
        gm_s = E(nc.sbuf_tensor("gm_s", [128, 128], BF16))
        S_f = E(nc.sbuf_tensor("S_f", [128, 16 * 65], F32))
        S_bf = E(nc.sbuf_tensor("S_bf", [128, 65], BF16))
        Ssib = E(nc.sbuf_tensor("Ssib", [128, 16 * 65], F32))
        Ssib_bf = E(nc.sbuf_tensor("Ssib_bf", [128, 65], BF16))
        yext = E(nc.sbuf_tensor("yext", [128, 128], F32))
        den_s = E(nc.sbuf_tensor("den_s", [1, 128], F32))
        onesE_s = E(nc.sbuf_tensor("onesE_s", [1, 64], F32))
        bc_s = E(nc.sbuf_tensor("bc_s", [64, 128], F32))
        yfm = E(nc.sbuf_tensor("yfm", [128, NCH * TLOC], BF16))
        fx = [E(nc.sbuf_tensor(f"fx{i}", [128, D], F32)) for i in range(2)]

        ps_q = E(nc.psum_tensor("ps_q", [128, 512], F32))
        ps_k = E(nc.psum_tensor("ps_k", [128, 512], F32))
        ps_g = E(nc.psum_tensor("ps_g", [128, 128], F32))
        ps_y = E(nc.psum_tensor("ps_y", [128, 128], F32))
        ps_s = E(nc.psum_tensor("ps_s", [128, 128], F32))
        ps_t = E(nc.psum_tensor("ps_t", [64, 128], F32))

        vch = Sem(ctx, nc, "vch")
        sch = Sem(ctx, nc, "sch")
        ldc = Sem(ctx, nc, "ldc")
        gw = Sem(ctx, nc, "gw")
        ldx = [Sem(ctx, nc, f"ldx{i}") for i in range(2)]
        stl = [Sem(ctx, nc, f"stl{i}") for i in range(2)]
        trn = [Sem(ctx, nc, f"trn{i}") for i in range(NCH)]
        stq = Sem(ctx, nc, "stq")
        stk = Sem(ctx, nc, "stk")
        stv = Sem(ctx, nc, "stv")
        ldh = Sem(ctx, nc, "ldh")
        stqp = Sem(ctx, nc, "stqp")
        stye = Sem(ctx, nc, "stye")
        ldqp = Sem(ctx, nc, "ldqp")
        ldye = Sem(ctx, nc, "ldye")
        ghl = Sem(ctx, nc, "ghl")
        gex = Sem(ctx, nc, "gex")
        pse1 = Sem(ctx, nc, "pse1")
        ldf = [Sem(ctx, nc, f"ldf{i}") for i in range(2)]
        stx = [Sem(ctx, nc, f"stx{i}") for i in range(2)]
        mm = Sem(ctx, nc, "mm")
        ve = Sem(ctx, nc, "ve")
        se = Sem(ctx, nc, "se")

        block = E(nc.Block())

        @block.sync
        def _(sync):
            tinc(sync.dma_start(out=tri_s[:], in_=trimask), ldc, 16)
            tinc(sync.dma_start(out=hm[:], in_=hmask), ldc, 16)
            tinc(sync.dma_start(out=tag_s[:], in_=tag), ldc, 16)
            sync.wait_ge(ldc.h, ldc.v)

        @block.gpsimd
        def _(gpsimd):
            tinc(gpsimd.dma_start(out=wf_s[:], in_=wf), gw, 16)

        @block.vector
        def _(vector):
            vector.memset(epsb[:], 1e-5)
            vector.memset(lncb[:], LNC_FEAT)
            vector.memset(eps6[:], 1e-6)
            vector.memset(nhalf[:], -0.5)
            vector.memset(vh_s[:], 1.0)
            vector.memset(onesE_s[:], 1.0)
            tinc(vector.memset(S_f[:], 0.0), ve, 1)
        ve_init = ve.v

        tile_done = []
        se_tile = []
        for i in range(NTI):
            b = i % 2

            @block.sync
            def _(sync, i=i, b=b):
                if i >= 2:
                    sync.wait_ge(ve.h, tile_done[i - 2])
                    sync.wait_ge(se.h, se_tile[i - 2])
                tinc(sync.dma_start(out=xb[b][:], in_=x4[i * 128:(i + 1) * 128, :]), ldx[b], 16)
            ld_i = ldx[b].v

            @block.scalar
            def _(scalar, i=i, b=b, ld_i=ld_i):
                scalar.wait_ge(ldx[b].h, ld_i)
                if i >= 1:
                    scalar.wait_ge(sch.h, sch.v)
                if i >= 2:
                    scalar.wait_ge(ve.h, tile_done[i - 2])
                ch(scalar.activation(sq[:], xb[b][:], AF.Copy, accum_out=stat[b][:, 0:1]), sch, scalar)
                tinc(scalar.activation(sq[:], xb[b][:], AF.Square, accum_out=stat[b][:, 1:2]), se, 1)
            se_sums = se.v

            @block.vector
            def _(vector, i=i, b=b, se_sums=se_sums):
                vector.wait_ge(se.h, se_sums)
                ch(vector.tensor_scalar(stat[b][:, 2:3], stat[b][:, 0:1], 1.0 / D, None, ALU.mult), vch, vector)
                ch(vector.tensor_tensor(stat[b][:, 3:4], stat[b][:, 2:3], stat[b][:, 2:3], ALU.mult), vch, vector)
                ch(vector.tensor_scalar(stat[b][:, 5:6], stat[b][:, 1:2], 1.0 / D, None, ALU.mult), vch, vector)
                tinc(vector.tensor_tensor(stat[b][:, 3:4], stat[b][:, 5:6], stat[b][:, 3:4], ALU.subtract), ve, 1)
            ve_var = ve.v

            @block.scalar
            def _(scalar, i=i, b=b, ve_var=ve_var):
                scalar.wait_ge(ve.h, ve_var)
                tinc(scalar.activation(stat[b][:, 4:5], stat[b][:, 3:4], AF.Sqrt, bias=epsb[:, 0:1]), se, 1)
            se_tile.append(se.v)
            se_sqrt = se.v

            @block.vector
            def _(vector, i=i, b=b, se_sqrt=se_sqrt):
                vector.wait_ge(se.h, se_sqrt)
                ch(vector.reciprocal(stat[b][:, 4:5], stat[b][:, 4:5]), vch, vector)
                vector.wait_ge(ve.h, ve.v)
                ch(vector.tensor_scalar(lnf[:], xb[b][:], stat[b][:, 2:3], stat[b][:, 4:5],
                                        ALU.subtract, ALU.mult), vch, vector)
                vector.wait_ge(stl[b].h, stl[b].v)
                tinc(vector.tensor_copy(lnt[b][:], lnf[:]), ve, 1)
            tile_done.append(ve.v)
            ve_ln = ve.v

            @block.sync
            def _(sync, i=i, b=b, ve_ln=ve_ln):
                sync.wait_ge(ve.h, ve_ln)
                tinc(sync.dma_start(out=ln_dram[i * 128:(i + 1) * 128, :], in_=lnt[b][:]), stl[b], 16)
        stl_all = (stl[0].v, stl[1].v)

        def proj_fm(wsrc, dst_dram, stdst):
            mm_before = mm.v

            @block.gpsimd
            def _(gpsimd, mm_before=mm_before):
                gpsimd.wait_ge(mm.h, mm_before)
                for cc in range(NCH):
                    tinc(gpsimd.dma_start(out=wA[:, cc * 1024:(cc + 1) * 1024],
                                          in_=wsrc[cc * 128:(cc + 1) * 128, :]), gw, 16)
            gw_w = gw.v

            for t in range(NTT8):
                t0 = t * 512

                @block.scalar
                def _(scalar, t=t, t0=t0):
                    scalar.wait_ge(stl[0].h, stl_all[0])
                    scalar.wait_ge(stl[1].h, stl_all[1])
                    if t >= 1:
                        scalar.wait_ge(mm.h, mm.v)
                    for c in range(NCH):
                        tinc(scalar.dma_start(out=rfm[:, c * 512:(c + 1) * 512],
                                              in_=ln_dram[t0:t0 + 512, c * 128:(c + 1) * 128],
                                              transpose=True), trn[c], 16)

                for oc in range(NCH):
                    pbank = ps_q if oc % 2 == 0 else ps_k

                    @block.tensor
                    def _(tensor, t=t, oc=oc, pbank=pbank, gw_w=gw_w):
                        if oc == 0:
                            for c in range(NCH):
                                tensor.wait_ge(trn[c].h, trn[c].v)
                            tensor.wait_ge(gw.h, gw_w)
                        if not (t == 0 and oc < 2):
                            tensor.wait_ge(se.h, se.v - 1)
                        for c in range(NCH):
                            i_ = tensor.matmul(pbank[:], wA[:, c * 1024 + oc * 128: c * 1024 + (oc + 1) * 128],
                                               rfm[:, c * 512:(c + 1) * 512], start=(c == 0), stop=(c == NCH - 1))
                        tinc(i_, mm, 1)
                    mm_p = mm.v

                    @block.scalar
                    def _(scalar, t=t, oc=oc, pbank=pbank, mm_p=mm_p,
                          g=(stq.v, stk.v, stv.v)):
                        scalar.wait_ge(mm.h, mm_p)
                        scalar.wait_ge(stdst.h, stdst.v)
                        scalar.wait_ge(stq.h, g[0])
                        scalar.wait_ge(stk.h, g[1])
                        scalar.wait_ge(stv.h, g[2])
                        tinc(scalar.activation(evb[:], pbank[:], AF.Copy), se, 1)
                    se_p = se.v

                    @block.sync
                    def _(sync, t=t, t0=t0, oc=oc, se_p=se_p):
                        sync.wait_ge(se.h, se_p)
                        tinc(sync.dma_start(out=dst_dram[oc * 128:(oc + 1) * 128, t0:t0 + 512], in_=evb[:]), stdst, 16)

        proj_fm(wq, q_dram, stq)
        proj_fm(wk, k_dram, stk)

        mm_before_v = mm.v

        @block.gpsimd
        def _(gpsimd, mm_before_v=mm_before_v):
            gpsimd.wait_ge(mm.h, mm_before_v)
            for cc in range(NCH):
                tinc(gpsimd.dma_start(out=wA[:, cc * 1024:(cc + 1) * 1024],
                                      in_=wv[cc * 128:(cc + 1) * 128, :]), gw, 16)
        gw_v = gw.v

        for t in range(NTT8):
            t0 = t * 512

            @block.scalar
            def _(scalar, t=t, t0=t0):
                if t >= 1:
                    scalar.wait_ge(mm.h, mm.v)
                for c in range(NCH):
                    tinc(scalar.dma_start(out=rfm[:, c * 512:(c + 1) * 512],
                                          in_=ln_dram[t0:t0 + 512, c * 128:(c + 1) * 128],
                                          transpose=True), trn[c], 16)

            for sub in range(4):
                for half in range(2):
                    pbank = ps_q if half == 0 else ps_k

                    @block.tensor
                    def _(tensor, t=t, sub=sub, half=half, pbank=pbank, gw_v=gw_v):
                        if sub == 0 and half == 0:
                            for c in range(NCH):
                                tensor.wait_ge(trn[c].h, trn[c].v)
                            tensor.wait_ge(gw.h, gw_v)
                        if not (t == 0 and sub == 0):
                            tensor.wait_ge(se.h, se.v - 1)
                        for c in range(NCH):
                            i_ = tensor.matmul(pbank[:], rfm[:, c * 512 + sub * 128: c * 512 + (sub + 1) * 128],
                                               wA[:, c * 1024 + half * 512:(c * 1024) + (half + 1) * 512],
                                               start=(c == 0), stop=(c == NCH - 1))
                        tinc(i_, mm, 1)
                    mm_v = mm.v

                    @block.scalar
                    def _(scalar, t=t, sub=sub, half=half, pbank=pbank, mm_v=mm_v,
                          g=(stq.v, stk.v)):
                        scalar.wait_ge(mm.h, mm_v)
                        scalar.wait_ge(stv.h, stv.v)
                        scalar.wait_ge(stq.h, g[0])
                        scalar.wait_ge(stk.h, g[1])
                        tinc(scalar.activation(evb[:], pbank[:], AF.Copy), se, 1)
                    se_v = se.v

                    @block.sync
                    def _(sync, t=t, t0=t0, sub=sub, half=half, se_v=se_v):
                        sync.wait_ge(se.h, se_v)
                        tinc(sync.dma_start(
                            out=v_dram[t0 + sub * 128: t0 + (sub + 1) * 128, half * 512:(half + 1) * 512],
                            in_=evb[:]), stv, 16)
        stv_all = stv.v
        stq_all = stq.v
        stk_all = stk.v

        for h in range(16):
            hc = h // 2
            hb = 64 * (h % 2)

            @block.sync
            def _(sync, h=h, hc=hc, hb=hb):
                sync.wait_ge(stq.h, stq_all)
                sync.wait_ge(stk.h, stk_all)
                sync.wait_ge(stv.h, stv_all)
                if h >= 1:
                    sync.wait_ge(se.h, se.v)
                    sync.wait_ge(mm.h, mm.v)
                tinc(sync.dma_start(out=qh_s[:], in_=q_dram[64 * h:64 * (h + 1), :]), ldh, 16)
                tinc(sync.dma_start(out=kh_s[:], in_=k_dram[64 * h:64 * (h + 1), :]), ldh, 16)
                tinc(sync.dma_start(out=vh_s[:].rearrange("p (i e) -> p i e", e=65)[:, :, 0:64],
                                    in_=v_dram.rearrange("(i p) d -> p i d", p=128)[:, :, 64 * h:64 * (h + 1)]),
                     ldh, 16)
            ld_h = ldh.v

            @block.scalar
            def _(scalar, h=h, ld_h=ld_h):
                scalar.wait_ge(ldh.h, ld_h)
                ch(scalar.activation(qsq[:], qh_s[:], AF.Square), sch, scalar)
                tinc(scalar.activation(ksq[:], kh_s[:], AF.Square), se, 1)
            se_sq = se.v

            for ck in range(NCK):
                c0 = ck * 128

                @block.tensor
                def _(tensor, h=h, ck=ck, c0=c0, se_sq=se_sq):
                    tensor.wait_ge(se.h, se_sq)
                    if ck >= 1 or h >= 1:
                        tensor.wait_ge(se.h, se.v)
                        tensor.wait_ge(ve.h, ve.v)
                    tensor.matmul(ps_q[:, 0:128], wf_s[:], qh_s[:, c0:c0 + 128], start=True, stop=False)
                    tensor.matmul(ps_q[:, 0:128], nhalf[:], qsq[:, c0:c0 + 128], start=False, stop=True)
                    tensor.matmul(ps_k[:, 0:128], wf_s[:], kh_s[:, c0:c0 + 128], start=True, stop=False)
                    tensor.matmul(ps_k[:, 0:128], nhalf[:], ksq[:, c0:c0 + 128], start=False, stop=True)
                    tensor.matmul(ps_s[:], kh_s[:, c0:c0 + 128], wf_s[:], start=True, stop=False)
                    tinc(tensor.matmul(ps_s[:], ksq[:, c0:c0 + 128], nhalf[:], start=False, stop=True), mm, 1)
                mm_f = mm.v

                @block.scalar
                def _(scalar, h=h, ck=ck, mm_f=mm_f):
                    scalar.wait_ge(mm.h, mm_f)
                    scalar.wait_ge(ve.h, ve.v)
                    scalar.wait_ge(stqp.h, stqp.v)
                    ch(scalar.activation(qp_s[:], ps_q[:, 0:128], AF.Exp, bias=lncb[:, 0:1]), sch, scalar)
                    ch(scalar.activation(kp_s[:], ps_k[:, 0:128], AF.Exp, bias=lncb[:, 0:1]), sch, scalar)
                    tinc(scalar.activation(kpT_s[:], ps_s[:], AF.Exp, bias=lncb[:, 0:1]), se, 1)
                se_feat = se.v

                @block.sync
                def _(sync, h=h, ck=ck, c0=c0, se_feat=se_feat):
                    sync.wait_ge(se.h, se_feat)
                    tinc(sync.dma_start(out=qp_dram[h, :, c0:c0 + 128], in_=qp_s[:]), stqp, 16)

                @block.tensor
                def _(tensor, h=h, ck=ck, c0=c0, se_feat=se_feat):
                    tensor.wait_ge(se.h, se_feat)
                    tinc(tensor.matmul(ps_g[:], kp_s[:], qp_s[:], start=True, stop=True), mm, 1)
                mm_g = mm.v

                @block.vector
                def _(vector, h=h, ck=ck, mm_g=mm_g):
                    vector.wait_ge(mm.h, mm_g)
                    tinc(vector.tensor_tensor(gm_s[:], ps_g[:], tri_s[:], ALU.mult), ve, 1)
                ve_gm = ve.v

                @block.vector
                def _(vector, h=h, ck=ck, ve_gm=ve_gm):
                    vector.wait_ge(ve.h, ve_gm)
                    tinc(vector.tensor_copy(S_bf[:], S_f[:, h * 65:(h + 1) * 65]), ve, 1)
                ve_sbf = ve.v

                @block.tensor
                def _(tensor, h=h, ck=ck, c0=c0, ve_gm=ve_gm, ve_sbf=ve_sbf):
                    tensor.wait_ge(ve.h, ve_sbf)
                    tensor.matmul(ps_y[0:65, :], vh_s[:, ck * 65:(ck + 1) * 65], gm_s[:], start=True, stop=False)
                    tinc(tensor.matmul(ps_y[0:65, :], S_bf[:], qp_s[:], start=False, stop=True), mm, 1)
                    tinc(tensor.matmul(ps_s[:, 0:65], kpT_s[:], vh_s[:, ck * 65:(ck + 1) * 65],
                                       start=True, stop=True), mm, 1)
                mm_y = mm.v

                @block.scalar
                def _(scalar, h=h, ck=ck, c0=c0, mm_y=mm_y):
                    scalar.wait_ge(mm.h, mm_y)
                    scalar.wait_ge(stye.h, stye.v)
                    tinc(scalar.activation(yext[0:65, :], ps_y[0:65, :], AF.Copy), se, 1)
                se_ye = se.v

                @block.sync
                def _(sync, h=h, ck=ck, c0=c0, se_ye=se_ye):
                    sync.wait_ge(se.h, se_ye)
                    tinc(sync.dma_start(out=ye_dram[h, :, c0:c0 + 128], in_=yext[0:65, :]), stye, 16)

                @block.vector
                def _(vector, h=h, ck=ck, mm_y=mm_y):
                    vector.wait_ge(mm.h, mm_y)
                    tinc(vector.tensor_tensor(S_f[:, h * 65:(h + 1) * 65], S_f[:, h * 65:(h + 1) * 65],
                                              ps_s[:, 0:65], ALU.add), ve, 1)

        ve_Sall = ve.v

        @block.gpsimd
        def _(gpsimd):
            gpsimd.wait_ge(ve.h, ve_Sall)
            gpsimd.wait_ge(ldc.h, ldc.v)
            pid = gpsimd.partition_id()
            slot = pid % 2
            sib = slot if self_exchange else 1 - (pid % 2)
            tinc(gpsimd.dma_start(out=sh_S[slot, :, :], in_=S_f[:]), ghl, 16)
            gpsimd.wait_ge(ghl.h, ghl.v)
            tinc(gpsimd.dma_start(out=sh_flag[slot, :][None, :], in_=tag_s[:]), ghl, 16)
            gpsimd.wait_ge(ghl.h, ghl.v)
            poll_flag(gpsimd, sh_flag[sib, :][None, :], flag_s, tag_s, pse1)
            tinc(gpsimd.dma_start(out=Ssib[:], in_=sh_S[sib, :, :]), ghl, 16)
            gpsimd.wait_ge(ghl.h, ghl.v)
            gpsimd.sem_inc(gex.h, 1)
            gex.inc(1)
        gex_S = gex.v

        @block.vector
        def _(vector):
            vector.wait_ge(gex.h, gex_S)
            tinc(vector.tensor_scalar(Ssib[:], Ssib[:], hm[:, 0:1], None, ALU.mult), ve, 1)
        ve_Ssib = ve.v

        for h in range(16):
            hc = h // 2
            hb = 64 * (h % 2)

            @block.vector
            def _(vector, h=h):
                vector.wait_ge(ve.h, ve_Ssib)
                vector.wait_ge(mm.h, mm.v)
                tinc(vector.tensor_copy(Ssib_bf[:], Ssib[:, h * 65:(h + 1) * 65]), ve, 1)
            ve_sb = ve.v

            for ck in range(NCK):
                c0 = ck * 128
                sb = ck % 2

                @block.sync
                def _(sync, h=h, ck=ck, c0=c0, g=(stqp.v, stye.v)):
                    sync.wait_ge(stqp.h, g[0])
                    sync.wait_ge(stye.h, g[1])
                    if h + ck > 0:
                        sync.wait_ge(ve.h, ve.v)
                    tinc(sync.dma_start(out=qp_s[:], in_=qp_dram[h, :, c0:c0 + 128]), ldqp, 16)
                    tinc(sync.dma_start(out=yext[0:65, :], in_=ye_dram[h, :, c0:c0 + 128]), ldye, 16)
                ld_c = (ldqp.v, ldye.v)

                @block.tensor
                def _(tensor, h=h, ck=ck, ld_c=ld_c, ve_sb=ve_sb):
                    tensor.wait_ge(ldqp.h, ld_c[0])
                    tensor.wait_ge(ve.h, ve_sb)
                    if h + ck > 0:
                        tensor.wait_ge(ve.h, ve.v - 1)
                    tinc(tensor.matmul(ps_y[0:65, :], Ssib_bf[:], qp_s[:], start=True, stop=True), mm, 1)
                mm_c = mm.v

                @block.vector
                def _(vector, h=h, hc=hc, hb=hb, ck=ck, c0=c0, mm_c=mm_c, ld_c=ld_c):
                    vector.wait_ge(mm.h, mm_c)
                    vector.wait_ge(ldye.h, ld_c[1])
                    ch(vector.tensor_tensor(yext[0:65, :], yext[0:65, :], ps_y[0:65, :], ALU.add), vch, vector)
                    ch(vector.tensor_scalar(den_s[:], yext[64:65, :], eps6[0:1, 0:1], None, ALU.add), vch, vector)
                    ch(vector.reciprocal(den_s[:], den_s[:]), vch, vector)
                    vector.sem_inc(ve.h, 1)
                    ve.inc(1)
                ve_dn = ve.v

                @block.tensor
                def _(tensor, h=h, ck=ck, ve_dn=ve_dn):
                    tensor.wait_ge(ve.h, ve_dn)
                    tinc(tensor.matmul(ps_t[:], onesE_s[0:1, 0:64], den_s[:], start=True, stop=True), mm, 1)
                mm_bc = mm.v

                @block.vector
                def _(vector, h=h, hc=hc, hb=hb, ck=ck, c0=c0, mm_bc=mm_bc):
                    vector.wait_ge(mm.h, mm_bc)
                    ch(vector.tensor_copy(bc_s[:], ps_t[:]), vch, vector)
                    tinc(vector.tensor_tensor(yfm[hb:hb + 64, hc * TLOC + c0: hc * TLOC + c0 + 128],
                                              yext[0:64, :], bc_s[:], ALU.mult), ve, 1)
        ve_yfm = ve.v

        @block.gpsimd
        def _(gpsimd):
            gpsimd.wait_ge(mm.h, mm.v)
            for cc in range(NCH):
                tinc(gpsimd.dma_start(out=wA[:, cc * 1024:(cc + 1) * 1024],
                                      in_=wo[cc * 128:(cc + 1) * 128, :]), gw, 16)
        gw_o = gw.v

        for t in range(NTT8):
            t0 = t * 512
            for oc in range(NCH):
                pbank = ps_q if oc % 2 == 0 else ps_k

                @block.tensor
                def _(tensor, t=t, t0=t0, oc=oc, pbank=pbank, gw_o=gw_o):
                    if t == 0 and oc == 0:
                        tensor.wait_ge(gw.h, gw_o)
                        tensor.wait_ge(ve.h, ve_yfm)
                    if not (t == 0 and oc < 2):
                        tensor.wait_ge(se.h, se.v - 1)
                    for c in range(NCH):
                        i_ = tensor.matmul(pbank[:], wA[:, c * 1024 + oc * 128: c * 1024 + (oc + 1) * 128],
                                           yfm[:, c * TLOC + t0: c * TLOC + t0 + 512], start=(c == 0), stop=(c == NCH - 1))
                    tinc(i_, mm, 1)
                mm_o = mm.v
                @block.scalar
                def _(scalar, t=t, t0=t0, oc=oc, pbank=pbank, mm_o=mm_o):
                    scalar.wait_ge(mm.h, mm_o)
                    scalar.wait_ge(stq.h, stq.v)
                    tinc(scalar.activation(evb[:], pbank[:], AF.Copy), se, 1)
                se_o = se.v

                @block.sync
                def _(sync, t=t, t0=t0, oc=oc, se_o=se_o):
                    sync.wait_ge(se.h, se_o)
                    tinc(sync.dma_start(out=q_dram[oc * 128:(oc + 1) * 128, t0:t0 + 512], in_=evb[:]), stq, 16)
        sto_all = stq.v

        for i in range(NTI):
            b = i % 2

            @block.sync
            def _(sync, i=i, b=b):
                sync.wait_ge(stq.h, sto_all)
                sync.wait_ge(stx[b].h, stx[b].v)
                tinc(sync.dma_start(out=fx[b][:], in_=x4[i * 128:(i + 1) * 128, :]), ldf[b], 16)
            ld_f = ldf[b].v

            @block.scalar
            def _(scalar, i=i, b=b):
                scalar.wait_ge(stq.h, sto_all)
                scalar.wait_ge(stx[b].h, stx[b].v)
                scalar.wait_ge(ldx[b].h, ldx[b].v)
                tinc(scalar.dma_start(out=lnt[b][:], in_=q_dram[:, i * 128:(i + 1) * 128],
                                      transpose=True), ldx[b], 16)
            tr_f = ldx[b].v

            @block.vector
            def _(vector, i=i, b=b, ld_f=ld_f, tr_f=tr_f):
                vector.wait_ge(ldf[b].h, ld_f)
                vector.wait_ge(ldx[b].h, tr_f)
                tinc(vector.tensor_tensor(fx[b][:], fx[b][:], lnt[b][:], ALU.add), ve, 1)
            ve_f = ve.v

            @block.sync
            def _(sync, i=i, b=b, ve_f=ve_f):
                sync.wait_ge(ve.h, ve_f)
                tinc(sync.dma_start(out=x5[i * 128:(i + 1) * 128, :], in_=fx[b][:]), stx[b], 16)

        @block.sync
        def _(sync):
            sync.wait_ge(stx[0].h, stx[0].v)
            sync.wait_ge(stx[1].h, stx[1].v)

    return nc



BIGNEG = 1.0e8


def build_F(detect=True):
    nc = bass.Bass("TRN2", target_bir_lowering=False, debug=False,
                   detect_race_conditions=detect, dynamic_dma_scratch_size=8192)
    xin = nc.dram_tensor("xin", [TLOC, D], F32, kind="ExternalInput").ap()
    wgT = nc.dram_tensor("wgT", [D, 16], F32, kind="ExternalInput").ap()
    bias = nc.dram_tensor("bias", [1, 16], F32, kind="ExternalInput").ap()
    w1T = nc.dram_tensor("w1T", [16, D, 512], F32, kind="ExternalInput").ap()
    w2T = nc.dram_tensor("w2T", [16, 512, D], F32, kind="ExternalInput").ap()
    ws1T = nc.dram_tensor("ws1T", [D, D], F32, kind="ExternalInput").ap()
    ws2T = nc.dram_tensor("ws2T", [D, D], F32, kind="ExternalInput").ap()
    onesc = nc.dram_tensor("onesc", [128, 1], F32, kind="ExternalInput").ap()
    onesr = nc.dram_tensor("onesr", [1, 128], F32, kind="ExternalInput").ap()
    iota16 = nc.dram_tensor("iota16", [128, 16], F32, kind="ExternalInput").ap()
    ident = nc.dram_tensor("ident", [128, 128], F32, kind="ExternalInput").ap()
    xout = nc.dram_tensor("xout", [TLOC, D], F32, kind="ExternalOutput").ap()

    ln_dram = nc.dram_tensor("ln_dram", [TLOC, D], BF16, kind="Internal").ap()
    cw_dram = nc.dram_tensor("cw_dram", [16, TLOC], F32, kind="Internal").ap()
    o_dram = nc.dram_tensor("o_dram", [D, TLOC], BF16, kind="Internal").ap()

    NTI = TLOC // 128
    NTT8 = TLOC // 512

    with contextlib.ExitStack() as ctx:
        E = ctx.enter_context
        sq = E(nc.sbuf_tensor("sq", [128, D], F32))
        lnf = E(nc.sbuf_tensor("lnf", [128, D], F32))
        stat = [E(nc.sbuf_tensor(f"stat{i}", [128, 8], F32)) for i in range(2)]
        epsb = E(nc.sbuf_tensor("epsb", [128, 1], F32))
        xb = [E(nc.sbuf_tensor(f"xb{i}", [128, D], F32)) for i in range(2)]
        lnt = [E(nc.sbuf_tensor(f"lnt{i}", [128, D], BF16)) for i in range(2)]
        ones_s = E(nc.sbuf_tensor("ones_s", [128, 1], F32))
        onesr_s = E(nc.sbuf_tensor("onesr_s", [1, 128], F32))
        iota_s = E(nc.sbuf_tensor("iota_s", [128, 16], F32))
        iM_s = E(nc.sbuf_tensor("iM_s", [128, 16], F32))
        ident_s = E(nc.sbuf_tensor("ident_s", [128, 128], F32))
        wgs = E(nc.sbuf_tensor("wgs", [128, NCH * 16], BF16))
        bcb = E(nc.sbuf_tensor("bcb", [128, 16], F32))
        rfm = E(nc.sbuf_tensor("rfm", [128, NCH * 512], BF16))
        probs = E(nc.sbuf_tensor("probs", [128, 16], F32))
        schoice = E(nc.sbuf_tensor("schoice", [128, 16], F32))
        t16 = E(nc.sbuf_tensor("t16", [128, 16], F32))
        u16 = E(nc.sbuf_tensor("u16", [128, 16], F32))
        g4 = E(nc.sbuf_tensor("g4", [128, 4], F32))
        oh = E(nc.sbuf_tensor("oh", [128, 16], F32))
        cwt = E(nc.sbuf_tensor("cwt", [128, 16], F32))
        cwT = E(nc.sbuf_tensor("cwT", [16, 128], F32))
        colF = E(nc.sbuf_tensor("colF", [128, 8], F32))
        w1s = [E(nc.sbuf_tensor(f"w1s{i}", [128, NCH * 512], BF16)) for i in range(2)]
        w2s = [E(nc.sbuf_tensor(f"w2s{i}", [128, 4 * D], BF16)) for i in range(2)]
        hidb_all = E(nc.sbuf_tensor("hidb_all", [128, 64 * 512], BF16))
        hshb = E(nc.sbuf_tensor("hshb", [128, NCH * 512], BF16))
        relb = E(nc.sbuf_tensor("relb", [128, 512], BF16))
        cwrow = E(nc.sbuf_tensor("cwrow", [1, TLOC], F32))
        cwbc = E(nc.sbuf_tensor("cwbc", [128, 512], BF16))
        evb = E(nc.sbuf_tensor("evb", [128, 512], BF16))
        fx = [E(nc.sbuf_tensor(f"fx{i}", [128, D], F32)) for i in range(2)]
        fo = [E(nc.sbuf_tensor(f"fo{i}", [128, D], BF16)) for i in range(2)]

        ps8 = E(nc.psum_tensor("ps8", [128, 4096], F32))

        vch = Sem(ctx, nc, "vch")
        sch = Sem(ctx, nc, "sch")
        ldc = Sem(ctx, nc, "ldc")
        gw = [Sem(ctx, nc, f"gw{i}") for i in range(2)]
        gws = Sem(ctx, nc, "gws")
        ldx = [Sem(ctx, nc, f"ldx{i}") for i in range(2)]
        stl = [Sem(ctx, nc, f"stl{i}") for i in range(2)]
        trn = [Sem(ctx, nc, f"trn{i}") for i in range(NCH)]
        stcw = Sem(ctx, nc, "stcw")
        ldcw = Sem(ctx, nc, "ldcw")
        sto = Sem(ctx, nc, "sto")
        ldf = [Sem(ctx, nc, f"ldf{i}") for i in range(2)]
        trf = [Sem(ctx, nc, f"trf{i}") for i in range(2)]
        stx = [Sem(ctx, nc, f"stx{i}") for i in range(2)]
        mm = Sem(ctx, nc, "mm")
        ve = Sem(ctx, nc, "ve")
        se = Sem(ctx, nc, "se")

        block = E(nc.Block())

        @block.sync
        def _(sync):
            tinc(sync.dma_start(out=ones_s[:], in_=onesc), ldc, 16)
            tinc(sync.dma_start(out=onesr_s[:], in_=onesr), ldc, 16)
            tinc(sync.dma_start(out=iota_s[:], in_=iota16), ldc, 16)
            tinc(sync.dma_start(out=ident_s[:], in_=ident), ldc, 16)
            tinc(sync.dma_start(out=bcb[0:1, :], in_=bias), ldc, 16)
            sync.wait_ge(ldc.h, ldc.v)

        @block.gpsimd
        def _(gpsimd):
            for cc in range(NCH):
                tinc(gpsimd.dma_start(out=wgs[:, cc * 16:(cc + 1) * 16],
                                      in_=wgT[cc * 128:(cc + 1) * 128, :]), gws, 16)
        gws_w = gws.v

        @block.vector
        def _(vector):
            vector.memset(epsb[:], 1e-5)
            vector.wait_ge(ldc.h, ldc.v)
            tinc(vector.tensor_scalar(iM_s[:], iota_s[:], 64.0, None, ALU.subtract), ve, 1)
        ve_init = ve.v

        @block.tensor
        def _(tensor):
            tensor.wait_ge(ldc.h, ldc.v)
            tinc(tensor.matmul(ps8[:, 3072:3088], onesr_s[:], bcb[0:1, :], start=True, stop=True), mm, 1)
        mm_bc = mm.v

        @block.vector
        def _(vector):
            vector.wait_ge(mm.h, mm_bc)
            tinc(vector.tensor_copy(bcb[:], ps8[:, 3072:3088]), ve, 1)
        ve_bcb = ve.v

        tile_done = []
        se_tile = []
        for i in range(NTI):
            b = i % 2

            @block.sync
            def _(sync, i=i, b=b):
                if i >= 2:
                    sync.wait_ge(ve.h, tile_done[i - 2])
                    sync.wait_ge(se.h, se_tile[i - 2])
                tinc(sync.dma_start(out=xb[b][:], in_=xin[i * 128:(i + 1) * 128, :]), ldx[b], 16)
            ld_i = ldx[b].v

            @block.scalar
            def _(scalar, i=i, b=b, ld_i=ld_i):
                scalar.wait_ge(ldx[b].h, ld_i)
                if i >= 1:
                    scalar.wait_ge(sch.h, sch.v)
                if i >= 2:
                    scalar.wait_ge(ve.h, tile_done[i - 2])
                ch(scalar.activation(sq[:], xb[b][:], AF.Copy, accum_out=stat[b][:, 0:1]), sch, scalar)
                tinc(scalar.activation(sq[:], xb[b][:], AF.Square, accum_out=stat[b][:, 1:2]), se, 1)
            se_sums = se.v

            @block.vector
            def _(vector, i=i, b=b, se_sums=se_sums):
                vector.wait_ge(se.h, se_sums)
                ch(vector.tensor_scalar(stat[b][:, 2:3], stat[b][:, 0:1], 1.0 / D, None, ALU.mult), vch, vector)
                ch(vector.tensor_tensor(stat[b][:, 3:4], stat[b][:, 2:3], stat[b][:, 2:3], ALU.mult), vch, vector)
                ch(vector.tensor_scalar(stat[b][:, 5:6], stat[b][:, 1:2], 1.0 / D, None, ALU.mult), vch, vector)
                tinc(vector.tensor_tensor(stat[b][:, 3:4], stat[b][:, 5:6], stat[b][:, 3:4], ALU.subtract), ve, 1)
            ve_var = ve.v

            @block.scalar
            def _(scalar, i=i, b=b, ve_var=ve_var):
                scalar.wait_ge(ve.h, ve_var)
                tinc(scalar.activation(stat[b][:, 4:5], stat[b][:, 3:4], AF.Sqrt, bias=epsb[:, 0:1]), se, 1)
            se_tile.append(se.v)
            se_sqrt = se.v

            @block.vector
            def _(vector, i=i, b=b, se_sqrt=se_sqrt):
                vector.wait_ge(se.h, se_sqrt)
                ch(vector.reciprocal(stat[b][:, 4:5], stat[b][:, 4:5]), vch, vector)
                vector.wait_ge(ve.h, ve.v)
                ch(vector.tensor_scalar(lnf[:], xb[b][:], stat[b][:, 2:3], stat[b][:, 4:5],
                                        ALU.subtract, ALU.mult), vch, vector)
                vector.wait_ge(stl[b].h, stl[b].v)
                tinc(vector.tensor_copy(lnt[b][:], lnf[:]), ve, 1)
            tile_done.append(ve.v)
            ve_ln = ve.v

            @block.sync
            def _(sync, i=i, b=b, ve_ln=ve_ln):
                sync.wait_ge(ve.h, ve_ln)
                tinc(sync.dma_start(out=ln_dram[i * 128:(i + 1) * 128, :], in_=lnt[b][:]), stl[b], 16)
        stl_all = (stl[0].v, stl[1].v)

        for t in range(NTT8):
            t0 = t * 512

            @block.scalar
            def _(scalar, t=t, t0=t0):
                scalar.wait_ge(stl[0].h, stl_all[0])
                scalar.wait_ge(stl[1].h, stl_all[1])
                if t >= 1:
                    scalar.wait_ge(mm.h, mm.v)
                for c in range(NCH):
                    tinc(scalar.dma_start(out=rfm[:, c * 512:(c + 1) * 512],
                                          in_=ln_dram[t0:t0 + 512, c * 128:(c + 1) * 128],
                                          transpose=True), trn[c], 16)

            for sub in range(4):
                i = t * 4 + sub

                @block.tensor
                def _(tensor, t=t, sub=sub, i=i):
                    if sub == 0:
                        for c in range(NCH):
                            tensor.wait_ge(trn[c].h, trn[c].v)
                        tensor.wait_ge(gws.h, gws_w)
                    if i >= 1:
                        tensor.wait_ge(se.h, se.v)
                        tensor.wait_ge(ve.h, ve.v)
                    for c in range(NCH):
                        i_ = tensor.matmul(ps8[:, 3072:3088], rfm[:, c * 512 + sub * 128: c * 512 + (sub + 1) * 128],
                                           wgs[:, c * 16:(c + 1) * 16], start=(c == 0), stop=(c == NCH - 1))
                    tinc(i_, mm, 1)
                mm_lg = mm.v

                @block.scalar
                def _(scalar, i=i, mm_lg=mm_lg):
                    scalar.wait_ge(mm.h, mm_lg)
                    scalar.wait_ge(ve.h, ve.v)
                    tinc(scalar.activation(probs[:], ps8[:, 3072:3088], AF.Sigmoid), se, 1)
                se_sc = se.v

                @block.vector
                def _(vector, i=i, se_sc=se_sc):
                    vector.wait_ge(se.h, se_sc)
                    vector.wait_ge(ve.h, ve_bcb)
                    vector.wait_ge(ve.h, ve_init)
                    ch(vector.tensor_tensor(schoice[:], probs[:], bcb[:], ALU.add), vch, vector)
                    for g in range(4):
                        sl = schoice[:, 4 * g:4 * (g + 1)]
                        ch(vector.reduce_max(colF[:, 0:1], sl, axis=AXX), vch, vector)
                        ch(vector.tensor_scalar(u16[:, 0:4], sl, colF[:, 0:1], None, ALU.is_lt), vch, vector)
                        ch(vector.tensor_scalar(t16[:, 0:4], u16[:, 0:4], -BIGNEG, BIGNEG, ALU.mult, ALU.add), vch, vector)
                        ch(vector.tensor_tensor(t16[:, 4:8], sl, t16[:, 0:4], ALU.subtract), vch, vector)
                        ch(vector.reduce_max(colF[:, 1:2], t16[:, 4:8], axis=AXX), vch, vector)
                        ch(vector.tensor_tensor(g4[:, g:g + 1], colF[:, 0:1], colF[:, 1:2], ALU.add), vch, vector)
                    ch(vector.reduce_max(colF[:, 2:3], g4[:], axis=AXX), vch, vector)
                    ch(vector.tensor_scalar(u16[:, 0:4], g4[:], colF[:, 2:3], None, ALU.is_lt), vch, vector)
                    ch(vector.tensor_scalar(t16[:, 0:4], u16[:, 0:4], -BIGNEG, BIGNEG, ALU.mult, ALU.add), vch, vector)
                    ch(vector.tensor_tensor(t16[:, 4:8], g4[:], t16[:, 0:4], ALU.subtract), vch, vector)
                    ch(vector.reduce_max(colF[:, 3:4], t16[:, 4:8], axis=AXX), vch, vector)
                    ch(vector.tensor_scalar(u16[:, 0:4], g4[:], colF[:, 3:4], None, ALU.is_ge), vch, vector)
                    for g in range(4):
                        ch(vector.tensor_scalar(t16[:, 4 * g:4 * (g + 1)],
                                                iota_s[:, 0:4] if False else schoice[:, 4 * g:4 * (g + 1)],
                                                u16[:, g:g + 1], None, ALU.mult), vch, vector)
                    ch(vector.memset(cwt[:], 0.0), vch, vector)
                    ch(vector.memset(colF[:, 6:7], 0.0), vch, vector)
                    for kk in range(4):
                        ch(vector.reduce_max(colF[:, 0:1], t16[:], axis=AXX), vch, vector)
                        ch(vector.tensor_scalar(u16[:], t16[:], colF[:, 0:1], None, ALU.is_equal), vch, vector)
                        ch(vector.tensor_tensor(u16[:], u16[:], iM_s[:], ALU.mult), vch, vector)
                        ch(vector.tensor_scalar(u16[:], u16[:], 64.0, None, ALU.add), vch, vector)
                        ch(vector.tensor_reduce(colF[:, 1:2], u16[:], AXX, ALU.min), vch, vector)
                        ch(vector.tensor_scalar(oh[:], iota_s[:], colF[:, 1:2], None, ALU.is_equal), vch, vector)
                        ch(vector.tensor_tensor(u16[:], probs[:], oh[:], ALU.mult), vch, vector)
                        ch(vector.reduce_sum(colF[:, 2:3], u16[:], axis=AXX), vch, vector)
                        ch(vector.tensor_tensor(colF[:, 6:7], colF[:, 6:7], colF[:, 2:3], ALU.add), vch, vector)
                        ch(vector.tensor_scalar(u16[:], oh[:], colF[:, 2:3], None, ALU.mult), vch, vector)
                        ch(vector.tensor_tensor(cwt[:], cwt[:], u16[:], ALU.add), vch, vector)
                        if kk < 3:
                            ch(vector.tensor_scalar(u16[:], oh[:], BIGNEG, None, ALU.mult), vch, vector)
                            ch(vector.tensor_tensor(t16[:], t16[:], u16[:], ALU.subtract), vch, vector)
                    ch(vector.tensor_scalar(colF[:, 6:7], colF[:, 6:7], 1e-20, None, ALU.add), vch, vector)
                    ch(vector.reciprocal(colF[:, 6:7], colF[:, 6:7]), vch, vector)
                    ch(vector.tensor_scalar(cwt[:], cwt[:], colF[:, 6:7], 2.5, ALU.mult, ALU.mult), vch, vector)
                    vector.sem_inc(ve.h, 1)
                    ve.inc(1)
                ve_cw = ve.v

                @block.tensor
                def _(tensor, i=i, ve_cw=ve_cw):
                    tensor.wait_ge(ve.h, ve_cw)
                    tinc(tensor.transpose(ps8[0:16, 3584:3712], cwt[:], ident_s[:]), mm, 1)
                mm_ct = mm.v

                @block.vector
                def _(vector, i=i, mm_ct=mm_ct):
                    vector.wait_ge(mm.h, mm_ct)
                    vector.wait_ge(stcw.h, stcw.v)
                    ch(vector.tensor_copy(cwT[:], ps8[0:16, 3584:3712]), vch, vector)
                    vector.sem_inc(ve.h, 1)
                    ve.inc(1)
                ve_ctt = ve.v

                @block.sync
                def _(sync, i=i, ve_ctt=ve_ctt):
                    sync.wait_ge(ve.h, ve_ctt)
                    tinc(sync.dma_start(out=cw_dram[:, i * 128:(i + 1) * 128], in_=cwT[:]), stcw, 16)
        stcw_all = stcw.v

        for t in range(NTT8):
            t0 = t * 512

            @block.scalar
            def _(scalar, t=t, t0=t0):
                if t >= 1:
                    scalar.wait_ge(mm.h, mm.v)
                for c in range(NCH):
                    tinc(scalar.dma_start(out=rfm[:, c * 512:(c + 1) * 512],
                                          in_=ln_dram[t0:t0 + 512, c * 128:(c + 1) * 128],
                                          transpose=True), trn[c], 16)

            mm_before_sh = mm.v

            @block.gpsimd
            def _(gpsimd, t=t, mm_before_sh=mm_before_sh):
                gpsimd.wait_ge(mm.h, mm_before_sh)
                for cc in range(NCH):
                    tinc(gpsimd.dma_start(out=w1s[0][:, cc * 512:(cc + 1) * 512],
                                          in_=ws1T[cc * 128:(cc + 1) * 128, 0:512]), gw[0], 16)
                for cc in range(NCH):
                    tinc(gpsimd.dma_start(out=w1s[1][:, cc * 512:(cc + 1) * 512],
                                          in_=ws1T[cc * 128:(cc + 1) * 128, 512:1024]), gw[1], 16)
            gw_sh = (gw[0].v, gw[1].v)

            for hh in range(2):
                for fc in range(4):
                    pb0 = (hh * 4 + fc) % 2
                    pslice = ps8[:, pb0 * 512:(pb0 + 1) * 512]

                    @block.tensor
                    def _(tensor, t=t, hh=hh, fc=fc, pslice=pslice, gw_sh=gw_sh):
                        if hh == 0 and fc == 0:
                            for c in range(NCH):
                                tensor.wait_ge(trn[c].h, trn[c].v)
                            tensor.wait_ge(gw[0].h, gw_sh[0])
                            tensor.wait_ge(gw[1].h, gw_sh[1])
                            if t >= 1:
                                tensor.wait_ge(se.h, se.v)
                        else:
                            tensor.wait_ge(se.h, se.v - 1)
                        for c in range(NCH):
                            i_ = tensor.matmul(pslice, w1s[hh][:, c * 512 + fc * 128: c * 512 + (fc + 1) * 128],
                                               rfm[:, c * 512:(c + 1) * 512], start=(c == 0), stop=(c == NCH - 1))
                        tinc(i_, mm, 1)
                    mm_h1 = mm.v

                    @block.scalar
                    def _(scalar, t=t, hh=hh, fc=fc, pslice=pslice, mm_h1=mm_h1):
                        scalar.wait_ge(mm.h, mm_h1)
                        scalar.wait_ge(ve.h, ve.v)
                        tinc(scalar.activation(relb[:], pslice, AF.Relu), se, 1)
                    se_r = se.v

                    @block.vector
                    def _(vector, t=t, hh=hh, fc=fc, se_r=se_r):
                        vector.wait_ge(se.h, se_r)
                        tinc(vector.tensor_tensor(hshb[:, (hh * 4 + fc) * 512:(hh * 4 + fc + 1) * 512],
                                                  relb[:], relb[:], ALU.mult), ve, 1)
            ve_hsh = ve.v

            for e in range(16):
                eb = e % 2
                mm_we = mm.v

                @block.gpsimd
                def _(gpsimd, t=t, e=e, eb=eb, mm_we=mm_we):
                    gpsimd.wait_ge(mm.h, mm_we)
                    for cc in range(NCH):
                        tinc(gpsimd.dma_start(out=w1s[eb][:, cc * 512:(cc + 1) * 512],
                                              in_=w1T[e, cc * 128:(cc + 1) * 128, :]), gw[eb], 16)
                gw_e = gw[eb].v

                @block.sync
                def _(sync, t=t, t0=t0, e=e):
                    sync.wait_ge(stcw.h, stcw_all)
                    sync.wait_ge(ve.h, ve.v)
                    tinc(sync.dma_start(out=cwrow[0:1, 0:512], in_=cw_dram[e:e + 1, t0:t0 + 512]), ldcw, 16)
                ld_cw = ldcw.v

                @block.tensor
                def _(tensor, t=t, e=e, ld_cw=ld_cw):
                    tensor.wait_ge(ldcw.h, ld_cw)
                    tensor.wait_ge(ve.h, ve.v)
                    tinc(tensor.matmul(ps8[:, 1024:1536], onesr_s[:], cwrow[0:1, 0:512],
                                       start=True, stop=True), mm, 1)
                mm_cb = mm.v

                @block.vector
                def _(vector, t=t, e=e, mm_cb=mm_cb):
                    vector.wait_ge(mm.h, mm_cb)
                    tinc(vector.tensor_copy(cwbc[:], ps8[:, 1024:1536]), ve, 1)
                ve_cb = ve.v

                for fc in range(4):
                    pb0 = fc % 2
                    pslice = ps8[:, pb0 * 512:(pb0 + 1) * 512]

                    @block.tensor
                    def _(tensor, t=t, e=e, eb=eb, fc=fc, pslice=pslice, gw_e=gw_e):
                        tensor.wait_ge(gw[eb].h, gw_e)
                        tensor.wait_ge(se.h, se.v - 1 if fc >= 1 else se.v)
                        for c in range(NCH):
                            i_ = tensor.matmul(pslice, w1s[eb][:, c * 512 + fc * 128: c * 512 + (fc + 1) * 128],
                                               rfm[:, c * 512:(c + 1) * 512], start=(c == 0), stop=(c == NCH - 1))
                        tinc(i_, mm, 1)
                    mm_h1 = mm.v

                    @block.scalar
                    def _(scalar, t=t, e=e, fc=fc, pslice=pslice, mm_h1=mm_h1):
                        scalar.wait_ge(mm.h, mm_h1)
                        scalar.wait_ge(ve.h, ve.v)
                        tinc(scalar.activation(relb[:], pslice, AF.Relu), se, 1)
                    se_r = se.v

                    @block.vector
                    def _(vector, t=t, e=e, fc=fc, se_r=se_r, ve_cb=ve_cb):
                        vector.wait_ge(se.h, se_r)
                        vector.wait_ge(ve.h, ve_cb)
                        ch(vector.tensor_tensor(relb[:], relb[:], relb[:], ALU.mult), vch, vector)
                        tinc(vector.tensor_tensor(hidb_all[:, (e * 4 + fc) * 512:(e * 4 + fc + 1) * 512],
                                                  relb[:], cwbc[:], ALU.mult), ve, 1)
            ve_hide = ve.v

            @block.gpsimd
            def _(gpsimd, t=t, ve_hide=ve_hide):
                gpsimd.wait_ge(mm.h, mm.v)
                for cc in range(4):
                    tinc(gpsimd.dma_start(out=w2s[0][:, cc * D:(cc + 1) * D],
                                          in_=ws2T[cc * 128:(cc + 1) * 128, :]), gw[0], 16)
                for cc in range(4):
                    tinc(gpsimd.dma_start(out=w2s[1][:, cc * D:(cc + 1) * D],
                                          in_=ws2T[(4 + cc) * 128:(5 + cc) * 128, :]), gw[1], 16)
            gw_s2 = (gw[0].v, gw[1].v)

            @block.tensor
            def _(tensor, t=t, ve_hide=ve_hide, gw_s2=gw_s2):
                tensor.wait_ge(ve.h, ve_hide)
                tensor.wait_ge(gw[0].h, gw_s2[0])
                tensor.wait_ge(gw[1].h, gw_s2[1])
                if t >= 1:
                    tensor.wait_ge(se.h, se.v)
                for oc in range(NCH):
                    for fc in range(8):
                        wbuf = w2s[0] if fc < 4 else w2s[1]
                        fcl = fc % 4
                        i_ = tensor.matmul(ps8[:, oc * 512:(oc + 1) * 512],
                                           wbuf[:, fcl * D + oc * 128: fcl * D + (oc + 1) * 128],
                                           hshb[:, fc * 512:(fc + 1) * 512],
                                           start=(fc == 0), stop=False)
                tinc(i_, mm, 1)

            for e in range(16):
                eb = e % 2
                mm_we2 = mm.v

                @block.gpsimd
                def _(gpsimd, t=t, e=e, eb=eb, mm_we2=mm_we2):
                    gpsimd.wait_ge(mm.h, mm_we2)
                    for cc in range(4):
                        tinc(gpsimd.dma_start(out=w2s[eb][:, cc * D:(cc + 1) * D],
                                              in_=w2T[e, cc * 128:(cc + 1) * 128, :]), gw[eb], 16)
                gw_e2 = gw[eb].v

                @block.tensor
                def _(tensor, t=t, e=e, eb=eb, gw_e2=gw_e2):
                    tensor.wait_ge(gw[eb].h, gw_e2)
                    for oc in range(NCH):
                        for fc in range(4):
                            i_ = tensor.matmul(ps8[:, oc * 512:(oc + 1) * 512],
                                               w2s[eb][:, fc * D + oc * 128: fc * D + (oc + 1) * 128],
                                               hidb_all[:, (e * 4 + fc) * 512:(e * 4 + fc + 1) * 512],
                                               start=False, stop=(e == 15 and fc == 3))
                    tinc(i_, mm, 1)
            mm_out = mm.v

            for oc in range(NCH):
                @block.scalar
                def _(scalar, t=t, t0=t0, oc=oc, mm_out=mm_out):
                    scalar.wait_ge(mm.h, mm_out)
                    scalar.wait_ge(sto.h, sto.v)
                    tinc(scalar.activation(evb[:], ps8[:, oc * 512:(oc + 1) * 512], AF.Copy), se, 1)
                se_o = se.v

                @block.sync
                def _(sync, t=t, t0=t0, oc=oc, se_o=se_o):
                    sync.wait_ge(se.h, se_o)
                    tinc(sync.dma_start(out=o_dram[oc * 128:(oc + 1) * 128, t0:t0 + 512], in_=evb[:]), sto, 16)
        sto_all = sto.v

        for i in range(NTI):
            b = i % 2

            @block.sync
            def _(sync, i=i, b=b):
                sync.wait_ge(sto.h, sto_all)
                sync.wait_ge(stx[b].h, stx[b].v)
                tinc(sync.dma_start(out=fx[b][:], in_=xin[i * 128:(i + 1) * 128, :]), ldf[b], 16)
            ld_f = ldf[b].v

            @block.scalar
            def _(scalar, i=i, b=b):
                scalar.wait_ge(sto.h, sto_all)
                scalar.wait_ge(stx[b].h, stx[b].v)
                scalar.wait_ge(trf[b].h, trf[b].v)
                tinc(scalar.dma_start(out=fo[b][:], in_=o_dram[:, i * 128:(i + 1) * 128],
                                      transpose=True), trf[b], 16)
            tr_f = trf[b].v

            @block.vector
            def _(vector, i=i, b=b, ld_f=ld_f, tr_f=tr_f):
                vector.wait_ge(ldf[b].h, ld_f)
                vector.wait_ge(trf[b].h, tr_f)
                tinc(vector.tensor_tensor(fx[b][:], fx[b][:], fo[b][:], ALU.add), ve, 1)
            ve_f = ve.v

            @block.sync
            def _(sync, i=i, b=b, ve_f=ve_f):
                sync.wait_ge(ve.h, ve_f)
                tinc(sync.dma_start(out=xout[i * 128:(i + 1) * 128, :], in_=fx[b][:]), stx[b], 16)

        @block.sync
        def _(sync):
            sync.wait_ge(stx[0].h, stx[0].v)
            sync.wait_ge(stx[1].h, stx[1].v)

    return nc



def build_G(detect=True, self_exchange=False, local_shared=False):
    nc = bass.Bass("TRN2", target_bir_lowering=False, debug=False,
                   detect_race_conditions=detect, dynamic_dma_scratch_size=8192)
    x6 = nc.dram_tensor("x6", [TLOC, D], F32, kind="ExternalInput").ap()
    wqkv = nc.dram_tensor("wqkv", [D, 1536], F32, kind="ExternalInput").ap()
    wproj = nc.dram_tensor("wproj", [D, D], F32, kind="ExternalInput").ap()
    cosq = nc.dram_tensor("cosq", [8192, 512], BF16, kind="ExternalInput").ap()
    sinq = nc.dram_tensor("sinq", [8192, 512], BF16, kind="ExternalInput").ap()
    cosk = nc.dram_tensor("cosk", [8192, 512], BF16, kind="ExternalInput").ap()
    sink = nc.dram_tensor("sink", [8192, 512], BF16, kind="ExternalInput").ap()
    bproj = nc.dram_tensor("bproj", [128, NCH], F32, kind="ExternalInput").ap()
    tag = nc.dram_tensor("tag", [1, 16], I32, kind="ExternalInput").ap()
    x7 = nc.dram_tensor("x7", [TLOC, D], F32, kind="ExternalOutput").ap()

    q_dram = nc.dram_tensor("q_dram", [8192, 512], BF16, kind="Internal").ap()
    k_dram = nc.dram_tensor("k_dram", [8192, 512], BF16, kind="Internal").ap()
    v_dram = nc.dram_tensor("v_dram", [8192, 512], BF16, kind="Internal").ap()
    o2_dram = nc.dram_tensor("o2_dram", [D, TLOC], BF16, kind="Internal").ap()
    ln_local = nc.dram_tensor("ln_local", [TLOC, D], BF16, kind="Internal").ap()
    o_local = nc.dram_tensor("o_local", [512, 8192], BF16, kind="Internal").ap()
    of_my = nc.dram_tensor("of_my", [D, TLOC], BF16, kind="Internal").ap()
    _spc = "Local" if local_shared else "Shared"
    sh_batch = nc.dram_tensor("sh_batch", [8192, D], BF16, kind="Internal", addr_space=_spc).ap()
    sh_ofull = nc.dram_tensor("sh_ofull", [D, 8192], BF16, kind="Internal", addr_space=_spc).ap()
    sh_flag = nc.dram_tensor("sh_flag", [4, 16], I32, kind="Internal", addr_space=_spc).ap()

    NTI = TLOC // 128
    NTT8 = TLOC // 512
    NPT = 8192 // 128
    NQT = 8192 // 512

    with contextlib.ExitStack() as ctx:
        E = ctx.enter_context
        sq = E(nc.sbuf_tensor("sq", [128, D], F32))
        lnf = E(nc.sbuf_tensor("lnf", [128, D], F32))
        stat = [E(nc.sbuf_tensor(f"stat{i}", [128, 8], F32)) for i in range(2)]
        epsb = E(nc.sbuf_tensor("epsb", [128, 1], F32))
        eps0 = E(nc.sbuf_tensor("eps0", [1, 1], F32))
        xb = [E(nc.sbuf_tensor(f"xb{i}", [128, D], F32)) for i in range(2)]
        lnt = [E(nc.sbuf_tensor(f"lnt{i}", [128, D], BF16)) for i in range(2)]
        tag_s = E(nc.sbuf_tensor("tag_s", [1, 16], I32))
        flag_s = E(nc.sbuf_tensor("flag_s", [1, 16], I32))
        wA = E(nc.sbuf_tensor("wA", [128, NCH * 1536], BF16))
        rfm = E(nc.sbuf_tensor("rfm", [128, NCH * 128], BF16))
        qkvb = E(nc.sbuf_tensor("qkvb", [128, 1536], F32))
        ropeb = E(nc.sbuf_tensor("ropeb", [128, 2 * 512], BF16))
        qrb = E(nc.sbuf_tensor("qrb", [128, 512], BF16))
        krb = E(nc.sbuf_tensor("krb", [128, 512], BF16))
        vbb = E(nc.sbuf_tensor("vbb", [128, 512], BF16))
        kfm2 = E(nc.sbuf_tensor("kfm2", [128, 8192], BF16))
        qfm2 = E(nc.sbuf_tensor("qfm2", [128, 8192], BF16))
        vext = E(nc.sbuf_tensor("vext", [128, 64 * 65], BF16))
        pbf = E(nc.sbuf_tensor("pbf", [128, 512], BF16))
        den_s = E(nc.sbuf_tensor("den_s", [1, 512], F32))
        bc_s = E(nc.sbuf_tensor("bc_s", [64, 512], F32))
        ofm = E(nc.sbuf_tensor("ofm", [64, 512], BF16))
        rfm2 = E(nc.sbuf_tensor("rfm2", [128, NCH * 512], BF16))
        onesG_s = E(nc.sbuf_tensor("onesG_s", [1, 64], F32))
        obuf = E(nc.sbuf_tensor("obuf", [128, 512], BF16))
        bpr = E(nc.sbuf_tensor("bpr", [128, NCH], F32))
        fx = [E(nc.sbuf_tensor(f"fx{i}", [128, D], F32)) for i in range(2)]
        fo = [E(nc.sbuf_tensor(f"fo{i}", [128, D], BF16)) for i in range(2)]

        ps_qkv = E(nc.psum_tensor("ps_qkv", [128, 1536], F32))
        ps_sc = E(nc.psum_tensor("ps_sc", [128, 512], F32))
        ps_o = E(nc.psum_tensor("ps_o", [128, 512], F32))
        ps_t = E(nc.psum_tensor("ps_t", [64, 512], F32))
        ps_p = E(nc.psum_tensor("ps_p", [128, 512], F32))

        vch = Sem(ctx, nc, "vch")
        sch = Sem(ctx, nc, "sch")
        ldc = Sem(ctx, nc, "ldc")
        gw = Sem(ctx, nc, "gw")
        ldx = [Sem(ctx, nc, f"ldx{i}") for i in range(2)]
        gst = Sem(ctx, nc, "gst")
        stl2 = [Sem(ctx, nc, f"stl2{i}") for i in range(2)]
        ghl = Sem(ctx, nc, "ghl")
        gex = Sem(ctx, nc, "gex")
        pse1 = Sem(ctx, nc, "pse1")
        pse2 = Sem(ctx, nc, "pse2")
        trq = Sem(ctx, nc, "trq")
        ldrp = Sem(ctx, nc, "ldrp")
        stq2 = Sem(ctx, nc, "stq2")
        stk2 = Sem(ctx, nc, "stk2")
        stv2 = Sem(ctx, nc, "stv2")
        ldkf = Sem(ctx, nc, "ldkf")
        ldqf = Sem(ctx, nc, "ldqf")
        ldvf = Sem(ctx, nc, "ldvf")
        sto2 = Sem(ctx, nc, "sto2")
        ldof = Sem(ctx, nc, "ldof")
        ldf = [Sem(ctx, nc, f"ldf{i}") for i in range(2)]
        trf = [Sem(ctx, nc, f"trf{i}") for i in range(2)]
        stx = [Sem(ctx, nc, f"stx{i}") for i in range(2)]
        mm = Sem(ctx, nc, "mm")
        ve = Sem(ctx, nc, "ve")
        se = Sem(ctx, nc, "se")

        block = E(nc.Block())

        @block.sync
        def _(sync):
            tinc(sync.dma_start(out=tag_s[:], in_=tag), ldc, 16)
            tinc(sync.dma_start(out=bpr[:], in_=bproj), ldc, 16)
            sync.wait_ge(ldc.h, ldc.v)

        @block.vector
        def _(vector):
            vector.memset(epsb[:], 1e-5)
            vector.memset(eps0[:], 0.0)
            vector.memset(onesG_s[:], 1.0)
            tinc(vector.memset(vext[:], 1.0), ve, 1)
        ve_init = ve.v

        if local_shared:
            @block.vector
            def _(vector):
                tinc(vector.memset(lnt[0][:], 0.0), ve, 1)
            ve_z0 = ve.v

            @block.gpsimd
            def _(gpsimd):
                gpsimd.wait_ge(ve.h, ve_z0)
                for zz in range(8192 // 128):
                    tinc(gpsimd.dma_start(out=sh_batch[zz * 128:(zz + 1) * 128, :], in_=lnt[0][:]), ghl, 16)
                for zz in range(D // 128):
                    tinc(gpsimd.dma_start(
                        out=sh_ofull[zz * 128:(zz + 1) * 128, :].rearrange("p (a b) -> p a b", b=1024),
                        in_=lnt[0][:].rearrange("p d -> p 1 d") if False else lnt[0][:][:, None, :]), ghl, 16) if False else None
                for zz in range(D // 128):
                    for seg in range(8):
                        tinc(gpsimd.dma_start(out=sh_ofull[zz * 128:(zz + 1) * 128, seg * 1024:(seg + 1) * 1024],
                                              in_=lnt[0][:]), ghl, 16)
                gpsimd.wait_ge(ghl.h, ghl.v)
                gpsimd.sem_inc(gex.h, 1)
                gex.inc(1)
            gez0 = gex.v
        else:
            gez0 = None

        tile_done = []
        se_tile = []
        for i in range(NTI):
            b = i % 2

            @block.sync
            def _(sync, i=i, b=b):
                if i >= 2:
                    sync.wait_ge(ve.h, tile_done[i - 2])
                    sync.wait_ge(se.h, se_tile[i - 2])
                tinc(sync.dma_start(out=xb[b][:], in_=x6[i * 128:(i + 1) * 128, :]), ldx[b], 16)
            ld_i = ldx[b].v

            @block.scalar
            def _(scalar, i=i, b=b, ld_i=ld_i):
                scalar.wait_ge(ldx[b].h, ld_i)
                if i >= 1:
                    scalar.wait_ge(sch.h, sch.v)
                if i >= 2:
                    scalar.wait_ge(ve.h, tile_done[i - 2])
                ch(scalar.activation(sq[:], xb[b][:], AF.Copy, accum_out=stat[b][:, 0:1]), sch, scalar)
                tinc(scalar.activation(sq[:], xb[b][:], AF.Square, accum_out=stat[b][:, 1:2]), se, 1)
            se_sums = se.v

            @block.vector
            def _(vector, i=i, b=b, se_sums=se_sums):
                vector.wait_ge(se.h, se_sums)
                ch(vector.tensor_scalar(stat[b][:, 2:3], stat[b][:, 0:1], 1.0 / D, None, ALU.mult), vch, vector)
                ch(vector.tensor_tensor(stat[b][:, 3:4], stat[b][:, 2:3], stat[b][:, 2:3], ALU.mult), vch, vector)
                ch(vector.tensor_scalar(stat[b][:, 5:6], stat[b][:, 1:2], 1.0 / D, None, ALU.mult), vch, vector)
                tinc(vector.tensor_tensor(stat[b][:, 3:4], stat[b][:, 5:6], stat[b][:, 3:4], ALU.subtract), ve, 1)
            ve_var = ve.v

            @block.scalar
            def _(scalar, i=i, b=b, ve_var=ve_var):
                scalar.wait_ge(ve.h, ve_var)
                tinc(scalar.activation(stat[b][:, 4:5], stat[b][:, 3:4], AF.Sqrt, bias=epsb[:, 0:1]), se, 1)
            se_tile.append(se.v)
            se_sqrt = se.v

            @block.vector
            def _(vector, i=i, b=b, se_sqrt=se_sqrt):
                vector.wait_ge(se.h, se_sqrt)
                ch(vector.reciprocal(stat[b][:, 4:5], stat[b][:, 4:5]), vch, vector)
                vector.wait_ge(ve.h, ve.v)
                ch(vector.tensor_scalar(lnf[:], xb[b][:], stat[b][:, 2:3], stat[b][:, 4:5],
                                        ALU.subtract, ALU.mult), vch, vector)
                vector.wait_ge(stl2[b].h, stl2[b].v)
                if gez0 is not None and i < 2:
                    vector.wait_ge(gex.h, gez0)
                tinc(vector.tensor_copy(lnt[b][:], lnf[:]), ve, 1)
            tile_done.append(ve.v)
            ve_ln = ve.v

            @block.sync
            def _(sync, i=i, b=b, ve_ln=ve_ln):
                sync.wait_ge(ve.h, ve_ln)
                tinc(sync.dma_start(out=ln_local[i * 128:(i + 1) * 128, :], in_=lnt[b][:]), stl2[b], 16)
        stl2_all = (stl2[0].v, stl2[1].v)

        @block.gpsimd
        def _(gpsimd):
            gpsimd.wait_ge(stl2[0].h, stl2_all[0])
            gpsimd.wait_ge(stl2[1].h, stl2_all[1])
            gpsimd.wait_ge(ldc.h, ldc.v)
            pid = gpsimd.partition_id()
            slot = pid % 2
            sib = slot if self_exchange else 1 - (pid % 2)
            tinc(gpsimd.dma_start(out=sh_batch.rearrange("(s q) d -> s q d", q=TLOC)[slot],
                                  in_=ln_local), ghl, 16)
            gpsimd.wait_ge(ghl.h, ghl.v)
            tinc(gpsimd.dma_start(out=sh_flag[slot, :][None, :], in_=tag_s[:]), ghl, 16)
            gpsimd.wait_ge(ghl.h, ghl.v)
            poll_flag(gpsimd, sh_flag[sib, :][None, :], flag_s, tag_s, pse1)
            gpsimd.sem_inc(gex.h, 1)
            gex.inc(1)
        gex_ln = gex.v

        @block.gpsimd
        def _(gpsimd):
            for cc in range(NCH):
                tinc(gpsimd.dma_start(out=wA[:, cc * 1536:(cc + 1) * 1536],
                                      in_=wqkv[cc * 128:(cc + 1) * 128, :]), gw, 16)
        gw_qkv = gw.v

        for p in range(NPT):
            p0 = p * 128

            @block.scalar
            def _(scalar, p=p, p0=p0):
                scalar.wait_ge(gex.h, gex_ln)
                if p >= 1:
                    scalar.wait_ge(mm.h, mm.v)
                for c in range(NCH):
                    tinc(scalar.dma_start(out=rfm[:, c * 128:(c + 1) * 128],
                                          in_=sh_batch[p0:p0 + 128, c * 128:(c + 1) * 128],
                                          transpose=True), trq, 16)
            tr_p = trq.v

            @block.sync
            def _(sync, p=p, p0=p0):
                if p >= 1:
                    sync.wait_ge(ve.h, ve.v)
                tinc(sync.dma_start(out=ropeb[:, 0:512], in_=cosq[p0:p0 + 128, :] if False else cosq[p0:p0 + 128, :]), ldrp, 16)
                tinc(sync.dma_start(out=ropeb[:, 512:1024], in_=sinq[p0:p0 + 128, :]), ldrp, 16)
            ld_rp = ldrp.v

            @block.tensor
            def _(tensor, p=p, tr_p=tr_p, gw_qkv=gw_qkv):
                tensor.wait_ge(trq.h, tr_p)
                tensor.wait_ge(gw.h, gw_qkv)
                if p >= 1:
                    tensor.wait_ge(se.h, se.v)
                    tensor.wait_ge(ve.h, ve.v)
                for c in range(NCH):
                    tensor.matmul(ps_qkv[:, 0:512], rfm[:, c * 128:(c + 1) * 128],
                                  wA[:, c * 1536:c * 1536 + 512], start=(c == 0), stop=(c == NCH - 1))
                for c in range(NCH):
                    tensor.matmul(ps_qkv[:, 512:1024], rfm[:, c * 128:(c + 1) * 128],
                                  wA[:, c * 1536 + 512:c * 1536 + 1024], start=(c == 0), stop=(c == NCH - 1))
                for c in range(NCH):
                    i_ = tensor.matmul(ps_qkv[:, 1024:1536], rfm[:, c * 128:(c + 1) * 128],
                                       wA[:, c * 1536 + 1024:c * 1536 + 1536], start=(c == 0), stop=(c == NCH - 1))
                tinc(i_, mm, 1)
            mm_qkv = mm.v

            @block.scalar
            def _(scalar, p=p, p0=p0, mm_qkv=mm_qkv):
                scalar.wait_ge(mm.h, mm_qkv)
                scalar.wait_ge(stv2.h, stv2.v)
                tinc(scalar.activation(vbb[:], ps_qkv[:, 1024:1536], AF.Copy), se, 1)
            se_v = se.v

            @block.sync
            def _(sync, p=p, p0=p0, se_v=se_v):
                sync.wait_ge(se.h, se_v)
                tinc(sync.dma_start(out=v_dram[p0:p0 + 128, :], in_=vbb[:]), stv2, 16)

            def rope_emit(src_off, dstbuf, dst_dram, stsem, cos_t, sin_t, ld_gate):
                @block.vector
                def _(vector, p=p, src_off=src_off, dstbuf=dstbuf, ld_gate=ld_gate):
                    vector.wait_ge(mm.h, mm.v)
                    vector.wait_ge(ldrp.h, ld_gate)
                    vector.wait_ge(stsem.h, stsem.v)
                    first = [slice(h * 64, h * 64 + 32) for h in range(8)]
                    qv = ps_qkv[:, src_off:src_off + 512].rearrange("p (h k) -> p h k", k=64)
                    cv = ropeb[:, 0:512].rearrange("p (h k) -> p h k", k=64)
                    sv = ropeb[:, 512:1024].rearrange("p (h k) -> p h k", k=64)
                    dv = dstbuf[:].rearrange("p (h k) -> p h k", k=64)
                    ch(vector.tensor_tensor(dv[:, :, 0:32], qv[:, :, 0:32], cv[:, :, 0:32], ALU.mult), vch, vector)
                    ch(vector.tensor_tensor(sq[:, 0:256].rearrange("p (h k) -> p h k", k=32),
                                            qv[:, :, 32:64], sv[:, :, 0:32], ALU.mult), vch, vector)
                    ch(vector.tensor_tensor(dv[:, :, 0:32], dv[:, :, 0:32],
                                            sq[:, 0:256].rearrange("p (h k) -> p h k", k=32), ALU.subtract), vch, vector)
                    ch(vector.tensor_tensor(dv[:, :, 32:64], qv[:, :, 32:64], cv[:, :, 32:64], ALU.mult), vch, vector)
                    ch(vector.tensor_tensor(sq[:, 0:256].rearrange("p (h k) -> p h k", k=32),
                                            qv[:, :, 0:32], sv[:, :, 32:64], ALU.mult), vch, vector)
                    tinc(vector.tensor_tensor(dv[:, :, 32:64], dv[:, :, 32:64],
                                              sq[:, 0:256].rearrange("p (h k) -> p h k", k=32), ALU.add), ve, 1)
                ve_r = ve.v

                @block.sync
                def _(sync, p=p, p0=p0, ve_r=ve_r, dstbuf=dstbuf):
                    sync.wait_ge(ve.h, ve_r)
                    tinc(sync.dma_start(out=dst_dram[p0:p0 + 128, :], in_=dstbuf[:]), stsem, 16)

            rope_emit(0, qrb, q_dram, stq2, cosq, sinq, ld_rp)

            @block.sync
            def _(sync, p=p, p0=p0):
                sync.wait_ge(ve.h, ve.v)
                tinc(sync.dma_start(out=ropeb[:, 0:512], in_=cosk[p0:p0 + 128, :]), ldrp, 16)
                tinc(sync.dma_start(out=ropeb[:, 512:1024], in_=sink[p0:p0 + 128, :]), ldrp, 16)
            ld_rp2 = ldrp.v

            rope_emit(0 + 512, krb, k_dram, stk2, cosk, sink, ld_rp2)
        stq_all = stq2.v
        stk_all = stk2.v
        stv_all = stv2.v

        for h in range(8):
            hp, hh = h // 2, h % 2
            if hh == 0:
                @block.scalar
                def _(scalar, h=h, hp=hp):
                    scalar.wait_ge(stq2.h, stq_all)
                    scalar.wait_ge(stk2.h, stk_all)
                    if hp >= 1:
                        scalar.wait_ge(mm.h, mm.v)
                    tinc(scalar.dma_start(out=kfm2[:], in_=k_dram[:, hp * 128:(hp + 1) * 128],
                                          transpose=True), ldkf, 16)
                    tinc(scalar.dma_start(out=qfm2[:], in_=q_dram[:, hp * 128:(hp + 1) * 128],
                                          transpose=True), ldqf, 16)
            ld_kq = (ldkf.v, ldqf.v)
            kfm = kfm2[hh * 64:(hh + 1) * 64, :]
            qfm = qfm2[hh * 64:(hh + 1) * 64, :]

            @block.sync
            def _(sync, h=h):
                sync.wait_ge(stv2.h, stv_all)
                if h >= 1:
                    sync.wait_ge(mm.h, mm.v)
                tinc(sync.dma_start(out=vext[:].rearrange("p (i e) -> p i e", e=65)[:, :, 0:64],
                                    in_=v_dram.rearrange("(i p) d -> p i d", p=128)[:, :, h * 64:(h + 1) * 64]),
                     ldvf, 16)
            ld_v = ldvf.v

            for qt in range(NQT):
                q0 = qt * 512
                mm_o0 = mm.v

                for kt in range(64):
                    @block.tensor
                    def _(tensor, h=h, qt=qt, q0=q0, kt=kt, ld_kq=ld_kq, ld_v=ld_v):
                        if kt == 0 and qt == 0:
                            tensor.wait_ge(ldkf.h, ld_kq[0])
                            tensor.wait_ge(ldqf.h, ld_kq[1])
                            tensor.wait_ge(ldvf.h, ld_v)
                            tensor.wait_ge(ve.h, ve_init)
                        if not (h == 0 and qt == 0 and kt == 0):
                            tensor.wait_ge(se.h, se.v)
                        tinc(tensor.matmul(ps_sc[:], kfm[:, kt * 128:(kt + 1) * 128],
                                           qfm[:, q0:q0 + 512], start=True, stop=True), mm, 1)
                    mm_sc = mm.v

                    @block.scalar
                    def _(scalar, h=h, qt=qt, kt=kt, mm_sc=mm_sc):
                        scalar.wait_ge(mm.h, mm_sc)
                        if kt >= 1:
                            scalar.wait_ge(mm.h, mm.v)
                        tinc(scalar.activation(pbf[:], ps_sc[:], AF.Exp), se, 1)
                    se_e = se.v

                    @block.tensor
                    def _(tensor, h=h, qt=qt, kt=kt, se_e=se_e):
                        tensor.wait_ge(se.h, se_e)
                        if kt == 0 and not (h == 0 and qt == 0):
                            tensor.wait_ge(ve.h, ve.v)
                        tinc(tensor.matmul(ps_o[0:65, :], vext[:, kt * 65:(kt + 1) * 65], pbf[:],
                                           start=(kt == 0), stop=(kt == 63)), mm, 1)
                mm_oend = mm.v

                @block.vector
                def _(vector, h=h, qt=qt, mm_oend=mm_oend):
                    vector.wait_ge(mm.h, mm_oend)
                    ch(vector.tensor_scalar(den_s[:], ps_o[64:65, :], eps0[0:1, 0:1], None, ALU.add), vch, vector)
                    tinc(vector.reciprocal(den_s[:], den_s[:]), ve, 1)
                ve_dn = ve.v

                @block.tensor
                def _(tensor, h=h, qt=qt, ve_dn=ve_dn):
                    tensor.wait_ge(ve.h, ve_dn)
                    tensor.wait_ge(ve.h, ve.v)
                    tinc(tensor.matmul(ps_t[:], vext[0:1, 0:64] if False else onesG_s[0:1, :], den_s[:],
                                       start=True, stop=True), mm, 1)
                mm_bc = mm.v

                @block.vector
                def _(vector, h=h, qt=qt, mm_bc=mm_bc):
                    vector.wait_ge(mm.h, mm_bc)
                    ch(vector.tensor_copy(bc_s[:], ps_t[:]), vch, vector)
                    vector.wait_ge(gst.h, gst.v)
                    tinc(vector.tensor_tensor(ofm[:], ps_o[0:64, :], bc_s[:], ALU.mult), ve, 1)
                ve_of = ve.v

                @block.sync
                def _(sync, h=h, qt=qt, q0=q0, ve_of=ve_of):
                    sync.wait_ge(ve.h, ve_of)
                    tinc(sync.dma_start(out=o_local[h * 64:(h + 1) * 64, q0:q0 + 512],
                                        in_=ofm[:]), gst, 16)
        gst_o = gst.v

        @block.gpsimd
        def _(gpsimd):
            gpsimd.wait_ge(gst.h, gst_o)
            pid = gpsimd.partition_id()
            slot = pid % 2
            sib = slot if self_exchange else 1 - (pid % 2)
            tinc(gpsimd.dma_start(out=sh_ofull.rearrange("(s p) q -> s p q", p=512)[slot],
                                  in_=o_local), ghl, 16)
            gpsimd.wait_ge(ghl.h, ghl.v)
            tinc(gpsimd.dma_start(out=sh_flag[2 + slot, :][None, :], in_=tag_s[:]), ghl, 16)
            gpsimd.wait_ge(ghl.h, ghl.v)
            poll_flag(gpsimd, sh_flag[2 + sib, :][None, :], flag_s, tag_s, pse2)
            tinc(gpsimd.dma_start(out=of_my,
                                  in_=sh_ofull.rearrange("p (s q) -> p s q", q=TLOC)[:, slot]), ghl, 16)
            gpsimd.wait_ge(ghl.h, ghl.v)
            gpsimd.sem_inc(gex.h, 1)
            gex.inc(1)
        gex_o = gex.v

        @block.gpsimd
        def _(gpsimd):
            gpsimd.wait_ge(mm.h, mm.v)
            for cc in range(NCH):
                tinc(gpsimd.dma_start(out=wA[:, cc * 1024:(cc + 1) * 1024],
                                      in_=wproj[cc * 128:(cc + 1) * 128, :]), gw, 16)
        gw_p = gw.v

        for t in range(NTT8):
            t0 = t * 512

            @block.sync
            def _(sync, t=t, t0=t0):
                sync.wait_ge(gex.h, gex_o)
                if t >= 1:
                    sync.wait_ge(mm.h, mm.v)
                for c in range(NCH):
                    tinc(sync.dma_start(out=rfm2[:, c * 512:(c + 1) * 512],
                                        in_=of_my[c * 128:(c + 1) * 128, t0:t0 + 512]), ldof, 16)
            ld_o = ldof.v

            for oc in range(NCH):
                @block.tensor
                def _(tensor, t=t, oc=oc, ld_o=ld_o, gw_p=gw_p):
                    if oc == 0:
                        tensor.wait_ge(ldof.h, ld_o)
                        tensor.wait_ge(gw.h, gw_p)
                    if not (t == 0 and oc < 2):
                        tensor.wait_ge(se.h, se.v - 1)
                    for c in range(NCH):
                        i_ = tensor.matmul(ps_p[:] if oc % 2 == 0 else ps_sc[:],
                                           wA[:, c * 1024 + oc * 128: c * 1024 + (oc + 1) * 128],
                                           rfm2[:, c * 512:(c + 1) * 512], start=(c == 0), stop=(c == NCH - 1))
                    tinc(i_, mm, 1)
                mm_pj = mm.v

                @block.scalar
                def _(scalar, t=t, oc=oc, mm_pj=mm_pj):
                    scalar.wait_ge(mm.h, mm_pj)
                    scalar.wait_ge(sto2.h, sto2.v)
                    pb = ps_p if oc % 2 == 0 else ps_sc
                    tinc(scalar.activation(obuf[:], pb[:], AF.Identity, bias=bpr[:, oc:oc + 1]), se, 1)
                se_pj = se.v

                @block.sync
                def _(sync, t=t, t0=t0, oc=oc, se_pj=se_pj):
                    sync.wait_ge(se.h, se_pj)
                    tinc(sync.dma_start(out=o2_dram[oc * 128:(oc + 1) * 128, t0:t0 + 512], in_=obuf[:]), sto2, 16)
        sto2_all = sto2.v

        for i in range(NTI):
            b = i % 2

            @block.sync
            def _(sync, i=i, b=b):
                sync.wait_ge(sto2.h, sto2_all)
                sync.wait_ge(stx[b].h, stx[b].v)
                tinc(sync.dma_start(out=fx[b][:], in_=x6[i * 128:(i + 1) * 128, :]), ldf[b], 16)
            ld_f = ldf[b].v

            @block.scalar
            def _(scalar, i=i, b=b):
                scalar.wait_ge(sto2.h, sto2_all)
                scalar.wait_ge(stx[b].h, stx[b].v)
                scalar.wait_ge(trf[b].h, trf[b].v)
                tinc(scalar.dma_start(out=fo[b][:], in_=o2_dram[:, i * 128:(i + 1) * 128],
                                      transpose=True), trf[b], 16)
            tr_f = trf[b].v

            @block.vector
            def _(vector, i=i, b=b, ld_f=ld_f, tr_f=tr_f):
                vector.wait_ge(ldf[b].h, ld_f)
                vector.wait_ge(trf[b].h, tr_f)
                tinc(vector.tensor_tensor(fx[b][:], fx[b][:], fo[b][:], ALU.add), ve, 1)
            ve_f = ve.v

            @block.sync
            def _(sync, i=i, b=b, ve_f=ve_f):
                sync.wait_ge(ve.h, ve_f)
                tinc(sync.dma_start(out=x7[i * 128:(i + 1) * 128, :], in_=fx[b][:]), stx[b], 16)

        @block.sync
        def _(sync):
            sync.wait_ge(stx[0].h, stx[0].v)
            sync.wait_ge(stx[1].h, stx[1].v)

    return nc



def _consts():
    ident = np.eye(128, dtype=np.float32)
    onesc = np.ones((128, 1), dtype=np.float32)
    onesr = np.ones((1, 128), dtype=np.float32)
    return ident, onesc, onesr


def _tag():
    return np.full((1, 16), np.random.randint(1, 2**31 - 1, dtype=np.int32), dtype=np.int32)


def _perm_x_for_core(x4, c):
    b, par = c // 2, c % 2
    border = [b] + [bb for bb in range(4) if bb != b]
    pos = np.concatenate([np.arange(par * 4096, (par + 1) * 4096),
                          np.arange((1 - par) * 4096, (2 - par) * 4096) if par == 0
                          else np.arange(0, 4096)])
    xr = x4[border][:, pos, :]
    return np.ascontiguousarray(xr.reshape(NTOK, D))


def run_A(x4):
    ident, onesc, onesr = _consts()
    nc = build_A(detect=False)
    in_maps = []
    for c in range(NCORE):
        in_maps.append({
            "x_all": _perm_x_for_core(x4, c),
            "ident": ident, "onesc": onesc,
        })
    res = run_bass_kernel_spmd(nc, in_maps, core_ids=list(range(NCORE)))
    return [r["x1_tm"] for r in res.results]


def _b1_consts():
    iota16 = np.tile(np.arange(16, dtype=np.float32), (128, 1))
    return iota16


def _b2_consts():
    iota16 = _b1_consts()
    eoffC = np.tile((np.arange(16, dtype=np.float32) * CL), (128, 1))
    iotacol = np.arange(128, dtype=np.float32).reshape(128, 1)
    pfill = np.zeros((TOTSLOT, 4), dtype=np.float32)
    pfill[:, 1] = 2 * TLOC
    return iota16, eoffC, iotacol, pfill


def b1_inmap(c, x1_c, inputs):
    ident, onesc, onesr = _consts()
    return {
        "x1": x1_c,
        "rw": np.ascontiguousarray(inputs["L0_ffn_router_w"], dtype=np.float32),
        "rb": np.ascontiguousarray(inputs["L0_ffn_router_b"], dtype=np.float32).reshape(1, 16),
        "onesc": onesc,
        "iota16": _b1_consts(),
    }


def b2_inmap(c, x1_c, ln_c, route_c, counts_all, inputs):
    ident, onesc, onesr = _consts()
    iota16, eoffC, iotacol, pfill = _b2_consts()
    cm = np.zeros((8, 1), dtype=np.float32)
    cm[:c] = 1.0
    return {
        "x1": x1_c, "ln_tm": ln_c, "route": route_c,
        "counts_all": counts_all, "coremask": cm,
        "ident": ident, "onesc": onesc, "iota16": iota16,
        "eoffC": eoffC, "iotacol": iotacol, "pfill": pfill,
        "swg": np.ascontiguousarray(inputs["L0_ffn_sw_gate"], dtype=np.float32),
        "swu": np.ascontiguousarray(inputs["L0_ffn_sw_up"], dtype=np.float32),
        "swd": np.ascontiguousarray(inputs["L0_ffn_sw_down"], dtype=np.float32),
        "wg": np.ascontiguousarray(inputs["L0_ffn_w_gate"], dtype=np.float32),
        "wu": np.ascontiguousarray(inputs["L0_ffn_w_up"], dtype=np.float32),
        "wdn": np.ascontiguousarray(inputs["L0_ffn_w_down"], dtype=np.float32),
    }


def c_inmap(c, x2_c, inputs, tag):
    def T(a):
        return np.ascontiguousarray(np.asarray(a, dtype=np.float32).T)
    def chunkcol(a):
        return np.ascontiguousarray(np.asarray(a, dtype=np.float32).reshape(8, 128).T)
    convw = np.asarray(inputs["L0_attn_b_conv_w"], dtype=np.float32).reshape(1024, 4)
    convw = np.ascontiguousarray(convw.reshape(8, 128, 4).transpose(1, 0, 2).reshape(128, 32))
    gb = np.ascontiguousarray(np.asarray(inputs["L0_attn_b_gates_b"], dtype=np.float32).reshape(16, 128).T)
    hm = np.full((128, 1), float(c % 2), dtype=np.float32)
    return {
        "x2": x2_c,
        "inwT": T(inputs["L0_attn_b_input_w"]),
        "gwT": T(inputs["L0_attn_b_gates_w"]),
        "outT": T(inputs["L0_attn_b_output_w"]),
        "growT": T(inputs["L0_attn_b_grow_w"]),
        "shrT": T(inputs["L0_attn_b_shrink_w"]),
        "convw": convw,
        "convb": chunkcol(inputs["L0_attn_b_conv_b"]),
        "fbase": chunkcol(inputs["L0_attn_b_forget_base"]),
        "gam1": chunkcol(inputs["L0_attn_b_hawk_norm_gamma"]),
        "gam2": chunkcol(inputs["L0_attn_b_gmlp_norm_gamma"]),
        "gb": gb,
        "hmask": hm,
        "tag": tag,
    }


def e_inmap(c, x4_c, inputs, tag):
    hm = np.full((128, 1), float(c % 2), dtype=np.float32)
    return {
        "x4": x4_c,
        "wq": np.ascontiguousarray(inputs["L1_attn_a_Wq"], dtype=np.float32),
        "wk": np.ascontiguousarray(inputs["L1_attn_a_Wk"], dtype=np.float32),
        "wv": np.ascontiguousarray(inputs["L1_attn_a_Wv"], dtype=np.float32),
        "wo": np.ascontiguousarray(inputs["L1_attn_a_Wo"], dtype=np.float32),
        "wf": np.ascontiguousarray(inputs["L1_attn_a_W_feat"], dtype=np.float32),
        "trimask": np.triu(np.ones((128, 128), dtype=np.float32)),
        "hmask": hm,
        "tag": tag,
    }


def f_inmap(c, x_c, inputs):
    ident, onesc, onesr = _consts()
    def T(a):
        return np.ascontiguousarray(np.asarray(a, dtype=np.float32).T)
    w1T = np.ascontiguousarray(np.asarray(inputs["L1_ffn_W1"], dtype=np.float32).transpose(0, 2, 1))
    w2T = np.ascontiguousarray(np.asarray(inputs["L1_ffn_W2"], dtype=np.float32).transpose(0, 2, 1))
    return {
        "xin": x_c,
        "wgT": T(inputs["L1_ffn_Wg"]),
        "bias": np.asarray(inputs["L1_ffn_bias_corr"], dtype=np.float32).reshape(1, 16),
        "w1T": w1T, "w2T": w2T,
        "ws1T": T(inputs["L1_ffn_Ws1"]),
        "ws2T": T(inputs["L1_ffn_Ws2"]),
        "onesc": onesc, "onesr": onesr,
        "iota16": _b1_consts(),
        "ident": ident,
    }


def g_inmap(c, x6_c, inputs, tag):
    par = c % 2
    h0 = par * 8
    wqkv = np.asarray(inputs["L1_attn_b_w_qkv"], dtype=np.float32)
    secs = [wqkv[s * 1024 + h0 * 64: s * 1024 + (h0 + 8) * 64, :] for s in range(3)]
    wqkv_my = np.ascontiguousarray(np.concatenate(secs, axis=0).T)
    inv_freq = 1.0 / (10000.0 ** (np.arange(0, 64, 2, dtype=np.float32) / 64.0))
    t = np.arange(8192, dtype=np.float32)
    freqs = t[:, None] * inv_freq[None, :]
    emb = np.concatenate([freqs, freqs], axis=1)
    cos = np.cos(emb); sin = np.sin(emb)
    cos8 = np.tile(cos, (1, 8)); sin8 = np.tile(sin, (1, 8))
    scale = 1.0 / 8.0
    bp = np.ascontiguousarray(np.asarray(inputs["L1_attn_b_b_proj"], dtype=np.float32).reshape(8, 128).T)
    return {
        "x6": x6_c,
        "wqkv": wqkv_my,
        "wproj": np.ascontiguousarray(np.asarray(inputs["L1_attn_b_w_proj"], dtype=np.float32).T),
        "cosq": (cos8 * scale).astype(BF16NP),
        "sinq": (sin8 * scale).astype(BF16NP),
        "cosk": cos8.astype(BF16NP),
        "sink": sin8.astype(BF16NP),
        "bproj": bp,
        "tag": tag,
    }


def kernel(**inputs):
    x4 = np.ascontiguousarray(inputs["L0_attn_a_x"], dtype=np.float32)
    x1s = run_A(x4)

    nc1 = build_B1(detect=False)
    res1 = run_bass_kernel_spmd(nc1, [b1_inmap(c, x1s[c], inputs) for c in range(NCORE)],
                                core_ids=list(range(NCORE)))
    counts_all = np.concatenate([r["counts"] for r in res1.results], axis=0)

    nc2 = build_B2(detect=False)
    res2 = run_bass_kernel_spmd(
        nc2,
        [b2_inmap(c, x1s[c], res1.results[c]["ln_tm"], res1.results[c]["route"], counts_all, inputs)
         for c in range(NCORE)],
        core_ids=list(range(NCORE)))
    x2s = [r["x2"] for r in res2.results]

    tag = _tag()
    ncC = build_C(detect=False)
    resC = run_bass_kernel_spmd(ncC, [c_inmap(c, x2s[c], inputs, tag) for c in range(NCORE)],
                                core_ids=list(range(NCORE)))
    x3s = [r["x3"] for r in resC.results]

    res1b = run_bass_kernel_spmd(nc1, [b1_inmap(c, x3s[c], inputs) for c in range(NCORE)],
                                 core_ids=list(range(NCORE)))
    counts_all2 = np.concatenate([r["counts"] for r in res1b.results], axis=0)
    res2b = run_bass_kernel_spmd(
        nc2,
        [b2_inmap(c, x3s[c], res1b.results[c]["ln_tm"], res1b.results[c]["route"], counts_all2, inputs)
         for c in range(NCORE)],
        core_ids=list(range(NCORE)))
    x4s = [r["x2"] for r in res2b.results]

    ncE = build_E(detect=False)
    resE = run_bass_kernel_spmd(ncE, [e_inmap(c, x4s[c], inputs, tag) for c in range(NCORE)],
                                core_ids=list(range(NCORE)))
    x5s = [r["x5"] for r in resE.results]

    ncF = build_F(detect=False)
    resF = run_bass_kernel_spmd(ncF, [f_inmap(c, x5s[c], inputs) for c in range(NCORE)],
                                core_ids=list(range(NCORE)))
    x6s = [r["xout"] for r in resF.results]

    ncG = build_G(detect=False)
    resG = run_bass_kernel_spmd(ncG, [g_inmap(c, x6s[c], inputs, tag) for c in range(NCORE)],
                                core_ids=list(range(NCORE)))
    x7s = [r["x7"] for r in resG.results]

    resF2 = run_bass_kernel_spmd(ncF, [f_inmap(c, x7s[c], inputs) for c in range(NCORE)],
                                 core_ids=list(range(NCORE)))
    out = np.concatenate([r["xout"] for r in resF2.results], axis=0)
    return out.reshape(4, 8192, D)


# version 02
# speedup vs baseline: 1.3241x; 1.3241x over previous
import contextlib
import numpy as np
import ml_dtypes
import concourse.bass as bass
import concourse.mybir as mybir
from concourse.bass_utils import run_bass_kernel_spmd
from concourse.bass_core_rust import IndirectOffsetOnAxis

F32 = mybir.dt.float32
BF16 = mybir.dt.bfloat16
I32 = mybir.dt.int32
AF = mybir.ActivationFunctionType
ALU = mybir.AluOpType
AXX = mybir.AxisListType.X

NCORE = 8
D = 1024
NCH = 8
TLOC = 4096
NTOK = 32768
TT = 512
NTT = TLOC // TT
BF16NP = ml_dtypes.bfloat16
PI = float(np.pi)


class Sem:
    def __init__(self, ctx, nc, name):
        self.h = ctx.enter_context(nc.semaphore(name))
        self.v = 0

    def inc(self, n=1):
        self.v += n
        return self.v


def tinc(instr, sem, n=1):
    instr.then_inc(sem.h, n)
    sem.inc(n)
    return sem.v


def ch(instr, sem, eng):
    instr.then_inc(sem.h, 1)
    sem.inc(1)
    eng.wait_ge(sem.h, sem.v)


def poll_flag(gpsimd, sh_flag_row, flag_s, tagp_s, psem):
    with gpsimd.register(None) as cnt:
        gpsimd.reg_mov(cnt, 0)

        def cond_fn():
            gpsimd.dma_start(out=flag_s[:], in_=sh_flag_row).then_inc(psem.h, 16)
            gpsimd.reg_add(cnt, cnt, 16)
            gpsimd.wait_ge(psem.h, cnt)
            val = gpsimd.value_load(flag_s[0:1, 0:1])
            tv = gpsimd.value_load(tagp_s[0:1, 0:1])
            return val != tv

        with gpsimd.While(cond_fn):
            pass



def build_A(detect=True):
    nc = bass.Bass("TRN2", target_bir_lowering=False, debug=False,
                   detect_race_conditions=detect, dynamic_dma_scratch_size=4096)
    x_all = nc.dram_tensor("x_all", [NTOK, D], F32, kind="ExternalInput").ap()
    onesc = nc.dram_tensor("onesc", [128, 1], F32, kind="ExternalInput").ap()
    ident = nc.dram_tensor("ident", [128, 128], F32, kind="ExternalInput").ap()
    x1_tm = nc.dram_tensor("x1_tm", [TLOC, D], F32, kind="ExternalOutput").ap()

    sin2_dram = nc.dram_tensor("sin2_dram", [8192, TLOC], BF16, kind="Internal").ap()

    NT_ALL = NTOK // 128
    NT_B = 8192 // 128
    NG = TLOC // 256

    with contextlib.ExitStack() as ctx:
        E = ctx.enter_context
        xt = [E(nc.sbuf_tensor(f"xt{i}", [128, D], F32)) for i in range(2)]
        sq = E(nc.sbuf_tensor("sq", [128, D], F32))
        lnf = E(nc.sbuf_tensor("lnf", [128, D], F32))
        vt = E(nc.sbuf_tensor("vt", [128, 512], F32))
        stat = [E(nc.sbuf_tensor(f"stat{i}", [128, 8], F32)) for i in range(2)]
        epsb = E(nc.sbuf_tensor("epsb", [128, 1], F32))
        ones_s = E(nc.sbuf_tensor("ones_s", [128, 1], F32))
        ones_b = E(nc.sbuf_tensor("ones_b", [128, 1], BF16))
        one1 = E(nc.sbuf_tensor("one1", [1, 1], F32))
        ident_s = E(nc.sbuf_tensor("ident_s", [128, 128], F32))
        lnb = E(nc.sbuf_tensor("lnb", [128, NT_B * D], BF16))
        m_tm = E(nc.sbuf_tensor("m_tm", [128, 64 * 64], F32))
        m_fm = E(nc.sbuf_tensor("m_fm", [64, 8192], F32))
        el2 = E(nc.sbuf_tensor("el2", [128, 2048], F32))
        sin2b = E(nc.sbuf_tensor("sin2b", [128, 2048], BF16))
        pbuf = E(nc.sbuf_tensor("pbuf", [128, 256], BF16))
        ptile = E(nc.sbuf_tensor("ptile", [128, 256], BF16))
        zrow = E(nc.sbuf_tensor("zrow", [1, 256], F32))
        zrec = E(nc.sbuf_tensor("zrec", [128, 1], F32))
        outs = E(nc.sbuf_tensor("outs", [128, D], F32))
        xres = E(nc.sbuf_tensor("xres", [128, D], F32))

        ps_log = E(nc.psum_tensor("ps_log", [128, 512], F32))
        ps_z = E(nc.psum_tensor("ps_z", [1, 256], F32))
        ps_o = [E(nc.psum_tensor(f"ps_o{i}", [128, 512], F32)) for i in range(4)]
        ps_t = E(nc.psum_tensor("ps_t", [128, 128], F32))

        vch = Sem(ctx, nc, "vch")
        sch = Sem(ctx, nc, "sch")
        ldc = Sem(ctx, nc, "ldc")
        ldx = [Sem(ctx, nc, f"ldx{i}") for i in range(2)]
        lds = [Sem(ctx, nc, f"lds{i}") for i in range(2)]
        ldr = Sem(ctx, nc, "ldr")
        sts = Sem(ctx, nc, "sts")
        stx = Sem(ctx, nc, "stx")
        mm = Sem(ctx, nc, "mm")
        ve = Sem(ctx, nc, "ve")
        se = Sem(ctx, nc, "se")

        block = E(nc.Block())

        @block.vector
        def _(vector):
            vector.memset(epsb[:], 1e-5)
            vector.memset(ones_b[:], 1.0)
            tinc(vector.memset(m_tm[:], 0.0), ve, 1)
        ve_init = ve.v

        @block.sync
        def _(sync):
            tinc(sync.dma_start(out=ones_s[:], in_=onesc), ldc, 16)
            tinc(sync.dma_start(out=one1[:], in_=onesc[0:1, 0:1]), ldc, 16)
            tinc(sync.dma_start(out=ident_s[:], in_=ident), ldc, 16)
            sync.wait_ge(ldc.h, ldc.v)

        tile_done = []
        se_tile = []
        for i in range(NT_ALL):
            b = i % 2

            @block.sync
            def _(sync, i=i, b=b):
                if i >= 2:
                    sync.wait_ge(ve.h, tile_done[i - 2])
                    sync.wait_ge(se.h, se_tile[i - 2])
                tinc(sync.dma_start(out=xt[b][:], in_=x_all[i * 128:(i + 1) * 128, :]), ldx[b], 16)
            ld_i = ldx[b].v

            @block.scalar
            def _(scalar, i=i, b=b, ld_i=ld_i):
                scalar.wait_ge(ldx[b].h, ld_i)
                if i >= 1:
                    scalar.wait_ge(sch.h, sch.v)
                if i >= 2:
                    scalar.wait_ge(ve.h, tile_done[i - 2])
                ch(scalar.activation(sq[:], xt[b][:], AF.Copy, accum_out=stat[b][:, 0:1]), sch, scalar)
                tinc(scalar.activation(sq[:], xt[b][:], AF.Square, accum_out=stat[b][:, 1:2]), se, 1)
            se_sums = se.v

            @block.vector
            def _(vector, i=i, b=b, se_sums=se_sums):
                vector.wait_ge(se.h, se_sums)
                ch(vector.tensor_scalar(stat[b][:, 2:3], stat[b][:, 0:1], 1.0 / D, None, ALU.mult), vch, vector)
                ch(vector.tensor_tensor(stat[b][:, 3:4], stat[b][:, 2:3], stat[b][:, 2:3], ALU.mult), vch, vector)
                ch(vector.tensor_scalar(stat[b][:, 5:6], stat[b][:, 1:2], 1.0 / D, None, ALU.mult), vch, vector)
                tinc(vector.tensor_tensor(stat[b][:, 3:4], stat[b][:, 5:6], stat[b][:, 3:4], ALU.subtract), ve, 1)
            ve_var = ve.v

            @block.scalar
            def _(scalar, i=i, b=b, ve_var=ve_var):
                scalar.wait_ge(ve.h, ve_var)
                tinc(scalar.activation(stat[b][:, 4:5], stat[b][:, 3:4], AF.Sqrt, bias=epsb[:, 0:1]), se, 1)
            se_tile.append(se.v)
            se_sqrt = se.v

            @block.vector
            def _(vector, i=i, b=b, se_sqrt=se_sqrt):
                vector.wait_ge(se.h, se_sqrt)
                ch(vector.reciprocal(stat[b][:, 4:5], stat[b][:, 4:5]), vch, vector)
                ch(vector.tensor_scalar(lnf[:], xt[b][:], stat[b][:, 2:3], stat[b][:, 4:5],
                                        ALU.subtract, ALU.mult), vch, vector)
                if i < NT_B:
                    ch(vector.tensor_copy(lnb[:, i * D:(i + 1) * D], lnf[:]), vch, vector)
                ch(vector.tensor_tensor(vt[:, 0:512], lnf[:, 0:512], lnf[:, 512:1024], ALU.add), vch, vector)
                ch(vector.tensor_tensor(vt[:, 0:256], vt[:, 0:256], vt[:, 256:512], ALU.add), vch, vector)
                ch(vector.tensor_tensor(vt[:, 0:128], vt[:, 0:128], vt[:, 128:256], ALU.add), vch, vector)
                ch(vector.tensor_tensor(vt[:, 0:64], vt[:, 0:64], vt[:, 64:128], ALU.add), vch, vector)
                pt = i % 64
                tinc(vector.tensor_tensor(m_tm[:, pt * 64:(pt + 1) * 64], m_tm[:, pt * 64:(pt + 1) * 64],
                                          vt[:, 0:64], ALU.add), ve, 1)
            tile_done.append(ve.v)

        @block.vector
        def _(vector):
            vector.wait_ge(ve.h, ve.v)
            tinc(vector.tensor_scalar(m_tm[:], m_tm[:], 1.0 / 64.0, None, ALU.mult), ve, 1)
        ve_ms = ve.v

        for p in range(64):
            @block.tensor
            def _(tensor, p=p):
                tensor.wait_ge(ve.h, ve_ms + p)
                tensor.wait_ge(ldc.h, ldc.v)
                tinc(tensor.transpose(ps_t[0:64, :], m_tm[:, p * 64:(p + 1) * 64], ident_s[:]), mm, 1)
            mm_t = mm.v

            @block.vector
            def _(vector, p=p, mm_t=mm_t):
                vector.wait_ge(mm.h, mm_t)
                tinc(vector.tensor_copy(m_fm[:, p * 128:(p + 1) * 128], ps_t[0:64, :]), ve, 1)
        ve_mfm = ve.v

        se_s2_prev = None
        last_clip = None
        for j in range(NT_B):
            for hlf in range(2):
                clip_done = []
                for q in range(4):
                    @block.tensor
                    def _(tensor, j=j, hlf=hlf, q=q, last_clip=last_clip):
                        if j == 0 and hlf == 0 and q == 0:
                            tensor.wait_ge(ve.h, ve_mfm)
                        gate = clip_done[-1] if clip_done else last_clip
                        if gate is not None:
                            tensor.wait_ge(ve.h, gate)
                        tinc(tensor.matmul(ps_log[:], m_fm[:, j * 128:(j + 1) * 128],
                                           m_fm[:, hlf * 2048 + q * 512: hlf * 2048 + (q + 1) * 512],
                                           start=True, stop=True), mm, 1)
                    mm_l = mm.v

                    @block.vector
                    def _(vector, j=j, hlf=hlf, q=q, mm_l=mm_l, prev=se_s2_prev):
                        vector.wait_ge(mm.h, mm_l)
                        if prev is not None:
                            vector.wait_ge(se.h, prev)
                        tinc(vector.tensor_scalar(el2[:, q * 512:(q + 1) * 512], ps_log[:],
                                                  PI, -PI, ALU.min, ALU.max), ve, 1)
                    clip_done.append(ve.v)
                last_clip = clip_done[-1]

                @block.scalar
                def _(scalar, j=j, hlf=hlf, clip_done=tuple(clip_done)):
                    scalar.wait_ge(ve.h, clip_done[-1])
                    if j + hlf > 0:
                        scalar.wait_ge(sts.h, sts.v)
                    ch(scalar.activation(el2[:], el2[:], AF.Sin, scale=0.5), sch, scalar)
                    tinc(scalar.activation(sin2b[:], el2[:], AF.Square), se, 1)
                se_s2_prev = se.v
                se_s2 = se.v

                @block.sync
                def _(sync, j=j, hlf=hlf, se_s2=se_s2):
                    sync.wait_ge(se.h, se_s2)
                    tinc(sync.dma_start(out=sin2_dram[j * 128:(j + 1) * 128, hlf * 2048:(hlf + 1) * 2048],
                                        in_=sin2b[:]), sts, 16)
        sts_all = sts.v

        for g in range(NG):
            mm_j = []
            for j in range(NT_B):
                sb = j % 2

                @block.sync
                def _(sync, g=g, j=j, sb=sb, mm_j=mm_j):
                    if g == 0 and j == 0:
                        sync.wait_ge(sts.h, sts_all)
                    if j >= 2:
                        sync.wait_ge(mm.h, mm_j[j - 2])
                    elif g > 0 and j < 2:
                        sync.wait_ge(mm.h, mm_j_prev[NT_B - 2 + j])
                    tinc(sync.dma_start(out=(ptile if sb else pbuf)[:],
                                        in_=sin2_dram[j * 128:(j + 1) * 128, g * 256:(g + 1) * 256]),
                         lds[sb], 16)
                ld_j = lds[sb].v

                @block.scalar
                def _(scalar, g=g, j=j, sb=sb, ld_j=ld_j):
                    scalar.wait_ge(lds[sb].h, ld_j)
                    tinc(scalar.activation((ptile if sb else pbuf)[:],
                                           (ptile if sb else pbuf)[:], AF.Exp), se, 1)
                se_e = se.v

                @block.tensor
                def _(tensor, g=g, j=j, sb=sb, se_e=se_e):
                    tensor.wait_ge(se.h, se_e)
                    src_ = ptile if sb else pbuf
                    tensor.matmul(ps_z[:], ones_b[:], src_[:], start=(j == 0), stop=(j == NT_B - 1))
                    for s in range(2):
                        for hh in range(2):
                            tinc(tensor.matmul(ps_o[s * 2 + hh][:],
                                               src_[:, s * 128:(s + 1) * 128],
                                               lnb[:, j * D + hh * 512: j * D + (hh + 1) * 512],
                                               start=(j == 0), stop=(j == NT_B - 1)), mm, 1)
                mm_j.append(mm.v)
            mm_j_prev = mm_j
            mm_grp = mm.v

            @block.vector
            def _(vector, g=g, mm_grp=mm_grp):
                vector.wait_ge(mm.h, mm_grp)
                tinc(vector.tensor_copy(zrow[:], ps_z[:]), ve, 1)
            ve_z = ve.v

            for s in range(2):
                @block.tensor
                def _(tensor, g=g, s=s, ve_z=ve_z):
                    tensor.wait_ge(ve.h, ve.v)
                    tensor.wait_ge(ve.h, ve_z)
                    tinc(tensor.matmul(ps_t[:, 0:1], zrow[0:1, s * 128:(s + 1) * 128], one1[:],
                                       start=True, stop=True), mm, 1)
                mm_zc = mm.v

                @block.sync
                def _(sync, g=g, s=s):
                    sync.wait_ge(stx.h, stx.v)
                    r0 = g * 256 + s * 128
                    tinc(sync.dma_start(out=xres[:], in_=x_all[r0:r0 + 128, :]), ldr, 16)
                ld_r = ldr.v

                @block.vector
                def _(vector, g=g, s=s, mm_zc=mm_zc, ld_r=ld_r):
                    vector.wait_ge(mm.h, mm_zc)
                    vector.wait_ge(stx.h, stx.v)
                    ch(vector.tensor_copy(zrec[:], ps_t[:, 0:1]), vch, vector)
                    ch(vector.reciprocal(zrec[:], zrec[:]), vch, vector)
                    for hh in range(2):
                        ch(vector.tensor_scalar(outs[:, hh * 512:(hh + 1) * 512], ps_o[s * 2 + hh][:],
                                                zrec[:, 0:1], None, ALU.mult), vch, vector)
                    vector.wait_ge(ldr.h, ld_r)
                    tinc(vector.tensor_tensor(outs[:], outs[:], xres[:], ALU.add), ve, 1)
                ve_r = ve.v

                @block.sync
                def _(sync, g=g, s=s, ve_r=ve_r):
                    sync.wait_ge(ve.h, ve_r)
                    r0 = g * 256 + s * 128
                    tinc(sync.dma_start(out=x1_tm[r0:r0 + 128, :], in_=outs[:]), stx, 16)

        @block.sync
        def _(sync):
            sync.wait_ge(stx.h, stx.v)

    return nc



def build_B1(detect=True):
    nc = bass.Bass("TRN2", target_bir_lowering=False, debug=False,
                   detect_race_conditions=detect)
    x1 = nc.dram_tensor("x1", [TLOC, D], F32, kind="ExternalInput").ap()
    rw = nc.dram_tensor("rw", [D, 16], F32, kind="ExternalInput").ap()
    rb = nc.dram_tensor("rb", [1, 16], F32, kind="ExternalInput").ap()
    onesc = nc.dram_tensor("onesc", [128, 1], F32, kind="ExternalInput").ap()
    iota16 = nc.dram_tensor("iota16", [128, 16], F32, kind="ExternalInput").ap()
    ln_tm = nc.dram_tensor("ln_tm", [TLOC, D], BF16, kind="ExternalOutput").ap()
    route = nc.dram_tensor("route", [TLOC, 4], F32, kind="ExternalOutput").ap()
    counts = nc.dram_tensor("counts", [1, 16], F32, kind="ExternalOutput").ap()

    NTI = TLOC // 128

    with contextlib.ExitStack() as ctx:
        E = ctx.enter_context
        xb = [E(nc.sbuf_tensor(f"xb{i}", [128, D], F32)) for i in range(2)]
        sq = E(nc.sbuf_tensor("sq", [128, D], F32))
        lnf = E(nc.sbuf_tensor("lnf", [128, D], F32))
        lnt = [E(nc.sbuf_tensor(f"lnt{i}", [128, D], BF16)) for i in range(2)]
        stat = [E(nc.sbuf_tensor(f"stat{i}", [128, 8], F32)) for i in range(2)]
        epsb = E(nc.sbuf_tensor("epsb", [128, 1], F32))
        ones_s = E(nc.sbuf_tensor("ones_s", [128, 1], F32))
        onesr_b = E(nc.sbuf_tensor("onesr_b", [1, 128], BF16))
        rw_s = E(nc.sbuf_tensor("rw_s", [128, 8 * 16], BF16))
        rb_s = E(nc.sbuf_tensor("rb_s", [1, 16], BF16))
        iota_s = E(nc.sbuf_tensor("iota_s", [128, 16], F32))
        iM_s = E(nc.sbuf_tensor("iM_s", [128, 16], F32))
        ln_fm = E(nc.sbuf_tensor("ln_fm", [128, NCH * TLOC], BF16))
        probs = E(nc.sbuf_tensor("probs", [128, 16], F32))
        t16 = E(nc.sbuf_tensor("t16", [128, 16], F32))
        oh1 = E(nc.sbuf_tensor("oh1", [128, 16], F32))
        oh2 = E(nc.sbuf_tensor("oh2", [128, 16], F32))
        col = E(nc.sbuf_tensor("col", [128, 8], F32))
        rtile = [E(nc.sbuf_tensor(f"rtile{i}", [128, 4], F32)) for i in range(2)]
        cnt_s = E(nc.sbuf_tensor("cnt_s", [1, 16], F32))

        ps_logit = E(nc.psum_tensor("ps_logit", [128, 16], F32))
        ps_cnt = E(nc.psum_tensor("ps_cnt", [1, 16], F32))

        vch = Sem(ctx, nc, "vch")
        sch = Sem(ctx, nc, "sch")
        ldc = Sem(ctx, nc, "ldc")
        ldx = [Sem(ctx, nc, f"ldx{i}") for i in range(2)]
        ldt = [Sem(ctx, nc, f"ldt{i}") for i in range(NCH)]
        stl = [Sem(ctx, nc, f"stl{i}") for i in range(2)]
        str_ = [Sem(ctx, nc, f"str{i}") for i in range(2)]
        stc = Sem(ctx, nc, "stc")
        mm = Sem(ctx, nc, "mm")
        ve = Sem(ctx, nc, "ve")
        se = Sem(ctx, nc, "se")
        gld = Sem(ctx, nc, "gld")

        block = E(nc.Block())

        @block.vector
        def _(vector):
            vector.memset(epsb[:], 1e-5)
            tinc(vector.memset(onesr_b[:], 1.0), ve, 1)
        ve_init = ve.v

        @block.sync
        def _(sync):
            tinc(sync.dma_start(out=ones_s[:], in_=onesc), ldc, 16)
            tinc(sync.dma_start(out=iota_s[:], in_=iota16), ldc, 16)

        @block.gpsimd
        def _(gpsimd):
            for c in range(NCH):
                gpsimd.dma_start(out=rw_s[:, c * 16:(c + 1) * 16], in_=rw[c * 128:(c + 1) * 128, :]).then_inc(gld.h, 16)
                gld.inc(16)
            tinc(gpsimd.dma_start(out=rb_s[:], in_=rb), gld, 16)
        gld_w = gld.v

        @block.vector
        def _(vector):
            vector.wait_ge(ldc.h, ldc.v)
            tinc(vector.tensor_scalar(iM_s[:], iota_s[:], 64.0, None, ALU.subtract), ve, 1)
        ve_iM = ve.v

        tile_done = []
        se_tile = []
        for i in range(NTI):
            b = i % 2

            @block.sync
            def _(sync, i=i, b=b):
                if i >= 2:
                    sync.wait_ge(ve.h, tile_done[i - 2])
                    sync.wait_ge(se.h, se_tile[i - 2])
                tinc(sync.dma_start(out=xb[b][:], in_=x1[i * 128:(i + 1) * 128, :]), ldx[b], 16)
            ld_i = ldx[b].v

            @block.scalar
            def _(scalar, i=i, b=b, ld_i=ld_i):
                scalar.wait_ge(ldx[b].h, ld_i)
                if i >= 1:
                    scalar.wait_ge(sch.h, sch.v)
                if i >= 2:
                    scalar.wait_ge(ve.h, tile_done[i - 2])
                ch(scalar.activation(sq[:], xb[b][:], AF.Copy, accum_out=stat[b][:, 0:1]), sch, scalar)
                tinc(scalar.activation(sq[:], xb[b][:], AF.Square, accum_out=stat[b][:, 1:2]), se, 1)
            se_sums = se.v

            @block.vector
            def _(vector, i=i, b=b, se_sums=se_sums):
                vector.wait_ge(se.h, se_sums)
                ch(vector.tensor_scalar(stat[b][:, 2:3], stat[b][:, 0:1], 1.0 / D, None, ALU.mult), vch, vector)
                ch(vector.tensor_tensor(stat[b][:, 3:4], stat[b][:, 2:3], stat[b][:, 2:3], ALU.mult), vch, vector)
                ch(vector.tensor_scalar(stat[b][:, 5:6], stat[b][:, 1:2], 1.0 / D, None, ALU.mult), vch, vector)
                tinc(vector.tensor_tensor(stat[b][:, 3:4], stat[b][:, 5:6], stat[b][:, 3:4], ALU.subtract), ve, 1)
            ve_var = ve.v

            @block.scalar
            def _(scalar, i=i, b=b, ve_var=ve_var):
                scalar.wait_ge(ve.h, ve_var)
                tinc(scalar.activation(stat[b][:, 4:5], stat[b][:, 3:4], AF.Sqrt, bias=epsb[:, 0:1]), se, 1)
            se_tile.append(se.v)
            se_sqrt = se.v

            @block.vector
            def _(vector, i=i, b=b, se_sqrt=se_sqrt):
                vector.wait_ge(se.h, se_sqrt)
                ch(vector.reciprocal(stat[b][:, 4:5], stat[b][:, 4:5]), vch, vector)
                vector.wait_ge(ve.h, ve.v)
                ch(vector.tensor_scalar(lnf[:], xb[b][:], stat[b][:, 2:3], stat[b][:, 4:5],
                                        ALU.subtract, ALU.mult), vch, vector)
                vector.wait_ge(stl[b].h, stl[b].v)
                tinc(vector.tensor_copy(lnt[b][:], lnf[:]), ve, 1)
            tile_done.append(ve.v)
            ve_ln = ve.v

            @block.sync
            def _(sync, i=i, b=b, ve_ln=ve_ln):
                sync.wait_ge(ve.h, ve_ln)
                tinc(sync.dma_start(out=ln_tm[i * 128:(i + 1) * 128, :], in_=lnt[b][:]), stl[b], 16)
        stl_all = (stl[0].v, stl[1].v)

        for c in range(NCH):
            @block.scalar
            def _(scalar, c=c):
                scalar.wait_ge(stl[0].h, stl_all[0])
                scalar.wait_ge(stl[1].h, stl_all[1])
                tinc(scalar.dma_start(out=ln_fm[:, c * TLOC:(c + 1) * TLOC],
                                      in_=ln_tm[:, c * 128:(c + 1) * 128], transpose=True), ldt[c], 16)

        route_done = []
        for i in range(NTI):
            b = i % 2
            mm_l0 = mm.v

            @block.tensor
            def _(tensor, i=i, mm_l0=mm_l0):
                if i == 0:
                    for c in range(NCH):
                        tensor.wait_ge(ldt[c].h, ldt[c].v)
                    tensor.wait_ge(gld.h, gld_w)
                    tensor.wait_ge(ve.h, ve_init)
                else:
                    tensor.wait_ge(ve.h, ve.v)
                    tensor.wait_ge(se.h, se.v)
                for c in range(NCH):
                    tensor.matmul(ps_logit[:], ln_fm[:, c * TLOC + i * 128: c * TLOC + (i + 1) * 128],
                                  rw_s[:, c * 16:(c + 1) * 16], start=(c == 0), stop=False)
                tinc(tensor.matmul(ps_logit[:], onesr_b[0:1, 0:128], rb_s[:],
                                   start=False, stop=True), mm, 1)
            mm_log = mm.v

            @block.vector
            def _(vector, i=i, mm_log=mm_log):
                vector.wait_ge(mm.h, mm_log)
                ch(vector.reduce_max(col[:, 0:1], ps_logit[:], axis=AXX), vch, vector)
                tinc(vector.tensor_scalar(col[:, 1:2], col[:, 0:1], -1.0, None, ALU.mult), ve, 1)
            ve_mx = ve.v

            @block.scalar
            def _(scalar, i=i, ve_mx=ve_mx):
                scalar.wait_ge(ve.h, ve_mx)
                tinc(scalar.activation(probs[:], ps_logit[:], AF.Exp, bias=col[:, 1:2]), se, 1)
            se_exp = se.v

            @block.vector
            def _(vector, i=i, b=b, se_exp=se_exp):
                vector.wait_ge(se.h, se_exp)
                if i >= 1:
                    vector.wait_ge(mm.h, mm.v)
                ch(vector.reduce_sum(col[:, 2:3], probs[:], axis=AXX), vch, vector)
                ch(vector.reciprocal(col[:, 2:3], col[:, 2:3]), vch, vector)
                ch(vector.tensor_scalar(probs[:], probs[:], col[:, 2:3], None, ALU.mult), vch, vector)
                ch(vector.reduce_max(col[:, 0:1], probs[:], axis=AXX), vch, vector)
                ch(vector.tensor_scalar(t16[:], probs[:], col[:, 0:1], None, ALU.is_equal), vch, vector)
                ch(vector.tensor_tensor(t16[:], t16[:], iM_s[:], ALU.mult), vch, vector)
                ch(vector.tensor_scalar(t16[:], t16[:], 64.0, None, ALU.add), vch, vector)
                ch(vector.tensor_reduce(col[:, 4:5], t16[:], AXX, ALU.min), vch, vector)
                ch(vector.tensor_scalar(oh1[:], iota_s[:], col[:, 4:5], None, ALU.is_equal), vch, vector)
                ch(vector.tensor_tensor(t16[:], probs[:], oh1[:], ALU.mult), vch, vector)
                ch(vector.reduce_sum(col[:, 5:6], t16[:], axis=AXX), vch, vector)
                ch(vector.tensor_tensor(t16[:], probs[:], t16[:], ALU.subtract), vch, vector)
                ch(vector.reduce_max(col[:, 0:1], t16[:], axis=AXX), vch, vector)
                ch(vector.tensor_scalar(oh2[:], t16[:], col[:, 0:1], None, ALU.is_equal), vch, vector)
                ch(vector.tensor_tensor(oh2[:], oh2[:], iM_s[:], ALU.mult), vch, vector)
                ch(vector.tensor_scalar(oh2[:], oh2[:], 64.0, None, ALU.add), vch, vector)
                ch(vector.tensor_reduce(col[:, 6:7], oh2[:], AXX, ALU.min), vch, vector)
                ch(vector.tensor_scalar(oh2[:], iota_s[:], col[:, 6:7], None, ALU.is_equal), vch, vector)
                ch(vector.tensor_tensor(t16[:], t16[:], oh2[:], ALU.mult), vch, vector)
                ch(vector.reduce_sum(col[:, 7:8], t16[:], axis=AXX), vch, vector)
                ch(vector.tensor_tensor(col[:, 2:3], col[:, 5:6], col[:, 7:8], ALU.add), vch, vector)
                ch(vector.tensor_scalar(col[:, 2:3], col[:, 2:3], 1e-9, None, ALU.add), vch, vector)
                ch(vector.reciprocal(col[:, 2:3], col[:, 2:3]), vch, vector)
                vector.wait_ge(str_[b].h, str_[b].v)
                ch(vector.tensor_tensor(rtile[b][:, 0:1], col[:, 4:5], ones_s[:, 0:1], ALU.mult), vch, vector)
                ch(vector.tensor_tensor(rtile[b][:, 1:2], col[:, 6:7], ones_s[:, 0:1], ALU.mult), vch, vector)
                ch(vector.tensor_tensor(rtile[b][:, 2:3], col[:, 5:6], col[:, 2:3], ALU.mult), vch, vector)
                ch(vector.tensor_tensor(rtile[b][:, 3:4], col[:, 7:8], col[:, 2:3], ALU.mult), vch, vector)
                tinc(vector.tensor_tensor(t16[:], oh1[:], oh2[:], ALU.add), ve, 1)
            ve_rt = ve.v

            @block.tensor
            def _(tensor, i=i, ve_rt=ve_rt):
                tensor.wait_ge(ve.h, ve_rt)
                tensor.wait_ge(ldc.h, 32)
                tinc(tensor.matmul(ps_cnt[:], ones_s[:], t16[:],
                                   start=(i == 0), stop=(i == NTI - 1)), mm, 1)

            @block.sync
            def _(sync, i=i, b=b, ve_rt=ve_rt):
                sync.wait_ge(ve.h, ve_rt)
                tinc(sync.dma_start(out=route[i * 128:(i + 1) * 128, :], in_=rtile[b][:]), str_[b], 16)
            route_done.append((str_[b].v, b))
        mm_cnt = mm.v

        @block.vector
        def _(vector):
            vector.wait_ge(mm.h, mm_cnt)
            tinc(vector.tensor_copy(cnt_s[:], ps_cnt[:]), ve, 1)
        ve_cnt = ve.v

        @block.sync
        def _(sync):
            sync.wait_ge(ve.h, ve_cnt)
            tinc(sync.dma_start(out=counts, in_=cnt_s[:]), stc, 16)
            sync.wait_ge(stc.h, stc.v)
            sync.wait_ge(str_[0].h, str_[0].v)
            sync.wait_ge(str_[1].h, str_[1].v)

    return nc



CL = 1536
NBLK = CL // 128
TOTSLOT = 16 * CL
CAP_C = 5120.0
BIGOOB = 1.0e8


def build_B2(detect=True):
    nc = bass.Bass("TRN2", target_bir_lowering=False, debug=False,
                   detect_race_conditions=detect)
    x1 = nc.dram_tensor("x1", [TLOC, D], F32, kind="ExternalInput").ap()
    ln_tm = nc.dram_tensor("ln_tm", [TLOC, D], BF16, kind="ExternalInput").ap()
    route = nc.dram_tensor("route", [TLOC, 4], F32, kind="ExternalInput").ap()
    counts_all = nc.dram_tensor("counts_all", [8, 16], F32, kind="ExternalInput").ap()
    coremask = nc.dram_tensor("coremask", [8, 1], F32, kind="ExternalInput").ap()
    ident = nc.dram_tensor("ident", [128, 128], F32, kind="ExternalInput").ap()
    onesc = nc.dram_tensor("onesc", [128, 1], F32, kind="ExternalInput").ap()
    iota16 = nc.dram_tensor("iota16", [128, 16], F32, kind="ExternalInput").ap()
    eoffC = nc.dram_tensor("eoffC", [128, 16], F32, kind="ExternalInput").ap()
    iotacol = nc.dram_tensor("iotacol", [128, 1], F32, kind="ExternalInput").ap()
    pfill = nc.dram_tensor("pfill", [TOTSLOT, 4], F32, kind="ExternalInput").ap()
    swg = nc.dram_tensor("swg", [D, 1408], F32, kind="ExternalInput").ap()
    swu = nc.dram_tensor("swu", [D, 1408], F32, kind="ExternalInput").ap()
    swd = nc.dram_tensor("swd", [1408, D], F32, kind="ExternalInput").ap()
    wg = nc.dram_tensor("wg", [16, D, 1408], F32, kind="ExternalInput").ap()
    wu = nc.dram_tensor("wu", [16, D, 1408], F32, kind="ExternalInput").ap()
    wdn = nc.dram_tensor("wdn", [16, 1408, D], F32, kind="ExternalInput").ap()
    x2 = nc.dram_tensor("x2", [TLOC, D], F32, kind="ExternalOutput").ap()

    ptable = nc.dram_tensor("ptable", [TOTSLOT + 128, 4], F32, kind="Internal").ap()
    gtmp = [nc.dram_tensor(f"gtmp{i}", [CL, D], BF16, kind="Internal").ap() for i in range(2)]
    ofm_dram = [nc.dram_tensor(f"ofm_dram{i}", [D, CL], BF16, kind="Internal").ap() for i in range(2)]
    swgb = nc.dram_tensor("swgb", [D, 1408], BF16, kind="Internal").ap()
    swub = nc.dram_tensor("swub", [D, 1408], BF16, kind="Internal").ap()
    swdb = nc.dram_tensor("swdb", [1408, D], BF16, kind="Internal").ap()
    wgb = nc.dram_tensor("wgb", [16 * D, 1408], BF16, kind="Internal").ap()
    wub = nc.dram_tensor("wub", [16 * D, 1408], BF16, kind="Internal").ap()
    wdb = nc.dram_tensor("wdb", [16 * 1408, D], BF16, kind="Internal").ap()
    sfm_dram = nc.dram_tensor("sfm_dram", [D, TLOC], BF16, kind="Internal").ap()
    img2 = nc.dram_tensor("img2", [2 * TLOC + 128, D], F32, kind="Internal").ap()

    NTI = TLOC // 128
    NF = 11
    NTT8 = TLOC // 512

    with contextlib.ExitStack() as ctx:
        E = ctx.enter_context
        ident_s = E(nc.sbuf_tensor("ident_s", [128, 128], F32))
        ones_s = E(nc.sbuf_tensor("ones_s", [128, 1], F32))
        iota_s = E(nc.sbuf_tensor("iota_s", [128, 16], F32))
        eoff_s = E(nc.sbuf_tensor("eoff_s", [128, 16], F32))
        icol_s = E(nc.sbuf_tensor("icol_s", [128, 1], F32))
        cmask_s = E(nc.sbuf_tensor("cmask_s", [8, 1], F32))
        call_s = E(nc.sbuf_tensor("call_s", [8, 16], F32))
        pre_row = E(nc.sbuf_tensor("pre_row", [1, 16], F32))
        pre_col = E(nc.sbuf_tensor("pre_col", [16, 1], F32))
        one1 = E(nc.sbuf_tensor("one1", [1, 1], F32))
        route_sb = E(nc.sbuf_tensor("route_sb", [128, NTI * 4], F32))
        pick_fm = E(nc.sbuf_tensor("pick_fm", [16, TLOC], F32))
        posE_fm = E(nc.sbuf_tensor("posE_fm", [16, TLOC], F32))
        ploc_fm = E(nc.sbuf_tensor("ploc_fm", [16, TLOC], F32))
        posg_tm = E(nc.sbuf_tensor("posg_tm", [128, 16], F32))
        ploc_tm = E(nc.sbuf_tensor("ploc_tm", [128, 16], F32))
        t16 = E(nc.sbuf_tensor("t16", [128, 16], F32))
        u16 = E(nc.sbuf_tensor("u16", [128, 16], F32))
        oh = E(nc.sbuf_tensor("oh", [128, 16], F32))
        colB = E(nc.sbuf_tensor("colB", [128, 8], F32))
        stile = [E(nc.sbuf_tensor(f"stile{i}", [128, 4], F32)) for i in range(2)]
        di32 = [E(nc.sbuf_tensor(f"di32{i}", [128, 1], I32)) for i in range(2)]
        wgt = E(nc.sbuf_tensor("wgt", [128, NCH * 1408], BF16))
        wut = E(nc.sbuf_tensor("wut", [128, NCH * 1408], BF16))
        wdt = E(nc.sbuf_tensor("wdt", [128, NF * 1024], BF16))
        actb = [E(nc.sbuf_tensor(f"actb{i}", [128, NCH * 512], BF16)) for i in range(2)]
        mbuf = E(nc.sbuf_tensor("mbuf", [128, NF * 512], BF16))
        s1f = [E(nc.sbuf_tensor(f"s1f{i}", [128, 512], F32)) for i in range(2)]
        obuf = [E(nc.sbuf_tensor(f"obuf{i}", [128, 512], BF16)) for i in range(2)]
        pt_all = E(nc.sbuf_tensor("pt_all", [128, 4 * NBLK], F32))
        gidx_all = E(nc.sbuf_tensor("gidx_all", [128, NBLK], I32))
        gt_s = [E(nc.sbuf_tensor(f"gt_s{i}", [128, D], BF16)) for i in range(2)]
        srows_s = [E(nc.sbuf_tensor(f"srows_s{i}", [128, NBLK], I32)) for i in range(2)]
        ws_s = [E(nc.sbuf_tensor(f"ws_s{i}", [128, NBLK], F32)) for i in range(2)]
        ot_s = [E(nc.sbuf_tensor(f"ot_s{i}", [128, D], BF16)) for i in range(2)]
        ow_f = [E(nc.sbuf_tensor(f"ow_f{i}", [128, D], F32)) for i in range(2)]
        fx = [E(nc.sbuf_tensor(f"fx{i}", [128, D], F32)) for i in range(2)]
        fl = [E(nc.sbuf_tensor(f"fl{i}", [128, D], BF16)) for i in range(2)]
        fs = [E(nc.sbuf_tensor(f"fs{i}", [128, D], BF16)) for i in range(2)]
        fg = [E(nc.sbuf_tensor(f"fg{i}", [128, 2 * D], F32)) for i in range(2)]

        ps_h1 = [E(nc.psum_tensor(f"ps_h1{i}", [128, 512], F32)) for i in range(2)]
        ps_h2 = [E(nc.psum_tensor(f"ps_h2{i}", [128, 512], F32)) for i in range(2)]
        ps_o = [E(nc.psum_tensor(f"ps_o{i}", [128, 512], F32)) for i in range(2)]
        ps_t = E(nc.psum_tensor("ps_t", [128, 128], F32))
        ps_pre = E(nc.psum_tensor("ps_pre", [16, 16], F32))

        vch = Sem(ctx, nc, "vch")
        sch = Sem(ctx, nc, "sch")
        ldc = Sem(ctx, nc, "ldc")
        gwc = Sem(ctx, nc, "gwc")
        gwcs = Sem(ctx, nc, "gwcs")
        hwl = Sem(ctx, nc, "hwl")
        stdone = Sem(ctx, nc, "stdone")
        scdone = Sem(ctx, nc, "scdone")
        gthdone = Sem(ctx, nc, "gthdone")
        gsc = Sem(ctx, nc, "gsc")
        gpt = Sem(ctx, nc, "gpt")
        ggth = Sem(ctx, nc, "ggth")
        gst2p = [Sem(ctx, nc, f"gst2p{i}") for i in range(2)]
        trnp = [Sem(ctx, nc, f"trnp{i}") for i in range(2)]
        trbp = [Sem(ctx, nc, f"trbp{i}") for i in range(2)]
        trf = [Sem(ctx, nc, f"trf{i}") for i in range(2)]
        stoP = [Sem(ctx, nc, f"stoP{i}") for i in range(2)]
        stz = Sem(ctx, nc, "stz")
        stsh = Sem(ctx, nc, "stsh")
        gsbP = [Sem(ctx, nc, f"gsbP{i}") for i in range(2)]
        stx = [Sem(ctx, nc, f"stx{i}") for i in range(2)]
        ldf = [Sem(ctx, nc, f"ldf{i}") for i in range(2)]
        mm = Sem(ctx, nc, "mm")
        ve = Sem(ctx, nc, "ve")
        se = Sem(ctx, nc, "se")

        block = E(nc.Block())

        @block.sync
        def _(sync):
            tinc(sync.dma_start(out=ident_s[:], in_=ident), ldc, 16)
            tinc(sync.dma_start(out=ones_s[:], in_=onesc), ldc, 16)
            tinc(sync.dma_start(out=one1[:], in_=onesc[0:1, 0:1]), ldc, 16)
            tinc(sync.dma_start(out=iota_s[:], in_=iota16), ldc, 16)
            tinc(sync.dma_start(out=eoff_s[:], in_=eoffC), ldc, 16)
            tinc(sync.dma_start(out=icol_s[:], in_=iotacol), ldc, 16)
            tinc(sync.dma_start(out=cmask_s[:], in_=coremask), ldc, 16)
            tinc(sync.dma_start(out=call_s[:], in_=counts_all), ldc, 16)
            tinc(sync.dma_start(out=route_sb[:].rearrange("p (i c) -> p i c", c=4),
                                in_=route.rearrange("(i p) c -> p i c", p=128)), ldc, 16)
            sync.wait_ge(ldc.h, ldc.v)

        @block.gpsimd
        def _(gpsimd):
            tinc(gpsimd.dma_start(out=ptable[0:TOTSLOT, :], in_=pfill), gsc, 16)
        gsc_fill = gsc.v

        @block.vector
        def _(vector):
            tinc(vector.memset(fx[0][:], 0.0), ve, 1)
        ve_z = ve.v

        @block.scalar
        def _(scalar):
            scalar.wait_ge(ve.h, ve_z)
            for zz in range(2 * TLOC // 128):
                tinc(scalar.dma_start(out=img2[zz * 128:(zz + 1) * 128, :], in_=fx[0][:]), stz, 16)
        stz_zero = stz.v

        cast_snap = []

        @block.gpsimd
        def _(gpsimd):
            tinc(gpsimd.dma_start(out=swgb, in_=swg), gwcs, 16)
            tinc(gpsimd.dma_start(out=swub, in_=swu), gwcs, 16)
            tinc(gpsimd.dma_start(out=swdb, in_=swd), gwcs, 16)
        gwc_sh = gwcs.v

        @block.gpsimd
        def _(gpsimd):
            for e in range(16):
                tinc(gpsimd.dma_start(out=wgb[e * D:(e + 1) * D, :], in_=wg[e, :, :]), gwc, 16)
                tinc(gpsimd.dma_start(out=wub[e * D:(e + 1) * D, :], in_=wu[e, :, :]), gwc, 16)
                tinc(gpsimd.dma_start(out=wdb[e * 1408:(e + 1) * 1408, :], in_=wdn[e, :, :]), gwc, 16)
                cast_snap.append(gwc.v)
        cast_all = gwc.v

        @block.sync
        def _(sync):
            sync.wait_ge(gwcs.h, gwc_sh)
            for c in range(NCH):
                tinc(sync.dma_start(out=wgt[:, c * 1408:(c + 1) * 1408],
                                    in_=swgb[c * 128:(c + 1) * 128, :]), hwl, 16)
                tinc(sync.dma_start(out=wut[:, c * 1408:(c + 1) * 1408],
                                    in_=swub[c * 128:(c + 1) * 128, :]), hwl, 16)
            for f in range(NF):
                tinc(sync.dma_start(out=wdt[:, f * 1024:(f + 1) * 1024],
                                    in_=swdb[f * 128:(f + 1) * 128, :]), hwl, 16)
        hwl_sh = hwl.v

        @block.tensor
        def _(tensor):
            tensor.wait_ge(ldc.h, ldc.v)
            tinc(tensor.matmul(ps_pre[0:1, :], cmask_s[:], call_s[:], start=True, stop=True), mm, 1)

        @block.vector
        def _(vector):
            vector.wait_ge(mm.h, mm.v)
            tinc(vector.tensor_copy(pre_row[:], ps_pre[0:1, :]), ve, 1)
        ve_pr = ve.v

        @block.tensor
        def _(tensor):
            tensor.wait_ge(ve.h, ve_pr)
            tinc(tensor.matmul(ps_t[0:16, 0:1], pre_row[:], one1[:], start=True, stop=True), mm, 1)

        @block.vector
        def _(vector):
            vector.wait_ge(mm.h, mm.v)
            tinc(vector.tensor_copy(pre_col[:], ps_t[0:16, 0:1]), ve, 1)
        ve_prc = ve.v

        for i in range(NTI):
            @block.vector
            def _(vector, i=i):
                if i == 0:
                    vector.wait_ge(ldc.h, ldc.v)
                else:
                    vector.wait_ge(mm.h, mm.v)
                ch(vector.tensor_scalar(t16[:], iota_s[:], route_sb[:, i * 4:i * 4 + 1], None, ALU.is_equal), vch, vector)
                ch(vector.tensor_scalar(u16[:], iota_s[:], route_sb[:, i * 4 + 1:i * 4 + 2], None, ALU.is_equal), vch, vector)
                tinc(vector.tensor_tensor(t16[:], t16[:], u16[:], ALU.add), ve, 1)
            ve_pk = ve.v

            @block.tensor
            def _(tensor, i=i, ve_pk=ve_pk):
                tensor.wait_ge(ve.h, ve_pk)
                if i >= 1:
                    tensor.wait_ge(ve.h, ve.v)
                tinc(tensor.transpose(ps_t[0:16, :], t16[:], ident_s[:]), mm, 1)
            mm_tp = mm.v

            @block.vector
            def _(vector, i=i, mm_tp=mm_tp):
                vector.wait_ge(mm.h, mm_tp)
                tinc(vector.tensor_copy(pick_fm[:, i * 128:(i + 1) * 128], ps_t[0:16, :]), ve, 1)

        @block.vector
        def _(vector):
            vector.wait_ge(ve.h, ve.v)
            vector.wait_ge(ve.h, ve_prc)
            ch(vector.tensor_tensor_scan(posE_fm[:], pick_fm[:], pick_fm[:], pre_col[:, 0:1],
                                         ALU.add, ALU.bypass), vch, vector)
            ch(vector.tensor_tensor(posE_fm[:], posE_fm[:], pick_fm[:], ALU.subtract), vch, vector)
            tinc(vector.tensor_scalar(ploc_fm[:], posE_fm[:], pre_col[:, 0:1], None, ALU.subtract), ve, 1)
        ve_scan = ve.v

        for i in range(NTI):
            @block.tensor
            def _(tensor, i=i, ve_scan=ve_scan):
                tensor.wait_ge(ve.h, ve_scan)
                tensor.wait_ge(ve.h, ve.v)
                tinc(tensor.transpose(ps_t[0:128, 0:16], posE_fm[:, i * 128:(i + 1) * 128],
                                      ident_s[0:16, 0:16]), mm, 1)
            mm_t1 = mm.v

            @block.vector
            def _(vector, i=i, mm_t1=mm_t1):
                vector.wait_ge(mm.h, mm_t1)
                tinc(vector.tensor_copy(posg_tm[:], ps_t[0:128, 0:16]), ve, 1)
            ve_t1 = ve.v

            @block.tensor
            def _(tensor, i=i, ve_t1=ve_t1):
                tensor.wait_ge(ve.h, ve_t1)
                tinc(tensor.transpose(ps_t[0:128, 0:16], ploc_fm[:, i * 128:(i + 1) * 128],
                                      ident_s[0:16, 0:16]), mm, 1)
            mm_t2 = mm.v

            @block.vector
            def _(vector, i=i, mm_t2=mm_t2):
                vector.wait_ge(mm.h, mm_t2)
                ch(vector.tensor_copy(ploc_tm[:], ps_t[0:128, 0:16]), vch, vector)
                tinc(vector.tensor_scalar(u16[:], posg_tm[:], CAP_C, None, ALU.is_lt), ve, 1)
            ve_t2 = ve.v

            for k in range(2):
                sb = (2 * i + k) % 2

                @block.vector
                def _(vector, i=i, k=k, sb=sb, ve_t2=ve_t2):
                    vector.wait_ge(ve.h, ve_t2)
                    vector.wait_ge(gsc.h, gsc.v)
                    ch(vector.tensor_scalar(oh[:], iota_s[:], route_sb[:, i * 4 + k:i * 4 + k + 1],
                                            None, ALU.is_equal), vch, vector)
                    ch(vector.tensor_tensor(t16[:], oh[:], ploc_tm[:], ALU.mult), vch, vector)
                    ch(vector.reduce_sum(colB[:, 0:1], t16[:], axis=AXX), vch, vector)
                    ch(vector.tensor_tensor(t16[:], oh[:], u16[:], ALU.mult), vch, vector)
                    ch(vector.reduce_sum(colB[:, 1:2], t16[:], axis=AXX), vch, vector)
                    ch(vector.tensor_tensor(t16[:], ploc_tm[:], eoff_s[:], ALU.add), vch, vector)
                    ch(vector.tensor_tensor(t16[:], oh[:], t16[:], ALU.mult), vch, vector)
                    ch(vector.reduce_sum(colB[:, 2:3], t16[:], axis=AXX), vch, vector)
                    ch(vector.tensor_scalar(colB[:, 3:4], colB[:, 0:1], float(CL), None, ALU.is_ge), vch, vector)
                    ch(vector.tensor_tensor(colB[:, 4:5], colB[:, 3:4], colB[:, 2:3], ALU.mult), vch, vector)
                    ch(vector.tensor_tensor(colB[:, 2:3], colB[:, 2:3], colB[:, 4:5], ALU.subtract), vch, vector)
                    ch(vector.scalar_tensor_tensor(colB[:, 2:3], colB[:, 3:4], float(TOTSLOT), colB[:, 2:3],
                                                   ALU.mult, ALU.add), vch, vector)
                    ch(vector.tensor_scalar(stile[sb][:, 0:1], icol_s[:], float(i * 128), None, ALU.add), vch, vector)
                    ch(vector.tensor_scalar(stile[sb][:, 1:2], stile[sb][:, 0:1], float(k * TLOC), None, ALU.add), vch, vector)
                    ch(vector.tensor_tensor(stile[sb][:, 2:3], route_sb[:, i * 4 + 2 + k:i * 4 + 3 + k],
                                            colB[:, 1:2], ALU.mult), vch, vector)
                    ch(vector.tensor_copy(stile[sb][:, 3:4], stile[sb][:, 2:3]), vch, vector)
                    tinc(vector.tensor_copy(di32[sb][:], colB[:, 2:3]), ve, 1)
                ve_st = ve.v

                @block.gpsimd
                def _(gpsimd, i=i, k=k, sb=sb, ve_st=ve_st):
                    gpsimd.wait_ge(ve.h, ve_st)
                    gpsimd.wait_ge(gsc.h, gsc_fill)
                    tinc(gpsimd.indirect_dma_start(
                        out=ptable, out_offset=IndirectOffsetOnAxis(di32[sb][:, 0:1], 0),
                        in_=stile[sb][:], in_offset=None), gsc, 16)
        gsc_all = gsc.v

        NTILE = NTT8 + 48
        act_hist = []
        trn_hist = [[], []]
        stoP_hist = {0: [], 1: []}
        silu_hist = []
        mult_hist = []
        ob_hist = []
        gidx_hist = []
        scat_hist = []
        gath_cnt = [0]
        wt_hist = {0: [], 1: []}
        ow_hist = {0: [], 1: []}
        st_glob = {0: [], 1: []}
        fcg = [0]
        dcg = [0]

        def tile_desc(ti):
            if ti < NTT8:
                return ("s", ti)
            return ((ti - NTT8) // 3, (ti - NTT8) % 3)

        def emit_transposes(ti):
            kind, sub = tile_desc(ti)
            p = ti % 2

            @block.scalar
            def _(scalar, ti=ti, kind=kind, sub=sub, p=p):
                if ti >= 2:
                    scalar.wait_ge(mm.h, act_hist[ti - 2])
                if kind != "s" and sub == 0:
                    scalar.wait_ge(gthdone.h, kind + 1)
                for c in range(NCH):
                    if kind == "s":
                        srcv = ln_tm[sub * 512:(sub + 1) * 512, c * 128:(c + 1) * 128]
                    else:
                        srcv = gtmp[kind % 2][sub * 512:(sub + 1) * 512, c * 128:(c + 1) * 128]
                    tinc(scalar.dma_start(out=actb[p][:, c * 512:(c + 1) * 512], in_=srcv,
                                          transpose=True), trnp[p], 16)
            trn_hist[p].append(trnp[p].v)
            return trnp[p].v

        def emit_mlp_512(ti, dst_store, trn_snap, hwl_gate=None, mid_emit=None):
            p = ti % 2
            src_buf = actb[p]
            for fc in range(NF):
                q = fcg[0] % 2
                fcg[0] += 1

                @block.tensor
                def _(tensor, fc=fc, q=q):
                    if fc == 0:
                        tensor.wait_ge(trnp[p].h, trn_snap)
                        if hwl_gate is not None:
                            tensor.wait_ge(hwl.h, hwl_gate)
                    if len(mult_hist) >= 2:
                        tensor.wait_ge(ve.h, mult_hist[-2])
                        tensor.wait_ge(se.h, silu_hist[-2])
                    for c in range(NCH):
                        tensor.matmul(ps_h1[q][:], wgt[:, c * 1408 + fc * 128:c * 1408 + (fc + 1) * 128],
                                      src_buf[:, c * 512:c * 512 + 512],
                                      start=(c == 0), stop=(c == NCH - 1))
                    for c in range(NCH):
                        i_ = tensor.matmul(ps_h2[q][:], wut[:, c * 1408 + fc * 128:c * 1408 + (fc + 1) * 128],
                                           src_buf[:, c * 512:c * 512 + 512],
                                           start=(c == 0), stop=(c == NCH - 1))
                    tinc(i_, mm, 1)
                mm_h = mm.v

                @block.scalar
                def _(scalar, fc=fc, q=q, mm_h=mm_h):
                    scalar.wait_ge(mm.h, mm_h)
                    if len(mult_hist) >= 2:
                        scalar.wait_ge(ve.h, mult_hist[-2])
                    tinc(scalar.activation(s1f[q][:], ps_h1[q][:], AF.Silu), se, 1)
                silu_hist.append(se.v)
                se_si = se.v

                @block.vector
                def _(vector, fc=fc, q=q, se_si=se_si):
                    vector.wait_ge(se.h, se_si)
                    tinc(vector.tensor_tensor(mbuf[:, fc * 512:(fc + 1) * 512], s1f[q][:], ps_h2[q][:], ALU.mult), ve, 1)
                mult_hist.append(ve.v)
            act_hist.append(mm.v)
            ve_m = ve.v
            if mid_emit is not None:
                mid_emit()

            for dc in range(NCH):
                q = dcg[0] % 2
                dcg[0] += 1

                @block.tensor
                def _(tensor, dc=dc, q=q, ve_m=ve_m):
                    if dc == 0:
                        tensor.wait_ge(ve.h, ve_m)
                    if len(ob_hist) >= 2:
                        tensor.wait_ge(se.h, ob_hist[-2])
                    for fc in range(NF):
                        i_ = tensor.matmul(ps_o[q][:], wdt[:, fc * 1024 + dc * 128:fc * 1024 + (dc + 1) * 128],
                                           mbuf[:, fc * 512:(fc + 1) * 512], start=(fc == 0), stop=(fc == NF - 1))
                    tinc(i_, mm, 1)
                mm_o = mm.v

                @block.scalar
                def _(scalar, dc=dc, q=q, mm_o=mm_o):
                    scalar.wait_ge(mm.h, mm_o)
                    if stoP_hist[q]:
                        scalar.wait_ge(stoP[q].h, stoP_hist[q][-1])
                    tinc(scalar.activation(obuf[q][:], ps_o[q][:], AF.Copy), se, 1)
                ob_hist.append(se.v)
                dst_store(dc, q, se.v)

        def mk_store_s(t0):
            def store(dc, q, se_ob):
                @block.sync
                def _(sync, dc=dc, q=q, se_ob=se_ob, t0=t0):
                    sync.wait_ge(se.h, se_ob)
                    tinc(sync.dma_start(out=sfm_dram[dc * 128:(dc + 1) * 128, t0:t0 + 512],
                                        in_=obuf[q][:]), stoP[q], 16)
                stoP_hist[q].append(stoP[q].v)
            return store

        def mk_store_e(e, c0, gate0):
            def store(dc, q, se_ob):
                @block.sync
                def _(sync, dc=dc, q=q, se_ob=se_ob):
                    if gate0 is not None and dc == 0:
                        sync.wait_ge(scdone.h, gate0)
                    sync.wait_ge(se.h, se_ob)
                    tinc(sync.dma_start(out=ofm_dram[e % 2][dc * 128:(dc + 1) * 128, c0:c0 + 512],
                                        in_=obuf[q][:]), stoP[q], 16)
                stoP_hist[q].append(stoP[q].v)
            return store

        def emit_gathers(e):
            par = e % 2

            @block.gpsimd
            def _(gpsimd, e=e, par=par):
                if e == 0:
                    gpsimd.wait_ge(gsc.h, gsc_all)
                if gidx_hist:
                    gpsimd.wait_ge(ve.h, gidx_hist[-1])
                if act_hist:
                    gpsimd.wait_ge(mm.h, act_hist[-1])
                for blk in range(NBLK):
                    r0 = e * CL + blk * 128
                    tinc(gpsimd.dma_start(out=pt_all[:, blk * 4:(blk + 1) * 4],
                                          in_=ptable[r0:r0 + 128, :]), gpt, 16)
            gpt_l = gpt.v

            @block.vector
            def _(vector, e=e, par=par, gpt_l=gpt_l):
                vector.wait_ge(gpt.h, gpt_l)
                if len(scat_hist) >= 1 and e >= 2:
                    vector.wait_ge(scdone.h, e - 1)
                for blk in range(NBLK):
                    ch(vector.tensor_copy(gidx_all[:, blk:blk + 1], pt_all[:, blk * 4:blk * 4 + 1]), vch, vector)
                    ch(vector.tensor_copy(srows_s[par][:, blk:blk + 1], pt_all[:, blk * 4 + 1:blk * 4 + 2]), vch, vector)
                    if blk < NBLK - 1:
                        ch(vector.tensor_copy(ws_s[par][:, blk:blk + 1], pt_all[:, blk * 4 + 2:blk * 4 + 3]), vch, vector)
                    else:
                        tinc(vector.tensor_copy(ws_s[par][:, blk:blk + 1], pt_all[:, blk * 4 + 2:blk * 4 + 3]), ve, 1)
            gidx_hist.append(ve.v)
            ve_gi = ve.v

            @block.gpsimd
            def _(gpsimd, e=e, par=par, ve_gi=ve_gi):
                gpsimd.wait_ge(ve.h, ve_gi)
                for blk in range(NBLK):
                    q = blk % 2
                    if st_glob[q]:
                        gpsimd.wait_ge(gst2p[q].h, st_glob[q][-1])
                    tinc(gpsimd.indirect_dma_start(
                        out=gt_s[q][:], out_offset=None,
                        in_=ln_tm, in_offset=IndirectOffsetOnAxis(gidx_all[:, blk:blk + 1], 0)), ggth, 16)
                    gpsimd.wait_ge(ggth.h, ggth.v)
                    tinc(gpsimd.dma_start(out=gtmp[par][blk * 128:(blk + 1) * 128, :], in_=gt_s[q][:]), gst2p[q], 16)
                    st_glob[q].append(gst2p[q].v)
                gpsimd.wait_ge(gst2p[0].h, st_glob[0][-1])
                gpsimd.wait_ge(gst2p[1].h, st_glob[1][-1])
                gpsimd.sem_inc(gthdone.h, 1)
            gthdone.v += 1
            gath_cnt[0] += 1

        def emit_scatters(e):
            par = e % 2
            for blk in range(NBLK):
                q = blk % 2

                @block.scalar
                def _(scalar, e=e, blk=blk, q=q):
                    if blk == 0:
                        scalar.wait_ge(stdone.h, e + 1)
                    if wt_hist[q]:
                        scalar.wait_ge(ve.h, wt_hist[q][-1])
                    tinc(scalar.dma_start(out=ot_s[q][:], in_=ofm_dram[par][:, blk * 128:(blk + 1) * 128],
                                          transpose=True), trbp[q], 16)
                trb_l = trbp[q].v

                @block.vector
                def _(vector, e=e, blk=blk, q=q, par=par, trb_l=trb_l):
                    vector.wait_ge(trbp[q].h, trb_l)
                    vector.wait_ge(gsbP[0].h, gsbP[0].v)
                    tinc(vector.tensor_scalar(ow_f[q][:], ot_s[q][:], ws_s[par][:, blk:blk + 1], None, ALU.mult), ve, 1)
                wt_hist[q].append(ve.v)
                ve_ow = ve.v

                @block.gpsimd
                def _(gpsimd, e=e, blk=blk, q=q, par=par, ve_ow=ve_ow):
                    gpsimd.wait_ge(ve.h, ve_ow)
                    if e == 0 and blk == 0:
                        gpsimd.wait_ge(stz.h, stz_zero)
                    tinc(gpsimd.indirect_dma_start(
                        out=img2, out_offset=IndirectOffsetOnAxis(srows_s[par][:, blk:blk + 1], 0),
                        in_=ow_f[q][:], in_offset=None), gsbP[0], 16)

            @block.gpsimd
            def _(gpsimd, e=e):
                gpsimd.wait_ge(gsbP[0].h, gsbP[0].v)
                gpsimd.sem_inc(scdone.h, 1)
            scdone.v += 1
            scat_hist.append(gsbP[0].v)

        emit_gathers(0)
        tr_snaps = {0: emit_transposes(0)}

        def mid_for(ti):
            if ti + 1 < NTILE:
                def f(ti=ti):
                    tr_snaps[ti + 1] = emit_transposes(ti + 1)
                return f
            return None

        for t in range(NTT8):
            emit_mlp_512(t, mk_store_s(t * 512), tr_snaps[t],
                         hwl_gate=(hwl_sh if t == 0 else None), mid_emit=mid_for(t))

        def emit_store_marker(eidx):
            @block.sync
            def _(sync, eidx=eidx):
                sync.wait_ge(stoP[0].h, stoP[0].v)
                sync.wait_ge(stoP[1].h, stoP[1].v)
                sync.sem_inc(stdone.h, 1)
            stdone.v += 1

        prev_gu = act_hist[NTT8 - 1]
        prev_end = mm.v
        for e in range(16):
            @block.sync
            def _(sync, e=e, prev_gu=prev_gu, prev_end=prev_end):
                sync.wait_ge(gwc.h, cast_all)
                sync.wait_ge(mm.h, prev_gu)
                for c in range(NCH):
                    tinc(sync.dma_start(out=wgt[:, c * 1408:(c + 1) * 1408],
                                        in_=wgb[e * D + c * 128:e * D + (c + 1) * 128, :]), hwl, 16)
                    tinc(sync.dma_start(out=wut[:, c * 1408:(c + 1) * 1408],
                                        in_=wub[e * D + c * 128:e * D + (c + 1) * 128, :]), hwl, 16)
                sync.wait_ge(mm.h, prev_end)
                for f in range(NF):
                    tinc(sync.dma_start(out=wdt[:, f * 1024:(f + 1) * 1024],
                                        in_=wdb[e * 1408 + f * 128:e * 1408 + (f + 1) * 128, :]), hwl, 16)
            hwl_e = hwl.v
            if e >= 1:
                emit_store_marker(e - 1)

            if e + 1 < 16:
                emit_gathers(e + 1)

            gate0 = (e - 1) if e >= 2 else None
            for ct in range(3):
                ti = NTT8 + e * 3 + ct
                emit_mlp_512(ti, mk_store_e(e, ct * 512, gate0 if ct == 0 else None),
                             tr_snaps[ti], hwl_gate=(hwl_e if ct == 0 else None),
                             mid_emit=mid_for(ti))
            prev_gu = act_hist[-1]
            prev_end = mm.v
            emit_scatters(e)
        emit_store_marker(15)
        gsb_fin = gsbP[0].v

        for i in range(NTI):
            b = i % 2

            @block.sync
            def _(sync, i=i, b=b):
                sync.wait_ge(gsbP[0].h, gsb_fin)
                if i == 0:
                    sync.wait_ge(stz.h, stz_zero)
                sync.wait_ge(stx[b].h, stx[b].v)
                tinc(sync.dma_start(out=fx[b][:], in_=x1[i * 128:(i + 1) * 128, :]), ldf[b], 16)
                tinc(sync.dma_start(out=fl[b][:], in_=ln_tm[i * 128:(i + 1) * 128, :]), ldf[b], 16)
                tinc(sync.dma_start(out=fg[b][:, 0:D], in_=img2[i * 128:(i + 1) * 128, :]), ldf[b], 16)
                tinc(sync.dma_start(out=fg[b][:, D:2 * D], in_=img2[TLOC + i * 128:TLOC + (i + 1) * 128, :]), ldf[b], 16)
            ld_f = ldf[b].v

            @block.scalar
            def _(scalar, i=i, b=b):
                scalar.wait_ge(stx[b].h, stx[b].v)
                tinc(scalar.dma_start(out=fs[b][:], in_=sfm_dram[:, i * 128:(i + 1) * 128],
                                      transpose=True), trf[b], 16)
            trb_f = trf[b].v

            @block.vector
            def _(vector, i=i, b=b, ld_f=ld_f, trb_f=trb_f):
                vector.wait_ge(ldf[b].h, ld_f)
                vector.wait_ge(trf[b].h, trb_f)
                ch(vector.tensor_tensor(fx[b][:], fx[b][:], fl[b][:], ALU.add), vch, vector)
                ch(vector.tensor_tensor(fx[b][:], fx[b][:], fs[b][:], ALU.add), vch, vector)
                ch(vector.tensor_tensor(fx[b][:], fx[b][:], fg[b][:, 0:D], ALU.add), vch, vector)
                tinc(vector.tensor_tensor(fx[b][:], fx[b][:], fg[b][:, D:2 * D], ALU.add), ve, 1)
            ve_f = ve.v

            @block.sync
            def _(sync, i=i, b=b, ve_f=ve_f):
                sync.wait_ge(ve.h, ve_f)
                tinc(sync.dma_start(out=x2[i * 128:(i + 1) * 128, :], in_=fx[b][:]), stx[b], 16)

        @block.sync
        def _(sync):
            sync.wait_ge(stx[0].h, stx[0].v)
            sync.wait_ge(stx[1].h, stx[1].v)

    return nc



def build_C(detect=True, self_exchange=False):
    nc = bass.Bass("TRN2", target_bir_lowering=False, debug=False,
                   detect_race_conditions=detect, dynamic_dma_scratch_size=8192)
    x2 = nc.dram_tensor("x2", [TLOC, D], F32, kind="ExternalInput").ap()
    inwT = nc.dram_tensor("inwT", [D, 2048], F32, kind="ExternalInput").ap()
    gwT = nc.dram_tensor("gwT", [D, 2048], F32, kind="ExternalInput").ap()
    outT = nc.dram_tensor("outT", [D, D], F32, kind="ExternalInput").ap()
    growT = nc.dram_tensor("growT", [D, 4096], F32, kind="ExternalInput").ap()
    shrT = nc.dram_tensor("shrT", [2048, D], F32, kind="ExternalInput").ap()
    convw = nc.dram_tensor("convw", [128, 4 * NCH], F32, kind="ExternalInput").ap()
    convb = nc.dram_tensor("convb", [128, NCH], F32, kind="ExternalInput").ap()
    fbase = nc.dram_tensor("fbase", [128, NCH], F32, kind="ExternalInput").ap()
    gam1 = nc.dram_tensor("gam1", [128, NCH], F32, kind="ExternalInput").ap()
    gam2 = nc.dram_tensor("gam2", [128, NCH], F32, kind="ExternalInput").ap()
    gb = nc.dram_tensor("gb", [128, 16], F32, kind="ExternalInput").ap()
    hmask = nc.dram_tensor("hmask", [128, 1], F32, kind="ExternalInput").ap()
    tag = nc.dram_tensor("tag", [1, 16], I32, kind="ExternalInput").ap()
    x3 = nc.dram_tensor("x3", [TLOC, D], F32, kind="ExternalOutput").ap()

    ln_dram = nc.dram_tensor("ln_dram", [TLOC, D], BF16, kind="Internal").ap()
    rms1_dram = nc.dram_tensor("rms1_dram", [TLOC, D], BF16, kind="Internal").ap()
    z0_dram = nc.dram_tensor("z0_dram", [NCH, 128, TLOC], F32, kind="Internal").ap()
    gg_dram = nc.dram_tensor("gg_dram", [NCH, 128, TLOC], BF16, kind="Internal").ap()
    z1_dram = nc.dram_tensor("z1_dram", [NCH, 128, TLOC], F32, kind="Internal").ap()
    hout_dram = nc.dram_tensor("hout_dram", [D, TLOC], BF16, kind="Internal").ap()
    rms2_dram = nc.dram_tensor("rms2_dram", [TLOC, D], BF16, kind="Internal").ap()
    gout_dram = nc.dram_tensor("gout_dram", [D, TLOC], BF16, kind="Internal").ap()
    sh_halo = nc.dram_tensor("sh_halo", [2, 128, 24], F32, kind="Internal", addr_space="Shared").ap()
    sh_carry = nc.dram_tensor("sh_carry", [2, 128, 8], F32, kind="Internal", addr_space="Shared").ap()
    sh_flag = nc.dram_tensor("sh_flag", [4, 16], I32, kind="Internal", addr_space="Shared").ap()

    NTI = TLOC // 128
    NTT8 = TLOC // 512

    with contextlib.ExitStack() as ctx:
        E = ctx.enter_context
        sq = E(nc.sbuf_tensor("sq", [128, D], F32))
        lnf = E(nc.sbuf_tensor("lnf", [128, D], F32))
        rmt = [E(nc.sbuf_tensor(f"rmt{i}", [128, D], BF16)) for i in range(2)]
        stat = [E(nc.sbuf_tensor(f"stat{i}", [128, 8], F32)) for i in range(2)]
        epsb = E(nc.sbuf_tensor("epsb", [128, 1], F32))
        oneb = E(nc.sbuf_tensor("oneb", [128, 1], F32))
        wA = E(nc.sbuf_tensor("wA", [128, 16384], BF16))
        wB = E(nc.sbuf_tensor("wB", [128, 32768], BF16))
        gamc = E(nc.sbuf_tensor("gamc", [128, 16], F32))
        cnv = E(nc.sbuf_tensor("cnv", [128, 4 * NCH], F32))
        cnb = E(nc.sbuf_tensor("cnb", [128, NCH], F32))
        cneg = E(nc.sbuf_tensor("cneg", [128, NCH], F32))
        fbs = E(nc.sbuf_tensor("fbs", [128, NCH], F32))
        gbC = E(nc.sbuf_tensor("gbC", [128, 16], F32))
        hm = E(nc.sbuf_tensor("hm", [128, 1], F32))
        tag_s = E(nc.sbuf_tensor("tag_s", [1, 16], I32))
        flag_s = E(nc.sbuf_tensor("flag_s", [1, 16], I32))
        rfm = E(nc.sbuf_tensor("rfm", [128, NCH * 512], BF16))
        zext = E(nc.sbuf_tensor("zext", [128, 3 + TLOC], F32))
        zacc = E(nc.sbuf_tensor("zacc", [128, TLOC], F32))
        halo = E(nc.sbuf_tensor("halo", [128, 24], F32))
        carrybuf = E(nc.sbuf_tensor("carrybuf", [128, 8], F32))
        myfin = E(nc.sbuf_tensor("myfin", [128, 8], F32))
        alph = E(nc.sbuf_tensor("alph", [128, TLOC], F32))
        xsb = E(nc.sbuf_tensor("xsb", [128, TLOC], F32))
        sgf = E(nc.sbuf_tensor("sgf", [128, 512], F32))
        evb = E(nc.sbuf_tensor("evb", [128, 512], BF16))
        g2buf = E(nc.sbuf_tensor("g2buf", [128, 16 * 512], BF16))
        fx = [E(nc.sbuf_tensor(f"fx{i}", [128, D], F32)) for i in range(2)]
        fl = [E(nc.sbuf_tensor(f"fl{i}", [128, D], BF16)) for i in range(2)]
        xb = fx
        lnt = fl
        fh = [E(nc.sbuf_tensor(f"fh{i}", [128, D], BF16)) for i in range(2)]
        fg2 = rmt

        ps_a = E(nc.psum_tensor("ps_a", [128, 512], F32))
        ps_b = E(nc.psum_tensor("ps_b", [128, 512], F32))

        vch = Sem(ctx, nc, "vch")
        sch = Sem(ctx, nc, "sch")
        ldc = Sem(ctx, nc, "ldc")
        gw = Sem(ctx, nc, "gw")
        ldx = [Sem(ctx, nc, f"ldx{i}") for i in range(2)]
        stl = [Sem(ctx, nc, f"stl{i}") for i in range(2)]
        stm = [Sem(ctx, nc, f"stm{i}") for i in range(2)]
        trn = [Sem(ctx, nc, f"trn{i}") for i in range(NCH)]
        stz0 = Sem(ctx, nc, "stz0")
        stgg = Sem(ctx, nc, "stgg")
        ldz = Sem(ctx, nc, "ldz")
        stz1 = Sem(ctx, nc, "stz1")
        ldz1 = Sem(ctx, nc, "ldz1")
        stho = Sem(ctx, nc, "stho")
        stgo = Sem(ctx, nc, "stgo")
        strm2 = [Sem(ctx, nc, f"strm2{i}") for i in range(2)]
        ldy = [Sem(ctx, nc, f"ldy{i}") for i in range(2)]
        gex = Sem(ctx, nc, "gex")
        ghl = Sem(ctx, nc, "ghl")
        pse1 = Sem(ctx, nc, "pse1")
        pse2 = Sem(ctx, nc, "pse2")
        ldf = [Sem(ctx, nc, f"ldf{i}") for i in range(2)]
        trf = [Sem(ctx, nc, f"trf{i}") for i in range(2)]
        stx = [Sem(ctx, nc, f"stx{i}") for i in range(2)]
        mm = Sem(ctx, nc, "mm")
        ve = Sem(ctx, nc, "ve")
        se = Sem(ctx, nc, "se")

        block = E(nc.Block())

        @block.sync
        def _(sync):
            tinc(sync.dma_start(out=tag_s[:], in_=tag), ldc, 16)
            tinc(sync.dma_start(out=hm[:], in_=hmask), ldc, 16)
            tinc(sync.dma_start(out=cnv[:], in_=convw), ldc, 16)
            tinc(sync.dma_start(out=cnb[:], in_=convb), ldc, 16)
            tinc(sync.dma_start(out=fbs[:], in_=fbase), ldc, 16)
            tinc(sync.dma_start(out=gamc[:, 0:8], in_=gam1), ldc, 16)
            tinc(sync.dma_start(out=gamc[:, 8:16], in_=gam2), ldc, 16)
            tinc(sync.dma_start(out=gbC[:], in_=gb), ldc, 16)
            sync.wait_ge(ldc.h, ldc.v)

        @block.vector
        def _(vector):
            vector.memset(epsb[:], 1e-5)
            vector.memset(oneb[:], 1.0 + 1e-6)
            vector.wait_ge(ldc.h, ldc.v)
            tinc(vector.tensor_scalar(gamc[:], gamc[:], 32.0, None, ALU.mult), ve, 1)
        ve_init = ve.v

        @block.scalar
        def _(scalar):
            scalar.wait_ge(ldc.h, ldc.v)
            scalar.wait_ge(ve.h, ve_init)
            ch(scalar.activation(cneg[:], fbs[:], AF.Abs), sch, scalar)
            ch(scalar.activation(cneg[:], cneg[:], AF.Exp, scale=-1.0), sch, scalar)
            tinc(scalar.activation(cneg[:], cneg[:], AF.Ln, bias=oneb[:, 0:1]), se, 1)
        se_sp = se.v

        @block.vector
        def _(vector):
            vector.wait_ge(se.h, se_sp)
            ch(vector.tensor_scalar(fbs[:], fbs[:], 0.0, None, ALU.max), vch, vector)
            ch(vector.tensor_tensor(cneg[:], cneg[:], fbs[:], ALU.add), vch, vector)
            tinc(vector.tensor_scalar(cneg[:], cneg[:], -8.0, None, ALU.mult), ve, 1)
        ve_cneg = ve.v

        tile_done = []
        se_tile = []
        for i in range(NTI):
            b = i % 2

            @block.sync
            def _(sync, i=i, b=b):
                if i >= 2:
                    sync.wait_ge(ve.h, tile_done[i - 2])
                    sync.wait_ge(se.h, se_tile[i - 2])
                tinc(sync.dma_start(out=xb[b][:], in_=x2[i * 128:(i + 1) * 128, :]), ldx[b], 16)
            ld_i = ldx[b].v

            @block.scalar
            def _(scalar, i=i, b=b, ld_i=ld_i):
                scalar.wait_ge(ldx[b].h, ld_i)
                if i >= 1:
                    scalar.wait_ge(sch.h, sch.v)
                if i >= 2:
                    scalar.wait_ge(ve.h, tile_done[i - 2])
                ch(scalar.activation(sq[:], xb[b][:], AF.Copy, accum_out=stat[b][:, 0:1]), sch, scalar)
                tinc(scalar.activation(sq[:], xb[b][:], AF.Square, accum_out=stat[b][:, 1:2]), se, 1)
            se_sums = se.v

            @block.vector
            def _(vector, i=i, b=b, se_sums=se_sums):
                vector.wait_ge(se.h, se_sums)
                ch(vector.tensor_scalar(stat[b][:, 2:3], stat[b][:, 0:1], 1.0 / D, None, ALU.mult), vch, vector)
                ch(vector.tensor_tensor(stat[b][:, 3:4], stat[b][:, 2:3], stat[b][:, 2:3], ALU.mult), vch, vector)
                ch(vector.tensor_scalar(stat[b][:, 5:6], stat[b][:, 1:2], 1.0 / D, None, ALU.mult), vch, vector)
                tinc(vector.tensor_tensor(stat[b][:, 3:4], stat[b][:, 5:6], stat[b][:, 3:4], ALU.subtract), ve, 1)
            ve_var = ve.v

            @block.scalar
            def _(scalar, i=i, b=b, ve_var=ve_var):
                scalar.wait_ge(ve.h, ve_var)
                tinc(scalar.activation(stat[b][:, 4:5], stat[b][:, 3:4], AF.Sqrt, bias=epsb[:, 0:1]), se, 1)
            se_tile.append(se.v)
            se_sqrt = se.v

            @block.vector
            def _(vector, i=i, b=b, se_sqrt=se_sqrt):
                vector.wait_ge(se.h, se_sqrt)
                ch(vector.reciprocal(stat[b][:, 4:5], stat[b][:, 4:5]), vch, vector)
                vector.wait_ge(ve.h, ve.v)
                ch(vector.tensor_scalar(lnf[:], xb[b][:], stat[b][:, 2:3], stat[b][:, 4:5],
                                        ALU.subtract, ALU.mult), vch, vector)
                vector.wait_ge(stl[b].h, stl[b].v)
                ch(vector.tensor_copy(lnt[b][:], lnf[:]), vch, vector)
                ch(vector.tensor_tensor(sq[:], lnf[:], lnf[:], ALU.mult), vch, vector)
                ch(vector.reduce_sum(stat[b][:, 6:7], sq[:], axis=AXX), vch, vector)
                tinc(vector.tensor_copy(stat[b][:, 7:8], stat[b][:, 6:7]), ve, 1)
            ve_n2 = ve.v

            @block.scalar
            def _(scalar, i=i, b=b, ve_n2=ve_n2):
                scalar.wait_ge(ve.h, ve_n2)
                tinc(scalar.activation(stat[b][:, 6:7], stat[b][:, 6:7], AF.Sqrt), se, 1)
            se_n = se.v

            @block.vector
            def _(vector, i=i, b=b, se_n=se_n):
                vector.wait_ge(se.h, se_n)
                ch(vector.reciprocal(stat[b][:, 6:7], stat[b][:, 6:7]), vch, vector)
                vector.wait_ge(stm[b].h, stm[b].v)
                tinc(vector.tensor_scalar(rmt[b][:], lnf[:], stat[b][:, 6:7], None, ALU.mult), ve, 1)
            tile_done.append(ve.v)
            ve_r = ve.v

            @block.sync
            def _(sync, i=i, b=b, ve_r=ve_r):
                sync.wait_ge(ve.h, ve_r)
                tinc(sync.dma_start(out=ln_dram[i * 128:(i + 1) * 128, :], in_=lnt[b][:]), stl[b], 16)
                tinc(sync.dma_start(out=rms1_dram[i * 128:(i + 1) * 128, :], in_=rmt[b][:]), stm[b], 16)
        stl_all = (stl[0].v, stl[1].v)
        stm_all = (stm[0].v, stm[1].v)

        def cast_weight(gpsimd, dst, dst_off, src, ncol, gam_col=None):
            for cc in range(NCH):
                tinc(gpsimd.dma_start(out=dst[:, dst_off + cc * ncol: dst_off + (cc + 1) * ncol],
                                      in_=src[cc * 128:(cc + 1) * 128, :]), gw, 16)

        @block.gpsimd
        def _(gpsimd):
            cast_weight(gpsimd, wA, 0, inwT, 2048)
        gw_in = gw.v

        @block.vector
        def _(vector):
            vector.wait_ge(gw.h, gw_in)
            vector.wait_ge(ve.h, ve_init)
            for cc in range(NCH):
                ii = vector.tensor_scalar(wA[:, cc * 2048:(cc + 1) * 2048],
                                          wA[:, cc * 2048:(cc + 1) * 2048],
                                          gamc[:, cc:cc + 1], None, ALU.mult)
                ch(ii, vch, vector)
            vector.sem_inc(ve.h, 1)
            ve.inc(1)
        ve_win = ve.v

        for t in range(NTT8):
            t0 = t * 512

            @block.scalar
            def _(scalar, t=t, t0=t0):
                scalar.wait_ge(stm[0].h, stm_all[0])
                scalar.wait_ge(stm[1].h, stm_all[1])
                if t >= 1:
                    scalar.wait_ge(mm.h, mm.v)
                for c in range(NCH):
                    tinc(scalar.dma_start(out=rfm[:, c * 512:(c + 1) * 512],
                                          in_=rms1_dram[t0:t0 + 512, c * 128:(c + 1) * 128],
                                          transpose=True), trn[c], 16)

            for oc in range(16):
                pbank = ps_a if oc % 2 == 0 else ps_b

                @block.tensor
                def _(tensor, t=t, oc=oc, pbank=pbank):
                    if oc == 0:
                        for c in range(NCH):
                            tensor.wait_ge(trn[c].h, trn[c].v)
                        tensor.wait_ge(ve.h, ve_win)
                    if oc >= 2:
                        tensor.wait_ge(se.h, se.v - 1)
                    for c in range(NCH):
                        i_ = tensor.matmul(pbank[:], wA[:, c * 2048 + oc * 128: c * 2048 + (oc + 1) * 128],
                                           rfm[:, c * 512:(c + 1) * 512], start=(c == 0), stop=(c == NCH - 1))
                    tinc(i_, mm, 1)
                mm_u = mm.v

                @block.scalar
                def _(scalar, t=t, t0=t0, oc=oc, pbank=pbank, mm_u=mm_u):
                    scalar.wait_ge(mm.h, mm_u)
                    scalar.wait_ge(stz0.h, stz0.v)
                    scalar.wait_ge(stgg.h, stgg.v)
                    if oc < NCH:
                        tinc(scalar.activation(evb[:], pbank[:], AF.Gelu), se, 1)
                    else:
                        tinc(scalar.activation(sgf[:], pbank[:], AF.Copy), se, 1)
                se_ev = se.v

                @block.sync
                def _(sync, t=t, t0=t0, oc=oc, se_ev=se_ev):
                    sync.wait_ge(se.h, se_ev)
                    if oc < NCH:
                        tinc(sync.dma_start(out=gg_dram[oc, :, t0:t0 + 512], in_=evb[:]), stgg, 16)
                    else:
                        tinc(sync.dma_start(out=z0_dram[oc - NCH, :, t0:t0 + 512], in_=sgf[:]), stz0, 16)
        stz0_all = stz0.v
        stgg_all = stgg.v

        @block.gpsimd
        def _(gpsimd):
            gpsimd.wait_ge(stz0.h, stz0_all)
            gpsimd.wait_ge(ldc.h, ldc.v)
            pid = gpsimd.partition_id()
            slot = pid % 2
            sib = slot if self_exchange else 1 - (pid % 2)
            tinc(gpsimd.dma_start(out=sh_halo[slot, :, :].rearrange("p (c f) -> p c f", f=3),
                                  in_=z0_dram[:, :, TLOC - 3:TLOC].rearrange("c p f -> p c f")), ghl, 16)
            gpsimd.wait_ge(ghl.h, ghl.v)
            tinc(gpsimd.dma_start(out=sh_flag[slot, :][None, :], in_=tag_s[:]), ghl, 16)
            gpsimd.wait_ge(ghl.h, ghl.v)
            poll_flag(gpsimd, sh_flag[sib, :][None, :], flag_s, tag_s, pse1)
            tinc(gpsimd.dma_start(out=halo[:], in_=sh_halo[sib, :, :]), ghl, 16)
            gpsimd.wait_ge(ghl.h, ghl.v)
            gpsimd.sem_inc(gex.h, 1)
            gex.inc(1)
        gex_halo = gex.v

        @block.vector
        def _(vector):
            vector.wait_ge(gex.h, gex_halo)
            tinc(vector.tensor_scalar(halo[:], halo[:], hm[:, 0:1], None, ALU.mult), ve, 1)
        ve_halo = ve.v

        @block.gpsimd
        def _(gpsimd):
            gpsimd.wait_ge(mm.h, mm.v)
            cast_weight(gpsimd, wA, 0, gwT, 2048)
        gw_g = gw.v

        scan_done = []
        for c in range(NCH):
            @block.sync
            def _(sync, c=c):
                if c >= 1:
                    sync.wait_ge(ve.h, ve.v)
                tinc(sync.dma_start(out=zext[:, 3:3 + TLOC], in_=z0_dram[c, :, :]), ldz, 16)
            ld_z = ldz.v

            @block.vector
            def _(vector, c=c, ld_z=ld_z):
                vector.wait_ge(ldz.h, ld_z)
                vector.wait_ge(ve.h, ve_halo)
                vector.wait_ge(stz1.h, stz1.v)
                ch(vector.tensor_copy(zext[:, 0:3], halo[:, c * 3:(c + 1) * 3]), vch, vector)
                ch(vector.tensor_scalar(zacc[:], zext[:, 0:TLOC], cnv[:, 4 * c:4 * c + 1], None, ALU.mult), vch, vector)
                for j in range(1, 4):
                    ch(vector.scalar_tensor_tensor(zacc[:], zext[:, j:j + TLOC], cnv[:, 4 * c + j:4 * c + j + 1],
                                                   zacc[:], ALU.mult, ALU.add), vch, vector)
                tinc(vector.tensor_scalar(zacc[:], zacc[:], cnb[:, c:c + 1], None, ALU.add), ve, 1)
            ve_cv = ve.v

            @block.vector
            def _(vector, c=c, ve_cv=ve_cv):
                vector.wait_ge(ve.h, ve_cv)
                tinc(vector.tensor_copy(wB[:, c * 4096:(c + 1) * 4096], zacc[:]), ve, 1)
            ve_zbf = ve.v

            @block.sync
            def _(sync, c=c, ve_cv=ve_cv):
                sync.wait_ge(ve.h, ve_cv)
                tinc(sync.dma_start(out=z1_dram[c, :, :], in_=zacc[:]), stz1, 16)
        stz1_all = stz1.v
        ve_z1all = ve.v

        for c in range(NCH):
            for half, oc in (("f", c), ("i", c + NCH)):
                for st_ in range(NTT8):
                    pbank = ps_a if half == "f" else ps_b

                    @block.tensor
                    def _(tensor, c=c, oc=oc, st_=st_, pbank=pbank, half=half):
                        if c == 0 and half == "f" and st_ == 0:
                            tensor.wait_ge(gw.h, gw_g)
                            tensor.wait_ge(ve.h, ve_z1all)
                        if not (c == 0 and half == "f" and st_ == 0):
                            tensor.wait_ge(se.h, se.v)
                        for cc in range(NCH):
                            i_ = tensor.matmul(pbank[:], wA[:, cc * 2048 + oc * 128: cc * 2048 + (oc + 1) * 128],
                                               wB[:, cc * 4096 + st_ * 512: cc * 4096 + (st_ + 1) * 512],
                                               start=(cc == 0), stop=(cc == NCH - 1))
                        tinc(i_, mm, 1)
                    mm_fg = mm.v

                    @block.scalar
                    def _(scalar, c=c, oc=oc, st_=st_, pbank=pbank, half=half, mm_fg=mm_fg):
                        scalar.wait_ge(mm.h, mm_fg)
                        if c >= 1 and st_ == 0:
                            scalar.wait_ge(ve.h, scan_done[c - 1])
                            scalar.wait_ge(stz0.h, stz0.v)
                            scalar.wait_ge(stz1.h, stz1.v)
                        if half == "f":
                            ch(scalar.activation(sgf[:], pbank[:], AF.Sigmoid, bias=gbC[:, c:c + 1]), sch, scalar)
                            tinc(scalar.activation(alph[:, st_ * 512:(st_ + 1) * 512], sgf[:],
                                                   AF.Exp, scale=cneg[:, c:c + 1]), se, 1)
                        else:
                            tinc(scalar.activation(xsb[:, st_ * 512:(st_ + 1) * 512], pbank[:],
                                                   AF.Sigmoid, bias=gbC[:, c + NCH:c + NCH + 1]), se, 1)
            se_fg = se.v

            @block.sync
            def _(sync, c=c):
                sync.wait_ge(stz1.h, stz1_all)
                if c >= 1:
                    sync.wait_ge(ve.h, ve.v)
                tinc(sync.dma_start(out=zext[:, 0:TLOC], in_=z1_dram[c, :, :]), ldz1, 16)
            ld_z1 = ldz1.v

            @block.vector
            def _(vector, c=c, se_fg=se_fg, ld_z1=ld_z1):
                vector.wait_ge(se.h, se_fg)
                vector.wait_ge(ldz1.h, ld_z1)
                ch(vector.tensor_tensor(xsb[:], xsb[:], zext[:, 0:TLOC], ALU.mult), vch, vector)
                ch(vector.tensor_tensor(zacc[:], alph[:], alph[:], ALU.mult), vch, vector)
                vector.sem_inc(ve.h, 1)
                ve.inc(1)
            ve_xs1 = ve.v

            @block.scalar
            def _(scalar, c=c, ve_xs1=ve_xs1):
                scalar.wait_ge(ve.h, ve_xs1)
                tinc(scalar.activation(zacc[:], zacc[:], AF.Sqrt, bias=oneb[:, 0:1], scale=-1.0), se, 1)
            se_beta = se.v

            @block.vector
            def _(vector, c=c, se_beta=se_beta):
                vector.wait_ge(se.h, se_beta)
                ch(vector.tensor_tensor(xsb[:], xsb[:], zacc[:], ALU.mult), vch, vector)
                ch(vector.tensor_tensor_scan(xsb[:], alph[:], xsb[:], 0.0, ALU.mult, ALU.add), vch, vector)
                ch(vector.tensor_tensor_scan(alph[:], alph[:], alph[:], 1.0, ALU.mult, ALU.bypass), vch, vector)
                tinc(vector.tensor_copy(myfin[:, c:c + 1], xsb[:, TLOC - 1:TLOC]), ve, 1)
            scan_done.append(ve.v)

            @block.sync
            def _(sync, c=c):
                sync.wait_ge(ve.h, scan_done[-1])
                tinc(sync.dma_start(out=z0_dram[c, :, :], in_=xsb[:]), stz0, 16)
            @block.sync
            def _(sync, c=c):
                tinc(sync.dma_start(out=z1_dram[c, :, :], in_=alph[:]), stz1, 16)
        sth0_all = stz0.v
        stA_all = stz1.v

        @block.gpsimd
        def _(gpsimd):
            gpsimd.wait_ge(ve.h, scan_done[-1])
            pid = gpsimd.partition_id()
            slot = pid % 2
            sib = slot if self_exchange else 1 - (pid % 2)
            tinc(gpsimd.dma_start(out=sh_carry[slot, :, :], in_=myfin[:]), ghl, 16)
            gpsimd.wait_ge(ghl.h, ghl.v)
            tinc(gpsimd.dma_start(out=sh_flag[2 + slot, :][None, :], in_=tag_s[:]), ghl, 16)
            gpsimd.wait_ge(ghl.h, ghl.v)
            poll_flag(gpsimd, sh_flag[2 + sib, :][None, :], flag_s, tag_s, pse2)
            tinc(gpsimd.dma_start(out=carrybuf[:], in_=sh_carry[sib, :, :]), ghl, 16)
            gpsimd.wait_ge(ghl.h, ghl.v)
            gpsimd.sem_inc(gex.h, 1)
            gex.inc(1)
        gex_carry = gex.v

        @block.vector
        def _(vector):
            vector.wait_ge(gex.h, gex_carry)
            tinc(vector.tensor_scalar(carrybuf[:], carrybuf[:], hm[:, 0:1], None, ALU.mult), ve, 1)
        ve_carry = ve.v

        for c in range(NCH):
            @block.sync
            def _(sync, c=c):
                sync.wait_ge(stz0.h, sth0_all)
                sync.wait_ge(stz1.h, stA_all)
                if c >= 1:
                    sync.wait_ge(ve.h, ve.v)
                tinc(sync.dma_start(out=zext[:, 0:TLOC], in_=z0_dram[c, :, :]), ldz, 16)
                tinc(sync.dma_start(out=zacc[:], in_=z1_dram[c, :, :]), ldz, 16)
            ld_h = ldz.v

            @block.scalar
            def _(scalar, c=c):
                if c >= 1:
                    scalar.wait_ge(ve.h, ve.v)
                tinc(scalar.dma_start(out=rfm[:, 0:TLOC // 8] if False else alph[:],
                                      in_=gg_dram[c, :, :].rearrange("p f -> p f")), ldz1, 16) if False else None
            @block.sync
            def _(sync, c=c):
                tinc(sync.dma_start(out=wA[:, 0:TLOC].rearrange("p f -> p f") if False else lnt[0][:, 0:D],
                                    in_=gg_dram[c, :, 0:0 + D]), ldz1, 16) if False else None

            @block.sync
            def _(sync, c=c):
                tinc(sync.dma_start(out=g2buf[:, 0:TLOC], in_=gg_dram[c, :, :]), ldz1, 16)
            ld_gg = ldz1.v

            @block.vector
            def _(vector, c=c, ld_h=ld_h, ld_gg=ld_gg):
                vector.wait_ge(ldz.h, ld_h)
                vector.wait_ge(ldz1.h, ld_gg)
                vector.wait_ge(ve.h, ve_carry)
                ch(vector.scalar_tensor_tensor(zext[:, 0:TLOC], zacc[:], carrybuf[:, c:c + 1],
                                               zext[:, 0:TLOC], ALU.mult, ALU.add), vch, vector)
                tinc(vector.tensor_tensor(wB[:, c * 4096:(c + 1) * 4096], g2buf[:, 0:TLOC],
                                          zext[:, 0:TLOC], ALU.mult), ve, 1)
        ve_hg = ve.v

        @block.gpsimd
        def _(gpsimd):
            gpsimd.wait_ge(mm.h, mm.v)
            cast_weight(gpsimd, wA, 0, outT, 1024)
        gw_o = gw.v

        for t in range(NTT8):
            t0 = t * 512
            for oc in range(NCH):
                pbank = ps_a if oc % 2 == 0 else ps_b

                @block.tensor
                def _(tensor, t=t, oc=oc, pbank=pbank):
                    if t == 0 and oc == 0:
                        tensor.wait_ge(gw.h, gw_o)
                        tensor.wait_ge(ve.h, ve_hg)
                    if not (t == 0 and oc < 2):
                        tensor.wait_ge(se.h, se.v - 1)
                    for cc in range(NCH):
                        i_ = tensor.matmul(pbank[:], wA[:, cc * 1024 + oc * 128: cc * 1024 + (oc + 1) * 128],
                                           wB[:, cc * 4096 + t * 512:(cc * 4096) + (t + 1) * 512],
                                           start=(cc == 0), stop=(cc == NCH - 1))
                    tinc(i_, mm, 1)
                mm_ho = mm.v

                @block.scalar
                def _(scalar, t=t, oc=oc, pbank=pbank, mm_ho=mm_ho):
                    scalar.wait_ge(mm.h, mm_ho)
                    scalar.wait_ge(stho.h, stho.v)
                    tinc(scalar.activation(evb[:], pbank[:], AF.Copy), se, 1)
                se_ho = se.v

                @block.sync
                def _(sync, t=t, t0=t0, oc=oc, se_ho=se_ho):
                    sync.wait_ge(se.h, se_ho)
                    tinc(sync.dma_start(out=hout_dram[oc * 128:(oc + 1) * 128, t0:t0 + 512], in_=evb[:]), stho, 16)
        stho_all = stho.v

        for i in range(NTI):
            b = i % 2

            @block.sync
            def _(sync, i=i, b=b):
                sync.wait_ge(stho.h, stho_all)
                sync.wait_ge(stl[0].h, stl_all[0])
                sync.wait_ge(stl[1].h, stl_all[1])
                sync.wait_ge(strm2[b].h, strm2[b].v)
                sync.wait_ge(ldy[b].h, ldy[b].v)
                tinc(sync.dma_start(out=fl[b][:], in_=ln_dram[i * 128:(i + 1) * 128, :]), ldy[b], 16)
            ld_y = ldy[b].v

            @block.scalar
            def _(scalar, i=i, b=b):
                scalar.wait_ge(strm2[b].h, strm2[b].v)
                tinc(scalar.dma_start(out=fh[b][:], in_=hout_dram[:, i * 128:(i + 1) * 128],
                                      transpose=True), trf[b], 16)
            tr_y = trf[b].v

            @block.vector
            def _(vector, i=i, b=b, ld_y=ld_y, tr_y=tr_y):
                vector.wait_ge(ldy[b].h, ld_y)
                vector.wait_ge(trf[b].h, tr_y)
                vector.wait_ge(ve.h, ve.v)
                ch(vector.tensor_tensor(lnf[:], fl[b][:], fh[b][:], ALU.add), vch, vector)
                ch(vector.tensor_tensor(sq[:], lnf[:], lnf[:], ALU.mult), vch, vector)
                tinc(vector.reduce_sum(stat[b][:, 6:7], sq[:], axis=AXX), ve, 1)
            ve_y = ve.v

            @block.scalar
            def _(scalar, i=i, b=b, ve_y=ve_y):
                scalar.wait_ge(ve.h, ve_y)
                tinc(scalar.activation(stat[b][:, 6:7], stat[b][:, 6:7], AF.Sqrt), se, 1)
            se_y = se.v

            @block.vector
            def _(vector, i=i, b=b, se_y=se_y):
                vector.wait_ge(se.h, se_y)
                ch(vector.reciprocal(stat[b][:, 6:7], stat[b][:, 6:7]), vch, vector)
                tinc(vector.tensor_scalar(rmt[b][:], lnf[:], stat[b][:, 6:7], None, ALU.mult), ve, 1)
            ve_r2 = ve.v

            @block.sync
            def _(sync, i=i, b=b, ve_r2=ve_r2):
                sync.wait_ge(ve.h, ve_r2)
                tinc(sync.dma_start(out=rms2_dram[i * 128:(i + 1) * 128, :], in_=rmt[b][:]), strm2[b], 16)
        strm2_all = (strm2[0].v, strm2[1].v)

        @block.gpsimd
        def _(gpsimd):
            gpsimd.wait_ge(mm.h, mm.v)
            gpsimd.wait_ge(ve.h, ve.v)
            cast_weight(gpsimd, wB, 0, growT, 4096)
            cast_weight(gpsimd, wA, 0, shrT, 1024)
            for cc in range(NCH, 16):
                tinc(gpsimd.dma_start(out=wA[:, cc * 1024:(cc + 1) * 1024],
                                      in_=shrT[cc * 128:(cc + 1) * 128, :]), gw, 16)
        gw_g2 = gw.v

        @block.vector
        def _(vector):
            vector.wait_ge(gw.h, gw_g2)
            for cc in range(NCH):
                ii = vector.tensor_scalar(wB[:, cc * 4096:(cc + 1) * 4096],
                                          wB[:, cc * 4096:(cc + 1) * 4096],
                                          gamc[:, 8 + cc:9 + cc], None, ALU.mult)
                ch(ii, vch, vector)
            vector.sem_inc(ve.h, 1)
            ve.inc(1)
        ve_wg2 = ve.v

        for t in range(NTT8):
            t0 = t * 512

            @block.scalar
            def _(scalar, t=t, t0=t0):
                scalar.wait_ge(strm2[0].h, strm2_all[0])
                scalar.wait_ge(strm2[1].h, strm2_all[1])
                if t >= 1:
                    scalar.wait_ge(mm.h, mm.v)
                for c in range(NCH):
                    tinc(scalar.dma_start(out=rfm[:, c * 512:(c + 1) * 512],
                                          in_=rms2_dram[t0:t0 + 512, c * 128:(c + 1) * 128],
                                          transpose=True), trn[c], 16)

            for ocp in range(16):
                @block.tensor
                def _(tensor, t=t, ocp=ocp):
                    if ocp == 0:
                        for c in range(NCH):
                            tensor.wait_ge(trn[c].h, trn[c].v)
                        tensor.wait_ge(ve.h, ve_wg2)
                    else:
                        tensor.wait_ge(se.h, se.v)
                        tensor.wait_ge(ve.h, ve.v)
                    for c in range(NCH):
                        tensor.matmul(ps_a[:], wB[:, c * 4096 + ocp * 128: c * 4096 + (ocp + 1) * 128],
                                      rfm[:, c * 512:(c + 1) * 512], start=(c == 0), stop=(c == NCH - 1))
                    for c in range(NCH):
                        i_ = tensor.matmul(ps_b[:], wB[:, c * 4096 + (16 + ocp) * 128: c * 4096 + (17 + ocp) * 128],
                                           rfm[:, c * 512:(c + 1) * 512], start=(c == 0), stop=(c == NCH - 1))
                    tinc(i_, mm, 1)
                mm_g2 = mm.v

                @block.scalar
                def _(scalar, t=t, ocp=ocp, mm_g2=mm_g2):
                    scalar.wait_ge(mm.h, mm_g2)
                    if t >= 1 and ocp == 0:
                        scalar.wait_ge(mm.h, mm_g2)
                    tinc(scalar.activation(g2buf[:, ocp * 512:(ocp + 1) * 512], ps_a[:], AF.Gelu), se, 1)
                se_gg2 = se.v

                @block.vector
                def _(vector, t=t, ocp=ocp, se_gg2=se_gg2):
                    vector.wait_ge(se.h, se_gg2)
                    tinc(vector.tensor_tensor(g2buf[:, ocp * 512:(ocp + 1) * 512],
                                              g2buf[:, ocp * 512:(ocp + 1) * 512],
                                              ps_b[:], ALU.mult), ve, 1)
            ve_m2 = ve.v

            for od in range(NCH):
                pbank = ps_a if od % 2 == 0 else ps_b

                @block.tensor
                def _(tensor, t=t, od=od, pbank=pbank, ve_m2=ve_m2):
                    tensor.wait_ge(ve.h, ve_m2)
                    if od >= 2:
                        tensor.wait_ge(se.h, se.v - 1)
                    for fc in range(16):
                        i_ = tensor.matmul(pbank[:], wA[:, fc * 1024 + od * 128: fc * 1024 + (od + 1) * 128],
                                           g2buf[:, fc * 512:(fc + 1) * 512], start=(fc == 0), stop=(fc == 15))
                    tinc(i_, mm, 1)
                mm_sh = mm.v

                @block.scalar
                def _(scalar, t=t, od=od, pbank=pbank, mm_sh=mm_sh):
                    scalar.wait_ge(mm.h, mm_sh)
                    scalar.wait_ge(stgo.h, stgo.v)
                    tinc(scalar.activation(evb[:], pbank[:], AF.Copy), se, 1)
                se_go = se.v

                @block.sync
                def _(sync, t=t, t0=t0, od=od, se_go=se_go):
                    sync.wait_ge(se.h, se_go)
                    tinc(sync.dma_start(out=gout_dram[od * 128:(od + 1) * 128, t0:t0 + 512], in_=evb[:]), stgo, 16)
        stgo_all = stgo.v

        for i in range(NTI):
            b = i % 2

            @block.sync
            def _(sync, i=i, b=b):
                sync.wait_ge(stgo.h, stgo_all)
                sync.wait_ge(stx[b].h, stx[b].v)
                tinc(sync.dma_start(out=fx[b][:], in_=x2[i * 128:(i + 1) * 128, :]), ldf[b], 16)
                tinc(sync.dma_start(out=fl[b][:], in_=ln_dram[i * 128:(i + 1) * 128, :]), ldf[b], 16)
            ld_f = ldf[b].v

            @block.scalar
            def _(scalar, i=i, b=b):
                scalar.wait_ge(stgo.h, stgo_all)
                scalar.wait_ge(stx[b].h, stx[b].v)
                scalar.wait_ge(trf[b].h, trf[b].v)
                tinc(scalar.dma_start(out=fh[b][:], in_=hout_dram[:, i * 128:(i + 1) * 128],
                                      transpose=True), trf[b], 16)
                tinc(scalar.dma_start(out=fg2[b][:], in_=gout_dram[:, i * 128:(i + 1) * 128],
                                      transpose=True), trf[b], 16)
            tr_f = trf[b].v

            @block.vector
            def _(vector, i=i, b=b, ld_f=ld_f, tr_f=tr_f):
                vector.wait_ge(ldf[b].h, ld_f)
                vector.wait_ge(trf[b].h, tr_f)
                ch(vector.tensor_tensor(fx[b][:], fx[b][:], fl[b][:], ALU.add), vch, vector)
                ch(vector.tensor_tensor(fx[b][:], fx[b][:], fh[b][:], ALU.add), vch, vector)
                tinc(vector.tensor_tensor(fx[b][:], fx[b][:], fg2[b][:], ALU.add), ve, 1)
            ve_f = ve.v

            @block.sync
            def _(sync, i=i, b=b, ve_f=ve_f):
                sync.wait_ge(ve.h, ve_f)
                tinc(sync.dma_start(out=x3[i * 128:(i + 1) * 128, :], in_=fx[b][:]), stx[b], 16)

        @block.sync
        def _(sync):
            sync.wait_ge(stx[0].h, stx[0].v)
            sync.wait_ge(stx[1].h, stx[1].v)

    return nc



LNC_FEAT = float(np.log(1.0 / np.sqrt(128.0)))


def build_E(detect=True, self_exchange=False):
    nc = bass.Bass("TRN2", target_bir_lowering=False, debug=False,
                   detect_race_conditions=detect, dynamic_dma_scratch_size=8192)
    x4 = nc.dram_tensor("x4", [TLOC, D], F32, kind="ExternalInput").ap()
    wq = nc.dram_tensor("wq", [D, D], F32, kind="ExternalInput").ap()
    wk = nc.dram_tensor("wk", [D, D], F32, kind="ExternalInput").ap()
    wv = nc.dram_tensor("wv", [D, D], F32, kind="ExternalInput").ap()
    wo = nc.dram_tensor("wo", [D, D], F32, kind="ExternalInput").ap()
    wf = nc.dram_tensor("wf", [64, 128], F32, kind="ExternalInput").ap()
    trimask = nc.dram_tensor("trimask", [128, 128], F32, kind="ExternalInput").ap()
    hmask = nc.dram_tensor("hmask", [128, 1], F32, kind="ExternalInput").ap()
    tag = nc.dram_tensor("tag", [1, 16], I32, kind="ExternalInput").ap()
    x5 = nc.dram_tensor("x5", [TLOC, D], F32, kind="ExternalOutput").ap()

    ln_dram = nc.dram_tensor("ln_dram", [TLOC, D], BF16, kind="Internal").ap()
    q_dram = nc.dram_tensor("q_dram", [D, TLOC], BF16, kind="Internal").ap()
    k_dram = nc.dram_tensor("k_dram", [D, TLOC], BF16, kind="Internal").ap()
    v_dram = nc.dram_tensor("v_dram", [TLOC, D], BF16, kind="Internal").ap()
    qp_dram = nc.dram_tensor("qp_dram", [16, 128, TLOC], BF16, kind="Internal").ap()
    ye_dram = nc.dram_tensor("ye_dram", [16, 65, TLOC], F32, kind="Internal").ap()
    sh_S = nc.dram_tensor("sh_S", [2, 128, 16 * 65], F32, kind="Internal", addr_space="Shared").ap()
    sh_flag = nc.dram_tensor("sh_flag", [2, 16], I32, kind="Internal", addr_space="Shared").ap()

    NTI = TLOC // 128
    NTT8 = TLOC // 512
    NCK = 32

    with contextlib.ExitStack() as ctx:
        E = ctx.enter_context
        sq = E(nc.sbuf_tensor("sq", [128, D], F32))
        lnf = E(nc.sbuf_tensor("lnf", [128, D], F32))
        stat = [E(nc.sbuf_tensor(f"stat{i}", [128, 8], F32)) for i in range(2)]
        epsb = E(nc.sbuf_tensor("epsb", [128, 1], F32))
        lncb = E(nc.sbuf_tensor("lncb", [128, 1], F32))
        eps6 = E(nc.sbuf_tensor("eps6", [128, 1], F32))
        xb = [E(nc.sbuf_tensor(f"xb{i}", [128, D], F32)) for i in range(2)]
        lnt = [E(nc.sbuf_tensor(f"lnt{i}", [128, D], BF16)) for i in range(2)]
        wA = E(nc.sbuf_tensor("wA", [128, NCH * 1024], BF16))
        rfm = E(nc.sbuf_tensor("rfm", [128, NCH * 512], BF16))
        evb = E(nc.sbuf_tensor("evb", [128, 512], BF16))
        wf_s = E(nc.sbuf_tensor("wf_s", [64, 128], BF16))
        nhalf = E(nc.sbuf_tensor("nhalf", [64, 128], BF16))
        tri_s = E(nc.sbuf_tensor("tri_s", [128, 128], F32))
        hm = E(nc.sbuf_tensor("hm", [128, 1], F32))
        tag_s = E(nc.sbuf_tensor("tag_s", [1, 16], I32))
        flag_s = E(nc.sbuf_tensor("flag_s", [1, 16], I32))
        qh_s = E(nc.sbuf_tensor("qh_s", [64, TLOC], BF16))
        kh_s = E(nc.sbuf_tensor("kh_s", [64, TLOC], BF16))
        vh_s = E(nc.sbuf_tensor("vh_s", [128, NCK * 65], BF16))
        qsq = E(nc.sbuf_tensor("qsq", [64, TLOC], BF16))
        ksq = E(nc.sbuf_tensor("ksq", [64, TLOC], BF16))
        qp_s = E(nc.sbuf_tensor("qp_s", [128, 128], BF16))
        kp_s = E(nc.sbuf_tensor("kp_s", [128, 128], BF16))
        kpT_s = E(nc.sbuf_tensor("kpT_s", [128, 128], BF16))
        gm_s = E(nc.sbuf_tensor("gm_s", [128, 128], BF16))
        S_f = E(nc.sbuf_tensor("S_f", [128, 16 * 65], F32))
        S_bf = E(nc.sbuf_tensor("S_bf", [128, 65], BF16))
        Ssib = E(nc.sbuf_tensor("Ssib", [128, 16 * 65], F32))
        Ssib_bf = E(nc.sbuf_tensor("Ssib_bf", [128, 65], BF16))
        yext = E(nc.sbuf_tensor("yext", [128, 128], F32))
        den_s = E(nc.sbuf_tensor("den_s", [1, 128], F32))
        onesE_s = E(nc.sbuf_tensor("onesE_s", [1, 64], F32))
        bc_s = E(nc.sbuf_tensor("bc_s", [64, 128], F32))
        yfm = E(nc.sbuf_tensor("yfm", [128, NCH * TLOC], BF16))
        fx = [E(nc.sbuf_tensor(f"fx{i}", [128, D], F32)) for i in range(2)]

        ps_q = E(nc.psum_tensor("ps_q", [128, 512], F32))
        ps_k = E(nc.psum_tensor("ps_k", [128, 512], F32))
        ps_g = E(nc.psum_tensor("ps_g", [128, 128], F32))
        ps_y = E(nc.psum_tensor("ps_y", [128, 128], F32))
        ps_s = E(nc.psum_tensor("ps_s", [128, 128], F32))
        ps_t = E(nc.psum_tensor("ps_t", [64, 128], F32))

        vch = Sem(ctx, nc, "vch")
        sch = Sem(ctx, nc, "sch")
        ldc = Sem(ctx, nc, "ldc")
        gw = Sem(ctx, nc, "gw")
        ldx = [Sem(ctx, nc, f"ldx{i}") for i in range(2)]
        stl = [Sem(ctx, nc, f"stl{i}") for i in range(2)]
        trn = [Sem(ctx, nc, f"trn{i}") for i in range(NCH)]
        stq = Sem(ctx, nc, "stq")
        stk = Sem(ctx, nc, "stk")
        stv = Sem(ctx, nc, "stv")
        ldh = Sem(ctx, nc, "ldh")
        stqp = Sem(ctx, nc, "stqp")
        stye = Sem(ctx, nc, "stye")
        ldqp = Sem(ctx, nc, "ldqp")
        ldye = Sem(ctx, nc, "ldye")
        ghl = Sem(ctx, nc, "ghl")
        gex = Sem(ctx, nc, "gex")
        pse1 = Sem(ctx, nc, "pse1")
        ldf = [Sem(ctx, nc, f"ldf{i}") for i in range(2)]
        stx = [Sem(ctx, nc, f"stx{i}") for i in range(2)]
        mm = Sem(ctx, nc, "mm")
        ve = Sem(ctx, nc, "ve")
        se = Sem(ctx, nc, "se")

        block = E(nc.Block())

        @block.sync
        def _(sync):
            tinc(sync.dma_start(out=tri_s[:], in_=trimask), ldc, 16)
            tinc(sync.dma_start(out=hm[:], in_=hmask), ldc, 16)
            tinc(sync.dma_start(out=tag_s[:], in_=tag), ldc, 16)
            sync.wait_ge(ldc.h, ldc.v)

        @block.gpsimd
        def _(gpsimd):
            tinc(gpsimd.dma_start(out=wf_s[:], in_=wf), gw, 16)

        @block.vector
        def _(vector):
            vector.memset(epsb[:], 1e-5)
            vector.memset(lncb[:], LNC_FEAT)
            vector.memset(eps6[:], 1e-6)
            vector.memset(nhalf[:], -0.5)
            vector.memset(vh_s[:], 1.0)
            vector.memset(onesE_s[:], 1.0)
            tinc(vector.memset(S_f[:], 0.0), ve, 1)
        ve_init = ve.v

        tile_done = []
        se_tile = []
        for i in range(NTI):
            b = i % 2

            @block.sync
            def _(sync, i=i, b=b):
                if i >= 2:
                    sync.wait_ge(ve.h, tile_done[i - 2])
                    sync.wait_ge(se.h, se_tile[i - 2])
                tinc(sync.dma_start(out=xb[b][:], in_=x4[i * 128:(i + 1) * 128, :]), ldx[b], 16)
            ld_i = ldx[b].v

            @block.scalar
            def _(scalar, i=i, b=b, ld_i=ld_i):
                scalar.wait_ge(ldx[b].h, ld_i)
                if i >= 1:
                    scalar.wait_ge(sch.h, sch.v)
                if i >= 2:
                    scalar.wait_ge(ve.h, tile_done[i - 2])
                ch(scalar.activation(sq[:], xb[b][:], AF.Copy, accum_out=stat[b][:, 0:1]), sch, scalar)
                tinc(scalar.activation(sq[:], xb[b][:], AF.Square, accum_out=stat[b][:, 1:2]), se, 1)
            se_sums = se.v

            @block.vector
            def _(vector, i=i, b=b, se_sums=se_sums):
                vector.wait_ge(se.h, se_sums)
                ch(vector.tensor_scalar(stat[b][:, 2:3], stat[b][:, 0:1], 1.0 / D, None, ALU.mult), vch, vector)
                ch(vector.tensor_tensor(stat[b][:, 3:4], stat[b][:, 2:3], stat[b][:, 2:3], ALU.mult), vch, vector)
                ch(vector.tensor_scalar(stat[b][:, 5:6], stat[b][:, 1:2], 1.0 / D, None, ALU.mult), vch, vector)
                tinc(vector.tensor_tensor(stat[b][:, 3:4], stat[b][:, 5:6], stat[b][:, 3:4], ALU.subtract), ve, 1)
            ve_var = ve.v

            @block.scalar
            def _(scalar, i=i, b=b, ve_var=ve_var):
                scalar.wait_ge(ve.h, ve_var)
                tinc(scalar.activation(stat[b][:, 4:5], stat[b][:, 3:4], AF.Sqrt, bias=epsb[:, 0:1]), se, 1)
            se_tile.append(se.v)
            se_sqrt = se.v

            @block.vector
            def _(vector, i=i, b=b, se_sqrt=se_sqrt):
                vector.wait_ge(se.h, se_sqrt)
                ch(vector.reciprocal(stat[b][:, 4:5], stat[b][:, 4:5]), vch, vector)
                vector.wait_ge(ve.h, ve.v)
                ch(vector.tensor_scalar(lnf[:], xb[b][:], stat[b][:, 2:3], stat[b][:, 4:5],
                                        ALU.subtract, ALU.mult), vch, vector)
                vector.wait_ge(stl[b].h, stl[b].v)
                tinc(vector.tensor_copy(lnt[b][:], lnf[:]), ve, 1)
            tile_done.append(ve.v)
            ve_ln = ve.v

            @block.sync
            def _(sync, i=i, b=b, ve_ln=ve_ln):
                sync.wait_ge(ve.h, ve_ln)
                tinc(sync.dma_start(out=ln_dram[i * 128:(i + 1) * 128, :], in_=lnt[b][:]), stl[b], 16)
        stl_all = (stl[0].v, stl[1].v)

        def proj_fm(wsrc, dst_dram, stdst):
            mm_before = mm.v

            @block.gpsimd
            def _(gpsimd, mm_before=mm_before):
                gpsimd.wait_ge(mm.h, mm_before)
                for cc in range(NCH):
                    tinc(gpsimd.dma_start(out=wA[:, cc * 1024:(cc + 1) * 1024],
                                          in_=wsrc[cc * 128:(cc + 1) * 128, :]), gw, 16)
            gw_w = gw.v

            for t in range(NTT8):
                t0 = t * 512

                @block.scalar
                def _(scalar, t=t, t0=t0):
                    scalar.wait_ge(stl[0].h, stl_all[0])
                    scalar.wait_ge(stl[1].h, stl_all[1])
                    if t >= 1:
                        scalar.wait_ge(mm.h, mm.v)
                    for c in range(NCH):
                        tinc(scalar.dma_start(out=rfm[:, c * 512:(c + 1) * 512],
                                              in_=ln_dram[t0:t0 + 512, c * 128:(c + 1) * 128],
                                              transpose=True), trn[c], 16)

                for oc in range(NCH):
                    pbank = ps_q if oc % 2 == 0 else ps_k

                    @block.tensor
                    def _(tensor, t=t, oc=oc, pbank=pbank, gw_w=gw_w):
                        if oc == 0:
                            for c in range(NCH):
                                tensor.wait_ge(trn[c].h, trn[c].v)
                            tensor.wait_ge(gw.h, gw_w)
                        if not (t == 0 and oc < 2):
                            tensor.wait_ge(se.h, se.v - 1)
                        for c in range(NCH):
                            i_ = tensor.matmul(pbank[:], wA[:, c * 1024 + oc * 128: c * 1024 + (oc + 1) * 128],
                                               rfm[:, c * 512:(c + 1) * 512], start=(c == 0), stop=(c == NCH - 1))
                        tinc(i_, mm, 1)
                    mm_p = mm.v

                    @block.scalar
                    def _(scalar, t=t, oc=oc, pbank=pbank, mm_p=mm_p,
                          g=(stq.v, stk.v, stv.v)):
                        scalar.wait_ge(mm.h, mm_p)
                        scalar.wait_ge(stdst.h, stdst.v)
                        scalar.wait_ge(stq.h, g[0])
                        scalar.wait_ge(stk.h, g[1])
                        scalar.wait_ge(stv.h, g[2])
                        tinc(scalar.activation(evb[:], pbank[:], AF.Copy), se, 1)
                    se_p = se.v

                    @block.sync
                    def _(sync, t=t, t0=t0, oc=oc, se_p=se_p):
                        sync.wait_ge(se.h, se_p)
                        tinc(sync.dma_start(out=dst_dram[oc * 128:(oc + 1) * 128, t0:t0 + 512], in_=evb[:]), stdst, 16)

        proj_fm(wq, q_dram, stq)
        proj_fm(wk, k_dram, stk)

        mm_before_v = mm.v

        @block.gpsimd
        def _(gpsimd, mm_before_v=mm_before_v):
            gpsimd.wait_ge(mm.h, mm_before_v)
            for cc in range(NCH):
                tinc(gpsimd.dma_start(out=wA[:, cc * 1024:(cc + 1) * 1024],
                                      in_=wv[cc * 128:(cc + 1) * 128, :]), gw, 16)
        gw_v = gw.v

        for t in range(NTT8):
            t0 = t * 512

            @block.scalar
            def _(scalar, t=t, t0=t0):
                if t >= 1:
                    scalar.wait_ge(mm.h, mm.v)
                for c in range(NCH):
                    tinc(scalar.dma_start(out=rfm[:, c * 512:(c + 1) * 512],
                                          in_=ln_dram[t0:t0 + 512, c * 128:(c + 1) * 128],
                                          transpose=True), trn[c], 16)

            for sub in range(4):
                for half in range(2):
                    pbank = ps_q if half == 0 else ps_k

                    @block.tensor
                    def _(tensor, t=t, sub=sub, half=half, pbank=pbank, gw_v=gw_v):
                        if sub == 0 and half == 0:
                            for c in range(NCH):
                                tensor.wait_ge(trn[c].h, trn[c].v)
                            tensor.wait_ge(gw.h, gw_v)
                        if not (t == 0 and sub == 0):
                            tensor.wait_ge(se.h, se.v - 1)
                        for c in range(NCH):
                            i_ = tensor.matmul(pbank[:], rfm[:, c * 512 + sub * 128: c * 512 + (sub + 1) * 128],
                                               wA[:, c * 1024 + half * 512:(c * 1024) + (half + 1) * 512],
                                               start=(c == 0), stop=(c == NCH - 1))
                        tinc(i_, mm, 1)
                    mm_v = mm.v

                    @block.scalar
                    def _(scalar, t=t, sub=sub, half=half, pbank=pbank, mm_v=mm_v,
                          g=(stq.v, stk.v)):
                        scalar.wait_ge(mm.h, mm_v)
                        scalar.wait_ge(stv.h, stv.v)
                        scalar.wait_ge(stq.h, g[0])
                        scalar.wait_ge(stk.h, g[1])
                        tinc(scalar.activation(evb[:], pbank[:], AF.Copy), se, 1)
                    se_v = se.v

                    @block.sync
                    def _(sync, t=t, t0=t0, sub=sub, half=half, se_v=se_v):
                        sync.wait_ge(se.h, se_v)
                        tinc(sync.dma_start(
                            out=v_dram[t0 + sub * 128: t0 + (sub + 1) * 128, half * 512:(half + 1) * 512],
                            in_=evb[:]), stv, 16)
        stv_all = stv.v
        stq_all = stq.v
        stk_all = stk.v

        for h in range(16):
            hc = h // 2
            hb = 64 * (h % 2)

            @block.sync
            def _(sync, h=h, hc=hc, hb=hb):
                sync.wait_ge(stq.h, stq_all)
                sync.wait_ge(stk.h, stk_all)
                sync.wait_ge(stv.h, stv_all)
                if h >= 1:
                    sync.wait_ge(se.h, se.v)
                    sync.wait_ge(mm.h, mm.v)
                tinc(sync.dma_start(out=qh_s[:], in_=q_dram[64 * h:64 * (h + 1), :]), ldh, 16)
                tinc(sync.dma_start(out=kh_s[:], in_=k_dram[64 * h:64 * (h + 1), :]), ldh, 16)
                tinc(sync.dma_start(out=vh_s[:].rearrange("p (i e) -> p i e", e=65)[:, :, 0:64],
                                    in_=v_dram.rearrange("(i p) d -> p i d", p=128)[:, :, 64 * h:64 * (h + 1)]),
                     ldh, 16)
            ld_h = ldh.v

            @block.scalar
            def _(scalar, h=h, ld_h=ld_h):
                scalar.wait_ge(ldh.h, ld_h)
                ch(scalar.activation(qsq[:], qh_s[:], AF.Square), sch, scalar)
                tinc(scalar.activation(ksq[:], kh_s[:], AF.Square), se, 1)
            se_sq = se.v

            for ck in range(NCK):
                c0 = ck * 128

                @block.tensor
                def _(tensor, h=h, ck=ck, c0=c0, se_sq=se_sq):
                    tensor.wait_ge(se.h, se_sq)
                    if ck >= 1 or h >= 1:
                        tensor.wait_ge(se.h, se.v)
                        tensor.wait_ge(ve.h, ve.v)
                    tensor.matmul(ps_q[:, 0:128], wf_s[:], qh_s[:, c0:c0 + 128], start=True, stop=False)
                    tensor.matmul(ps_q[:, 0:128], nhalf[:], qsq[:, c0:c0 + 128], start=False, stop=True)
                    tensor.matmul(ps_k[:, 0:128], wf_s[:], kh_s[:, c0:c0 + 128], start=True, stop=False)
                    tensor.matmul(ps_k[:, 0:128], nhalf[:], ksq[:, c0:c0 + 128], start=False, stop=True)
                    tensor.matmul(ps_s[:], kh_s[:, c0:c0 + 128], wf_s[:], start=True, stop=False)
                    tinc(tensor.matmul(ps_s[:], ksq[:, c0:c0 + 128], nhalf[:], start=False, stop=True), mm, 1)
                mm_f = mm.v

                @block.scalar
                def _(scalar, h=h, ck=ck, mm_f=mm_f):
                    scalar.wait_ge(mm.h, mm_f)
                    scalar.wait_ge(ve.h, ve.v)
                    scalar.wait_ge(stqp.h, stqp.v)
                    ch(scalar.activation(qp_s[:], ps_q[:, 0:128], AF.Exp, bias=lncb[:, 0:1]), sch, scalar)
                    ch(scalar.activation(kp_s[:], ps_k[:, 0:128], AF.Exp, bias=lncb[:, 0:1]), sch, scalar)
                    tinc(scalar.activation(kpT_s[:], ps_s[:], AF.Exp, bias=lncb[:, 0:1]), se, 1)
                se_feat = se.v

                @block.sync
                def _(sync, h=h, ck=ck, c0=c0, se_feat=se_feat):
                    sync.wait_ge(se.h, se_feat)
                    tinc(sync.dma_start(out=qp_dram[h, :, c0:c0 + 128], in_=qp_s[:]), stqp, 16)

                @block.tensor
                def _(tensor, h=h, ck=ck, c0=c0, se_feat=se_feat):
                    tensor.wait_ge(se.h, se_feat)
                    tinc(tensor.matmul(ps_g[:], kp_s[:], qp_s[:], start=True, stop=True), mm, 1)
                mm_g = mm.v

                @block.vector
                def _(vector, h=h, ck=ck, mm_g=mm_g):
                    vector.wait_ge(mm.h, mm_g)
                    tinc(vector.tensor_tensor(gm_s[:], ps_g[:], tri_s[:], ALU.mult), ve, 1)
                ve_gm = ve.v

                @block.vector
                def _(vector, h=h, ck=ck, ve_gm=ve_gm):
                    vector.wait_ge(ve.h, ve_gm)
                    tinc(vector.tensor_copy(S_bf[:], S_f[:, h * 65:(h + 1) * 65]), ve, 1)
                ve_sbf = ve.v

                @block.tensor
                def _(tensor, h=h, ck=ck, c0=c0, ve_gm=ve_gm, ve_sbf=ve_sbf):
                    tensor.wait_ge(ve.h, ve_sbf)
                    tensor.matmul(ps_y[0:65, :], vh_s[:, ck * 65:(ck + 1) * 65], gm_s[:], start=True, stop=False)
                    tinc(tensor.matmul(ps_y[0:65, :], S_bf[:], qp_s[:], start=False, stop=True), mm, 1)
                    tinc(tensor.matmul(ps_s[:, 0:65], kpT_s[:], vh_s[:, ck * 65:(ck + 1) * 65],
                                       start=True, stop=True), mm, 1)
                mm_y = mm.v

                @block.scalar
                def _(scalar, h=h, ck=ck, c0=c0, mm_y=mm_y):
                    scalar.wait_ge(mm.h, mm_y)
                    scalar.wait_ge(stye.h, stye.v)
                    tinc(scalar.activation(yext[0:65, :], ps_y[0:65, :], AF.Copy), se, 1)
                se_ye = se.v

                @block.sync
                def _(sync, h=h, ck=ck, c0=c0, se_ye=se_ye):
                    sync.wait_ge(se.h, se_ye)
                    tinc(sync.dma_start(out=ye_dram[h, :, c0:c0 + 128], in_=yext[0:65, :]), stye, 16)

                @block.vector
                def _(vector, h=h, ck=ck, mm_y=mm_y):
                    vector.wait_ge(mm.h, mm_y)
                    tinc(vector.tensor_tensor(S_f[:, h * 65:(h + 1) * 65], S_f[:, h * 65:(h + 1) * 65],
                                              ps_s[:, 0:65], ALU.add), ve, 1)

        ve_Sall = ve.v

        @block.gpsimd
        def _(gpsimd):
            gpsimd.wait_ge(ve.h, ve_Sall)
            gpsimd.wait_ge(ldc.h, ldc.v)
            pid = gpsimd.partition_id()
            slot = pid % 2
            sib = slot if self_exchange else 1 - (pid % 2)
            tinc(gpsimd.dma_start(out=sh_S[slot, :, :], in_=S_f[:]), ghl, 16)
            gpsimd.wait_ge(ghl.h, ghl.v)
            tinc(gpsimd.dma_start(out=sh_flag[slot, :][None, :], in_=tag_s[:]), ghl, 16)
            gpsimd.wait_ge(ghl.h, ghl.v)
            poll_flag(gpsimd, sh_flag[sib, :][None, :], flag_s, tag_s, pse1)
            tinc(gpsimd.dma_start(out=Ssib[:], in_=sh_S[sib, :, :]), ghl, 16)
            gpsimd.wait_ge(ghl.h, ghl.v)
            gpsimd.sem_inc(gex.h, 1)
            gex.inc(1)
        gex_S = gex.v

        @block.vector
        def _(vector):
            vector.wait_ge(gex.h, gex_S)
            tinc(vector.tensor_scalar(Ssib[:], Ssib[:], hm[:, 0:1], None, ALU.mult), ve, 1)
        ve_Ssib = ve.v

        for h in range(16):
            hc = h // 2
            hb = 64 * (h % 2)

            @block.vector
            def _(vector, h=h):
                vector.wait_ge(ve.h, ve_Ssib)
                vector.wait_ge(mm.h, mm.v)
                tinc(vector.tensor_copy(Ssib_bf[:], Ssib[:, h * 65:(h + 1) * 65]), ve, 1)
            ve_sb = ve.v

            for ck in range(NCK):
                c0 = ck * 128
                sb = ck % 2

                @block.sync
                def _(sync, h=h, ck=ck, c0=c0, g=(stqp.v, stye.v)):
                    sync.wait_ge(stqp.h, g[0])
                    sync.wait_ge(stye.h, g[1])
                    if h + ck > 0:
                        sync.wait_ge(ve.h, ve.v)
                    tinc(sync.dma_start(out=qp_s[:], in_=qp_dram[h, :, c0:c0 + 128]), ldqp, 16)
                    tinc(sync.dma_start(out=yext[0:65, :], in_=ye_dram[h, :, c0:c0 + 128]), ldye, 16)
                ld_c = (ldqp.v, ldye.v)

                @block.tensor
                def _(tensor, h=h, ck=ck, ld_c=ld_c, ve_sb=ve_sb):
                    tensor.wait_ge(ldqp.h, ld_c[0])
                    tensor.wait_ge(ve.h, ve_sb)
                    if h + ck > 0:
                        tensor.wait_ge(ve.h, ve.v - 1)
                    tinc(tensor.matmul(ps_y[0:65, :], Ssib_bf[:], qp_s[:], start=True, stop=True), mm, 1)
                mm_c = mm.v

                @block.vector
                def _(vector, h=h, hc=hc, hb=hb, ck=ck, c0=c0, mm_c=mm_c, ld_c=ld_c):
                    vector.wait_ge(mm.h, mm_c)
                    vector.wait_ge(ldye.h, ld_c[1])
                    ch(vector.tensor_tensor(yext[0:65, :], yext[0:65, :], ps_y[0:65, :], ALU.add), vch, vector)
                    ch(vector.tensor_scalar(den_s[:], yext[64:65, :], eps6[0:1, 0:1], None, ALU.add), vch, vector)
                    ch(vector.reciprocal(den_s[:], den_s[:]), vch, vector)
                    vector.sem_inc(ve.h, 1)
                    ve.inc(1)
                ve_dn = ve.v

                @block.tensor
                def _(tensor, h=h, ck=ck, ve_dn=ve_dn):
                    tensor.wait_ge(ve.h, ve_dn)
                    tinc(tensor.matmul(ps_t[:], onesE_s[0:1, 0:64], den_s[:], start=True, stop=True), mm, 1)
                mm_bc = mm.v

                @block.vector
                def _(vector, h=h, hc=hc, hb=hb, ck=ck, c0=c0, mm_bc=mm_bc):
                    vector.wait_ge(mm.h, mm_bc)
                    ch(vector.tensor_copy(bc_s[:], ps_t[:]), vch, vector)
                    tinc(vector.tensor_tensor(yfm[hb:hb + 64, hc * TLOC + c0: hc * TLOC + c0 + 128],
                                              yext[0:64, :], bc_s[:], ALU.mult), ve, 1)
        ve_yfm = ve.v

        @block.gpsimd
        def _(gpsimd):
            gpsimd.wait_ge(mm.h, mm.v)
            for cc in range(NCH):
                tinc(gpsimd.dma_start(out=wA[:, cc * 1024:(cc + 1) * 1024],
                                      in_=wo[cc * 128:(cc + 1) * 128, :]), gw, 16)
        gw_o = gw.v

        for t in range(NTT8):
            t0 = t * 512
            for oc in range(NCH):
                pbank = ps_q if oc % 2 == 0 else ps_k

                @block.tensor
                def _(tensor, t=t, t0=t0, oc=oc, pbank=pbank, gw_o=gw_o):
                    if t == 0 and oc == 0:
                        tensor.wait_ge(gw.h, gw_o)
                        tensor.wait_ge(ve.h, ve_yfm)
                    if not (t == 0 and oc < 2):
                        tensor.wait_ge(se.h, se.v - 1)
                    for c in range(NCH):
                        i_ = tensor.matmul(pbank[:], wA[:, c * 1024 + oc * 128: c * 1024 + (oc + 1) * 128],
                                           yfm[:, c * TLOC + t0: c * TLOC + t0 + 512], start=(c == 0), stop=(c == NCH - 1))
                    tinc(i_, mm, 1)
                mm_o = mm.v
                @block.scalar
                def _(scalar, t=t, t0=t0, oc=oc, pbank=pbank, mm_o=mm_o):
                    scalar.wait_ge(mm.h, mm_o)
                    scalar.wait_ge(stq.h, stq.v)
                    tinc(scalar.activation(evb[:], pbank[:], AF.Copy), se, 1)
                se_o = se.v

                @block.sync
                def _(sync, t=t, t0=t0, oc=oc, se_o=se_o):
                    sync.wait_ge(se.h, se_o)
                    tinc(sync.dma_start(out=q_dram[oc * 128:(oc + 1) * 128, t0:t0 + 512], in_=evb[:]), stq, 16)
        sto_all = stq.v

        for i in range(NTI):
            b = i % 2

            @block.sync
            def _(sync, i=i, b=b):
                sync.wait_ge(stq.h, sto_all)
                sync.wait_ge(stx[b].h, stx[b].v)
                tinc(sync.dma_start(out=fx[b][:], in_=x4[i * 128:(i + 1) * 128, :]), ldf[b], 16)
            ld_f = ldf[b].v

            @block.scalar
            def _(scalar, i=i, b=b):
                scalar.wait_ge(stq.h, sto_all)
                scalar.wait_ge(stx[b].h, stx[b].v)
                scalar.wait_ge(ldx[b].h, ldx[b].v)
                tinc(scalar.dma_start(out=lnt[b][:], in_=q_dram[:, i * 128:(i + 1) * 128],
                                      transpose=True), ldx[b], 16)
            tr_f = ldx[b].v

            @block.vector
            def _(vector, i=i, b=b, ld_f=ld_f, tr_f=tr_f):
                vector.wait_ge(ldf[b].h, ld_f)
                vector.wait_ge(ldx[b].h, tr_f)
                tinc(vector.tensor_tensor(fx[b][:], fx[b][:], lnt[b][:], ALU.add), ve, 1)
            ve_f = ve.v

            @block.sync
            def _(sync, i=i, b=b, ve_f=ve_f):
                sync.wait_ge(ve.h, ve_f)
                tinc(sync.dma_start(out=x5[i * 128:(i + 1) * 128, :], in_=fx[b][:]), stx[b], 16)

        @block.sync
        def _(sync):
            sync.wait_ge(stx[0].h, stx[0].v)
            sync.wait_ge(stx[1].h, stx[1].v)

    return nc



BIGNEG = 1.0e8


def build_F(detect=True):
    nc = bass.Bass("TRN2", target_bir_lowering=False, debug=False,
                   detect_race_conditions=detect, dynamic_dma_scratch_size=8192)
    xin = nc.dram_tensor("xin", [TLOC, D], F32, kind="ExternalInput").ap()
    wgT = nc.dram_tensor("wgT", [D, 16], F32, kind="ExternalInput").ap()
    bias = nc.dram_tensor("bias", [1, 16], F32, kind="ExternalInput").ap()
    w1T = nc.dram_tensor("w1T", [16, D, 512], F32, kind="ExternalInput").ap()
    w2T = nc.dram_tensor("w2T", [16, 512, D], F32, kind="ExternalInput").ap()
    ws1T = nc.dram_tensor("ws1T", [D, D], F32, kind="ExternalInput").ap()
    ws2T = nc.dram_tensor("ws2T", [D, D], F32, kind="ExternalInput").ap()
    onesc = nc.dram_tensor("onesc", [128, 1], F32, kind="ExternalInput").ap()
    onesr = nc.dram_tensor("onesr", [1, 128], F32, kind="ExternalInput").ap()
    iota16 = nc.dram_tensor("iota16", [128, 16], F32, kind="ExternalInput").ap()
    ident = nc.dram_tensor("ident", [128, 128], F32, kind="ExternalInput").ap()
    xout = nc.dram_tensor("xout", [TLOC, D], F32, kind="ExternalOutput").ap()

    ln_dram = nc.dram_tensor("ln_dram", [TLOC, D], BF16, kind="Internal").ap()
    cw_dram = nc.dram_tensor("cw_dram", [16, TLOC], F32, kind="Internal").ap()
    o_dram = nc.dram_tensor("o_dram", [D, TLOC], BF16, kind="Internal").ap()

    NTI = TLOC // 128
    NTT8 = TLOC // 512

    with contextlib.ExitStack() as ctx:
        E = ctx.enter_context
        sq = E(nc.sbuf_tensor("sq", [128, D], F32))
        lnf = E(nc.sbuf_tensor("lnf", [128, D], F32))
        stat = [E(nc.sbuf_tensor(f"stat{i}", [128, 8], F32)) for i in range(2)]
        epsb = E(nc.sbuf_tensor("epsb", [128, 1], F32))
        xb = [E(nc.sbuf_tensor(f"xb{i}", [128, D], F32)) for i in range(2)]
        lnt = [E(nc.sbuf_tensor(f"lnt{i}", [128, D], BF16)) for i in range(2)]
        ones_s = E(nc.sbuf_tensor("ones_s", [128, 1], F32))
        onesr_s = E(nc.sbuf_tensor("onesr_s", [1, 128], F32))
        iota_s = E(nc.sbuf_tensor("iota_s", [128, 16], F32))
        iM_s = E(nc.sbuf_tensor("iM_s", [128, 16], F32))
        ident_s = E(nc.sbuf_tensor("ident_s", [128, 128], F32))
        wgs = E(nc.sbuf_tensor("wgs", [128, NCH * 16], BF16))
        bcb = E(nc.sbuf_tensor("bcb", [128, 16], F32))
        rfm = E(nc.sbuf_tensor("rfm", [128, NCH * 512], BF16))
        probs = E(nc.sbuf_tensor("probs", [128, 16], F32))
        schoice = E(nc.sbuf_tensor("schoice", [128, 16], F32))
        t16 = E(nc.sbuf_tensor("t16", [128, 16], F32))
        u16 = E(nc.sbuf_tensor("u16", [128, 16], F32))
        g4 = E(nc.sbuf_tensor("g4", [128, 4], F32))
        oh = E(nc.sbuf_tensor("oh", [128, 16], F32))
        cwt = E(nc.sbuf_tensor("cwt", [128, 16], F32))
        cwT = E(nc.sbuf_tensor("cwT", [16, 128], F32))
        colF = E(nc.sbuf_tensor("colF", [128, 8], F32))
        w1s = [E(nc.sbuf_tensor(f"w1s{i}", [128, NCH * 512], BF16)) for i in range(2)]
        w2s = [E(nc.sbuf_tensor(f"w2s{i}", [128, 4 * D], BF16)) for i in range(2)]
        hidb_all = E(nc.sbuf_tensor("hidb_all", [128, 64 * 512], BF16))
        hshb = E(nc.sbuf_tensor("hshb", [128, NCH * 512], BF16))
        relb = E(nc.sbuf_tensor("relb", [128, 512], BF16))
        cwrow = E(nc.sbuf_tensor("cwrow", [1, TLOC], F32))
        cwbc = E(nc.sbuf_tensor("cwbc", [128, 512], BF16))
        evb = E(nc.sbuf_tensor("evb", [128, 512], BF16))
        fx = [E(nc.sbuf_tensor(f"fx{i}", [128, D], F32)) for i in range(2)]
        fo = [E(nc.sbuf_tensor(f"fo{i}", [128, D], BF16)) for i in range(2)]

        ps8 = E(nc.psum_tensor("ps8", [128, 4096], F32))

        vch = Sem(ctx, nc, "vch")
        sch = Sem(ctx, nc, "sch")
        ldc = Sem(ctx, nc, "ldc")
        gw = [Sem(ctx, nc, f"gw{i}") for i in range(2)]
        gws = Sem(ctx, nc, "gws")
        ldx = [Sem(ctx, nc, f"ldx{i}") for i in range(2)]
        stl = [Sem(ctx, nc, f"stl{i}") for i in range(2)]
        trn = [Sem(ctx, nc, f"trn{i}") for i in range(NCH)]
        stcw = Sem(ctx, nc, "stcw")
        ldcw = Sem(ctx, nc, "ldcw")
        sto = Sem(ctx, nc, "sto")
        ldf = [Sem(ctx, nc, f"ldf{i}") for i in range(2)]
        trf = [Sem(ctx, nc, f"trf{i}") for i in range(2)]
        stx = [Sem(ctx, nc, f"stx{i}") for i in range(2)]
        mm = Sem(ctx, nc, "mm")
        ve = Sem(ctx, nc, "ve")
        se = Sem(ctx, nc, "se")

        block = E(nc.Block())

        @block.sync
        def _(sync):
            tinc(sync.dma_start(out=ones_s[:], in_=onesc), ldc, 16)
            tinc(sync.dma_start(out=onesr_s[:], in_=onesr), ldc, 16)
            tinc(sync.dma_start(out=iota_s[:], in_=iota16), ldc, 16)
            tinc(sync.dma_start(out=ident_s[:], in_=ident), ldc, 16)
            tinc(sync.dma_start(out=bcb[0:1, :], in_=bias), ldc, 16)
            sync.wait_ge(ldc.h, ldc.v)

        @block.gpsimd
        def _(gpsimd):
            for cc in range(NCH):
                tinc(gpsimd.dma_start(out=wgs[:, cc * 16:(cc + 1) * 16],
                                      in_=wgT[cc * 128:(cc + 1) * 128, :]), gws, 16)
        gws_w = gws.v

        @block.vector
        def _(vector):
            vector.memset(epsb[:], 1e-5)
            vector.wait_ge(ldc.h, ldc.v)
            tinc(vector.tensor_scalar(iM_s[:], iota_s[:], 64.0, None, ALU.subtract), ve, 1)
        ve_init = ve.v

        @block.tensor
        def _(tensor):
            tensor.wait_ge(ldc.h, ldc.v)
            tinc(tensor.matmul(ps8[:, 3072:3088], onesr_s[:], bcb[0:1, :], start=True, stop=True), mm, 1)
        mm_bc = mm.v

        @block.vector
        def _(vector):
            vector.wait_ge(mm.h, mm_bc)
            tinc(vector.tensor_copy(bcb[:], ps8[:, 3072:3088]), ve, 1)
        ve_bcb = ve.v

        tile_done = []
        se_tile = []
        for i in range(NTI):
            b = i % 2

            @block.sync
            def _(sync, i=i, b=b):
                if i >= 2:
                    sync.wait_ge(ve.h, tile_done[i - 2])
                    sync.wait_ge(se.h, se_tile[i - 2])
                tinc(sync.dma_start(out=xb[b][:], in_=xin[i * 128:(i + 1) * 128, :]), ldx[b], 16)
            ld_i = ldx[b].v

            @block.scalar
            def _(scalar, i=i, b=b, ld_i=ld_i):
                scalar.wait_ge(ldx[b].h, ld_i)
                if i >= 1:
                    scalar.wait_ge(sch.h, sch.v)
                if i >= 2:
                    scalar.wait_ge(ve.h, tile_done[i - 2])
                ch(scalar.activation(sq[:], xb[b][:], AF.Copy, accum_out=stat[b][:, 0:1]), sch, scalar)
                tinc(scalar.activation(sq[:], xb[b][:], AF.Square, accum_out=stat[b][:, 1:2]), se, 1)
            se_sums = se.v

            @block.vector
            def _(vector, i=i, b=b, se_sums=se_sums):
                vector.wait_ge(se.h, se_sums)
                ch(vector.tensor_scalar(stat[b][:, 2:3], stat[b][:, 0:1], 1.0 / D, None, ALU.mult), vch, vector)
                ch(vector.tensor_tensor(stat[b][:, 3:4], stat[b][:, 2:3], stat[b][:, 2:3], ALU.mult), vch, vector)
                ch(vector.tensor_scalar(stat[b][:, 5:6], stat[b][:, 1:2], 1.0 / D, None, ALU.mult), vch, vector)
                tinc(vector.tensor_tensor(stat[b][:, 3:4], stat[b][:, 5:6], stat[b][:, 3:4], ALU.subtract), ve, 1)
            ve_var = ve.v

            @block.scalar
            def _(scalar, i=i, b=b, ve_var=ve_var):
                scalar.wait_ge(ve.h, ve_var)
                tinc(scalar.activation(stat[b][:, 4:5], stat[b][:, 3:4], AF.Sqrt, bias=epsb[:, 0:1]), se, 1)
            se_tile.append(se.v)
            se_sqrt = se.v

            @block.vector
            def _(vector, i=i, b=b, se_sqrt=se_sqrt):
                vector.wait_ge(se.h, se_sqrt)
                ch(vector.reciprocal(stat[b][:, 4:5], stat[b][:, 4:5]), vch, vector)
                vector.wait_ge(ve.h, ve.v)
                ch(vector.tensor_scalar(lnf[:], xb[b][:], stat[b][:, 2:3], stat[b][:, 4:5],
                                        ALU.subtract, ALU.mult), vch, vector)
                vector.wait_ge(stl[b].h, stl[b].v)
                tinc(vector.tensor_copy(lnt[b][:], lnf[:]), ve, 1)
            tile_done.append(ve.v)
            ve_ln = ve.v

            @block.sync
            def _(sync, i=i, b=b, ve_ln=ve_ln):
                sync.wait_ge(ve.h, ve_ln)
                tinc(sync.dma_start(out=ln_dram[i * 128:(i + 1) * 128, :], in_=lnt[b][:]), stl[b], 16)
        stl_all = (stl[0].v, stl[1].v)

        for t in range(NTT8):
            t0 = t * 512

            @block.scalar
            def _(scalar, t=t, t0=t0):
                scalar.wait_ge(stl[0].h, stl_all[0])
                scalar.wait_ge(stl[1].h, stl_all[1])
                if t >= 1:
                    scalar.wait_ge(mm.h, mm.v)
                for c in range(NCH):
                    tinc(scalar.dma_start(out=rfm[:, c * 512:(c + 1) * 512],
                                          in_=ln_dram[t0:t0 + 512, c * 128:(c + 1) * 128],
                                          transpose=True), trn[c], 16)

            for sub in range(4):
                i = t * 4 + sub

                @block.tensor
                def _(tensor, t=t, sub=sub, i=i):
                    if sub == 0:
                        for c in range(NCH):
                            tensor.wait_ge(trn[c].h, trn[c].v)
                        tensor.wait_ge(gws.h, gws_w)
                    if i >= 1:
                        tensor.wait_ge(se.h, se.v)
                        tensor.wait_ge(ve.h, ve.v)
                    for c in range(NCH):
                        i_ = tensor.matmul(ps8[:, 3072:3088], rfm[:, c * 512 + sub * 128: c * 512 + (sub + 1) * 128],
                                           wgs[:, c * 16:(c + 1) * 16], start=(c == 0), stop=(c == NCH - 1))
                    tinc(i_, mm, 1)
                mm_lg = mm.v

                @block.scalar
                def _(scalar, i=i, mm_lg=mm_lg):
                    scalar.wait_ge(mm.h, mm_lg)
                    scalar.wait_ge(ve.h, ve.v)
                    tinc(scalar.activation(probs[:], ps8[:, 3072:3088], AF.Sigmoid), se, 1)
                se_sc = se.v

                @block.vector
                def _(vector, i=i, se_sc=se_sc):
                    vector.wait_ge(se.h, se_sc)
                    vector.wait_ge(ve.h, ve_bcb)
                    vector.wait_ge(ve.h, ve_init)
                    ch(vector.tensor_tensor(schoice[:], probs[:], bcb[:], ALU.add), vch, vector)
                    for g in range(4):
                        sl = schoice[:, 4 * g:4 * (g + 1)]
                        ch(vector.reduce_max(colF[:, 0:1], sl, axis=AXX), vch, vector)
                        ch(vector.tensor_scalar(u16[:, 0:4], sl, colF[:, 0:1], None, ALU.is_lt), vch, vector)
                        ch(vector.tensor_scalar(t16[:, 0:4], u16[:, 0:4], -BIGNEG, BIGNEG, ALU.mult, ALU.add), vch, vector)
                        ch(vector.tensor_tensor(t16[:, 4:8], sl, t16[:, 0:4], ALU.subtract), vch, vector)
                        ch(vector.reduce_max(colF[:, 1:2], t16[:, 4:8], axis=AXX), vch, vector)
                        ch(vector.tensor_tensor(g4[:, g:g + 1], colF[:, 0:1], colF[:, 1:2], ALU.add), vch, vector)
                    ch(vector.reduce_max(colF[:, 2:3], g4[:], axis=AXX), vch, vector)
                    ch(vector.tensor_scalar(u16[:, 0:4], g4[:], colF[:, 2:3], None, ALU.is_lt), vch, vector)
                    ch(vector.tensor_scalar(t16[:, 0:4], u16[:, 0:4], -BIGNEG, BIGNEG, ALU.mult, ALU.add), vch, vector)
                    ch(vector.tensor_tensor(t16[:, 4:8], g4[:], t16[:, 0:4], ALU.subtract), vch, vector)
                    ch(vector.reduce_max(colF[:, 3:4], t16[:, 4:8], axis=AXX), vch, vector)
                    ch(vector.tensor_scalar(u16[:, 0:4], g4[:], colF[:, 3:4], None, ALU.is_ge), vch, vector)
                    for g in range(4):
                        ch(vector.tensor_scalar(t16[:, 4 * g:4 * (g + 1)],
                                                iota_s[:, 0:4] if False else schoice[:, 4 * g:4 * (g + 1)],
                                                u16[:, g:g + 1], None, ALU.mult), vch, vector)
                    ch(vector.memset(cwt[:], 0.0), vch, vector)
                    ch(vector.memset(colF[:, 6:7], 0.0), vch, vector)
                    for kk in range(4):
                        ch(vector.reduce_max(colF[:, 0:1], t16[:], axis=AXX), vch, vector)
                        ch(vector.tensor_scalar(u16[:], t16[:], colF[:, 0:1], None, ALU.is_equal), vch, vector)
                        ch(vector.tensor_tensor(u16[:], u16[:], iM_s[:], ALU.mult), vch, vector)
                        ch(vector.tensor_scalar(u16[:], u16[:], 64.0, None, ALU.add), vch, vector)
                        ch(vector.tensor_reduce(colF[:, 1:2], u16[:], AXX, ALU.min), vch, vector)
                        ch(vector.tensor_scalar(oh[:], iota_s[:], colF[:, 1:2], None, ALU.is_equal), vch, vector)
                        ch(vector.tensor_tensor(u16[:], probs[:], oh[:], ALU.mult), vch, vector)
                        ch(vector.reduce_sum(colF[:, 2:3], u16[:], axis=AXX), vch, vector)
                        ch(vector.tensor_tensor(colF[:, 6:7], colF[:, 6:7], colF[:, 2:3], ALU.add), vch, vector)
                        ch(vector.tensor_scalar(u16[:], oh[:], colF[:, 2:3], None, ALU.mult), vch, vector)
                        ch(vector.tensor_tensor(cwt[:], cwt[:], u16[:], ALU.add), vch, vector)
                        if kk < 3:
                            ch(vector.tensor_scalar(u16[:], oh[:], BIGNEG, None, ALU.mult), vch, vector)
                            ch(vector.tensor_tensor(t16[:], t16[:], u16[:], ALU.subtract), vch, vector)
                    ch(vector.tensor_scalar(colF[:, 6:7], colF[:, 6:7], 1e-20, None, ALU.add), vch, vector)
                    ch(vector.reciprocal(colF[:, 6:7], colF[:, 6:7]), vch, vector)
                    ch(vector.tensor_scalar(cwt[:], cwt[:], colF[:, 6:7], 2.5, ALU.mult, ALU.mult), vch, vector)
                    vector.sem_inc(ve.h, 1)
                    ve.inc(1)
                ve_cw = ve.v

                @block.tensor
                def _(tensor, i=i, ve_cw=ve_cw):
                    tensor.wait_ge(ve.h, ve_cw)
                    tinc(tensor.transpose(ps8[0:16, 3584:3712], cwt[:], ident_s[:]), mm, 1)
                mm_ct = mm.v

                @block.vector
                def _(vector, i=i, mm_ct=mm_ct):
                    vector.wait_ge(mm.h, mm_ct)
                    vector.wait_ge(stcw.h, stcw.v)
                    ch(vector.tensor_copy(cwT[:], ps8[0:16, 3584:3712]), vch, vector)
                    vector.sem_inc(ve.h, 1)
                    ve.inc(1)
                ve_ctt = ve.v

                @block.sync
                def _(sync, i=i, ve_ctt=ve_ctt):
                    sync.wait_ge(ve.h, ve_ctt)
                    tinc(sync.dma_start(out=cw_dram[:, i * 128:(i + 1) * 128], in_=cwT[:]), stcw, 16)
        stcw_all = stcw.v

        for t in range(NTT8):
            t0 = t * 512

            @block.scalar
            def _(scalar, t=t, t0=t0):
                if t >= 1:
                    scalar.wait_ge(mm.h, mm.v)
                for c in range(NCH):
                    tinc(scalar.dma_start(out=rfm[:, c * 512:(c + 1) * 512],
                                          in_=ln_dram[t0:t0 + 512, c * 128:(c + 1) * 128],
                                          transpose=True), trn[c], 16)

            mm_before_sh = mm.v

            @block.gpsimd
            def _(gpsimd, t=t, mm_before_sh=mm_before_sh):
                gpsimd.wait_ge(mm.h, mm_before_sh)
                for cc in range(NCH):
                    tinc(gpsimd.dma_start(out=w1s[0][:, cc * 512:(cc + 1) * 512],
                                          in_=ws1T[cc * 128:(cc + 1) * 128, 0:512]), gw[0], 16)
                for cc in range(NCH):
                    tinc(gpsimd.dma_start(out=w1s[1][:, cc * 512:(cc + 1) * 512],
                                          in_=ws1T[cc * 128:(cc + 1) * 128, 512:1024]), gw[1], 16)
            gw_sh = (gw[0].v, gw[1].v)

            for hh in range(2):
                for fc in range(4):
                    pb0 = (hh * 4 + fc) % 2
                    pslice = ps8[:, pb0 * 512:(pb0 + 1) * 512]

                    @block.tensor
                    def _(tensor, t=t, hh=hh, fc=fc, pslice=pslice, gw_sh=gw_sh):
                        if hh == 0 and fc == 0:
                            for c in range(NCH):
                                tensor.wait_ge(trn[c].h, trn[c].v)
                            tensor.wait_ge(gw[0].h, gw_sh[0])
                            tensor.wait_ge(gw[1].h, gw_sh[1])
                            if t >= 1:
                                tensor.wait_ge(se.h, se.v)
                        else:
                            tensor.wait_ge(se.h, se.v - 1)
                        for c in range(NCH):
                            i_ = tensor.matmul(pslice, w1s[hh][:, c * 512 + fc * 128: c * 512 + (fc + 1) * 128],
                                               rfm[:, c * 512:(c + 1) * 512], start=(c == 0), stop=(c == NCH - 1))
                        tinc(i_, mm, 1)
                    mm_h1 = mm.v

                    @block.scalar
                    def _(scalar, t=t, hh=hh, fc=fc, pslice=pslice, mm_h1=mm_h1):
                        scalar.wait_ge(mm.h, mm_h1)
                        scalar.wait_ge(ve.h, ve.v)
                        tinc(scalar.activation(relb[:], pslice, AF.Relu), se, 1)
                    se_r = se.v

                    @block.vector
                    def _(vector, t=t, hh=hh, fc=fc, se_r=se_r):
                        vector.wait_ge(se.h, se_r)
                        tinc(vector.tensor_tensor(hshb[:, (hh * 4 + fc) * 512:(hh * 4 + fc + 1) * 512],
                                                  relb[:], relb[:], ALU.mult), ve, 1)
            ve_hsh = ve.v

            for e in range(16):
                eb = e % 2
                mm_we = mm.v

                @block.gpsimd
                def _(gpsimd, t=t, e=e, eb=eb, mm_we=mm_we):
                    gpsimd.wait_ge(mm.h, mm_we)
                    for cc in range(NCH):
                        tinc(gpsimd.dma_start(out=w1s[eb][:, cc * 512:(cc + 1) * 512],
                                              in_=w1T[e, cc * 128:(cc + 1) * 128, :]), gw[eb], 16)
                gw_e = gw[eb].v

                @block.sync
                def _(sync, t=t, t0=t0, e=e):
                    sync.wait_ge(stcw.h, stcw_all)
                    sync.wait_ge(ve.h, ve.v)
                    tinc(sync.dma_start(out=cwrow[0:1, 0:512], in_=cw_dram[e:e + 1, t0:t0 + 512]), ldcw, 16)
                ld_cw = ldcw.v

                @block.tensor
                def _(tensor, t=t, e=e, ld_cw=ld_cw):
                    tensor.wait_ge(ldcw.h, ld_cw)
                    tensor.wait_ge(ve.h, ve.v)
                    tinc(tensor.matmul(ps8[:, 1024:1536], onesr_s[:], cwrow[0:1, 0:512],
                                       start=True, stop=True), mm, 1)
                mm_cb = mm.v

                @block.vector
                def _(vector, t=t, e=e, mm_cb=mm_cb):
                    vector.wait_ge(mm.h, mm_cb)
                    tinc(vector.tensor_copy(cwbc[:], ps8[:, 1024:1536]), ve, 1)
                ve_cb = ve.v

                for fc in range(4):
                    pb0 = fc % 2
                    pslice = ps8[:, pb0 * 512:(pb0 + 1) * 512]

                    @block.tensor
                    def _(tensor, t=t, e=e, eb=eb, fc=fc, pslice=pslice, gw_e=gw_e):
                        tensor.wait_ge(gw[eb].h, gw_e)
                        tensor.wait_ge(se.h, se.v - 1 if fc >= 1 else se.v)
                        for c in range(NCH):
                            i_ = tensor.matmul(pslice, w1s[eb][:, c * 512 + fc * 128: c * 512 + (fc + 1) * 128],
                                               rfm[:, c * 512:(c + 1) * 512], start=(c == 0), stop=(c == NCH - 1))
                        tinc(i_, mm, 1)
                    mm_h1 = mm.v

                    @block.scalar
                    def _(scalar, t=t, e=e, fc=fc, pslice=pslice, mm_h1=mm_h1):
                        scalar.wait_ge(mm.h, mm_h1)
                        scalar.wait_ge(ve.h, ve.v)
                        tinc(scalar.activation(relb[:], pslice, AF.Relu), se, 1)
                    se_r = se.v

                    @block.vector
                    def _(vector, t=t, e=e, fc=fc, se_r=se_r, ve_cb=ve_cb):
                        vector.wait_ge(se.h, se_r)
                        vector.wait_ge(ve.h, ve_cb)
                        ch(vector.tensor_tensor(relb[:], relb[:], relb[:], ALU.mult), vch, vector)
                        tinc(vector.tensor_tensor(hidb_all[:, (e * 4 + fc) * 512:(e * 4 + fc + 1) * 512],
                                                  relb[:], cwbc[:], ALU.mult), ve, 1)
            ve_hide = ve.v

            @block.gpsimd
            def _(gpsimd, t=t, ve_hide=ve_hide):
                gpsimd.wait_ge(mm.h, mm.v)
                for cc in range(4):
                    tinc(gpsimd.dma_start(out=w2s[0][:, cc * D:(cc + 1) * D],
                                          in_=ws2T[cc * 128:(cc + 1) * 128, :]), gw[0], 16)
                for cc in range(4):
                    tinc(gpsimd.dma_start(out=w2s[1][:, cc * D:(cc + 1) * D],
                                          in_=ws2T[(4 + cc) * 128:(5 + cc) * 128, :]), gw[1], 16)
            gw_s2 = (gw[0].v, gw[1].v)

            @block.tensor
            def _(tensor, t=t, ve_hide=ve_hide, gw_s2=gw_s2):
                tensor.wait_ge(ve.h, ve_hide)
                tensor.wait_ge(gw[0].h, gw_s2[0])
                tensor.wait_ge(gw[1].h, gw_s2[1])
                if t >= 1:
                    tensor.wait_ge(se.h, se.v)
                for oc in range(NCH):
                    for fc in range(8):
                        wbuf = w2s[0] if fc < 4 else w2s[1]
                        fcl = fc % 4
                        i_ = tensor.matmul(ps8[:, oc * 512:(oc + 1) * 512],
                                           wbuf[:, fcl * D + oc * 128: fcl * D + (oc + 1) * 128],
                                           hshb[:, fc * 512:(fc + 1) * 512],
                                           start=(fc == 0), stop=False)
                tinc(i_, mm, 1)

            for e in range(16):
                eb = e % 2
                mm_we2 = mm.v

                @block.gpsimd
                def _(gpsimd, t=t, e=e, eb=eb, mm_we2=mm_we2):
                    gpsimd.wait_ge(mm.h, mm_we2)
                    for cc in range(4):
                        tinc(gpsimd.dma_start(out=w2s[eb][:, cc * D:(cc + 1) * D],
                                              in_=w2T[e, cc * 128:(cc + 1) * 128, :]), gw[eb], 16)
                gw_e2 = gw[eb].v

                @block.tensor
                def _(tensor, t=t, e=e, eb=eb, gw_e2=gw_e2):
                    tensor.wait_ge(gw[eb].h, gw_e2)
                    for oc in range(NCH):
                        for fc in range(4):
                            i_ = tensor.matmul(ps8[:, oc * 512:(oc + 1) * 512],
                                               w2s[eb][:, fc * D + oc * 128: fc * D + (oc + 1) * 128],
                                               hidb_all[:, (e * 4 + fc) * 512:(e * 4 + fc + 1) * 512],
                                               start=False, stop=(e == 15 and fc == 3))
                    tinc(i_, mm, 1)
            mm_out = mm.v

            for oc in range(NCH):
                @block.scalar
                def _(scalar, t=t, t0=t0, oc=oc, mm_out=mm_out):
                    scalar.wait_ge(mm.h, mm_out)
                    scalar.wait_ge(sto.h, sto.v)
                    tinc(scalar.activation(evb[:], ps8[:, oc * 512:(oc + 1) * 512], AF.Copy), se, 1)
                se_o = se.v

                @block.sync
                def _(sync, t=t, t0=t0, oc=oc, se_o=se_o):
                    sync.wait_ge(se.h, se_o)
                    tinc(sync.dma_start(out=o_dram[oc * 128:(oc + 1) * 128, t0:t0 + 512], in_=evb[:]), sto, 16)
        sto_all = sto.v

        for i in range(NTI):
            b = i % 2

            @block.sync
            def _(sync, i=i, b=b):
                sync.wait_ge(sto.h, sto_all)
                sync.wait_ge(stx[b].h, stx[b].v)
                tinc(sync.dma_start(out=fx[b][:], in_=xin[i * 128:(i + 1) * 128, :]), ldf[b], 16)
            ld_f = ldf[b].v

            @block.scalar
            def _(scalar, i=i, b=b):
                scalar.wait_ge(sto.h, sto_all)
                scalar.wait_ge(stx[b].h, stx[b].v)
                scalar.wait_ge(trf[b].h, trf[b].v)
                tinc(scalar.dma_start(out=fo[b][:], in_=o_dram[:, i * 128:(i + 1) * 128],
                                      transpose=True), trf[b], 16)
            tr_f = trf[b].v

            @block.vector
            def _(vector, i=i, b=b, ld_f=ld_f, tr_f=tr_f):
                vector.wait_ge(ldf[b].h, ld_f)
                vector.wait_ge(trf[b].h, tr_f)
                tinc(vector.tensor_tensor(fx[b][:], fx[b][:], fo[b][:], ALU.add), ve, 1)
            ve_f = ve.v

            @block.sync
            def _(sync, i=i, b=b, ve_f=ve_f):
                sync.wait_ge(ve.h, ve_f)
                tinc(sync.dma_start(out=xout[i * 128:(i + 1) * 128, :], in_=fx[b][:]), stx[b], 16)

        @block.sync
        def _(sync):
            sync.wait_ge(stx[0].h, stx[0].v)
            sync.wait_ge(stx[1].h, stx[1].v)

    return nc



def build_G(detect=True, self_exchange=False, local_shared=False):
    nc = bass.Bass("TRN2", target_bir_lowering=False, debug=False,
                   detect_race_conditions=detect, dynamic_dma_scratch_size=8192)
    x6 = nc.dram_tensor("x6", [TLOC, D], F32, kind="ExternalInput").ap()
    wqkv = nc.dram_tensor("wqkv", [D, 1536], F32, kind="ExternalInput").ap()
    wproj = nc.dram_tensor("wproj", [D, D], F32, kind="ExternalInput").ap()
    cosq = nc.dram_tensor("cosq", [8192, 512], BF16, kind="ExternalInput").ap()
    sinq = nc.dram_tensor("sinq", [8192, 512], BF16, kind="ExternalInput").ap()
    cosk = nc.dram_tensor("cosk", [8192, 512], BF16, kind="ExternalInput").ap()
    sink = nc.dram_tensor("sink", [8192, 512], BF16, kind="ExternalInput").ap()
    bproj = nc.dram_tensor("bproj", [128, NCH], F32, kind="ExternalInput").ap()
    tag = nc.dram_tensor("tag", [1, 16], I32, kind="ExternalInput").ap()
    x7 = nc.dram_tensor("x7", [TLOC, D], F32, kind="ExternalOutput").ap()

    q_dram = nc.dram_tensor("q_dram", [8192, 512], BF16, kind="Internal").ap()
    k_dram = nc.dram_tensor("k_dram", [8192, 512], BF16, kind="Internal").ap()
    v_dram = nc.dram_tensor("v_dram", [8192, 512], BF16, kind="Internal").ap()
    o2_dram = nc.dram_tensor("o2_dram", [D, TLOC], BF16, kind="Internal").ap()
    ln_local = nc.dram_tensor("ln_local", [TLOC, D], BF16, kind="Internal").ap()
    o_local = nc.dram_tensor("o_local", [512, 8192], BF16, kind="Internal").ap()
    of_my = nc.dram_tensor("of_my", [D, TLOC], BF16, kind="Internal").ap()
    _spc = "Local" if local_shared else "Shared"
    sh_batch = nc.dram_tensor("sh_batch", [8192, D], BF16, kind="Internal", addr_space=_spc).ap()
    sh_ofull = nc.dram_tensor("sh_ofull", [D, 8192], BF16, kind="Internal", addr_space=_spc).ap()
    sh_flag = nc.dram_tensor("sh_flag", [4, 16], I32, kind="Internal", addr_space=_spc).ap()

    NTI = TLOC // 128
    NTT8 = TLOC // 512
    NPT = 8192 // 128
    NQT = 8192 // 512

    with contextlib.ExitStack() as ctx:
        E = ctx.enter_context
        sq = E(nc.sbuf_tensor("sq", [128, D], F32))
        lnf = E(nc.sbuf_tensor("lnf", [128, D], F32))
        stat = [E(nc.sbuf_tensor(f"stat{i}", [128, 8], F32)) for i in range(2)]
        epsb = E(nc.sbuf_tensor("epsb", [128, 1], F32))
        eps0 = E(nc.sbuf_tensor("eps0", [1, 1], F32))
        xb = [E(nc.sbuf_tensor(f"xb{i}", [128, D], F32)) for i in range(2)]
        lnt = [E(nc.sbuf_tensor(f"lnt{i}", [128, D], BF16)) for i in range(2)]
        tag_s = E(nc.sbuf_tensor("tag_s", [1, 16], I32))
        flag_s = E(nc.sbuf_tensor("flag_s", [1, 16], I32))
        wA = E(nc.sbuf_tensor("wA", [128, NCH * 1536], BF16))
        rfm = E(nc.sbuf_tensor("rfm", [128, NCH * 128], BF16))
        qkvb = E(nc.sbuf_tensor("qkvb", [128, 1536], F32))
        ropeb = E(nc.sbuf_tensor("ropeb", [128, 2 * 512], BF16))
        qrb = E(nc.sbuf_tensor("qrb", [128, 512], BF16))
        krb = E(nc.sbuf_tensor("krb", [128, 512], BF16))
        vbb = E(nc.sbuf_tensor("vbb", [128, 512], BF16))
        kfm2 = E(nc.sbuf_tensor("kfm2", [128, 8192], BF16))
        qfm2 = E(nc.sbuf_tensor("qfm2", [128, 8192], BF16))
        vext = E(nc.sbuf_tensor("vext", [128, 64 * 65], BF16))
        pbf2 = E(nc.sbuf_tensor("pbf2", [128, 2048], BF16))
        den_s = E(nc.sbuf_tensor("den_s", [1, 512], F32))
        bc_s = E(nc.sbuf_tensor("bc_s", [64, 512], F32))
        ofm = E(nc.sbuf_tensor("ofm", [64, 512], BF16))
        rfm2 = E(nc.sbuf_tensor("rfm2", [128, NCH * 512], BF16))
        onesG_s = E(nc.sbuf_tensor("onesG_s", [1, 64], F32))
        obuf = E(nc.sbuf_tensor("obuf", [128, 512], BF16))
        bpr = E(nc.sbuf_tensor("bpr", [128, NCH], F32))
        fx = [E(nc.sbuf_tensor(f"fx{i}", [128, D], F32)) for i in range(2)]
        fo = [E(nc.sbuf_tensor(f"fo{i}", [128, D], BF16)) for i in range(2)]

        psBIG = E(nc.psum_tensor("psBIG", [128, 2048], F32))
        ps_qkv = psBIG[:, 0:1536]
        ps_o = E(nc.psum_tensor("ps_o", [128, 512], F32))
        ps_t = E(nc.psum_tensor("ps_t", [64, 512], F32))

        vch = Sem(ctx, nc, "vch")
        sch = Sem(ctx, nc, "sch")
        ldc = Sem(ctx, nc, "ldc")
        gw = Sem(ctx, nc, "gw")
        ldx = [Sem(ctx, nc, f"ldx{i}") for i in range(2)]
        gst = Sem(ctx, nc, "gst")
        stl2 = [Sem(ctx, nc, f"stl2{i}") for i in range(2)]
        ghl = Sem(ctx, nc, "ghl")
        gex = Sem(ctx, nc, "gex")
        pse1 = Sem(ctx, nc, "pse1")
        pse2 = Sem(ctx, nc, "pse2")
        trq = Sem(ctx, nc, "trq")
        ldrp = Sem(ctx, nc, "ldrp")
        stq2 = Sem(ctx, nc, "stq2")
        stk2 = Sem(ctx, nc, "stk2")
        stv2 = Sem(ctx, nc, "stv2")
        ldkf = Sem(ctx, nc, "ldkf")
        ldqf = Sem(ctx, nc, "ldqf")
        ldvf = Sem(ctx, nc, "ldvf")
        sto2 = Sem(ctx, nc, "sto2")
        ldof = Sem(ctx, nc, "ldof")
        ldf = [Sem(ctx, nc, f"ldf{i}") for i in range(2)]
        trf = [Sem(ctx, nc, f"trf{i}") for i in range(2)]
        stx = [Sem(ctx, nc, f"stx{i}") for i in range(2)]
        mm = Sem(ctx, nc, "mm")
        ve = Sem(ctx, nc, "ve")
        se = Sem(ctx, nc, "se")

        block = E(nc.Block())

        @block.sync
        def _(sync):
            tinc(sync.dma_start(out=tag_s[:], in_=tag), ldc, 16)
            tinc(sync.dma_start(out=bpr[:], in_=bproj), ldc, 16)
            sync.wait_ge(ldc.h, ldc.v)

        @block.vector
        def _(vector):
            vector.memset(epsb[:], 1e-5)
            vector.memset(eps0[:], 0.0)
            vector.memset(onesG_s[:], 1.0)
            tinc(vector.memset(vext[:], 1.0), ve, 1)
        ve_init = ve.v

        if local_shared:
            @block.vector
            def _(vector):
                tinc(vector.memset(lnt[0][:], 0.0), ve, 1)
            ve_z0 = ve.v

            @block.gpsimd
            def _(gpsimd):
                gpsimd.wait_ge(ve.h, ve_z0)
                for zz in range(8192 // 128):
                    tinc(gpsimd.dma_start(out=sh_batch[zz * 128:(zz + 1) * 128, :], in_=lnt[0][:]), ghl, 16)
                for zz in range(D // 128):
                    tinc(gpsimd.dma_start(
                        out=sh_ofull[zz * 128:(zz + 1) * 128, :].rearrange("p (a b) -> p a b", b=1024),
                        in_=lnt[0][:].rearrange("p d -> p 1 d") if False else lnt[0][:][:, None, :]), ghl, 16) if False else None
                for zz in range(D // 128):
                    for seg in range(8):
                        tinc(gpsimd.dma_start(out=sh_ofull[zz * 128:(zz + 1) * 128, seg * 1024:(seg + 1) * 1024],
                                              in_=lnt[0][:]), ghl, 16)
                gpsimd.wait_ge(ghl.h, ghl.v)
                gpsimd.sem_inc(gex.h, 1)
                gex.inc(1)
            gez0 = gex.v
        else:
            gez0 = None

        tile_done = []
        se_tile = []
        for i in range(NTI):
            b = i % 2

            @block.sync
            def _(sync, i=i, b=b):
                if i >= 2:
                    sync.wait_ge(ve.h, tile_done[i - 2])
                    sync.wait_ge(se.h, se_tile[i - 2])
                tinc(sync.dma_start(out=xb[b][:], in_=x6[i * 128:(i + 1) * 128, :]), ldx[b], 16)
            ld_i = ldx[b].v

            @block.scalar
            def _(scalar, i=i, b=b, ld_i=ld_i):
                scalar.wait_ge(ldx[b].h, ld_i)
                if i >= 1:
                    scalar.wait_ge(sch.h, sch.v)
                if i >= 2:
                    scalar.wait_ge(ve.h, tile_done[i - 2])
                ch(scalar.activation(sq[:], xb[b][:], AF.Copy, accum_out=stat[b][:, 0:1]), sch, scalar)
                tinc(scalar.activation(sq[:], xb[b][:], AF.Square, accum_out=stat[b][:, 1:2]), se, 1)
            se_sums = se.v

            @block.vector
            def _(vector, i=i, b=b, se_sums=se_sums):
                vector.wait_ge(se.h, se_sums)
                ch(vector.tensor_scalar(stat[b][:, 2:3], stat[b][:, 0:1], 1.0 / D, None, ALU.mult), vch, vector)
                ch(vector.tensor_tensor(stat[b][:, 3:4], stat[b][:, 2:3], stat[b][:, 2:3], ALU.mult), vch, vector)
                ch(vector.tensor_scalar(stat[b][:, 5:6], stat[b][:, 1:2], 1.0 / D, None, ALU.mult), vch, vector)
                tinc(vector.tensor_tensor(stat[b][:, 3:4], stat[b][:, 5:6], stat[b][:, 3:4], ALU.subtract), ve, 1)
            ve_var = ve.v

            @block.scalar
            def _(scalar, i=i, b=b, ve_var=ve_var):
                scalar.wait_ge(ve.h, ve_var)
                tinc(scalar.activation(stat[b][:, 4:5], stat[b][:, 3:4], AF.Sqrt, bias=epsb[:, 0:1]), se, 1)
            se_tile.append(se.v)
            se_sqrt = se.v

            @block.vector
            def _(vector, i=i, b=b, se_sqrt=se_sqrt):
                vector.wait_ge(se.h, se_sqrt)
                ch(vector.reciprocal(stat[b][:, 4:5], stat[b][:, 4:5]), vch, vector)
                vector.wait_ge(ve.h, ve.v)
                ch(vector.tensor_scalar(lnf[:], xb[b][:], stat[b][:, 2:3], stat[b][:, 4:5],
                                        ALU.subtract, ALU.mult), vch, vector)
                vector.wait_ge(stl2[b].h, stl2[b].v)
                if gez0 is not None and i < 2:
                    vector.wait_ge(gex.h, gez0)
                tinc(vector.tensor_copy(lnt[b][:], lnf[:]), ve, 1)
            tile_done.append(ve.v)
            ve_ln = ve.v

            @block.sync
            def _(sync, i=i, b=b, ve_ln=ve_ln):
                sync.wait_ge(ve.h, ve_ln)
                tinc(sync.dma_start(out=ln_local[i * 128:(i + 1) * 128, :], in_=lnt[b][:]), stl2[b], 16)
        stl2_all = (stl2[0].v, stl2[1].v)

        @block.gpsimd
        def _(gpsimd):
            gpsimd.wait_ge(stl2[0].h, stl2_all[0])
            gpsimd.wait_ge(stl2[1].h, stl2_all[1])
            gpsimd.wait_ge(ldc.h, ldc.v)
            pid = gpsimd.partition_id()
            slot = pid % 2
            sib = slot if self_exchange else 1 - (pid % 2)
            tinc(gpsimd.dma_start(out=sh_batch.rearrange("(s q) d -> s q d", q=TLOC)[slot],
                                  in_=ln_local), ghl, 16)
            gpsimd.wait_ge(ghl.h, ghl.v)
            tinc(gpsimd.dma_start(out=sh_flag[slot, :][None, :], in_=tag_s[:]), ghl, 16)
            gpsimd.wait_ge(ghl.h, ghl.v)
            poll_flag(gpsimd, sh_flag[sib, :][None, :], flag_s, tag_s, pse1)
            gpsimd.sem_inc(gex.h, 1)
            gex.inc(1)
        gex_ln = gex.v

        @block.gpsimd
        def _(gpsimd):
            for cc in range(NCH):
                tinc(gpsimd.dma_start(out=wA[:, cc * 1536:(cc + 1) * 1536],
                                      in_=wqkv[cc * 128:(cc + 1) * 128, :]), gw, 16)
        gw_qkv = gw.v

        for p in range(NPT):
            p0 = p * 128

            @block.scalar
            def _(scalar, p=p, p0=p0):
                scalar.wait_ge(gex.h, gex_ln)
                if p >= 1:
                    scalar.wait_ge(mm.h, mm.v)
                for c in range(NCH):
                    tinc(scalar.dma_start(out=rfm[:, c * 128:(c + 1) * 128],
                                          in_=sh_batch[p0:p0 + 128, c * 128:(c + 1) * 128],
                                          transpose=True), trq, 16)
            tr_p = trq.v

            @block.sync
            def _(sync, p=p, p0=p0):
                if p >= 1:
                    sync.wait_ge(ve.h, ve.v)
                tinc(sync.dma_start(out=ropeb[:, 0:512], in_=cosq[p0:p0 + 128, :] if False else cosq[p0:p0 + 128, :]), ldrp, 16)
                tinc(sync.dma_start(out=ropeb[:, 512:1024], in_=sinq[p0:p0 + 128, :]), ldrp, 16)
            ld_rp = ldrp.v

            @block.tensor
            def _(tensor, p=p, tr_p=tr_p, gw_qkv=gw_qkv):
                tensor.wait_ge(trq.h, tr_p)
                tensor.wait_ge(gw.h, gw_qkv)
                if p >= 1:
                    tensor.wait_ge(se.h, se.v)
                    tensor.wait_ge(ve.h, ve.v)
                for c in range(NCH):
                    tensor.matmul(psBIG[:, 0:512], rfm[:, c * 128:(c + 1) * 128],
                                  wA[:, c * 1536:c * 1536 + 512], start=(c == 0), stop=(c == NCH - 1))
                for c in range(NCH):
                    tensor.matmul(psBIG[:, 512:1024], rfm[:, c * 128:(c + 1) * 128],
                                  wA[:, c * 1536 + 512:c * 1536 + 1024], start=(c == 0), stop=(c == NCH - 1))
                for c in range(NCH):
                    i_ = tensor.matmul(psBIG[:, 1024:1536], rfm[:, c * 128:(c + 1) * 128],
                                       wA[:, c * 1536 + 1024:c * 1536 + 1536], start=(c == 0), stop=(c == NCH - 1))
                tinc(i_, mm, 1)
            mm_qkv = mm.v

            @block.scalar
            def _(scalar, p=p, p0=p0, mm_qkv=mm_qkv):
                scalar.wait_ge(mm.h, mm_qkv)
                scalar.wait_ge(stv2.h, stv2.v)
                tinc(scalar.activation(vbb[:], psBIG[:, 1024:1536], AF.Copy), se, 1)
            se_v = se.v

            @block.sync
            def _(sync, p=p, p0=p0, se_v=se_v):
                sync.wait_ge(se.h, se_v)
                tinc(sync.dma_start(out=v_dram[p0:p0 + 128, :], in_=vbb[:]), stv2, 16)

            def rope_emit(src_off, dstbuf, dst_dram, stsem, cos_t, sin_t, ld_gate):
                @block.vector
                def _(vector, p=p, src_off=src_off, dstbuf=dstbuf, ld_gate=ld_gate):
                    vector.wait_ge(mm.h, mm.v)
                    vector.wait_ge(ldrp.h, ld_gate)
                    vector.wait_ge(stsem.h, stsem.v)
                    first = [slice(h * 64, h * 64 + 32) for h in range(8)]
                    qv = psBIG[:, src_off:src_off + 512].rearrange("p (h k) -> p h k", k=64)
                    cv = ropeb[:, 0:512].rearrange("p (h k) -> p h k", k=64)
                    sv = ropeb[:, 512:1024].rearrange("p (h k) -> p h k", k=64)
                    dv = dstbuf[:].rearrange("p (h k) -> p h k", k=64)
                    ch(vector.tensor_tensor(dv[:, :, 0:32], qv[:, :, 0:32], cv[:, :, 0:32], ALU.mult), vch, vector)
                    ch(vector.tensor_tensor(sq[:, 0:256].rearrange("p (h k) -> p h k", k=32),
                                            qv[:, :, 32:64], sv[:, :, 0:32], ALU.mult), vch, vector)
                    ch(vector.tensor_tensor(dv[:, :, 0:32], dv[:, :, 0:32],
                                            sq[:, 0:256].rearrange("p (h k) -> p h k", k=32), ALU.subtract), vch, vector)
                    ch(vector.tensor_tensor(dv[:, :, 32:64], qv[:, :, 32:64], cv[:, :, 32:64], ALU.mult), vch, vector)
                    ch(vector.tensor_tensor(sq[:, 0:256].rearrange("p (h k) -> p h k", k=32),
                                            qv[:, :, 0:32], sv[:, :, 32:64], ALU.mult), vch, vector)
                    tinc(vector.tensor_tensor(dv[:, :, 32:64], dv[:, :, 32:64],
                                              sq[:, 0:256].rearrange("p (h k) -> p h k", k=32), ALU.add), ve, 1)
                ve_r = ve.v

                @block.sync
                def _(sync, p=p, p0=p0, ve_r=ve_r, dstbuf=dstbuf):
                    sync.wait_ge(ve.h, ve_r)
                    tinc(sync.dma_start(out=dst_dram[p0:p0 + 128, :], in_=dstbuf[:]), stsem, 16)

            rope_emit(0, qrb, q_dram, stq2, cosq, sinq, ld_rp)

            @block.sync
            def _(sync, p=p, p0=p0):
                sync.wait_ge(ve.h, ve.v)
                tinc(sync.dma_start(out=ropeb[:, 0:512], in_=cosk[p0:p0 + 128, :]), ldrp, 16)
                tinc(sync.dma_start(out=ropeb[:, 512:1024], in_=sink[p0:p0 + 128, :]), ldrp, 16)
            ld_rp2 = ldrp.v

            rope_emit(0 + 512, krb, k_dram, stk2, cosk, sink, ld_rp2)
        stq_all = stq2.v
        stk_all = stk2.v
        stv_all = stv2.v

        for h in range(8):
            hp, hh = h // 2, h % 2
            if hh == 0:
                @block.scalar
                def _(scalar, h=h, hp=hp):
                    scalar.wait_ge(stq2.h, stq_all)
                    scalar.wait_ge(stk2.h, stk_all)
                    if hp >= 1:
                        scalar.wait_ge(mm.h, mm.v)
                    tinc(scalar.dma_start(out=kfm2[:], in_=k_dram[:, hp * 128:(hp + 1) * 128],
                                          transpose=True), ldkf, 16)
                    tinc(scalar.dma_start(out=qfm2[:], in_=q_dram[:, hp * 128:(hp + 1) * 128],
                                          transpose=True), ldqf, 16)
            ld_kq = (ldkf.v, ldqf.v)
            kfm = kfm2[hh * 64:(hh + 1) * 64, :]
            qfm = qfm2[hh * 64:(hh + 1) * 64, :]

            @block.sync
            def _(sync, h=h):
                sync.wait_ge(stv2.h, stv_all)
                if h >= 1:
                    sync.wait_ge(mm.h, mm.v)
                tinc(sync.dma_start(out=vext[:].rearrange("p (i e) -> p i e", e=65)[:, :, 0:64],
                                    in_=v_dram.rearrange("(i p) d -> p i d", p=128)[:, :, h * 64:(h + 1) * 64]),
                     ldvf, 16)
            ld_v = ldvf.v

            for qt in range(NQT):
                q0 = qt * 512
                exp_hist = []
                omm_hist = []

                for j in range(32):
                    jb = j % 2
                    sc0 = jb * 1024

                    @block.tensor
                    def _(tensor, h=h, qt=qt, q0=q0, j=j, sc0=sc0, ld_kq=ld_kq, ld_v=ld_v):
                        if j == 0 and qt == 0:
                            tensor.wait_ge(ldkf.h, ld_kq[0])
                            tensor.wait_ge(ldqf.h, ld_kq[1])
                            tensor.wait_ge(ldvf.h, ld_v)
                            tensor.wait_ge(ve.h, ve_init)
                        if len(exp_hist) >= 2:
                            tensor.wait_ge(se.h, exp_hist[-2])
                        elif not (h == 0 and qt == 0) and len(exp_hist) == 0:
                            tensor.wait_ge(se.h, se.v)
                        tensor.matmul(psBIG[:, sc0:sc0 + 512], kfm[:, (2 * j) * 128:(2 * j + 1) * 128],
                                      qfm[:, q0:q0 + 512], start=True, stop=True)
                        tinc(tensor.matmul(psBIG[:, sc0 + 512:sc0 + 1024], kfm[:, (2 * j + 1) * 128:(2 * j + 2) * 128],
                                           qfm[:, q0:q0 + 512], start=True, stop=True), mm, 1)
                    mm_sc = mm.v

                    @block.scalar
                    def _(scalar, h=h, qt=qt, j=j, jb=jb, sc0=sc0, mm_sc=mm_sc):
                        scalar.wait_ge(mm.h, mm_sc)
                        if len(omm_hist) >= 2:
                            scalar.wait_ge(mm.h, omm_hist[-2])
                        tinc(scalar.activation(pbf2[:, jb * 1024:(jb + 1) * 1024],
                                               psBIG[:, sc0:sc0 + 1024], AF.Exp), se, 1)
                    exp_hist.append(se.v)
                    se_e = se.v

                    @block.tensor
                    def _(tensor, h=h, qt=qt, j=j, jb=jb, se_e=se_e):
                        tensor.wait_ge(se.h, se_e)
                        if j == 0 and not (h == 0 and qt == 0):
                            tensor.wait_ge(ve.h, ve.v)
                        tensor.matmul(ps_o[0:65, :], vext[:, (2 * j) * 65:(2 * j + 1) * 65],
                                      pbf2[:, jb * 1024:jb * 1024 + 512], start=(j == 0), stop=False)
                        tinc(tensor.matmul(ps_o[0:65, :], vext[:, (2 * j + 1) * 65:(2 * j + 2) * 65],
                                           pbf2[:, jb * 1024 + 512:jb * 1024 + 1024],
                                           start=False, stop=(j == 31)), mm, 1)
                    omm_hist.append(mm.v)
                mm_oend = mm.v

                @block.vector
                def _(vector, h=h, qt=qt, mm_oend=mm_oend):
                    vector.wait_ge(mm.h, mm_oend)
                    ch(vector.tensor_scalar(den_s[:], ps_o[64:65, :], eps0[0:1, 0:1], None, ALU.add), vch, vector)
                    tinc(vector.reciprocal(den_s[:], den_s[:]), ve, 1)
                ve_dn = ve.v

                @block.tensor
                def _(tensor, h=h, qt=qt, ve_dn=ve_dn):
                    tensor.wait_ge(ve.h, ve_dn)
                    tensor.wait_ge(ve.h, ve.v)
                    tinc(tensor.matmul(ps_t[:], vext[0:1, 0:64] if False else onesG_s[0:1, :], den_s[:],
                                       start=True, stop=True), mm, 1)
                mm_bc = mm.v

                @block.vector
                def _(vector, h=h, qt=qt, mm_bc=mm_bc):
                    vector.wait_ge(mm.h, mm_bc)
                    ch(vector.tensor_copy(bc_s[:], ps_t[:]), vch, vector)
                    vector.wait_ge(gst.h, gst.v)
                    tinc(vector.tensor_tensor(ofm[:], ps_o[0:64, :], bc_s[:], ALU.mult), ve, 1)
                ve_of = ve.v

                @block.sync
                def _(sync, h=h, qt=qt, q0=q0, ve_of=ve_of):
                    sync.wait_ge(ve.h, ve_of)
                    tinc(sync.dma_start(out=o_local[h * 64:(h + 1) * 64, q0:q0 + 512],
                                        in_=ofm[:]), gst, 16)
        gst_o = gst.v

        @block.gpsimd
        def _(gpsimd):
            gpsimd.wait_ge(gst.h, gst_o)
            pid = gpsimd.partition_id()
            slot = pid % 2
            sib = slot if self_exchange else 1 - (pid % 2)
            tinc(gpsimd.dma_start(out=sh_ofull.rearrange("(s p) q -> s p q", p=512)[slot],
                                  in_=o_local), ghl, 16)
            gpsimd.wait_ge(ghl.h, ghl.v)
            tinc(gpsimd.dma_start(out=sh_flag[2 + slot, :][None, :], in_=tag_s[:]), ghl, 16)
            gpsimd.wait_ge(ghl.h, ghl.v)
            poll_flag(gpsimd, sh_flag[2 + sib, :][None, :], flag_s, tag_s, pse2)
            tinc(gpsimd.dma_start(out=of_my,
                                  in_=sh_ofull.rearrange("p (s q) -> p s q", q=TLOC)[:, slot]), ghl, 16)
            gpsimd.wait_ge(ghl.h, ghl.v)
            gpsimd.sem_inc(gex.h, 1)
            gex.inc(1)
        gex_o = gex.v

        @block.gpsimd
        def _(gpsimd):
            gpsimd.wait_ge(mm.h, mm.v)
            for cc in range(NCH):
                tinc(gpsimd.dma_start(out=wA[:, cc * 1024:(cc + 1) * 1024],
                                      in_=wproj[cc * 128:(cc + 1) * 128, :]), gw, 16)
        gw_p = gw.v

        for t in range(NTT8):
            t0 = t * 512

            @block.sync
            def _(sync, t=t, t0=t0):
                sync.wait_ge(gex.h, gex_o)
                if t >= 1:
                    sync.wait_ge(mm.h, mm.v)
                for c in range(NCH):
                    tinc(sync.dma_start(out=rfm2[:, c * 512:(c + 1) * 512],
                                        in_=of_my[c * 128:(c + 1) * 128, t0:t0 + 512]), ldof, 16)
            ld_o = ldof.v

            for oc in range(NCH):
                @block.tensor
                def _(tensor, t=t, oc=oc, ld_o=ld_o, gw_p=gw_p):
                    if oc == 0:
                        tensor.wait_ge(ldof.h, ld_o)
                        tensor.wait_ge(gw.h, gw_p)
                    if not (t == 0 and oc < 2):
                        tensor.wait_ge(se.h, se.v - 1)
                    for c in range(NCH):
                        i_ = tensor.matmul(psBIG[:, (oc % 2) * 512:(oc % 2) * 512 + 512],
                                           wA[:, c * 1024 + oc * 128: c * 1024 + (oc + 1) * 128],
                                           rfm2[:, c * 512:(c + 1) * 512], start=(c == 0), stop=(c == NCH - 1))
                    tinc(i_, mm, 1)
                mm_pj = mm.v

                @block.scalar
                def _(scalar, t=t, oc=oc, mm_pj=mm_pj):
                    scalar.wait_ge(mm.h, mm_pj)
                    scalar.wait_ge(sto2.h, sto2.v)
                    tinc(scalar.activation(obuf[:], psBIG[:, (oc % 2) * 512:(oc % 2) * 512 + 512],
                                           AF.Identity, bias=bpr[:, oc:oc + 1]), se, 1)
                se_pj = se.v

                @block.sync
                def _(sync, t=t, t0=t0, oc=oc, se_pj=se_pj):
                    sync.wait_ge(se.h, se_pj)
                    tinc(sync.dma_start(out=o2_dram[oc * 128:(oc + 1) * 128, t0:t0 + 512], in_=obuf[:]), sto2, 16)
        sto2_all = sto2.v

        for i in range(NTI):
            b = i % 2

            @block.sync
            def _(sync, i=i, b=b):
                sync.wait_ge(sto2.h, sto2_all)
                sync.wait_ge(stx[b].h, stx[b].v)
                tinc(sync.dma_start(out=fx[b][:], in_=x6[i * 128:(i + 1) * 128, :]), ldf[b], 16)
            ld_f = ldf[b].v

            @block.scalar
            def _(scalar, i=i, b=b):
                scalar.wait_ge(sto2.h, sto2_all)
                scalar.wait_ge(stx[b].h, stx[b].v)
                scalar.wait_ge(trf[b].h, trf[b].v)
                tinc(scalar.dma_start(out=fo[b][:], in_=o2_dram[:, i * 128:(i + 1) * 128],
                                      transpose=True), trf[b], 16)
            tr_f = trf[b].v

            @block.vector
            def _(vector, i=i, b=b, ld_f=ld_f, tr_f=tr_f):
                vector.wait_ge(ldf[b].h, ld_f)
                vector.wait_ge(trf[b].h, tr_f)
                tinc(vector.tensor_tensor(fx[b][:], fx[b][:], fo[b][:], ALU.add), ve, 1)
            ve_f = ve.v

            @block.sync
            def _(sync, i=i, b=b, ve_f=ve_f):
                sync.wait_ge(ve.h, ve_f)
                tinc(sync.dma_start(out=x7[i * 128:(i + 1) * 128, :], in_=fx[b][:]), stx[b], 16)

        @block.sync
        def _(sync):
            sync.wait_ge(stx[0].h, stx[0].v)
            sync.wait_ge(stx[1].h, stx[1].v)

    return nc



def _consts():
    ident = np.eye(128, dtype=np.float32)
    onesc = np.ones((128, 1), dtype=np.float32)
    onesr = np.ones((1, 128), dtype=np.float32)
    return ident, onesc, onesr


def _tag():
    return np.full((1, 16), np.random.randint(1, 2**31 - 1, dtype=np.int32), dtype=np.int32)


def _perm_x_for_core(x4, c):
    b, par = c // 2, c % 2
    border = [b] + [bb for bb in range(4) if bb != b]
    pos = np.concatenate([np.arange(par * 4096, (par + 1) * 4096),
                          np.arange((1 - par) * 4096, (2 - par) * 4096) if par == 0
                          else np.arange(0, 4096)])
    xr = x4[border][:, pos, :]
    return np.ascontiguousarray(xr.reshape(NTOK, D))


def run_A(x4):
    ident, onesc, onesr = _consts()
    nc = build_A(detect=False)
    in_maps = []
    for c in range(NCORE):
        in_maps.append({
            "x_all": _perm_x_for_core(x4, c),
            "ident": ident, "onesc": onesc,
        })
    res = run_bass_kernel_spmd(nc, in_maps, core_ids=list(range(NCORE)))
    return [r["x1_tm"] for r in res.results]


def _b1_consts():
    iota16 = np.tile(np.arange(16, dtype=np.float32), (128, 1))
    return iota16


def _b2_consts():
    iota16 = _b1_consts()
    eoffC = np.tile((np.arange(16, dtype=np.float32) * CL), (128, 1))
    iotacol = np.arange(128, dtype=np.float32).reshape(128, 1)
    pfill = np.zeros((TOTSLOT, 4), dtype=np.float32)
    pfill[:, 1] = 2 * TLOC
    return iota16, eoffC, iotacol, pfill


def b1_inmap(c, x1_c, inputs):
    ident, onesc, onesr = _consts()
    return {
        "x1": x1_c,
        "rw": np.ascontiguousarray(inputs["L0_ffn_router_w"], dtype=np.float32),
        "rb": np.ascontiguousarray(inputs["L0_ffn_router_b"], dtype=np.float32).reshape(1, 16),
        "onesc": onesc,
        "iota16": _b1_consts(),
    }


def b2_inmap(c, x1_c, ln_c, route_c, counts_all, inputs):
    ident, onesc, onesr = _consts()
    iota16, eoffC, iotacol, pfill = _b2_consts()
    cm = np.zeros((8, 1), dtype=np.float32)
    cm[:c] = 1.0
    return {
        "x1": x1_c, "ln_tm": ln_c, "route": route_c,
        "counts_all": counts_all, "coremask": cm,
        "ident": ident, "onesc": onesc, "iota16": iota16,
        "eoffC": eoffC, "iotacol": iotacol, "pfill": pfill,
        "swg": np.ascontiguousarray(inputs["L0_ffn_sw_gate"], dtype=np.float32),
        "swu": np.ascontiguousarray(inputs["L0_ffn_sw_up"], dtype=np.float32),
        "swd": np.ascontiguousarray(inputs["L0_ffn_sw_down"], dtype=np.float32),
        "wg": np.ascontiguousarray(inputs["L0_ffn_w_gate"], dtype=np.float32),
        "wu": np.ascontiguousarray(inputs["L0_ffn_w_up"], dtype=np.float32),
        "wdn": np.ascontiguousarray(inputs["L0_ffn_w_down"], dtype=np.float32),
    }


def c_inmap(c, x2_c, inputs, tag):
    def T(a):
        return np.ascontiguousarray(np.asarray(a, dtype=np.float32).T)
    def chunkcol(a):
        return np.ascontiguousarray(np.asarray(a, dtype=np.float32).reshape(8, 128).T)
    convw = np.asarray(inputs["L0_attn_b_conv_w"], dtype=np.float32).reshape(1024, 4)
    convw = np.ascontiguousarray(convw.reshape(8, 128, 4).transpose(1, 0, 2).reshape(128, 32))
    gb = np.ascontiguousarray(np.asarray(inputs["L0_attn_b_gates_b"], dtype=np.float32).reshape(16, 128).T)
    hm = np.full((128, 1), float(c % 2), dtype=np.float32)
    return {
        "x2": x2_c,
        "inwT": T(inputs["L0_attn_b_input_w"]),
        "gwT": T(inputs["L0_attn_b_gates_w"]),
        "outT": T(inputs["L0_attn_b_output_w"]),
        "growT": T(inputs["L0_attn_b_grow_w"]),
        "shrT": T(inputs["L0_attn_b_shrink_w"]),
        "convw": convw,
        "convb": chunkcol(inputs["L0_attn_b_conv_b"]),
        "fbase": chunkcol(inputs["L0_attn_b_forget_base"]),
        "gam1": chunkcol(inputs["L0_attn_b_hawk_norm_gamma"]),
        "gam2": chunkcol(inputs["L0_attn_b_gmlp_norm_gamma"]),
        "gb": gb,
        "hmask": hm,
        "tag": tag,
    }


def e_inmap(c, x4_c, inputs, tag):
    hm = np.full((128, 1), float(c % 2), dtype=np.float32)
    return {
        "x4": x4_c,
        "wq": np.ascontiguousarray(inputs["L1_attn_a_Wq"], dtype=np.float32),
        "wk": np.ascontiguousarray(inputs["L1_attn_a_Wk"], dtype=np.float32),
        "wv": np.ascontiguousarray(inputs["L1_attn_a_Wv"], dtype=np.float32),
        "wo": np.ascontiguousarray(inputs["L1_attn_a_Wo"], dtype=np.float32),
        "wf": np.ascontiguousarray(inputs["L1_attn_a_W_feat"], dtype=np.float32),
        "trimask": np.triu(np.ones((128, 128), dtype=np.float32)),
        "hmask": hm,
        "tag": tag,
    }


def f_inmap(c, x_c, inputs):
    ident, onesc, onesr = _consts()
    def T(a):
        return np.ascontiguousarray(np.asarray(a, dtype=np.float32).T)
    w1T = np.ascontiguousarray(np.asarray(inputs["L1_ffn_W1"], dtype=np.float32).transpose(0, 2, 1))
    w2T = np.ascontiguousarray(np.asarray(inputs["L1_ffn_W2"], dtype=np.float32).transpose(0, 2, 1))
    return {
        "xin": x_c,
        "wgT": T(inputs["L1_ffn_Wg"]),
        "bias": np.asarray(inputs["L1_ffn_bias_corr"], dtype=np.float32).reshape(1, 16),
        "w1T": w1T, "w2T": w2T,
        "ws1T": T(inputs["L1_ffn_Ws1"]),
        "ws2T": T(inputs["L1_ffn_Ws2"]),
        "onesc": onesc, "onesr": onesr,
        "iota16": _b1_consts(),
        "ident": ident,
    }


def g_inmap(c, x6_c, inputs, tag):
    par = c % 2
    h0 = par * 8
    wqkv = np.asarray(inputs["L1_attn_b_w_qkv"], dtype=np.float32)
    secs = [wqkv[s * 1024 + h0 * 64: s * 1024 + (h0 + 8) * 64, :] for s in range(3)]
    wqkv_my = np.ascontiguousarray(np.concatenate(secs, axis=0).T)
    inv_freq = 1.0 / (10000.0 ** (np.arange(0, 64, 2, dtype=np.float32) / 64.0))
    t = np.arange(8192, dtype=np.float32)
    freqs = t[:, None] * inv_freq[None, :]
    emb = np.concatenate([freqs, freqs], axis=1)
    cos = np.cos(emb); sin = np.sin(emb)
    cos8 = np.tile(cos, (1, 8)); sin8 = np.tile(sin, (1, 8))
    scale = 1.0 / 8.0
    bp = np.ascontiguousarray(np.asarray(inputs["L1_attn_b_b_proj"], dtype=np.float32).reshape(8, 128).T)
    return {
        "x6": x6_c,
        "wqkv": wqkv_my,
        "wproj": np.ascontiguousarray(np.asarray(inputs["L1_attn_b_w_proj"], dtype=np.float32).T),
        "cosq": (cos8 * scale).astype(BF16NP),
        "sinq": (sin8 * scale).astype(BF16NP),
        "cosk": cos8.astype(BF16NP),
        "sink": sin8.astype(BF16NP),
        "bproj": bp,
        "tag": tag,
    }


def kernel(**inputs):
    x4 = np.ascontiguousarray(inputs["L0_attn_a_x"], dtype=np.float32)
    x1s = run_A(x4)

    nc1 = build_B1(detect=False)
    res1 = run_bass_kernel_spmd(nc1, [b1_inmap(c, x1s[c], inputs) for c in range(NCORE)],
                                core_ids=list(range(NCORE)))
    counts_all = np.concatenate([r["counts"] for r in res1.results], axis=0)

    nc2 = build_B2(detect=False)
    res2 = run_bass_kernel_spmd(
        nc2,
        [b2_inmap(c, x1s[c], res1.results[c]["ln_tm"], res1.results[c]["route"], counts_all, inputs)
         for c in range(NCORE)],
        core_ids=list(range(NCORE)))
    x2s = [r["x2"] for r in res2.results]

    tag = _tag()
    ncC = build_C(detect=False)
    resC = run_bass_kernel_spmd(ncC, [c_inmap(c, x2s[c], inputs, tag) for c in range(NCORE)],
                                core_ids=list(range(NCORE)))
    x3s = [r["x3"] for r in resC.results]

    res1b = run_bass_kernel_spmd(nc1, [b1_inmap(c, x3s[c], inputs) for c in range(NCORE)],
                                 core_ids=list(range(NCORE)))
    counts_all2 = np.concatenate([r["counts"] for r in res1b.results], axis=0)
    res2b = run_bass_kernel_spmd(
        nc2,
        [b2_inmap(c, x3s[c], res1b.results[c]["ln_tm"], res1b.results[c]["route"], counts_all2, inputs)
         for c in range(NCORE)],
        core_ids=list(range(NCORE)))
    x4s = [r["x2"] for r in res2b.results]

    ncE = build_E(detect=False)
    resE = run_bass_kernel_spmd(ncE, [e_inmap(c, x4s[c], inputs, tag) for c in range(NCORE)],
                                core_ids=list(range(NCORE)))
    x5s = [r["x5"] for r in resE.results]

    ncF = build_F(detect=False)
    resF = run_bass_kernel_spmd(ncF, [f_inmap(c, x5s[c], inputs) for c in range(NCORE)],
                                core_ids=list(range(NCORE)))
    x6s = [r["xout"] for r in resF.results]

    ncG = build_G(detect=False)
    resG = run_bass_kernel_spmd(ncG, [g_inmap(c, x6s[c], inputs, tag) for c in range(NCORE)],
                                core_ids=list(range(NCORE)))
    x7s = [r["x7"] for r in resG.results]

    resF2 = run_bass_kernel_spmd(ncF, [f_inmap(c, x7s[c], inputs) for c in range(NCORE)],
                                 core_ids=list(range(NCORE)))
    out = np.concatenate([r["xout"] for r in resF2.results], axis=0)
    return out.reshape(4, 8192, D)
